# version 06
# baseline (speedup 1.0000x reference)
import numpy as np
import ml_dtypes
from contextlib import ExitStack
import concourse.bass as bass
import concourse.mybir as mybir
from concourse.bass_utils import run_bass_kernel_spmd

F32 = mybir.dt.float32
BF16 = mybir.dt.bfloat16
AF = mybir.ActivationFunctionType
ALU = mybir.AluOpType
AX = mybir.AxisListType

NEG = -1.0e30
B_LOC = 2
S = 2048
D = 1024
NT = 16
HID = 2730
NHC = 22

PASSES = "full"
FORCE_SYNC = False

ENGINES = ("sync", "tensor", "vector", "scalar", "gpsimd")
DMA_POOL = {"sync": 32, "tensor": 2, "vector": 2, "scalar": 8, "gpsimd": 24}


class Prog:
    def __init__(self):
        self.ops = {e: [] for e in ENGINES}
        self.writer = {}
        self.readers = {}
        self.fence_deps = ()

    def op(self, eng, fn, reads=(), writes=(), dma=False, tag="", accum=False):
        deps = set(self.fence_deps)
        for k in reads:
            for w in self.writer.get(k, {}).values():
                deps.add(w)
        for k in writes:
            for w in self.writer.get(k, {}).values():
                deps.add(w)
            for r in self.readers.get(k, ()):
                deps.add(r)
        idx = len(self.ops[eng])
        me = (eng, idx)
        deps.discard(me)
        self.ops[eng].append([fn, dma, deps, False, tag, accum])
        for k in writes:
            self.writer.setdefault(k, {})[eng] = me
            self.readers[k] = []
        for k in reads:
            self.readers.setdefault(k, []).append(me)

    def dma(self, queue, out, in_, reads=(), writes=(), tag="", partial=False,
            **kw):
        self.op(queue, (lambda q, o=out, i=in_, kw=kw: q.dma_start(o, i, **kw)),
                reads=reads, writes=writes, dma=True, tag=tag)

    def fence(self):
        self.fence_deps = tuple(
            (e, len(self.ops[e]) - 1) for e in ENGINES if self.ops[e])

    def finalize(self):
        need = {e: set() for e in ENGINES}
        for e in ENGINES:
            for i, o in enumerate(self.ops[e]):
                for (de, di) in o[2]:
                    dop = self.ops[de][di]
                    if de != e or dop[1] or dop[5] or FORCE_SYNC:
                        need[de].add(di)
        self.mnum = {}
        self.dma_final = {}
        for e in ENGINES:
            c = 0
            d = 0
            K = DMA_POOL[e]
            for i, o in enumerate(self.ops[e]):
                if o[1]:
                    s = d % K
                    r = d // K + 1
                    self.mnum[(e, i)] = ("s", s, r)
                    self.dma_final[(e, s)] = r
                    d += 1
                elif i in need[e]:
                    c += 1
                    o[3] = True
                    self.mnum[(e, i)] = ("c", c)

    def emit_engine(self, q, e, sems):
        wm = {}
        for i, (fn, dma, deps, ms, tag, accum) in enumerate(self.ops[e]):
            req = {}
            for (de, di) in deps:
                dop = self.ops[de][di]
                if de == e and not dop[1] and not dop[5] and not FORCE_SYNC:
                    continue
                mn = self.mnum[(de, di)]
                if mn[0] == "c":
                    kk = (de, "c")
                    if mn[1] > req.get(kk, 0):
                        req[kk] = mn[1]
                else:
                    kk = (de, "s", mn[1])
                    if mn[2] > req.get(kk, 0):
                        req[kk] = mn[2]
            for kk in sorted(req):
                val = req[kk]
                if wm.get(kk, 0) >= val:
                    continue
                if kk[1] == "c":
                    q.wait_ge(sems[kk], val)
                else:
                    q.wait_ge(sems[kk], val * 16)
                wm[kk] = val
            if dma:
                kind, s, r = self.mnum[(e, i)]
                kk = (e, "s", s)
                if r > 1 and wm.get(kk, 0) < r - 1:
                    q.wait_ge(sems[kk], 16 * (r - 1))
                    wm[kk] = r - 1
            inst = fn(q)
            if dma:
                kind, s, r = self.mnum[(e, i)]
                inst.then_inc(sems[(e, "s", s)], 16)
            elif ms:
                inst.then_inc(sems[(e, "c")], 1)
        if e == "sync":
            for (qe, s), r in sorted(self.dma_final.items()):
                kk = (qe, "s", s)
                if wm.get(kk, 0) < r:
                    q.wait_ge(sems[kk], 16 * r)


def host_consts():
    c = {}
    c["ident"] = np.eye(128, dtype=ml_dtypes.bfloat16)
    cm = np.zeros((4, 128, 512), dtype=np.float32)
    for d_ in range(4):
        i = np.arange(128)[:, None]
        j = np.arange(512)[None, :]
        cm[d_] = np.where(j <= d_ * 128 + i, 0.0, NEG)
    c["cmask"] = cm.astype(ml_dtypes.bfloat16).reshape(4 * 128, 512)
    fm = np.zeros((2, 4, 128, 1024), dtype=np.float32)
    for first in range(2):
        for jt in range(4):
            pp = (128 * jt + np.arange(128))[:, None]
            m = np.arange(1024)[None, :]
            ok = (m >= pp) & (m <= pp + 512)
            if first:
                ok &= m >= 512
            fm[first, jt] = np.where(ok, 0.0, NEG)
    c["fmask"] = fm.astype(ml_dtypes.bfloat16).reshape(8 * 128, 1024)
    hd = 64
    pos = np.arange(S, dtype=np.float64)
    inv = 1.0 / (10000.0 ** (np.arange(0, hd, 2, dtype=np.float64) / hd))
    fr = np.concatenate([np.outer(pos, inv), np.outer(pos, inv)], -1)
    cosb, sinb = np.cos(fr), np.sin(fr)
    Bt = sinb.copy()
    Bt[:, 0::2] = -sinb[:, 0::2]
    c["ropeA"] = cosb.astype(np.float32)
    c["ropeB"] = Bt.astype(np.float32)
    half = np.arange(0, hd, 2, dtype=np.float64)
    invf = 1.0 / (10000.0 ** (half / hd))
    fr2 = np.outer(pos, invf)
    fr2 = np.concatenate([fr2, fr2], -1)
    base = (half + 0.4 * hd) / (1.4 * hd)
    power = (pos - S // 2) / 256.0
    scale = base[None, :] ** power[:, None]
    scale = np.concatenate([scale, scale], -1)
    cosx, sinx = np.cos(fr2), np.sin(fr2)
    sgn = np.ones(hd)
    sgn[:32] = -1.0
    c["xqA"] = (cosx * scale * 8.0).astype(np.float32)
    c["xqB"] = (sgn[None, :] * sinx * scale * 8.0).astype(np.float32)
    c["xkA"] = (cosx / scale).astype(np.float32)
    c["xkB"] = (sgn[None, :] * sinx / scale).astype(np.float32)
    c["zrow"] = np.zeros((1, 1024), dtype=ml_dtypes.bfloat16)
    c["eps"] = np.array([1e-5, 1e-6], dtype=np.float32)
    return c


def _is_small(out):
    try:
        return out.free_size() <= 64
    except Exception:
        return False


class Ctx:
    def __init__(self, nc, prog):
        self.nc = nc
        self.p = prog
        self.ident = None
        self.ps = []
        self.psT = []
        self.pro_steps = []

    def step_prologue(self, n=1):
        for _ in range(n):
            if self.pro_steps:
                self.pro_steps.pop(0)()

    def mm(self, ps_ap, lhsT_ap, rhs_ap, reads, wkey, start, stop, skip=False):
        self.p.op("tensor",
                  lambda q, o=ps_ap, a=lhsT_ap, b=rhs_ap, s1=start, s2=stop,
                  sk=skip: q.matmul(o, a, b, start=s1, stop=s2,
                                    skip_group_check=sk),
                  reads=reads, writes=(wkey,), tag="mm")

    def tr(self, ps_ap, in_ap, reads, wkey):
        self.p.op("tensor",
                  lambda q, o=ps_ap, i=in_ap, idn=self.ident:
                  q.transpose(o, i, idn[:]),
                  reads=reads + ("ident",), writes=(wkey,), tag="tr")

    def act(self, out, in_, func, reads, writes, bias=None, scale=None,
            accum=None):
        def f(q, o=out, i=in_, fu=func, b=bias, s=scale, a=accum):
            kw = {}
            if b is not None:
                kw["bias"] = b
            if s is not None:
                kw["scale"] = s
            if a is not None:
                kw["accum_out"] = a
            return q.activation(o, i, fu, **kw)
        self.p.op("scalar", f, reads=reads, writes=writes, tag="act",
                  accum=(accum is not None) or _is_small(out))

    def tt(self, out, a, b, op, reads, writes, eng="vector"):
        self.p.op(eng,
                  lambda q, o=out, x=a, y=b, p_=op: q.tensor_tensor(o, x, y, p_),
                  reads=reads, writes=writes, tag="tt", accum=_is_small(out))

    def ts(self, out, a, s1, op0, reads, writes, s2=None, op1=None, accum=None,
           eng="vector"):
        def f(q, o=out, x=a, s1=s1, s2=s2, o0=op0, o1=op1, ac=accum):
            kw = {}
            if ac is not None:
                kw["accum_out"] = ac
            if o1 is None:
                return q.tensor_scalar(o, x, s1, s2, o0, **kw)
            return q.tensor_scalar(o, x, s1, s2, o0, o1, **kw)
        self.p.op(eng, f, reads=reads, writes=writes, tag="ts",
                  accum=(accum is not None) or _is_small(out))

    def ttr(self, out, a, b, op0, op1, accum, reads, writes, scale=1.0,
            seed=0.0):
        self.p.op("vector",
                  lambda q, o=out, x=a, y=b, s=scale, sd=seed, o0=op0, o1=op1,
                  ac=accum: q.tensor_tensor_reduce(o, x, y, s, sd, o0, o1, ac),
                  reads=reads, writes=writes, tag="ttr")

    def stt(self, out, a, s, b, op0, op1, reads, writes, eng="vector"):
        self.p.op(eng,
                  lambda q, o=out, x=a, sc=s, y=b, o0=op0, o1=op1:
                  q.scalar_tensor_tensor(o, x, sc, y, o0, o1),
                  reads=reads, writes=writes, tag="stt", accum=_is_small(out))

    def cp(self, out, in_, reads, writes, eng="vector"):
        if eng == "scalar":
            self.p.op(eng, lambda q, o=out, i=in_: q.copy(o, i),
                      reads=reads, writes=writes, tag="cp",
                      accum=_is_small(out))
        else:
            self.p.op(eng, lambda q, o=out, i=in_: q.tensor_copy(o, i),
                      reads=reads, writes=writes, tag="cp",
                      accum=_is_small(out))

    def red(self, out, in_, op, reads, writes, negate=False):
        self.p.op("vector",
                  lambda q, o=out, i=in_, p_=op, n=negate:
                  q.tensor_reduce(o, i, axis=AX.X, op=p_, negate=n),
                  reads=reads, writes=writes, tag="red")

    def recip(self, out, in_, reads, writes):
        self.p.op("vector", lambda q, o=out, i=in_: q.reciprocal(o, i),
                  reads=reads, writes=writes, tag="recip",
                  accum=_is_small(out))

    def rsqrt(self, out, in_, reads, writes, scale=None, bias=None, minv=None):
        self.act(out, in_, AF.Ln, reads, writes, bias=bias, scale=scale)
        self.act(out, out, AF.Exp, writes, writes, scale=-0.5)
        if minv is not None:
            self.ts(out, out, minv, ALU.min, writes, writes)

    def bcast(self, dst, src_t, n, writes, offset=0, queue="sync"):
        ap = bass.AP(src_t, offset, [[0, 128], [1, n]])
        self.p.dma(queue, dst, ap, writes=writes)

    def ln_tile(self, x_ap, out_ap, kx, kout, scr, scr2, stats, sl, kst):
        st = lambda i: stats[:, sl + i: sl + i + 1]
        self.act(scr[:], x_ap, AF.Square, reads=(kx,), writes=(kst + "sA", kst),
                 accum=st(0))
        self.ts(scr2[:], x_ap, 0.0, ALU.add, reads=(kx,), writes=(kst + "sB", kst),
                s2=0.0, op1=ALU.add, accum=st(1))
        self.ts(st(2), st(1), 1.0 / D, ALU.mult, reads=(kst,), writes=(kst,))
        self.tt(st(5), st(2), st(2), ALU.mult, reads=(kst,), writes=(kst,))
        self.stt(st(3), st(0), 1.0 / D, st(5), ALU.mult, ALU.subtract,
                 reads=(kst,), writes=(kst,))
        self.rsqrt(st(4), st(3), reads=(kst, "epsc"), writes=(kst,),
                   bias=self.eps5)
        self.ts(out_ap, x_ap, st(2), ALU.subtract, reads=(kx, kst),
                writes=(kout,), s2=st(4), op1=ALU.mult)


def dram_row_ap(t, g, width=D):
    return bass.AP(t, g * 128 * width, [[width, 128], [1, width]])


def grow(g):
    return (g // NT) * (S + 128) + 128 + (g % NT) * 128


def gr_ap(t, g, width=D):
    return bass.AP(t, grow(g) * width, [[width, 128], [1, width]])


def gr_shift_ap(t, g):
    return bass.AP(t, (grow(g) - 1) * D + 512, [[D, 128], [1, 512]])


def build(passes=PASSES):
    nc = bass.Bass("TRN2", target_bir_lowering=False)
    prog = Prog()
    ctx = Ctx(nc, prog)
    p = prog

    dt_in = {}

    def din(name, shape, dtype=F32):
        dt_in[name] = nc.dram_tensor(name, list(shape), dtype,
                                     kind="ExternalInput")
        return dt_in[name]

    x_in = din("x", [B_LOC * S, D])
    din("aw_normw", [D]); din("aw_q", [D, D]); din("aw_k", [D, 512])
    din("aw_v", [D, 512]); din("aw_o", [D, D]); din("aw_gate", [1])
    din("aw_lt", [1])
    din("bw_q", [D, D]); din("bw_k", [D, D]); din("bw_v", [D, D])
    din("bw_o", [D, D])
    din("fw_qkv", [2, D, 1536]); din("fw_qs", [2, 64]); din("fw_ks", [2, 64])
    din("fw_wo", [2, 512, D]); din("fw_lng", [2, D]); din("fw_lnb", [2, D])
    din("fw_w1", [2, D, 2 * HID]); din("fw_b1", [2, 2 * HID])
    din("fw_w2", [2, HID, D]); din("fw_b2", [2, D])
    din("c_ident", [128, 128], BF16)
    din("c_cmask", [4 * 128, 512], BF16)
    din("c_fmask", [8 * 128, 1024], BF16)
    for nm in ("c_ropeA", "c_ropeB", "c_xqA", "c_xqB", "c_xkA", "c_xkB"):
        din(nm, [S, 64])
    din("c_zrow", [1, 1024], BF16)
    din("c_eps", [2])

    out_d = nc.dram_tensor("out", [B_LOC * S, D], F32, kind="ExternalOutput")
    dbg = {}
    if passes == "F1":
        ctx.dbg_c1 = {}
        for nm, shape, dt_ in (("Qw", [128, 4 * 512], BF16),
                               ("Kw", [128, 4 * 512], BF16),
                               ("Vw", [128, 4 * 512], BF16),
                               ("oall", [128, 4 * 512], BF16),
                               ("sc", [128, 1024], F32),
                               ("pn", [128, 1024], BF16),
                               ("qrot", [128, 512], BF16),
                               ("krot", [128, 512], BF16),
                               ("qf", [128, 512], F32),
                               ("kf", [128, 512], F32),
                               ("hbf", [128, 1024], BF16),
                               ("yl0", [128, 1024], F32),
                               ("qf_raw", [128, 512], F32),
                               ("kf_raw", [128, 512], F32),
                               ("qf_n", [128, 512], F32),
                               ("qrot0", [128, 512], BF16),
                               ("fac0", [128, 32], F32),
                               ("hbf0", [128, 1024], BF16),
                               ("hT0", [128, 1024], BF16)):
            ctx.dbg_c1[nm] = nc.dram_tensor("dbgC_" + nm, shape, dt_,
                                            kind="ExternalOutput")
    if passes == "A1":
        ctx.dbg_a1 = {}
        for nm, shape, dt_ in (("kfm", [128, 4 * 2048], BF16),
                               ("vtm", [128, 16 * 512], BF16),
                               ("qfm", [128, 8 * 512], BF16),
                               ("qn", [128, 1024], BF16),
                               ("kn", [128, 512], BF16),
                               ("qf", [128, 1024], F32),
                               ("hbf", [128, 1024], BF16),
                               ("sc", [128, 2048], F32),
                               ("pn", [128, 2048], BF16),
                               ("oall", [128, 8 * 512], BF16)):
            ctx.dbg_a1[nm] = nc.dram_tensor("dbgA_" + nm, shape, dt_,
                                            kind="ExternalOutput")
    if passes != "full":
        dbg["xr"] = nc.dram_tensor("dbg_xr", [B_LOC * S, D], F32,
                                   kind="ExternalOutput")
        dbg["yr"] = nc.dram_tensor("dbg_yr", [B_LOC * S, D], F32,
                                   kind="ExternalOutput")
        dbg["y1r"] = nc.dram_tensor("dbg_y1r", [B_LOC * S, D], F32,
                                    kind="ExternalOutput")

    xr = nc.dram_tensor("xr", [B_LOC * S, D], F32)
    yr = nc.dram_tensor("yr", [B_LOC * (S + 128), D], F32)
    y1r = nc.dram_tensor("y1r", [B_LOC * (S + 128), D], F32)
    wTd = nc.dram_tensor("wTd", [4, D, D], BF16)
    w1b = nc.dram_tensor("w1b", [2, 2 * NHC, 128, 1024], BF16)

    es = ExitStack()
    ident = es.enter_context(nc.sbuf_tensor("ident", [128, 128], BF16))
    identf = es.enter_context(nc.sbuf_tensor("identf", [128, 128], F32))
    scal = es.enter_context(nc.sbuf_tensor("scal", [128, 8], F32))
    ctx.ident = ident
    ctx.ps = [es.enter_context(nc.psum_tensor(f"ps{i}", [128, 512], F32))
              for i in range(6)]
    ctx.psT = [es.enter_context(nc.psum_tensor(f"psT{i}", [128, 1024], BF16))
               for i in range(2)]

    p.dma("sync", ident[:], dt_in["c_ident"].ap(), writes=("ident",))
    ctx.cp(identf[:], ident[:], reads=("ident",), writes=("identf",))
    ctx.identf = identf
    ctx.bcast(scal[:, 0:1], dt_in["aw_gate"], 1, ("gate",))
    ctx.bcast(scal[:, 1:2], dt_in["aw_lt"], 1, ("temp",))
    ctx.act(scal[:, 1:2], scal[:, 1:2], AF.Exp, reads=("temp",),
            writes=("temp",))
    p.dma("sync", scal[:, 2:4],
          bass.AP(dt_in["c_eps"], 0, [[0, 128], [1, 2]]), writes=("epsc",))
    ctx.eps5 = scal[:, 2:3]
    ctx.eps6 = scal[:, 3:4]

    for tns, knm in ((yr, "yr"), (y1r, "y1r")):
        for b in range(B_LOC):
            p.dma("gpsimd",
                  bass.AP(tns, b * (S + 128) * D, [[D, 128], [1, D]]),
                  bass.AP(dt_in["c_zrow"], 0, [[0, 128], [1, 1024]]),
                  writes=(f"{knm}guard:{b}",))

    make_prologue(ctx, dt_in, wTd, w1b, es)

    def scoped(fn, *args, **kw):
        st = ExitStack()
        sb = lambda name, shape, dtype: st.enter_context(
            nc.sbuf_tensor(name, shape, dtype))
        fn(ctx, dt_in, sb, *args, **kw)
        st.close()
        prog.fence()

    ncut = {"A": 1, "A1": 1, "F1": 2, "F1m": 3, "AF0": 5, "AF0B": 6,
            "full": 10}[passes]
    seq = [
        lambda last: scoped(pass_attn_a, x_in, xr, yr, scal, last=last,
                            out_d=out_d),
        lambda last: scoped(pass_ffn_attn, xr, yr, y1r, inv=0, l=0),
        lambda last: scoped(pass_ffn_mlp, xr, yr, y1r, w1b, inv=0, l=0,
                            last=last, out_d=out_d),
        lambda last: scoped(pass_ffn_attn, xr, yr, y1r, inv=0, l=1),
        lambda last: scoped(pass_ffn_mlp, xr, yr, y1r, w1b, inv=0, l=1,
                            last=last, out_d=out_d),
        lambda last: scoped(pass_attn_b, xr, yr, wTd, last=last, out_d=out_d),
        lambda last: scoped(pass_ffn_attn, xr, yr, y1r, inv=1, l=0),
        lambda last: scoped(pass_ffn_mlp, xr, yr, y1r, w1b, inv=1, l=0,
                            last=last, out_d=out_d),
        lambda last: scoped(pass_ffn_attn, xr, yr, y1r, inv=1, l=1),
        lambda last: scoped(pass_ffn_mlp, xr, yr, y1r, w1b, inv=1, l=1,
                            last=last, out_d=out_d),
    ]
    for i in range(ncut):
        seq[i](i == ncut - 1)

    if passes != "full" and passes != "A1":
        for g in range(B_LOC * NT):
            p.dma("sync", dram_row_ap(dbg["xr"], g), dram_row_ap(xr, g),
                  reads=(f"xr:{g}",))
            p.dma("sync", dram_row_ap(dbg["yr"], g), gr_ap(yr, g),
                  reads=(f"yr:{g}",))
            if passes in ("F1", "F1m"):
                p.dma("sync", dram_row_ap(dbg["y1r"], g),
                      gr_ap(y1r, g), reads=(f"y1r:{g}",))

    prog.finalize()
    sems = {}
    for e in ENGINES:
        sems[(e, "c")] = es.enter_context(nc.semaphore(f"s_{e}_c"))
        for s in range(DMA_POOL[e]):
            sems[(e, "s", s)] = es.enter_context(
                nc.semaphore(f"s_{e}_d{s}"))
    with nc.Block() as blk:
        for e in ENGINES:
            if prog.ops[e] or e == "sync":
                getattr(blk, e)(lambda q, ee=e: prog.emit_engine(q, ee, sems))
    es.close()
    return nc


def make_prologue(ctx, dt_in, wTd, w1b, es):
    nc = ctx.nc
    p = ctx.p
    stg = es.enter_context(nc.sbuf_tensor("pro_stg", [128, 1024], BF16))
    stg2 = es.enter_context(nc.sbuf_tensor("pro_stg2", [128, 1024], BF16))
    steps = ctx.pro_steps

    def w1_step(l, hc):
        def go():
            half, hcl = divmod(hc, NHC)
            col0 = half * HID + hcl * 128
            w = min(128, HID - hcl * 128)
            if w < 128:
                p.dma("gpsimd", stg[:],
                      bass.AP(dt_in["c_zrow"], 0, [[0, 128], [1, 1024]]),
                      reads=("pstg",), writes=("pstg",))
            ap = bass.AP(dt_in["fw_w1"], l * D * 2 * HID + col0,
                         [[2 * HID, 128], [2 * HID * 128, 8], [1, w]])
            dst = bass.AP(stg, 0, [[1024, 128], [128, 8], [1, w]])
            p.dma("gpsimd", dst, ap, reads=("pstg",), writes=("pstg",))
            p.dma("sync",
                  bass.AP(w1b, (l * 2 * NHC + hc) * 128 * 1024,
                          [[1024, 128], [1, 1024]]),
                  stg[:], reads=("pstg",), writes=(f"w1b:{l}:{hc}",))
        return go

    for l in range(2):
        for hc in range(2 * NHC):
            steps.append(w1_step(l, hc))

    def wt_step(wi, nm, i):
        def go():
            ap = bass.AP(dt_in[nm], i * 128 * D, [[D, 128], [1, 1024]])
            p.dma("gpsimd", stg2[:], ap, reads=("pstg2",), writes=("pstg2",))
            if nm == "bw_q":
                ctx.ts(stg2[:], stg2[:], 0.125, ALU.mult, reads=("pstg2",),
                       writes=("pstg2",))
            for jh in range(2):
                pk = f"psT{jh}"
                for j4 in range(4):
                    j = jh * 4 + j4
                    ctx.tr(ctx.psT[jh][:, j4 * 128:(j4 + 1) * 128],
                           stg2[:, j * 128:(j + 1) * 128],
                           reads=("pstg2",), wkey=pk)
            ctx.cp(stg[:, 0:512], ctx.psT[0][:, 0:512], reads=("psT0",),
                   writes=("pstg",))
            ctx.cp(stg[:, 512:1024], ctx.psT[1][:, 0:512], reads=("psT1",),
                   writes=("pstg2e",))
            p.dma("sync",
                  bass.AP(wTd, wi * D * D + i * 128,
                          [[D, 128], [D * 128, 4], [1, 128]]),
                  bass.AP(stg, 0, [[1024, 128], [128, 4], [1, 128]]),
                  reads=("pstg",), writes=(f"wTd:{nm}:{i}",))
            p.dma("sync",
                  bass.AP(wTd, wi * D * D + 4 * 128 * D + i * 128,
                          [[D, 128], [D * 128, 4], [1, 128]]),
                  bass.AP(stg, 512, [[1024, 128], [128, 4], [1, 128]]),
                  reads=("pstg2e",), writes=(f"wTd:{nm}:{i}",))
        return go

    for wi, nm in enumerate(["bw_q", "bw_k", "bw_v", "bw_o"]):
        for i in range(8):
            steps.append(wt_step(wi, nm, i))


def pass_attn_a(ctx, dt_in, sb, x_in, xr, yr, scal, last, out_d):
    p = ctx.p
    PS, PST = ctx.ps, ctx.psT

    wq_s = sb("A_wq", [128, 8 * 1024], BF16)
    wk_s = sb("A_wk", [128, 8 * 512], BF16)
    wv_s = sb("A_wv", [128, 8 * 512], BF16)
    wo_s = sb("A_wo", [128, 8 * 1024], BF16)
    normw = sb("A_normw", [128, 1024], F32)
    cmask = sb("A_cmask", [128, 4 * 512], BF16)
    QFM = sb("A_qfm", [128, 8 * 512], BF16)
    KFM = sb("A_kfm", [128, 4 * 2048], BF16)
    VTM = sb("A_vtm", [128, 16 * 512], BF16)
    xin = [sb(f"A_xin{i}", [128, 1024], F32) for i in range(2)]
    ybuf = [sb(f"A_y{i}", [128, 1024], F32) for i in range(2)]
    hbf = [sb(f"A_hbf{i}", [128, 1024], BF16) for i in range(2)]
    hT = sb("A_hT", [128, 1024], BF16)
    qf = sb("A_qf", [128, 1024], F32)
    kf = sb("A_kf", [128, 512], F32)
    qn = sb("A_qn", [128, 1024], BF16)
    kn = sb("A_kn", [128, 512], BF16)
    scr = [sb(f"A_scr{i}", [128, 1024], F32) for i in range(2)]
    scr2 = [sb(f"A_scr2{i}", [128, 1024], F32) for i in range(2)]
    stats = [sb(f"A_stats{i}", [128, 32], F32) for i in range(2)]
    st2 = [sb(f"A_st2{i}", [128, 8], F32) for i in range(2)]
    sc = sb("A_sc", [128, 2048], F32)
    Pn = [sb(f"A_pn{i}", [128, 2048], BF16) for i in range(2)]
    PT = sb("A_ptb", [128, 16 * 512], BF16)
    oall = sb("A_oall", [128, 8 * 512], BF16)
    xt2 = [sb("A_xt2", [128, 1024], F32)] * 2
    yt2 = [sb(f"A_yt2{i}", [128, 1024], F32) for i in range(2)]
    t1 = sb("A_t1", [128, 1024], F32)

    def wload(dst, name, cols, key):
        t = dt_in[name]
        for g2 in range(2):
            ap = bass.AP(t, g2 * 4 * 128 * cols,
                         [[cols, 128], [cols * 128, 4], [1, cols]])
            dstap = bass.AP(dst, g2 * 4 * cols,
                            [[8 * cols, 128], [cols, 4], [1, cols]])
            p.dma("gpsimd", dstap, ap, writes=(key,))

    wload(wq_s, "aw_q", 1024, "wq")
    wload(wk_s, "aw_k", 512, "wk")
    wload(wv_s, "aw_v", 512, "wv")
    wload(wo_s, "aw_o", 1024, "wo")
    ctx.bcast(normw[:], dt_in["aw_normw"], 1024, ("normw",))
    p.dma("sync", bass.AP(cmask, 0, [[2048, 128], [512, 4], [1, 512]]),
          dt_in["c_cmask"].ap().rearrange("(m p) n -> p m n", p=128),
          writes=("cmask",))

    for b in range(B_LOC):
        for qb in range(4):
            if getattr(ctx, "dbg_a1", None) is not None and (b, qb) == (0, 1):
                dbgd = ctx.dbg_a1
                for nm, buf in (("kfm", KFM), ("vtm", VTM), ("qfm", QFM),
                                ("qn", qn), ("kn", kn), ("qf", qf),
                                ("hbf", hbf[1]), ("sc", sc), ("pn", Pn),
                                ("oall", oall)):
                    p.dma("sync", dbgd[nm].ap(), buf[:],
                          reads=tuple(ctx.p.writer.keys()), writes=())
                return
            for tt in range(4):
                t = qb * 4 + tt
                g = b * NT + t
                pa = t % 2
                ctx.step_prologue(4)
                kx, ky = f"xin{pa}", f"ybuf{pa}"
                p.dma("sync", xin[pa][:], dram_row_ap(x_in, g), writes=(kx,))
                ctx.ln_tile(xin[pa][:], ybuf[pa][:], kx, ky, scr[pa],
                            scr2[pa], stats[pa], 0, f"stL{pa}")
                p.dma("sync", gr_ap(yr, g), ybuf[pa][:], reads=(ky,),
                      writes=(f"yr:{g}",))
                ctx.act(scr[pa][:], ybuf[pa][:], AF.Square, reads=(ky,),
                        writes=(f"scrA{pa}", f"stR{pa}"), scale=0.03125,
                        accum=stats[pa][:, 8:9])
                ctx.rsqrt(stats[pa][:, 9:10], stats[pa][:, 8:9],
                          reads=(f"stR{pa}", "epsc"), writes=(f"stR{pa}",),
                          bias=ctx.eps6)
                kh = f"hbf{pa}"
                ctx.stt(hbf[pa][:], ybuf[pa][:], stats[pa][:, 9:10], normw[:],
                        ALU.mult, ALU.mult, reads=(ky, f"stR{pa}", "normw"),
                        writes=(kh,))
                for c in range(8):
                    ctx.tr(PST[0][:, c * 128:(c + 1) * 128],
                           hbf[pa][:, c * 128:(c + 1) * 128],
                           reads=(kh,), wkey="psT0")
                ctx.cp(hT[:], PST[0][:], reads=("psT0",), writes=("hT",))
                for n in range(2):
                    for c in range(8):
                        ctx.mm(PS[n][:], hT[:, c * 128:(c + 1) * 128],
                               wq_s[:, c * 1024 + n * 512:c * 1024 + (n + 1) * 512],
                               reads=("hT", "wq"), wkey=f"ps{n}",
                               start=(c == 0), stop=(c == 7))
                for c in range(8):
                    ctx.mm(PS[2][:], hT[:, c * 128:(c + 1) * 128],
                           wk_s[:, c * 512:(c + 1) * 512],
                           reads=("hT", "wk"), wkey="ps2",
                           start=(c == 0), stop=(c == 7))
                for c in range(8):
                    ctx.mm(PS[3][:], hT[:, c * 128:(c + 1) * 128],
                           wv_s[:, c * 512:(c + 1) * 512],
                           reads=("hT", "wv"), wkey="ps3",
                           start=(c == 0), stop=(c == 7))
                ctx.cp(qf[:, 0:512], PS[0][:], reads=("ps0",), writes=("qf",),
                       eng="scalar")
                ctx.cp(qf[:, 512:1024], PS[1][:], reads=("ps1",),
                       writes=("qf",), eng="scalar")
                ctx.cp(kf[:], PS[2][:], reads=("ps2",), writes=("kf",),
                       eng="scalar")
                ctx.cp(VTM[:, t * 512:(t + 1) * 512], PS[3][:], reads=("ps3",),
                       writes=(f"VTM:{t}",))
                ctx.act(scr[pa][:], qf[:], AF.Square, reads=("qf",),
                        writes=(f"scrA{pa}",))
                ctx.act(scr2[pa][:, 0:512], kf[:], AF.Square, reads=("kf",),
                        writes=(f"scrB{pa}",))
                ctx.red(stats[pa][:, 16:20],
                        bass.AP(scr[pa], 0, [[1024, 128], [256, 4], [1, 256]]),
                        ALU.add, reads=(f"scrA{pa}",), writes=(f"stQ{pa}",))
                ctx.red(stats[pa][:, 20:22],
                        bass.AP(scr2[pa], 0, [[1024, 128], [256, 2], [1, 256]]),
                        ALU.add, reads=(f"scrB{pa}",), writes=(f"stQ{pa}",))
                ctx.rsqrt(stats[pa][:, 24:28], stats[pa][:, 16:20],
                          reads=(f"stQ{pa}",), writes=(f"stF{pa}",), minv=1e12)
                ctx.rsqrt(stats[pa][:, 28:30], stats[pa][:, 20:22],
                          reads=(f"stQ{pa}",), writes=(f"stF{pa}",), minv=1e12)
                ctx.ts(stats[pa][:, 24:28], stats[pa][:, 24:28], scal[:, 1:2],
                       ALU.mult, reads=(f"stF{pa}", "temp"),
                       writes=(f"stF{pa}",))
                ctx.tt(bass.AP(qn, 0, [[1024, 128], [256, 4], [1, 256]]),
                       bass.AP(qf, 0, [[1024, 128], [256, 4], [1, 256]]),
                       bass.AP(stats[pa], 24, [[32, 128], [1, 4], [0, 256]]),
                       ALU.mult, reads=("qf", f"stF{pa}"), writes=("qn",))
                ctx.tt(bass.AP(kn, 0, [[512, 128], [256, 2], [1, 256]]),
                       bass.AP(kf, 0, [[512, 128], [256, 2], [1, 256]]),
                       bass.AP(stats[pa], 28, [[32, 128], [1, 2], [0, 256]]),
                       ALU.mult, reads=("kf", f"stF{pa}"), writes=("kn",),
                       eng="vector")
                for c in range(8):
                    ctx.tr(PST[1][:, c * 128:(c + 1) * 128],
                           qn[:, c * 128:(c + 1) * 128],
                           reads=("qn",), wkey="psT1")
                ctx.cp(bass.AP(QFM, tt * 128,
                               [[8 * 512, 128], [512, 8], [1, 128]]),
                       bass.AP(PST[1], 0, [[1024, 128], [128, 8], [1, 128]]),
                       reads=("psT1",), writes=(f"QFM:{tt}",))
                for c in range(4):
                    ctx.tr(PST[0][:, c * 128:(c + 1) * 128],
                           kn[:, c * 128:(c + 1) * 128],
                           reads=("kn",), wkey="psT0")
                ctx.cp(bass.AP(KFM, t * 128,
                               [[4 * 2048, 128], [2048, 4], [1, 128]]),
                       bass.AP(PST[0], 0, [[1024, 128], [128, 4], [1, 128]]),
                       reads=("psT0",), writes=(f"KFM:{t}",), eng="scalar")
            nch = qb + 1
            nkc = nch * 4

            def a2_sm1(h, qt):
                kvh = h // 2
                kq = f"QFM:{qt}"
                sp = qt % 2
                for c in range(nch):
                    pb_ = 4 + (c % 2)
                    for cc in range(2):
                        ctx.mm(PS[pb_][:],
                               QFM[:, (2 * h + cc) * 512 + qt * 128:
                                   (2 * h + cc) * 512 + (qt + 1) * 128],
                               KFM[:, (2 * kvh + cc) * 2048 + c * 512:
                                   (2 * kvh + cc) * 2048 + (c + 1) * 512],
                               reads=(kq,) + tuple(
                                   f"KFM:{4 * c + i}" for i in range(4)),
                               wkey=f"ps{pb_}", start=(cc == 0),
                               stop=(cc == 1))
                    if c == qb:
                        ctx.tt(sc[:, c * 512:(c + 1) * 512], PS[pb_][:],
                               cmask[:, qt * 512:(qt + 1) * 512], ALU.add,
                               reads=(f"ps{pb_}", "cmask"),
                               writes=(f"sc:{c}",))
                    else:
                        ctx.cp(sc[:, c * 512:(c + 1) * 512], PS[pb_][:],
                               reads=(f"ps{pb_}",), writes=(f"sc:{c}",),
                               eng="scalar")
                ctx.red(st2[sp][:, 4:5], sc[:, :nch * 512], ALU.max,
                        reads=tuple(f"sc:{c}" for c in range(nch)),
                        writes=(f"stM{sp}",), negate=True)
                ctx.act(Pn[sp][:, :nch * 512], sc[:, :nch * 512], AF.Exp,
                        reads=(f"stM{sp}",) + tuple(
                            f"sc:{c}" for c in range(nch)),
                        writes=(f"Pn{sp}", f"stM{sp}"),
                        bias=st2[sp][:, 4:5], accum=st2[sp][:, 5:6])

            def a2_sm2pt(h, qt):
                sp = qt % 2
                ctx.recip(st2[sp][:, 5:6], st2[sp][:, 5:6],
                          reads=(f"stM{sp}",), writes=(f"stM{sp}",))
                ctx.ts(Pn[sp][:, :nch * 512], Pn[sp][:, :nch * 512],
                       st2[sp][:, 5:6], ALU.mult,
                       reads=(f"Pn{sp}", f"stM{sp}"), writes=(f"Pn{sp}",),
                       eng="vector")
                for kg in range((nkc + 7) // 8):
                    kcs = list(range(kg * 8, min(nkc, kg * 8 + 8)))
                    for kc in kcs:
                        ctx.tr(PST[1][:, (kc % 8) * 128:(kc % 8 + 1) * 128],
                               Pn[sp][:, kc * 128:(kc + 1) * 128],
                               reads=(f"Pn{sp}",), wkey="psT1")
                    nn = len(kcs)
                    ctx.cp(bass.AP(PT, kcs[0] * 512 + qt * 128,
                                   [[16 * 512, 128], [512, nn], [1, 128]]),
                           bass.AP(PST[1], 0,
                                   [[1024, 128], [128, nn], [1, 128]]),
                           reads=("psT1",),
                           writes=tuple(f"PT:{kc}:{qt}" for kc in kcs),
                           eng="scalar")

            pend = None
            for h in range(4):
                kvh = h // 2
                for qt in range(4):
                    a2_sm1(h, qt)
                    if pend is not None:
                        a2_sm2pt(*pend)
                    pend = (h, qt)
                a2_sm2pt(*pend)
                pend = None
                for vc in range(2):
                    for kc in range(nkc):
                        ctx.mm(PS[vc][:],
                               VTM[:, kc * 512 + kvh * 256 + vc * 128:
                                   kc * 512 + kvh * 256 + (vc + 1) * 128],
                               PT[:, kc * 512:(kc + 1) * 512],
                               reads=(f"VTM:{kc}",) + tuple(
                                   f"PT:{kc}:{q_}" for q_ in range(4)),
                               wkey=f"ps{vc}", start=(kc == 0),
                               stop=(kc == nkc - 1))
                    ctx.cp(oall[:, (2 * h + vc) * 512:(2 * h + vc + 1) * 512],
                           PS[vc][:], reads=(f"ps{vc}",),
                           writes=(f"oall:{2 * h + vc}",),
                           eng=("vector" if vc == 0 else "scalar"))
            for qt in range(4):
                t = qb * 4 + qt
                g = b * NT + t
                pa = t % 2
                kx2, ky2 = "xt2s", f"yt2{pa}"
                p.dma("sync", xt2[pa][:], dram_row_ap(x_in, g), writes=(kx2,))
                p.dma("sync", yt2[pa][:], gr_ap(yr, g),
                      reads=(f"yr:{g}",), writes=(ky2,))
                for n in range(2):
                    for c in range(8):
                        ctx.mm(PS[2 + n][:],
                               oall[:, c * 512 + qt * 128:c * 512 + (qt + 1) * 128],
                               wo_s[:, c * 1024 + n * 512:c * 1024 + (n + 1) * 512],
                               reads=(f"oall:{c}", "wo"), wkey=f"ps{2 + n}",
                               start=(c == 0), stop=(c == 7))
                for n in range(2):
                    ctx.stt(t1[:, n * 512:(n + 1) * 512], PS[2 + n][:],
                            scal[:, 0:1], yt2[pa][:, n * 512:(n + 1) * 512],
                            ALU.mult, ALU.add,
                            reads=(f"ps{2 + n}", "gate", ky2), writes=("t1",))
                ctx.tt(xt2[pa][:], t1[:], xt2[pa][:], ALU.add,
                       reads=("t1", kx2), writes=(kx2,), eng="vector")
                wb_store(ctx, xt2[pa][:], kx2, g, xr, yr, scr[pa], scr2[pa],
                         stats[pa], last, out_d, pa)


def wb_store(ctx, x1_ap, kx1, g, xr, yr, scr, scr2, stats, last, out_d,
             pa=0):
    p = ctx.p
    if last:
        p.dma("sync", dram_row_ap(out_d, g), x1_ap, reads=(kx1,),
              writes=(f"out:{g}",))
        p.dma("sync", dram_row_ap(xr, g), x1_ap, reads=(kx1,),
              writes=(f"xr:{g}",))
        return
    p.dma("sync", dram_row_ap(xr, g), x1_ap, reads=(kx1,), writes=(f"xr:{g}",))
    ctx.ln_tile(x1_ap, scr[:], kx1, f"scrA{pa}w", scr, scr2, stats, 10,
                f"stW{pa}")
    p.dma("sync", gr_ap(yr, g), scr[:], reads=(f"scrA{pa}w",),
          writes=(f"yr:{g}",))


def load_shifted(ctx, dst, src_t, g, t, key, dt_in, kpre):
    p = ctx.p
    b = g // NT
    p.dma("sync", dst[:, 0:512],
          bass.AP(src_t, grow(g) * D, [[D, 128], [1, 512]]),
          reads=(f"{kpre}:{g}",), writes=(key,))
    prev = (f"{kpre}guard:{b}",) if t == 0 else (f"{kpre}:{g - 1}",)
    p.dma("sync", dst[:, 512:1024], gr_shift_ap(src_t, g),
          reads=prev + (f"{kpre}:{g}",), writes=(key,))


def pass_ffn_attn(ctx, dt_in, sb, xr, yr, y1r, inv, l):
    p = ctx.p
    PS, PST = ctx.ps, ctx.psT
    pfx = f"F{inv}{l}a"
    N = lambda s: pfx + s

    wqkv = sb(N("wqkv"), [128, 8 * 1536], BF16)
    wo_s = sb(N("wo"), [128, 4 * 1024], BF16)
    fmask = sb(N("fmask"), [128, 8 * 1024], BF16)
    tabb = sb(N("tabb"), [128, 4 * 1024], F32)
    tabl = sb(N("tabl"), [128, 4 * 1024], F32)
    qsb = sb(N("qsb"), [128, 128], F32)
    Kw = [sb(N(f"K{i}"), [128, 4 * 512], BF16) for i in range(2)]
    Vw = [sb(N(f"V{i}"), [128, 4 * 512], BF16) for i in range(2)]
    Qw = sb(N("Q"), [128, 4 * 512], BF16)
    yl = [sb(N(f"yl{i}"), [128, 1024], F32) for i in range(2)]
    hbf = [sb(N(f"hbf{i}"), [128, 1024], BF16) for i in range(2)]
    hT = [sb(N(f"hT{i}"), [128, 1024], BF16) for i in range(2)]
    qf = [sb(N(f"qf{i}"), [128, 512], F32) for i in range(2)]
    kf = [sb(N(f"kf{i}"), [128, 512], F32) for i in range(2)]
    t1r = [sb(N(f"t1r{i}"), [128, 512], F32) for i in range(2)]
    t2r = [sb(N(f"t2r{i}"), [128, 512], F32) for i in range(2)]
    qrot = [sb(N(f"qrot{i}"), [128, 512], BF16) for i in range(2)]
    krot = [sb(N(f"krot{i}"), [128, 512], BF16) for i in range(2)]
    scr = [sb(N(f"scr{i}"), [128, 1024], F32) for i in range(2)]
    stats = [sb(N(f"stats{i}"), [128, 32], F32) for i in range(2)]
    st2 = [sb(N(f"st2{i}"), [128, 8], F32) for i in range(4)]
    sc = [sb(N(f"sc{i}"), [128, 1024], F32) for i in range(4)]
    Pn = [sb(N(f"pn{i}"), [128, 1024], BF16) for i in range(8)]
    PT = sb(N("pt"), [128, 8 * 512], BF16)
    oall = sb(N("oall"), [128, 4 * 512], BF16)
    yl2 = [sb(N(f"yl2{i}"), [128, 1024], F32) for i in range(2)]
    y1o = sb(N("y1o"), [128, 1024], F32)

    for g2 in range(4):
        ap = bass.AP(dt_in["fw_qkv"], l * D * 1536 + g2 * 2 * 128 * 1536,
                     [[1536, 128], [1536 * 128, 2], [1, 1536]])
        dst = bass.AP(wqkv, g2 * 2 * 1536,
                      [[8 * 1536, 128], [1536, 2], [1, 1536]])
        p.dma("gpsimd", dst, ap, writes=(N("wqkv"),))
    ap = bass.AP(dt_in["fw_wo"], l * 512 * D, [[D, 128], [D * 128, 4], [1, D]])
    p.dma("gpsimd", bass.AP(wo_s, 0, [[4 * 1024, 128], [1024, 4], [1, 1024]]),
          ap, writes=(N("wo"),))
    p.dma("sync", bass.AP(fmask, 0, [[8 * 1024, 128], [1024, 8], [1, 1024]]),
          dt_in["c_fmask"].ap().rearrange("(m p) n -> p m n", p=128),
          writes=(N("fmask"),))
    for ti, nm in enumerate(("c_xqA", "c_xqB", "c_xkA", "c_xkB")):
        p.dma("sync", bass.AP(tabb, ti * 1024, [[4096, 128], [64, 16], [1, 64]]),
              dt_in[nm].ap().rearrange("(t p) n -> p t n", p=128),
              writes=(N("tabb"),))
    ctx.bcast(qsb[:, 0:64], dt_in["fw_qs"], 64, (N("qsb"),), offset=l * 64)
    ctx.bcast(qsb[:, 64:128], dt_in["fw_ks"], 64, (N("qsb"),), offset=l * 64)
    for ti in range(4):
        qcol = 0 if ti < 2 else 64
        swap = (ti % 2 == 1)
        out3 = bass.AP(tabl, ti * 1024, [[4096, 128], [64, 16], [32, 2], [1, 32]])
        in3 = bass.AP(tabb, ti * 1024, [[4096, 128], [64, 16], [32, 2], [1, 32]])
        if swap:
            sc3 = bass.AP(qsb, qcol + 32, [[128, 128], [0, 16], [-32, 2], [1, 32]])
        else:
            sc3 = bass.AP(qsb, qcol, [[128, 128], [0, 16], [32, 2], [1, 32]])
        ctx.tt(out3, in3, sc3, ALU.mult, reads=(N("tabb"), N("qsb")),
               writes=(N("tabl"),))

    for b in range(B_LOC):
        p.dma("sync", Kw[1][:],
              bass.AP(dt_in["c_zrow"], 0, [[0, 128], [0, 2], [1, 1024]]),
              writes=(N("K1"),))
        p.dma("sync", Vw[1][:],
              bass.AP(dt_in["c_zrow"], 0, [[0, 128], [0, 2], [1, 1024]]),
              writes=(N("V1"),))
        cur, prv = 0, 1
        for wi in range(4):
            kK, kV = N(f"K{cur}"), N(f"V{cur}")
            pK, pV = N(f"K{prv}"), N(f"V{prv}")
            for tt in range(4):
                t = wi * 4 + tt
                g = b * NT + t
                pa = tt % 2
                kyl = N(f"yl{pa}")
                khb, kht = N(f"hbf{pa}"), N(f"hT{pa}")
                kqf, kkf = N(f"qf{pa}"), N(f"kf{pa}")
                kqr, kkr = N(f"qrot{pa}"), N(f"krot{pa}")
                load_shifted(ctx, yl[pa], yr, g, t, kyl, dt_in, "yr")
                ctx.cp(hbf[pa][:], yl[pa][:], reads=(kyl,), writes=(khb,))
                for c in range(8):
                    ctx.tr(PST[0][:, c * 128:(c + 1) * 128],
                           hbf[pa][:, c * 128:(c + 1) * 128],
                           reads=(khb,), wkey="psT0")
                ctx.cp(hT[pa][:], PST[0][:], reads=("psT0",), writes=(kht,))
                for o3 in range(3):
                    for c in range(8):
                        ctx.mm(PS[o3][:], hT[pa][:, c * 128:(c + 1) * 128],
                               wqkv[:, c * 1536 + o3 * 512:
                                    c * 1536 + (o3 + 1) * 512],
                               reads=(kht, N("wqkv")), wkey=f"ps{o3}",
                               start=(c == 0), stop=(c == 7))
                ctx.cp(qf[pa][:], PS[0][:], reads=("ps0",), writes=(kqf,),
                       eng="scalar")
                ctx.cp(kf[pa][:], PS[1][:], reads=("ps1",), writes=(kkf,),
                       eng="scalar")
                ctx.cp(Vw[cur][:, tt * 512:(tt + 1) * 512], PS[2][:],
                       reads=("ps2",), writes=(kV,))
                ctx.act(scr[pa][:, 0:512], qf[pa][:], AF.Square,
                        reads=(kqf,), writes=(N(f"scrA{pa}"),))
                ctx.act(scr[pa][:, 512:1024], kf[pa][:], AF.Square,
                        reads=(kkf,), writes=(N(f"scrA{pa}"),))
                ctx.red(stats[pa][:, 0:8],
                        bass.AP(scr[pa], 0, [[1024, 128], [64, 8], [1, 64]]),
                        ALU.add, reads=(N(f"scrA{pa}"),),
                        writes=(N(f"stQ{pa}"),))
                ctx.red(stats[pa][:, 8:16],
                        bass.AP(scr[pa], 512, [[1024, 128], [64, 8], [1, 64]]),
                        ALU.add, reads=(N(f"scrA{pa}"),),
                        writes=(N(f"stQ{pa}"),))
                ctx.rsqrt(stats[pa][:, 16:24], stats[pa][:, 0:8],
                          reads=(N(f"stQ{pa}"),), writes=(N(f"stF{pa}"),),
                          minv=1e12)
                ctx.rsqrt(stats[pa][:, 24:32], stats[pa][:, 8:16],
                          reads=(N(f"stQ{pa}"),), writes=(N(f"stF{pa}"),),
                          minv=1e12)
                ctx.tt(bass.AP(qf[pa], 0, [[512, 128], [64, 8], [1, 64]]),
                       bass.AP(qf[pa], 0, [[512, 128], [64, 8], [1, 64]]),
                       bass.AP(stats[pa], 16, [[32, 128], [1, 8], [0, 64]]),
                       ALU.mult, reads=(kqf, N(f"stF{pa}")), writes=(kqf,))
                ctx.tt(bass.AP(kf[pa], 0, [[512, 128], [64, 8], [1, 64]]),
                       bass.AP(kf[pa], 0, [[512, 128], [64, 8], [1, 64]]),
                       bass.AP(stats[pa], 24, [[32, 128], [1, 8], [0, 64]]),
                       ALU.mult, reads=(kkf, N(f"stF{pa}")), writes=(kkf,),
                       eng="vector")
                xrope(ctx, qf[pa], t1r[pa], t2r[pa], qrot[pa], tabl, 0, t,
                      kqf, kqr, N("tabl"), eng="vector", pfx2=N(f"r{pa}"))
                xrope(ctx, kf[pa], t1r[pa], t2r[pa], krot[pa], tabl, 2, t,
                      kkf, kkr, N("tabl"), eng="vector", pfx2=N(f"r{pa}"))
                for c in range(4):
                    ctx.tr(PST[1][:, c * 128:(c + 1) * 128],
                           qrot[pa][:, c * 128:(c + 1) * 128],
                           reads=(kqr,), wkey="psT1")
                ctx.cp(bass.AP(Qw, tt * 128, [[4 * 512, 128], [512, 4], [1, 128]]),
                       bass.AP(PST[1], 0, [[1024, 128], [128, 4], [1, 128]]),
                       reads=("psT1",), writes=(N("Q"),))
                for c in range(4):
                    ctx.tr(PST[0][:, c * 128:(c + 1) * 128],
                           krot[pa][:, c * 128:(c + 1) * 128],
                           reads=(kkr,), wkey="psT0")
                ctx.cp(bass.AP(Kw[cur], tt * 128,
                               [[4 * 512, 128], [512, 4], [1, 128]]),
                       bass.AP(PST[0], 0, [[1024, 128], [128, 4], [1, 128]]),
                       reads=("psT0",), writes=(kK,), eng="scalar")
            def c2_sm1(h, qt):
                hb = (h % 2) * 64
                gp = h // 2
                sp = qt
                pnn = (h % 2) * 4 + qt
                ksc, kpn, kstm = (N(f"sc{sp}"), N(f"pn{pnn}"), N(f"stM{sp}"))
                for ci in range(2):
                    kb = Kw[prv] if ci == 0 else Kw[cur]
                    kkey = pK if ci == 0 else kK
                    pb_ = 4 + ci
                    ctx.mm(PS[pb_][:],
                           Qw[hb:hb + 64, gp * 512 + qt * 128:
                              gp * 512 + (qt + 1) * 128],
                           kb[hb:hb + 64, gp * 512:(gp + 1) * 512],
                           reads=(N("Q"), kkey), wkey=f"ps{pb_}",
                           start=True, stop=True)
                    mrow = (4 if wi == 0 else 0) + qt
                    ctx.tt(sc[sp][:, ci * 512:(ci + 1) * 512], PS[pb_][:],
                           fmask[:, mrow * 1024 + ci * 512:
                                 mrow * 1024 + (ci + 1) * 512],
                           ALU.add, reads=(f"ps{pb_}", N("fmask")),
                           writes=(ksc,))
                ctx.red(st2[sp][:, 4:5], sc[sp][:], ALU.max,
                        reads=(ksc,), writes=(kstm,), negate=True)
                ctx.act(Pn[pnn][:], sc[sp][:], AF.Exp, reads=(ksc, kstm),
                        writes=(kpn, kstm), bias=st2[sp][:, 4:5],
                        accum=st2[sp][:, 5:6])

            def c2_sm2(h, qt):
                sp = qt
                pnn = (h % 2) * 4 + qt
                kpn, kstm = N(f"pn{pnn}"), N(f"stM{sp}")
                ctx.recip(st2[sp][:, 5:6], st2[sp][:, 5:6], reads=(kstm,),
                          writes=(kstm,))
                ctx.ts(Pn[pnn][:], Pn[pnn][:], st2[sp][:, 5:6], ALU.mult,
                       reads=(kpn, kstm), writes=(kpn,),
                       eng="vector")

            def c2_softmax(h):
                for qt in range(4):
                    c2_sm1(h, qt)
                    if ctx.sm_pend is not None:
                        c2_sm2(*ctx.sm_pend)
                    ctx.sm_pend = (h, qt)

            def c2_pv(h):
                hb = (h % 2) * 64
                gp = h // 2
                for qt in range(4):
                    pnn = (h % 2) * 4 + qt
                    kpn = N(f"pn{pnn}")
                    for kc in range(8):
                        ctx.tr(PST[1][:, kc * 128:(kc + 1) * 128],
                               Pn[pnn][:, kc * 128:(kc + 1) * 128],
                               reads=(kpn,), wkey="psT1")
                    ctx.cp(bass.AP(PT, qt * 128,
                                   [[8 * 512, 128], [512, 8], [1, 128]]),
                           bass.AP(PST[1], 0, [[1024, 128], [128, 8], [1, 128]]),
                           reads=("psT1",),
                           writes=tuple(f"{pfx}PT:{kc}:{qt}"
                                        for kc in range(8)), eng="scalar")
                psel = PS[0][0:64, :] if h % 2 == 0 else PS[0][64:128, :]
                for kc in range(8):
                    vb = Vw[prv] if kc < 4 else Vw[cur]
                    vkey = pV if kc < 4 else kV
                    ctx.mm(psel,
                           vb[:, (kc % 4) * 512 + h * 64:
                              (kc % 4) * 512 + (h + 1) * 64],
                           PT[:, kc * 512:(kc + 1) * 512],
                           reads=(vkey,) + tuple(
                               f"{pfx}PT:{kc}:{q_}" for q_ in range(4)),
                           wkey="ps0", start=(kc == 0), stop=(kc == 7),
                           skip=True)
                if h % 2 == 1:
                    ctx.cp(oall[:, gp * 512:(gp + 1) * 512], PS[0][:],
                           reads=("ps0",), writes=(f"{pfx}oall:{gp}",),
                           eng="scalar")

            ctx.sm_pend = None
            c2_softmax(0)
            for h in range(8):
                if h + 1 < 8:
                    c2_softmax(h + 1)
                if ctx.sm_pend is not None:
                    c2_sm2(*ctx.sm_pend)
                    ctx.sm_pend = None
                c2_pv(h)
            if getattr(ctx, "dbg_c1", None) is not None and (b, wi) == (0, 0)                     and inv == 0 and l == 0:
                dbgd = ctx.dbg_c1
                for nm, buf in (("Qw", Qw), ("Kw", Kw[cur]), ("Vw", Vw[cur]),
                                ("oall", oall), ("sc", sc[0]), ("pn", Pn[0]),
                                ("qrot", qrot[0]), ("krot", krot[0]),
                                ("qf", qf[0]), ("kf", kf[0]),
                                ("hbf", hbf[0])):
                    p.dma("sync", dbgd[nm].ap(), buf[:],
                          reads=tuple(ctx.p.writer.keys()), writes=())
            for qt in range(4):
                t = wi * 4 + qt
                g = b * NT + t
                pa = qt % 2
                ky2 = N(f"yl2{pa}")
                p.dma("sync", yl2[pa][:], gr_ap(yr, g),
                      reads=(f"yr:{g}",), writes=(ky2,))
                for n in range(2):
                    for c in range(4):
                        ctx.mm(PS[2 + n][:],
                               oall[:, c * 512 + qt * 128:c * 512 + (qt + 1) * 128],
                               wo_s[:, c * 1024 + n * 512:c * 1024 + (n + 1) * 512],
                               reads=(f"{pfx}oall:{c}", N("wo")),
                               wkey=f"ps{2 + n}", start=(c == 0), stop=(c == 3))
                for n in range(2):
                    ctx.tt(y1o[:, n * 512:(n + 1) * 512], PS[2 + n][:],
                           yl2[pa][:, n * 512:(n + 1) * 512], ALU.add,
                           reads=(f"ps{2 + n}", ky2), writes=(N("y1o"),))
                p.dma("sync", gr_ap(y1r, g), y1o[:], reads=(N("y1o"),),
                      writes=(f"y1r:{g}",))
            cur, prv = prv, cur


def xrope(ctx, src, t1r, t2r, outb, tabl, ti0, t, ksrc, kout, ktab,
          eng="vector", pfx2="r"):
    At = bass.AP(tabl, ti0 * 1024 + t * 64, [[4096, 128], [0, 8], [1, 64]])
    s3 = bass.AP(src, 0, [[512, 128], [64, 8], [1, 64]])
    ctx.tt(bass.AP(t1r, 0, [[512, 128], [64, 8], [1, 64]]), s3, At, ALU.mult,
           reads=(ksrc, ktab), writes=(pfx2 + "t1",), eng=eng)
    o2 = bass.AP(t2r, 0, [[512, 128], [64, 8], [32, 2], [1, 32]])
    prt = bass.AP(src, 32, [[512, 128], [64, 8], [-32, 2], [1, 32]])
    Bt = bass.AP(tabl, (ti0 + 1) * 1024 + t * 64,
                 [[4096, 128], [0, 8], [32, 2], [1, 32]])
    ctx.tt(o2, prt, Bt, ALU.mult, reads=(ksrc, ktab), writes=(pfx2 + "t2",),
           eng=eng)
    ctx.tt(outb[:], t1r[:], t2r[:], ALU.add,
           reads=(pfx2 + "t1", pfx2 + "t2"), writes=(kout,), eng=eng)


def pass_ffn_mlp(ctx, dt_in, sb, xr, yr, y1r, w1b, inv, l, last, out_d):
    p = ctx.p
    PS, PST = ctx.ps, ctx.psT
    endinv = (l == 1)
    pfx = f"F{inv}{l}m"
    N = lambda s: pfx + s

    lng = sb(N("lng"), [128, 1024], F32)
    lnb = sb(N("lnb"), [128, 1024], F32)
    b2b = sb(N("b2b"), [128, 1024], F32)
    b1c = sb(N("b1c"), [128, 2 * NHC], F32)
    h2T = sb(N("h2T"), [128, 8 * 1024], BF16)
    h4 = sb(N("h4"), [128, NHC * 1024], BF16)
    w2s = sb(N("w2s"), [128, NHC * 1024], BF16)
    w1s = [sb(N(f"w1s{i}"), [128, 1024], BF16) for i in range(4)]
    gact = sb(N("gact"), [128, 1024], BF16)
    abf = sb(N("abf"), [128, 1024], BF16)
    h2bf = [sb(N(f"h2bf{i}"), [128, 1024], BF16) for i in range(2)]
    hl = [sb(N(f"hl{i}"), [128, 1024], F32) for i in range(2)]
    h2n = [sb(N(f"h2n{i}"), [128, 1024], F32) for i in range(2)]
    scr = [sb(N(f"scr{i}"), [128, 1024], F32) for i in range(2)]
    scr2 = [sb(N(f"scr2{i}"), [128, 1024], F32) for i in range(2)]
    stats = [sb(N(f"stats{i}"), [128, 32], F32) for i in range(2)]
    y1l = [sb(N(f"y1l{i}"), [128, 1024], F32) for i in range(2)]
    xl = [sb(N(f"xl{i}"), [128, 1024], F32) for i in range(2)]
    yo = sb(N("yo"), [128, 1024], F32)

    ctx.bcast(lng[:], dt_in["fw_lng"], 1024, (N("lng"),), offset=l * D)
    ctx.bcast(lnb[:], dt_in["fw_lnb"], 1024, (N("lnb"),), offset=l * D)
    ctx.bcast(b2b[:], dt_in["fw_b2"], 1024, (N("b2b"),), offset=l * D)
    for half in range(2):
        base = l * 2 * HID + half * HID
        p.dma("gpsimd", scr2[0][0:22, 0:128],
              bass.AP(dt_in["c_zrow"], 0, [[0, 22], [1, 128]]),
              reads=("scrB0",), writes=("scrB0",))
        p.dma("sync", scr2[0][0:21, 0:128],
              bass.AP(dt_in["fw_b1"], base, [[128, 21], [1, 128]]),
              reads=("scrB0",), writes=("scrB0",))
        p.dma("sync", scr2[0][21:22, 0:42],
              bass.AP(dt_in["fw_b1"], base + 21 * 128, [[42, 1], [1, 42]]),
              reads=("scrB0",), writes=("scrB0",))
        ctx.p.op("tensor",
                 lambda q, o=PS[5][:, 0:22], i=scr2[0][0:22, 0:128],
                 idn=ctx.identf: q.transpose(o, i, idn[0:22, 0:22]),
                 reads=("scrB0", "identf"), writes=("ps5",), tag="tr")
        ctx.cp(b1c[:, half * NHC:(half + 1) * NHC], PS[5][:, 0:22],
               reads=("ps5",), writes=(N("b1c"),))

    for blk in range(4):
        b = blk // 2
        if blk == 0:
            for i in range(NHC):
                w = min(128, HID - i * 128)
                if w < 128:
                    p.dma("gpsimd", w2s[:, i * 1024:(i + 1) * 1024],
                          bass.AP(dt_in["c_zrow"], 0, [[0, 128], [1, 1024]]),
                          reads=(N("w2s"),), writes=(N("w2s"),))
                p.dma("gpsimd", w2s[0:w, i * 1024:(i + 1) * 1024],
                      bass.AP(dt_in["fw_w2"], l * HID * D + i * 128 * D,
                              [[D, w], [1, D]]),
                      reads=(N("w2s"),), writes=(N("w2s"),),
                      partial=(w < 128))
        for tt in range(8):
            t = (blk % 2) * 8 + tt
            g = b * NT + t
            pa = tt % 2
            khl = N(f"hl{pa}")
            kh2, kh2b = N(f"h2n{pa}"), N(f"h2bf{pa}")
            load_shifted(ctx, hl[pa], y1r, g, t, khl, dt_in, "y1r")
            ctx.ln_tile(hl[pa][:], h2n[pa][:], khl, kh2, scr[pa], scr2[pa],
                        stats[pa], 0, N(f"stL{pa}"))
            ctx.stt(h2n[pa][:], h2n[pa][:], 1.0, lng[:], ALU.mult, ALU.mult,
                    reads=(kh2, N("lng")), writes=(kh2,))
            ctx.tt(h2bf[pa][:], h2n[pa][:], lnb[:], ALU.add,
                   reads=(kh2, N("lnb")), writes=(kh2b,),
                   eng="vector")
            for c in range(8):
                ctx.tr(PST[0][:, c * 128:(c + 1) * 128],
                       h2bf[pa][:, c * 128:(c + 1) * 128],
                       reads=(kh2b,), wkey="psT0")
            ctx.cp(bass.AP(h2T, tt * 128, [[8 * 1024, 128], [1024, 8], [1, 128]]),
                   bass.AP(PST[0], 0, [[1024, 128], [128, 8], [1, 128]]),
                   reads=("psT0",), writes=(f"{pfx}h2T:{tt}",), eng="scalar")
        for i in range(NHC):
            pa = i % 2
            ka, kg = N(f"w1a{pa}"), N(f"w1g{pa}")
            p.dma("gpsimd", w1s[pa][:],
                  bass.AP(w1b, (l * 2 * NHC + i) * 128 * 1024,
                          [[1024, 128], [1, 1024]]),
                  reads=(f"w1b:{l}:{i}",), writes=(ka,))
            p.dma("gpsimd", w1s[2 + pa][:],
                  bass.AP(w1b, (l * 2 * NHC + NHC + i) * 128 * 1024,
                          [[1024, 128], [1, 1024]]),
                  reads=(f"w1b:{l}:{NHC + i}",), writes=(kg,))
            kpa, kpg = f"ps{4 + pa}", f"ps{pa}"
            for nn in range(2):
                psa, psg = PS[4 + pa], PS[pa]
                for c in range(8):
                    ctx.mm(psa[:], w1s[pa][:, c * 128:(c + 1) * 128],
                           h2T[:, c * 1024 + nn * 512:c * 1024 + (nn + 1) * 512],
                           reads=(ka,) + tuple(f"{pfx}h2T:{t_}"
                                               for t_ in range(8)),
                           wkey=kpa, start=(c == 0), stop=(c == 7))
                for c in range(8):
                    ctx.mm(psg[:], w1s[2 + pa][:, c * 128:(c + 1) * 128],
                           h2T[:, c * 1024 + nn * 512:c * 1024 + (nn + 1) * 512],
                           reads=(kg,) + tuple(f"{pfx}h2T:{t_}"
                                               for t_ in range(8)),
                           wkey=kpg, start=(c == 0), stop=(c == 7))
                ctx.act(gact[:, nn * 512:(nn + 1) * 512], psg[:], AF.Gelu,
                        reads=(kpg, N("b1c")), writes=(N("gact"),),
                        bias=b1c[:, NHC + i:NHC + i + 1])
                ctx.ts(abf[:, nn * 512:(nn + 1) * 512], psa[:],
                       b1c[:, i:i + 1], ALU.add, reads=(kpa, N("b1c")),
                       writes=(N("abf"),))
            ctx.tt(h4[:, i * 1024:(i + 1) * 1024], abf[:], gact[:], ALU.mult,
                   reads=(N("abf"), N("gact")), writes=(f"{pfx}h4:{i}",),
                   eng="vector")
        for tt in range(8):
            t = (blk % 2) * 8 + tt
            g = b * NT + t
            pa = tt % 2
            ky1, kx2 = N(f"y1l{pa}"), N(f"xl{pa}")
            p.dma("sync", y1l[pa][:], gr_ap(y1r, g),
                  reads=(f"y1r:{g}",), writes=(ky1,))
            if endinv:
                p.dma("sync", xl[pa][:], dram_row_ap(xr, g),
                      reads=(f"xr:{g}",), writes=(kx2,))
            pq = (2, 3) if tt % 2 == 0 else (4, 5)
            for n in range(2):
                for i in range(NHC):
                    ctx.mm(PS[pq[n]][:],
                           h4[:, i * 1024 + tt * 128:i * 1024 + (tt + 1) * 128],
                           w2s[:, i * 1024 + n * 512:i * 1024 + (n + 1) * 512],
                           reads=(f"{pfx}h4:{i}", N("w2s")),
                           wkey=f"ps{pq[n]}", start=(i == 0),
                           stop=(i == NHC - 1))
            for n in range(2):
                ctx.tt(yo[:, n * 512:(n + 1) * 512], PS[pq[n]][:],
                       y1l[pa][:, n * 512:(n + 1) * 512], ALU.add,
                       reads=(f"ps{pq[n]}", ky1), writes=(N("yo"),))
            ctx.tt(yo[:], yo[:], b2b[:], ALU.add, reads=(N("yo"), N("b2b")),
                   writes=(N("yo"),), eng="vector")
            if not endinv:
                p.dma("sync", gr_ap(yr, g), yo[:], reads=(N("yo"),),
                      writes=(f"yr:{g}",))
            else:
                ctx.tt(xl[pa][:], yo[:], xl[pa][:], ALU.add,
                       reads=(N("yo"), kx2), writes=(kx2,), eng="vector")
                wb_store(ctx, xl[pa][:], kx2, g, xr, yr, scr[pa], scr2[pa],
                         stats[pa], last, out_d, pa)


def pass_attn_b(ctx, dt_in, sb, xr, yr, wTd, last, out_d):
    p = ctx.p
    PS, PST = ctx.ps, ctx.psT

    wsl = [sb(f"B_w{i}", [128, 8 * 1024], BF16) for i in range(2)]
    cmask = sb("B_cmask", [128, 4 * 512], BF16)
    rtab = sb("B_rtab", [128, 2 * 1024], F32)
    QFM = sb("B_qfm", [128, 8 * 512], BF16)
    KFM = sb("B_kfm", [128, 8 * 2048], BF16)
    VTM = sb("B_vtm", [128, 16 * 1024], BF16)
    yl = [sb(f"B_yl{i}", [128, 1024], F32) for i in range(2)]
    ybf = sb("B_ybf", [128, 1024], BF16)
    yT = sb("B_yT", [128, 1024], BF16)
    t1r = sb("B_t1r", [128, 1024], F32)
    t2r = sb("B_t2r", [128, 1024], F32)
    qrot = sb("B_qrot", [128, 1024], BF16)
    krot = sb("B_krot", [128, 1024], BF16)
    kf = sb("B_kf", [128, 1024], F32)
    st2 = sb("B_st2", [128, 8], F32)
    sc = sb("B_sc", [128, 2048], F32)
    Pn = [sb(f"B_pn{i}", [128, 2048], BF16) for i in range(2)]
    PT = sb("B_ptb", [128, 16 * 512], BF16)
    oall = sb("B_oall", [128, 8 * 512], BF16)
    xt2 = [sb(f"B_xt2{i}", [128, 1024], F32) for i in range(2)]
    scr = sb("B_scr", [128, 1024], F32)
    scr2 = t2r
    stats = sb("B_stats", [128, 32], F32)

    p.dma("sync", bass.AP(cmask, 0, [[2048, 128], [512, 4], [1, 512]]),
          dt_in["c_cmask"].ap().rearrange("(m p) n -> p m n", p=128),
          writes=("cmask",))
    for ti, nm in enumerate(("c_ropeA", "c_ropeB")):
        p.dma("sync", bass.AP(rtab, ti * 1024, [[2048, 128], [64, 16], [1, 64]]),
              dt_in[nm].ap().rearrange("(t p) n -> p t n", p=128),
              writes=("Brtab",))

    wnames = ("bw_q", "bw_k", "bw_v", "bw_o")

    def wTload(slot, wi, key):
        nm = wnames[wi]
        for g2 in range(2):
            ap = bass.AP(wTd, wi * D * D + g2 * 4 * 128 * D,
                         [[D, 128], [D * 128, 4], [1, D]])
            dst = bass.AP(wsl[slot], g2 * 4 * 1024,
                          [[8 * 1024, 128], [1024, 4], [1, 1024]])
            p.dma("gpsimd", dst, ap,
                  reads=tuple(f"wTd:{nm}:{j}" for j in range(8)),
                  writes=(key,))

    for b in range(B_LOC):
        wTload(0, 1, "Bw0")
        wTload(1, 2, "Bw1")
        for t in range(NT):
            g = b * NT + t
            pa = t % 2
            kyl = f"Byl{pa}"
            p.dma("sync", yl[pa][:], gr_ap(yr, g), reads=(f"yr:{g}",),
                  writes=(kyl,))
            ctx.cp(ybf[:], yl[pa][:], reads=(kyl,), writes=("Bybf",))
            for c in range(8):
                ctx.tr(PST[0][:, c * 128:(c + 1) * 128],
                       ybf[:, c * 128:(c + 1) * 128], reads=("Bybf",),
                       wkey="psT0")
            ctx.cp(yT[:], PST[0][:], reads=("psT0",), writes=("ByT",))
            for n in range(2):
                for c in range(8):
                    ctx.mm(PS[n][:], yT[:, c * 128:(c + 1) * 128],
                           wsl[0][:, c * 1024 + n * 512:c * 1024 + (n + 1) * 512],
                           reads=("ByT", "Bw0"), wkey=f"ps{n}",
                           start=(c == 0), stop=(c == 7))
            for n in range(2):
                for c in range(8):
                    ctx.mm(PS[2 + n][:], yT[:, c * 128:(c + 1) * 128],
                           wsl[1][:, c * 1024 + n * 512:c * 1024 + (n + 1) * 512],
                           reads=("ByT", "Bw1"), wkey=f"ps{2 + n}",
                           start=(c == 0), stop=(c == 7))
            ctx.cp(kf[:, 0:512], PS[0][:], reads=("ps0",), writes=("Bkf",),
                   eng="scalar")
            ctx.cp(kf[:, 512:1024], PS[1][:], reads=("ps1",), writes=("Bkf",),
                   eng="scalar")
            brope(ctx, kf, "Bkf", t1r, t2r, krot, rtab, t, "Bkrot")
            ctx.cp(VTM[:, t * 1024:t * 1024 + 512], PS[2][:], reads=("ps2",),
                   writes=(f"BVTM:{t}:0",))
            ctx.cp(VTM[:, t * 1024 + 512:(t + 1) * 1024], PS[3][:],
                   reads=("ps3",), writes=(f"BVTM:{t}:1",), eng="scalar")
            for c in range(8):
                ctx.tr(PST[1][:, c * 128:(c + 1) * 128],
                       krot[:, c * 128:(c + 1) * 128], reads=("Bkrot",),
                       wkey="psT1")
            ctx.cp(bass.AP(KFM, t * 128, [[8 * 2048, 128], [2048, 8], [1, 128]]),
                   bass.AP(PST[1], 0, [[1024, 128], [128, 8], [1, 128]]),
                   reads=("psT1",), writes=(f"BKFM:{t}",))
        wTload(0, 0, "Bw0")
        wTload(1, 3, "Bw1")
        for qb in range(4):
            for tt in range(4):
                t = qb * 4 + tt
                g = b * NT + t
                pa = t % 2
                kyl = f"Byl{pa}"
                p.dma("sync", yl[pa][:], gr_ap(yr, g),
                      reads=(f"yr:{g}",), writes=(kyl,))
                ctx.cp(ybf[:], yl[pa][:], reads=(kyl,), writes=("Bybf",))
                for c in range(8):
                    ctx.tr(PST[0][:, c * 128:(c + 1) * 128],
                           ybf[:, c * 128:(c + 1) * 128], reads=("Bybf",),
                           wkey="psT0")
                ctx.cp(yT[:], PST[0][:], reads=("psT0",), writes=("ByT",))
                for n in range(2):
                    for c in range(8):
                        ctx.mm(PS[n][:], yT[:, c * 128:(c + 1) * 128],
                               wsl[0][:, c * 1024 + n * 512:
                                    c * 1024 + (n + 1) * 512],
                               reads=("ByT", "Bw0"), wkey=f"ps{n}",
                               start=(c == 0), stop=(c == 7))
                ctx.cp(kf[:, 0:512], PS[0][:], reads=("ps0",),
                       writes=("Bkf",), eng="scalar")
                ctx.cp(kf[:, 512:1024], PS[1][:], reads=("ps1",),
                       writes=("Bkf",), eng="scalar")
                brope(ctx, kf, "Bkf", t1r, t2r, qrot, rtab, t, "Bqrot")
                for c in range(8):
                    ctx.tr(PST[1][:, c * 128:(c + 1) * 128],
                           qrot[:, c * 128:(c + 1) * 128], reads=("Bqrot",),
                           wkey="psT1")
                ctx.cp(bass.AP(QFM, tt * 128,
                               [[8 * 512, 128], [512, 8], [1, 128]]),
                       bass.AP(PST[1], 0, [[1024, 128], [128, 8], [1, 128]]),
                       reads=("psT1",), writes=(f"BQFM:{tt}",))
            nch = qb + 1
            nkc = nch * 4

            def b2_sm1(gp, h2, qt):
                hb = h2 * 64
                sp = qt % 2
                for c in range(nch):
                    pb_ = 4 + (c % 2)
                    ctx.mm(PS[pb_][:],
                           QFM[hb:hb + 64, gp * 512 + qt * 128:
                               gp * 512 + (qt + 1) * 128],
                           KFM[hb:hb + 64, gp * 2048 + c * 512:
                               gp * 2048 + (c + 1) * 512],
                           reads=(f"BQFM:{qt}",) + tuple(
                               f"BKFM:{4 * c + i}" for i in range(4)),
                           wkey=f"ps{pb_}", start=True, stop=True)
                    if c == qb:
                        ctx.tt(sc[:, c * 512:(c + 1) * 512],
                               PS[pb_][:],
                               cmask[:, qt * 512:(qt + 1) * 512],
                               ALU.add, reads=(f"ps{pb_}", "cmask"),
                               writes=(f"Bsc:{c}",))
                    else:
                        ctx.cp(sc[:, c * 512:(c + 1) * 512],
                               PS[pb_][:], reads=(f"ps{pb_}",),
                               writes=(f"Bsc:{c}",), eng="scalar")
                ctx.red(st2[:, 4 + sp:5 + sp], sc[:, :nch * 512], ALU.max,
                        reads=tuple(f"Bsc:{c}" for c in range(nch)),
                        writes=(f"BstM{sp}",), negate=True)
                ctx.act(Pn[sp][:, :nch * 512], sc[:, :nch * 512], AF.Exp,
                        reads=(f"BstM{sp}",) + tuple(
                            f"Bsc:{c}" for c in range(nch)),
                        writes=(f"BPn{sp}", f"BstM{sp}"),
                        bias=st2[:, 4 + sp:5 + sp], accum=st2[:, 6 + sp:7 + sp])

            def b2_sm2pt(gp, h2, qt):
                sp = qt % 2
                ctx.recip(st2[:, 6 + sp:7 + sp], st2[:, 6 + sp:7 + sp],
                          reads=(f"BstM{sp}",), writes=(f"BstM{sp}",))
                ctx.ts(Pn[sp][:, :nch * 512], Pn[sp][:, :nch * 512],
                       st2[:, 6 + sp:7 + sp], ALU.mult,
                       reads=(f"BPn{sp}", f"BstM{sp}"),
                       writes=(f"BPn{sp}",), eng="vector")
                for kg in range((nkc + 7) // 8):
                    kcs = list(range(kg * 8, min(nkc, kg * 8 + 8)))
                    for kc in kcs:
                        ctx.tr(PST[1][:, (kc % 8) * 128:(kc % 8 + 1) * 128],
                               Pn[sp][:, kc * 128:(kc + 1) * 128],
                               reads=(f"BPn{sp}",), wkey="psT1")
                    nn = len(kcs)
                    ctx.cp(bass.AP(PT, kcs[0] * 512 + qt * 128,
                                   [[16 * 512, 128], [512, nn], [1, 128]]),
                           bass.AP(PST[1], 0,
                                   [[1024, 128], [128, nn], [1, 128]]),
                           reads=("psT1",),
                           writes=tuple(f"BPT:{kc}:{qt}" for kc in kcs),
                           eng="scalar")

            for gp in range(8):
                for h2 in range(2):
                    h = gp * 2 + h2
                    pend = None
                    for qt in range(4):
                        b2_sm1(gp, h2, qt)
                        if pend is not None:
                            b2_sm2pt(*pend)
                        pend = (gp, h2, qt)
                    b2_sm2pt(*pend)
                    psel = PS[0][0:64, :] if h2 == 0 else PS[0][64:128, :]
                    for kc in range(nkc):
                        ctx.mm(psel,
                               VTM[:, kc * 1024 + h * 64:
                                   kc * 1024 + (h + 1) * 64],
                               PT[:, kc * 512:(kc + 1) * 512],
                               reads=(f"BVTM:{kc}:{h // 8}",) + tuple(
                                   f"BPT:{kc}:{q_}" for q_ in range(4)),
                               wkey="ps0", start=(kc == 0),
                               stop=(kc == nkc - 1), skip=True)
                ctx.cp(oall[:, gp * 512:(gp + 1) * 512], PS[0][:],
                       reads=("ps0",), writes=(f"Boall:{gp}",))
            for qt in range(4):
                t = qb * 4 + qt
                g = b * NT + t
                pa = t % 2
                kx2 = f"Bxt2{pa}"
                p.dma("sync", xt2[pa][:], dram_row_ap(xr, g),
                      reads=(f"xr:{g}",), writes=(kx2,))
                for n in range(2):
                    for c in range(8):
                        ctx.mm(PS[2 + n][:],
                               oall[:, c * 512 + qt * 128:c * 512 + (qt + 1) * 128],
                               wsl[1][:, c * 1024 + n * 512:
                                    c * 1024 + (n + 1) * 512],
                               reads=(f"Boall:{c}", "Bw1"), wkey=f"ps{2 + n}",
                               start=(c == 0), stop=(c == 7))
                for n in range(2):
                    ctx.tt(xt2[pa][:, n * 512:(n + 1) * 512], PS[2 + n][:],
                           xt2[pa][:, n * 512:(n + 1) * 512], ALU.add,
                           reads=(f"ps{2 + n}", kx2), writes=(kx2,))
                wb_store(ctx, xt2[pa][:], kx2, g, xr, yr, scr, scr2, stats,
                         last, out_d)


def brope(ctx, src, ksrc, t1r, t2r, outb, rtab, t, kout):
    At = bass.AP(rtab, t * 64, [[2048, 128], [0, 16], [1, 64]])
    s3 = bass.AP(src, 0, [[1024, 128], [64, 16], [1, 64]])
    ctx.tt(bass.AP(t1r, 0, [[1024, 128], [64, 16], [1, 64]]), s3, At,
           ALU.mult, reads=(ksrc, "Brtab"), writes=("Brt1",))
    oe = bass.AP(t2r, 0, [[1024, 128], [64, 16], [2, 32]])
    pe = bass.AP(src, 1, [[1024, 128], [64, 16], [2, 32]])
    be = bass.AP(rtab, 1024 + t * 64, [[2048, 128], [0, 16], [2, 32]])
    ctx.tt(oe, pe, be, ALU.mult, reads=(ksrc, "Brtab"), writes=("Brt2",),
           eng="vector")
    oo = bass.AP(t2r, 1, [[1024, 128], [64, 16], [2, 32]])
    po = bass.AP(src, 0, [[1024, 128], [64, 16], [2, 32]])
    bo = bass.AP(rtab, 1024 + t * 64 + 1, [[2048, 128], [0, 16], [2, 32]])
    ctx.tt(oo, po, bo, ALU.mult, reads=(ksrc, "Brtab"), writes=("Brt2",),
           eng="vector")
    ctx.tt(outb[:], t1r[:], t2r[:], ALU.add, reads=("Brt1", "Brt2"),
           writes=(kout,), eng="vector")


def kernel(**inputs):
    consts = host_consts()
    x = np.ascontiguousarray(inputs["L0_attn_a_x"], dtype=np.float32)
    base = {
        "aw_normw": inputs["L0_attn_a_norm_w"],
        "aw_q": inputs["L0_attn_a_Wq"], "aw_k": inputs["L0_attn_a_Wk"],
        "aw_v": inputs["L0_attn_a_Wv"], "aw_o": inputs["L0_attn_a_Wo"],
        "aw_gate": np.reshape(inputs["L0_attn_a_gate"], [1]),
        "aw_lt": np.reshape(inputs["L0_attn_a_log_temperature"], [1]),
        "bw_q": inputs["L0_attn_b_Wq"], "bw_k": inputs["L0_attn_b_Wk"],
        "bw_v": inputs["L0_attn_b_Wv"], "bw_o": inputs["L0_attn_b_Wo"],
        "fw_qkv": inputs["L0_ffn_wqkv"], "fw_qs": inputs["L0_ffn_q_scale"],
        "fw_ks": inputs["L0_ffn_k_scale"], "fw_wo": inputs["L0_ffn_wo"],
        "fw_lng": inputs["L0_ffn_ln_g"], "fw_lnb": inputs["L0_ffn_ln_b"],
        "fw_w1": inputs["L0_ffn_w1"], "fw_b1": inputs["L0_ffn_b1"],
        "fw_w2": inputs["L0_ffn_w2"], "fw_b2": inputs["L0_ffn_b2"],
        "c_ident": consts["ident"], "c_cmask": consts["cmask"],
        "c_fmask": consts["fmask"], "c_ropeA": consts["ropeA"],
        "c_ropeB": consts["ropeB"], "c_xqA": consts["xqA"],
        "c_xqB": consts["xqB"], "c_xkA": consts["xkA"],
        "c_xkB": consts["xkB"], "c_zrow": consts["zrow"],
        "c_eps": consts["eps"],
    }
    base = {k: np.ascontiguousarray(v) for k, v in base.items()}
    nc = build(PASSES)
    in_maps = []
    for c in range(8):
        m = dict(base)
        m["x"] = np.ascontiguousarray(x[2 * c:2 * c + 2].reshape(B_LOC * S, D))
        in_maps.append(m)
    res = run_bass_kernel_spmd(nc, in_maps, core_ids=list(range(8)))
    if PASSES == "A1":
        kernel.dbg = [{k: np.asarray(v) for k, v in res.results[c].items()}
                      for c in range(8)]
        return np.zeros((16, S, D), np.float32)
    if PASSES in ("F1", "F1m"):
        kernel.dbg = [
            {k: np.asarray(res.results[c][f"dbg_{k}"])
             for k in ("xr", "yr", "y1r")} for c in range(8)]
        if PASSES == "F1":
            for c in range(8):
                for k in list(res.results[c].keys()):
                    if k.startswith("dbgC_"):
                        kernel.dbg[c][k] = np.asarray(res.results[c][k])
        return np.zeros((16, S, D), np.float32)
    outs = [np.asarray(res.results[c]["out"]).reshape(B_LOC, S, D)
            for c in range(8)]
    full = np.concatenate(outs, axis=0).astype(np.float32)
    if PASSES != "full":
        kernel.dbg = [
            {k: np.asarray(res.results[c][f"dbg_{k}"]) for k in ("xr", "yr")}
            for c in range(8)]
    return full


# version 07
# speedup vs baseline: 1.1737x; 1.1737x over previous
import numpy as np
import ml_dtypes
from contextlib import ExitStack
import concourse.bass as bass
import concourse.mybir as mybir
from concourse.bass_utils import run_bass_kernel_spmd

F32 = mybir.dt.float32
BF16 = mybir.dt.bfloat16
AF = mybir.ActivationFunctionType
ALU = mybir.AluOpType
AX = mybir.AxisListType

NEG = -1.0e30
B_LOC = 2
S = 2048
D = 1024
NT = 16
HID = 2730
NHC = 22

PASSES = "full"
FORCE_SYNC = False

ENGINES = ("sync", "tensor", "vector", "scalar", "gpsimd")
DMA_POOL = {"sync": 32, "tensor": 2, "vector": 2, "scalar": 8, "gpsimd": 24}


class Prog:
    def __init__(self):
        self.ops = {e: [] for e in ENGINES}
        self.writer = {}
        self.readers = {}
        self.fence_deps = ()

    def op(self, eng, fn, reads=(), writes=(), dma=False, tag="", accum=False):
        deps = set(self.fence_deps)
        for k in reads:
            for w in self.writer.get(k, {}).values():
                deps.add(w)
        for k in writes:
            for w in self.writer.get(k, {}).values():
                deps.add(w)
            for r in self.readers.get(k, ()):
                deps.add(r)
        idx = len(self.ops[eng])
        me = (eng, idx)
        deps.discard(me)
        self.ops[eng].append([fn, dma, deps, False, tag, accum])
        for k in writes:
            self.writer.setdefault(k, {})[eng] = me
            self.readers[k] = []
        for k in reads:
            self.readers.setdefault(k, []).append(me)

    def dma(self, queue, out, in_, reads=(), writes=(), tag="", partial=False,
            **kw):
        self.op(queue, (lambda q, o=out, i=in_, kw=kw: q.dma_start(o, i, **kw)),
                reads=reads, writes=writes, dma=True, tag=tag)

    def fence(self):
        self.fence_deps = tuple(
            (e, len(self.ops[e]) - 1) for e in ENGINES if self.ops[e])

    def finalize(self):
        need = {e: set() for e in ENGINES}
        for e in ENGINES:
            for i, o in enumerate(self.ops[e]):
                for (de, di) in o[2]:
                    dop = self.ops[de][di]
                    if de != e or dop[1] or dop[5] or FORCE_SYNC:
                        need[de].add(di)
        self.mnum = {}
        self.dma_final = {}
        for e in ENGINES:
            c = 0
            d = 0
            K = DMA_POOL[e]
            for i, o in enumerate(self.ops[e]):
                if o[1]:
                    s = d % K
                    r = d // K + 1
                    self.mnum[(e, i)] = ("s", s, r)
                    self.dma_final[(e, s)] = r
                    d += 1
                elif i in need[e]:
                    c += 1
                    o[3] = True
                    self.mnum[(e, i)] = ("c", c)

    def emit_engine(self, q, e, sems):
        wm = {}
        for i, (fn, dma, deps, ms, tag, accum) in enumerate(self.ops[e]):
            req = {}
            for (de, di) in deps:
                dop = self.ops[de][di]
                if de == e and not dop[1] and not dop[5] and not FORCE_SYNC:
                    continue
                mn = self.mnum[(de, di)]
                if mn[0] == "c":
                    kk = (de, "c")
                    if mn[1] > req.get(kk, 0):
                        req[kk] = mn[1]
                else:
                    kk = (de, "s", mn[1])
                    if mn[2] > req.get(kk, 0):
                        req[kk] = mn[2]
            for kk in sorted(req):
                val = req[kk]
                if wm.get(kk, 0) >= val:
                    continue
                if kk[1] == "c":
                    q.wait_ge(sems[kk], val)
                else:
                    q.wait_ge(sems[kk], val * 16)
                wm[kk] = val
            if dma:
                kind, s, r = self.mnum[(e, i)]
                kk = (e, "s", s)
                if r > 1 and wm.get(kk, 0) < r - 1:
                    q.wait_ge(sems[kk], 16 * (r - 1))
                    wm[kk] = r - 1
            inst = fn(q)
            if dma:
                kind, s, r = self.mnum[(e, i)]
                inst.then_inc(sems[(e, "s", s)], 16)
            elif ms:
                inst.then_inc(sems[(e, "c")], 1)
        if e == "sync":
            for (qe, s), r in sorted(self.dma_final.items()):
                kk = (qe, "s", s)
                if wm.get(kk, 0) < r:
                    q.wait_ge(sems[kk], 16 * r)


def host_consts():
    c = {}
    c["ident"] = np.eye(128, dtype=ml_dtypes.bfloat16)
    cm = np.zeros((4, 128, 512), dtype=np.float32)
    for d_ in range(4):
        i = np.arange(128)[:, None]
        j = np.arange(512)[None, :]
        cm[d_] = np.where(j <= d_ * 128 + i, 0.0, NEG)
    c["cmask"] = cm.astype(ml_dtypes.bfloat16).reshape(4 * 128, 512)
    fm = np.zeros((2, 4, 128, 1024), dtype=np.float32)
    for first in range(2):
        for jt in range(4):
            pp = (128 * jt + np.arange(128))[:, None]
            m = np.arange(1024)[None, :]
            ok = (m >= pp) & (m <= pp + 512)
            if first:
                ok &= m >= 512
            fm[first, jt] = np.where(ok, 0.0, NEG)
    c["fmask"] = fm.astype(ml_dtypes.bfloat16).reshape(8 * 128, 1024)
    hd = 64
    pos = np.arange(S, dtype=np.float64)
    inv = 1.0 / (10000.0 ** (np.arange(0, hd, 2, dtype=np.float64) / hd))
    fr = np.concatenate([np.outer(pos, inv), np.outer(pos, inv)], -1)
    cosb, sinb = np.cos(fr), np.sin(fr)
    Bt = sinb.copy()
    Bt[:, 0::2] = -sinb[:, 0::2]
    c["ropeA"] = cosb.astype(np.float32)
    c["ropeB"] = Bt.astype(np.float32)
    half = np.arange(0, hd, 2, dtype=np.float64)
    invf = 1.0 / (10000.0 ** (half / hd))
    fr2 = np.outer(pos, invf)
    fr2 = np.concatenate([fr2, fr2], -1)
    base = (half + 0.4 * hd) / (1.4 * hd)
    power = (pos - S // 2) / 256.0
    scale = base[None, :] ** power[:, None]
    scale = np.concatenate([scale, scale], -1)
    cosx, sinx = np.cos(fr2), np.sin(fr2)
    sgn = np.ones(hd)
    sgn[:32] = -1.0
    c["xqA"] = (cosx * scale * 8.0).astype(np.float32)
    c["xqB"] = (sgn[None, :] * sinx * scale * 8.0).astype(np.float32)
    c["xkA"] = (cosx / scale).astype(np.float32)
    c["xkB"] = (sgn[None, :] * sinx / scale).astype(np.float32)
    c["zrow"] = np.zeros((1, 1024), dtype=ml_dtypes.bfloat16)
    c["eps"] = np.array([1e-5, 1e-6], dtype=np.float32)
    return c


def _is_small(out):
    try:
        return out.free_size() <= 64
    except Exception:
        return False


class Ctx:
    def __init__(self, nc, prog):
        self.nc = nc
        self.p = prog
        self.ident = None
        self.ps = []
        self.psT = []
        self.pro_steps = []

    def step_prologue(self, n=1):
        for _ in range(n):
            if self.pro_steps:
                self.pro_steps.pop(0)()

    def mm(self, ps_ap, lhsT_ap, rhs_ap, reads, wkey, start, stop, skip=False):
        self.p.op("tensor",
                  lambda q, o=ps_ap, a=lhsT_ap, b=rhs_ap, s1=start, s2=stop,
                  sk=skip: q.matmul(o, a, b, start=s1, stop=s2,
                                    skip_group_check=sk),
                  reads=reads, writes=(wkey,), tag="mm")

    def tr(self, ps_ap, in_ap, reads, wkey):
        self.p.op("tensor",
                  lambda q, o=ps_ap, i=in_ap, idn=self.ident:
                  q.transpose(o, i, idn[:]),
                  reads=reads + ("ident",), writes=(wkey,), tag="tr")

    def act(self, out, in_, func, reads, writes, bias=None, scale=None,
            accum=None):
        def f(q, o=out, i=in_, fu=func, b=bias, s=scale, a=accum):
            kw = {}
            if b is not None:
                kw["bias"] = b
            if s is not None:
                kw["scale"] = s
            if a is not None:
                kw["accum_out"] = a
            return q.activation(o, i, fu, **kw)
        self.p.op("scalar", f, reads=reads, writes=writes, tag="act",
                  accum=(accum is not None) or _is_small(out))

    def tt(self, out, a, b, op, reads, writes, eng="vector"):
        self.p.op(eng,
                  lambda q, o=out, x=a, y=b, p_=op: q.tensor_tensor(o, x, y, p_),
                  reads=reads, writes=writes, tag="tt", accum=_is_small(out))

    def ts(self, out, a, s1, op0, reads, writes, s2=None, op1=None, accum=None,
           eng="vector"):
        def f(q, o=out, x=a, s1=s1, s2=s2, o0=op0, o1=op1, ac=accum):
            kw = {}
            if ac is not None:
                kw["accum_out"] = ac
            if o1 is None:
                return q.tensor_scalar(o, x, s1, s2, o0, **kw)
            return q.tensor_scalar(o, x, s1, s2, o0, o1, **kw)
        self.p.op(eng, f, reads=reads, writes=writes, tag="ts",
                  accum=(accum is not None) or _is_small(out))

    def ttr(self, out, a, b, op0, op1, accum, reads, writes, scale=1.0,
            seed=0.0):
        self.p.op("vector",
                  lambda q, o=out, x=a, y=b, s=scale, sd=seed, o0=op0, o1=op1,
                  ac=accum: q.tensor_tensor_reduce(o, x, y, s, sd, o0, o1, ac),
                  reads=reads, writes=writes, tag="ttr")

    def stt(self, out, a, s, b, op0, op1, reads, writes, eng="vector"):
        self.p.op(eng,
                  lambda q, o=out, x=a, sc=s, y=b, o0=op0, o1=op1:
                  q.scalar_tensor_tensor(o, x, sc, y, o0, o1),
                  reads=reads, writes=writes, tag="stt", accum=_is_small(out))

    def cp(self, out, in_, reads, writes, eng="vector"):
        if eng == "scalar":
            self.p.op(eng, lambda q, o=out, i=in_: q.copy(o, i),
                      reads=reads, writes=writes, tag="cp",
                      accum=_is_small(out))
        else:
            self.p.op(eng, lambda q, o=out, i=in_: q.tensor_copy(o, i),
                      reads=reads, writes=writes, tag="cp",
                      accum=_is_small(out))

    def red(self, out, in_, op, reads, writes, negate=False):
        self.p.op("vector",
                  lambda q, o=out, i=in_, p_=op, n=negate:
                  q.tensor_reduce(o, i, axis=AX.X, op=p_, negate=n),
                  reads=reads, writes=writes, tag="red")

    def recip(self, out, in_, reads, writes):
        self.p.op("vector", lambda q, o=out, i=in_: q.reciprocal(o, i),
                  reads=reads, writes=writes, tag="recip",
                  accum=_is_small(out))

    def rsqrt(self, out, in_, reads, writes, scale=None, bias=None, minv=None):
        self.act(out, in_, AF.Ln, reads, writes, bias=bias, scale=scale)
        self.act(out, out, AF.Exp, writes, writes, scale=-0.5)
        if minv is not None:
            self.ts(out, out, minv, ALU.min, writes, writes)

    def bcast(self, dst, src_t, n, writes, offset=0, queue="sync"):
        ap = bass.AP(src_t, offset, [[0, 128], [1, n]])
        self.p.dma(queue, dst, ap, writes=writes)

    def ln_tile(self, x_ap, out_ap, kx, kout, scr, scr2, stats, sl, kst):
        st = lambda i: stats[:, sl + i: sl + i + 1]
        self.act(scr[:], x_ap, AF.Square, reads=(kx,), writes=(kst + "sA", kst),
                 accum=st(0))
        self.ts(scr2[:], x_ap, 0.0, ALU.add, reads=(kx,), writes=(kst + "sB", kst),
                s2=0.0, op1=ALU.add, accum=st(1))
        self.ts(st(2), st(1), 1.0 / D, ALU.mult, reads=(kst,), writes=(kst,))
        self.tt(st(5), st(2), st(2), ALU.mult, reads=(kst,), writes=(kst,))
        self.stt(st(3), st(0), 1.0 / D, st(5), ALU.mult, ALU.subtract,
                 reads=(kst,), writes=(kst,))
        self.rsqrt(st(4), st(3), reads=(kst, "epsc"), writes=(kst,),
                   bias=self.eps5)
        self.ts(out_ap, x_ap, st(2), ALU.subtract, reads=(kx, kst),
                writes=(kout,), s2=st(4), op1=ALU.mult)


def dram_row_ap(t, g, width=D):
    return bass.AP(t, g * 128 * width, [[width, 128], [1, width]])


def grow(g):
    return (g // NT) * (S + 128) + 128 + (g % NT) * 128


def gr_ap(t, g, width=D):
    return bass.AP(t, grow(g) * width, [[width, 128], [1, width]])


def gr_shift_ap(t, g):
    return bass.AP(t, (grow(g) - 1) * D + 512, [[D, 128], [1, 512]])


def build(passes=PASSES):
    nc = bass.Bass("TRN2", target_bir_lowering=False)
    prog = Prog()
    ctx = Ctx(nc, prog)
    p = prog

    dt_in = {}

    def din(name, shape, dtype=F32):
        dt_in[name] = nc.dram_tensor(name, list(shape), dtype,
                                     kind="ExternalInput")
        return dt_in[name]

    x_in = din("x", [B_LOC * S, D])
    din("aw_normw", [D]); din("aw_q", [D, D]); din("aw_k", [D, 512])
    din("aw_v", [D, 512]); din("aw_o", [D, D]); din("aw_gate", [1])
    din("aw_lt", [1])
    din("bw_q", [D, D]); din("bw_k", [D, D]); din("bw_v", [D, D])
    din("bw_o", [D, D])
    din("fw_qkv", [2, D, 1536]); din("fw_qs", [2, 64]); din("fw_ks", [2, 64])
    din("fw_wo", [2, 512, D]); din("fw_lng", [2, D]); din("fw_lnb", [2, D])
    din("fw_w1", [2, D, 2 * HID]); din("fw_b1", [2, 2 * HID])
    din("fw_w2", [2, HID, D]); din("fw_b2", [2, D])
    din("c_ident", [128, 128], BF16)
    din("c_cmask", [4 * 128, 512], BF16)
    din("c_fmask", [8 * 128, 1024], BF16)
    for nm in ("c_ropeA", "c_ropeB", "c_xqA", "c_xqB", "c_xkA", "c_xkB"):
        din(nm, [S, 64])
    din("c_zrow", [1, 1024], BF16)
    din("c_eps", [2])

    out_d = nc.dram_tensor("out", [B_LOC * S, D], F32, kind="ExternalOutput")
    dbg = {}
    if passes == "F1":
        ctx.dbg_c1 = {}
        for nm, shape, dt_ in (("Qw", [128, 4 * 512], BF16),
                               ("Kw", [128, 4 * 512], BF16),
                               ("Vw", [128, 4 * 512], BF16),
                               ("oall", [128, 4 * 512], BF16),
                               ("sc", [128, 1024], F32),
                               ("pn", [128, 1024], BF16),
                               ("qrot", [128, 512], BF16),
                               ("krot", [128, 512], BF16),
                               ("qf", [128, 512], F32),
                               ("kf", [128, 512], F32),
                               ("hbf", [128, 1024], BF16),
                               ("yl0", [128, 1024], F32),
                               ("qf_raw", [128, 512], F32),
                               ("kf_raw", [128, 512], F32),
                               ("qf_n", [128, 512], F32),
                               ("qrot0", [128, 512], BF16),
                               ("fac0", [128, 32], F32),
                               ("hbf0", [128, 1024], BF16),
                               ("hT0", [128, 1024], BF16)):
            ctx.dbg_c1[nm] = nc.dram_tensor("dbgC_" + nm, shape, dt_,
                                            kind="ExternalOutput")
    if passes == "A1":
        ctx.dbg_a1 = {}
        for nm, shape, dt_ in (("kfm", [128, 4 * 2048], BF16),
                               ("vtm", [128, 16 * 512], BF16),
                               ("qfm", [128, 8 * 512], BF16),
                               ("qn", [128, 1024], BF16),
                               ("kn", [128, 512], BF16),
                               ("qf", [128, 1024], F32),
                               ("hbf", [128, 1024], BF16),
                               ("sc", [128, 2048], F32),
                               ("pn", [128, 2048], BF16),
                               ("oall", [128, 8 * 512], BF16)):
            ctx.dbg_a1[nm] = nc.dram_tensor("dbgA_" + nm, shape, dt_,
                                            kind="ExternalOutput")
    if passes != "full":
        dbg["xr"] = nc.dram_tensor("dbg_xr", [B_LOC * S, D], F32,
                                   kind="ExternalOutput")
        dbg["yr"] = nc.dram_tensor("dbg_yr", [B_LOC * S, D], F32,
                                   kind="ExternalOutput")
        dbg["y1r"] = nc.dram_tensor("dbg_y1r", [B_LOC * S, D], F32,
                                    kind="ExternalOutput")

    xr = nc.dram_tensor("xr", [B_LOC * S, D], F32)
    yr = nc.dram_tensor("yr", [B_LOC * (S + 128), D], F32)
    y1r = nc.dram_tensor("y1r", [B_LOC * (S + 128), D], F32)
    wTd = nc.dram_tensor("wTd", [4, D, D], BF16)
    w1b = nc.dram_tensor("w1b", [2, 2 * NHC, 128, 1024], BF16)

    es = ExitStack()
    ident = es.enter_context(nc.sbuf_tensor("ident", [128, 128], BF16))
    identf = es.enter_context(nc.sbuf_tensor("identf", [128, 128], F32))
    scal = es.enter_context(nc.sbuf_tensor("scal", [128, 8], F32))
    ctx.ident = ident
    ctx.ps = [es.enter_context(nc.psum_tensor(f"ps{i}", [128, 512], F32))
              for i in range(6)]
    ctx.psT = [es.enter_context(nc.psum_tensor(f"psT{i}", [128, 1024], BF16))
               for i in range(2)]

    p.dma("sync", ident[:], dt_in["c_ident"].ap(), writes=("ident",))
    ctx.cp(identf[:], ident[:], reads=("ident",), writes=("identf",))
    ctx.identf = identf
    ctx.bcast(scal[:, 0:1], dt_in["aw_gate"], 1, ("gate",))
    ctx.bcast(scal[:, 1:2], dt_in["aw_lt"], 1, ("temp",))
    ctx.act(scal[:, 1:2], scal[:, 1:2], AF.Exp, reads=("temp",),
            writes=("temp",))
    p.dma("sync", scal[:, 2:4],
          bass.AP(dt_in["c_eps"], 0, [[0, 128], [1, 2]]), writes=("epsc",))
    ctx.eps5 = scal[:, 2:3]
    ctx.eps6 = scal[:, 3:4]

    for tns, knm in ((yr, "yr"), (y1r, "y1r")):
        for b in range(B_LOC):
            p.dma("gpsimd",
                  bass.AP(tns, b * (S + 128) * D, [[D, 128], [1, D]]),
                  bass.AP(dt_in["c_zrow"], 0, [[0, 128], [1, 1024]]),
                  writes=(f"{knm}guard:{b}",))

    make_prologue(ctx, dt_in, wTd, w1b, es)

    def scoped(fn, *args, **kw):
        st = ExitStack()
        sb = lambda name, shape, dtype: st.enter_context(
            nc.sbuf_tensor(name, shape, dtype))
        fn(ctx, dt_in, sb, *args, **kw)
        st.close()
        prog.fence()

    ncut = {"A": 1, "A1": 1, "F1": 2, "F1m": 3, "AF0": 5, "AF0B": 6,
            "full": 10}[passes]
    seq = [
        lambda last: scoped(pass_attn_a, x_in, xr, yr, scal, last=last,
                            out_d=out_d),
        lambda last: scoped(pass_ffn_attn, xr, yr, y1r, inv=0, l=0),
        lambda last: scoped(pass_ffn_mlp, xr, yr, y1r, w1b, inv=0, l=0,
                            last=last, out_d=out_d),
        lambda last: scoped(pass_ffn_attn, xr, yr, y1r, inv=0, l=1),
        lambda last: scoped(pass_ffn_mlp, xr, yr, y1r, w1b, inv=0, l=1,
                            last=last, out_d=out_d),
        lambda last: scoped(pass_attn_b, xr, yr, wTd, last=last, out_d=out_d),
        lambda last: scoped(pass_ffn_attn, xr, yr, y1r, inv=1, l=0),
        lambda last: scoped(pass_ffn_mlp, xr, yr, y1r, w1b, inv=1, l=0,
                            last=last, out_d=out_d),
        lambda last: scoped(pass_ffn_attn, xr, yr, y1r, inv=1, l=1),
        lambda last: scoped(pass_ffn_mlp, xr, yr, y1r, w1b, inv=1, l=1,
                            last=last, out_d=out_d),
    ]
    for i in range(ncut):
        seq[i](i == ncut - 1)

    if passes != "full" and passes != "A1":
        for g in range(B_LOC * NT):
            p.dma("sync", dram_row_ap(dbg["xr"], g), dram_row_ap(xr, g),
                  reads=(f"xr:{g}",))
            p.dma("sync", dram_row_ap(dbg["yr"], g), gr_ap(yr, g),
                  reads=(f"yr:{g}",))
            if passes in ("F1", "F1m"):
                p.dma("sync", dram_row_ap(dbg["y1r"], g),
                      gr_ap(y1r, g), reads=(f"y1r:{g}",))

    prog.finalize()
    sems = {}
    for e in ENGINES:
        sems[(e, "c")] = es.enter_context(nc.semaphore(f"s_{e}_c"))
        for s in range(DMA_POOL[e]):
            sems[(e, "s", s)] = es.enter_context(
                nc.semaphore(f"s_{e}_d{s}"))
    with nc.Block() as blk:
        for e in ENGINES:
            if prog.ops[e] or e == "sync":
                getattr(blk, e)(lambda q, ee=e: prog.emit_engine(q, ee, sems))
    es.close()
    return nc


def make_prologue(ctx, dt_in, wTd, w1b, es):
    nc = ctx.nc
    p = ctx.p
    stg = es.enter_context(nc.sbuf_tensor("pro_stg", [128, 1024], BF16))
    stg2 = es.enter_context(nc.sbuf_tensor("pro_stg2", [128, 1024], BF16))
    steps = ctx.pro_steps

    def w1_step(l, hc):
        def go():
            half, hcl = divmod(hc, NHC)
            col0 = half * HID + hcl * 128
            w = min(128, HID - hcl * 128)
            if w < 128:
                p.dma("gpsimd", stg[:],
                      bass.AP(dt_in["c_zrow"], 0, [[0, 128], [1, 1024]]),
                      reads=("pstg",), writes=("pstg",))
            ap = bass.AP(dt_in["fw_w1"], l * D * 2 * HID + col0,
                         [[2 * HID, 128], [2 * HID * 128, 8], [1, w]])
            dst = bass.AP(stg, 0, [[1024, 128], [128, 8], [1, w]])
            p.dma("gpsimd", dst, ap, reads=("pstg",), writes=("pstg",))
            p.dma("sync",
                  bass.AP(w1b, (l * 2 * NHC + hc) * 128 * 1024,
                          [[1024, 128], [1, 1024]]),
                  stg[:], reads=("pstg",), writes=(f"w1b:{l}:{hc}",))
        return go

    for l in range(2):
        for hc in range(2 * NHC):
            steps.append(w1_step(l, hc))

    def wt_step(wi, nm, i):
        def go():
            ap = bass.AP(dt_in[nm], i * 128 * D, [[D, 128], [1, 1024]])
            p.dma("gpsimd", stg2[:], ap, reads=("pstg2",), writes=("pstg2",))
            if nm == "bw_q":
                ctx.ts(stg2[:], stg2[:], 0.125, ALU.mult, reads=("pstg2",),
                       writes=("pstg2",))
            for jh in range(2):
                pk = f"psT{jh}"
                for j4 in range(4):
                    j = jh * 4 + j4
                    ctx.tr(ctx.psT[jh][:, j4 * 128:(j4 + 1) * 128],
                           stg2[:, j * 128:(j + 1) * 128],
                           reads=("pstg2",), wkey=pk)
            ctx.cp(stg[:, 0:512], ctx.psT[0][:, 0:512], reads=("psT0",),
                   writes=("pstg",))
            ctx.cp(stg[:, 512:1024], ctx.psT[1][:, 0:512], reads=("psT1",),
                   writes=("pstg2e",))
            p.dma("sync",
                  bass.AP(wTd, wi * D * D + i * 128,
                          [[D, 128], [D * 128, 4], [1, 128]]),
                  bass.AP(stg, 0, [[1024, 128], [128, 4], [1, 128]]),
                  reads=("pstg",), writes=(f"wTd:{nm}:{i}",))
            p.dma("sync",
                  bass.AP(wTd, wi * D * D + 4 * 128 * D + i * 128,
                          [[D, 128], [D * 128, 4], [1, 128]]),
                  bass.AP(stg, 512, [[1024, 128], [128, 4], [1, 128]]),
                  reads=("pstg2e",), writes=(f"wTd:{nm}:{i}",))
        return go

    for wi, nm in enumerate(["bw_q", "bw_k", "bw_v", "bw_o"]):
        for i in range(8):
            steps.append(wt_step(wi, nm, i))


def pass_attn_a(ctx, dt_in, sb, x_in, xr, yr, scal, last, out_d):
    p = ctx.p
    PS, PST = ctx.ps, ctx.psT

    wq_s = sb("A_wq", [128, 8 * 1024], BF16)
    wk_s = sb("A_wk", [128, 8 * 512], BF16)
    wv_s = sb("A_wv", [128, 8 * 512], BF16)
    wo_s = sb("A_wo", [128, 8 * 1024], BF16)
    normw = sb("A_normw", [128, 1024], F32)
    cmask = sb("A_cmask", [128, 4 * 512], BF16)
    QFM = sb("A_qfm", [128, 8 * 512], BF16)
    KFM = sb("A_kfm", [128, 4 * 2048], BF16)
    VTM = sb("A_vtm", [128, 16 * 512], BF16)
    xin = [sb(f"A_xin{i}", [128, 1024], F32) for i in range(2)]
    ybuf = [sb(f"A_y{i}", [128, 1024], F32) for i in range(2)]
    hbf = [sb(f"A_hbf{i}", [128, 1024], BF16) for i in range(2)]
    hT = sb("A_hT", [128, 1024], BF16)
    qf = sb("A_qf", [128, 1024], F32)
    kf = sb("A_kf", [128, 512], F32)
    qn = sb("A_qn", [128, 1024], BF16)
    kn = sb("A_kn", [128, 512], BF16)
    scr = [sb(f"A_scr{i}", [128, 1024], F32) for i in range(2)]
    scr2 = [sb(f"A_scr2{i}", [128, 1024], F32) for i in range(2)]
    stats = [sb(f"A_stats{i}", [128, 32], F32) for i in range(2)]
    st2 = [sb(f"A_st2{i}", [128, 8], F32) for i in range(2)]
    sc = sb("A_sc", [128, 2048], F32)
    Pn = [sb(f"A_pn{i}", [128, 2048], BF16) for i in range(2)]
    PT = sb("A_ptb", [128, 16 * 512], BF16)
    oall = sb("A_oall", [128, 8 * 512], BF16)
    xt2 = [sb("A_xt2", [128, 1024], F32)] * 2
    yt2 = [sb(f"A_yt2{i}", [128, 1024], F32) for i in range(2)]
    t1 = sb("A_t1", [128, 1024], F32)

    def wload(dst, name, cols, key):
        t = dt_in[name]
        for g2 in range(2):
            ap = bass.AP(t, g2 * 4 * 128 * cols,
                         [[cols, 128], [cols * 128, 4], [1, cols]])
            dstap = bass.AP(dst, g2 * 4 * cols,
                            [[8 * cols, 128], [cols, 4], [1, cols]])
            p.dma("gpsimd", dstap, ap, writes=(key,))

    wload(wq_s, "aw_q", 1024, "wq")
    wload(wk_s, "aw_k", 512, "wk")
    wload(wv_s, "aw_v", 512, "wv")
    wload(wo_s, "aw_o", 1024, "wo")
    ctx.bcast(normw[:], dt_in["aw_normw"], 1024, ("normw",))
    p.dma("sync", bass.AP(cmask, 0, [[2048, 128], [512, 4], [1, 512]]),
          dt_in["c_cmask"].ap().rearrange("(m p) n -> p m n", p=128),
          writes=("cmask",))

    for b in range(B_LOC):
        for qb in range(4):
            if getattr(ctx, "dbg_a1", None) is not None and (b, qb) == (0, 1):
                dbgd = ctx.dbg_a1
                for nm, buf in (("kfm", KFM), ("vtm", VTM), ("qfm", QFM),
                                ("qn", qn), ("kn", kn), ("qf", qf),
                                ("hbf", hbf[1]), ("sc", sc), ("pn", Pn),
                                ("oall", oall)):
                    p.dma("sync", dbgd[nm].ap(), buf[:],
                          reads=tuple(ctx.p.writer.keys()), writes=())
                return
            for tt in range(4):
                t = qb * 4 + tt
                g = b * NT + t
                pa = t % 2
                ctx.step_prologue(4)
                kx, ky = f"xin{pa}", f"ybuf{pa}"
                p.dma("sync", xin[pa][:], dram_row_ap(x_in, g), writes=(kx,))
                ctx.ln_tile(xin[pa][:], ybuf[pa][:], kx, ky, scr[pa],
                            scr2[pa], stats[pa], 0, f"stL{pa}")
                p.dma("sync", gr_ap(yr, g), ybuf[pa][:], reads=(ky,),
                      writes=(f"yr:{g}",))
                ctx.act(scr[pa][:], ybuf[pa][:], AF.Square, reads=(ky,),
                        writes=(f"scrA{pa}", f"stR{pa}"), scale=0.03125,
                        accum=stats[pa][:, 8:9])
                ctx.rsqrt(stats[pa][:, 9:10], stats[pa][:, 8:9],
                          reads=(f"stR{pa}", "epsc"), writes=(f"stR{pa}",),
                          bias=ctx.eps6)
                kh = f"hbf{pa}"
                ctx.stt(hbf[pa][:], ybuf[pa][:], stats[pa][:, 9:10], normw[:],
                        ALU.mult, ALU.mult, reads=(ky, f"stR{pa}", "normw"),
                        writes=(kh,))
                for c in range(8):
                    ctx.tr(PST[0][:, c * 128:(c + 1) * 128],
                           hbf[pa][:, c * 128:(c + 1) * 128],
                           reads=(kh,), wkey="psT0")
                ctx.cp(hT[:], PST[0][:], reads=("psT0",), writes=("hT",))
                for n in range(2):
                    for c in range(8):
                        ctx.mm(PS[n][:], hT[:, c * 128:(c + 1) * 128],
                               wq_s[:, c * 1024 + n * 512:c * 1024 + (n + 1) * 512],
                               reads=("hT", "wq"), wkey=f"ps{n}",
                               start=(c == 0), stop=(c == 7))
                for c in range(8):
                    ctx.mm(PS[2][:], hT[:, c * 128:(c + 1) * 128],
                           wk_s[:, c * 512:(c + 1) * 512],
                           reads=("hT", "wk"), wkey="ps2",
                           start=(c == 0), stop=(c == 7))
                for c in range(8):
                    ctx.mm(PS[3][:], hT[:, c * 128:(c + 1) * 128],
                           wv_s[:, c * 512:(c + 1) * 512],
                           reads=("hT", "wv"), wkey="ps3",
                           start=(c == 0), stop=(c == 7))
                ctx.cp(qf[:, 0:512], PS[0][:], reads=("ps0",), writes=("qf",),
                       eng="scalar")
                ctx.cp(qf[:, 512:1024], PS[1][:], reads=("ps1",),
                       writes=("qf",), eng="scalar")
                ctx.cp(kf[:], PS[2][:], reads=("ps2",), writes=("kf",),
                       eng="scalar")
                ctx.cp(VTM[:, t * 512:(t + 1) * 512], PS[3][:], reads=("ps3",),
                       writes=(f"VTM:{t}",))
                ctx.act(scr[pa][:], qf[:], AF.Square, reads=("qf",),
                        writes=(f"scrA{pa}",))
                ctx.act(scr2[pa][:, 0:512], kf[:], AF.Square, reads=("kf",),
                        writes=(f"scrB{pa}",))
                ctx.red(stats[pa][:, 16:20],
                        bass.AP(scr[pa], 0, [[1024, 128], [256, 4], [1, 256]]),
                        ALU.add, reads=(f"scrA{pa}",), writes=(f"stQ{pa}",))
                ctx.red(stats[pa][:, 20:22],
                        bass.AP(scr2[pa], 0, [[1024, 128], [256, 2], [1, 256]]),
                        ALU.add, reads=(f"scrB{pa}",), writes=(f"stQ{pa}",))
                ctx.rsqrt(stats[pa][:, 24:28], stats[pa][:, 16:20],
                          reads=(f"stQ{pa}",), writes=(f"stF{pa}",), minv=1e12)
                ctx.rsqrt(stats[pa][:, 28:30], stats[pa][:, 20:22],
                          reads=(f"stQ{pa}",), writes=(f"stF{pa}",), minv=1e12)
                ctx.ts(stats[pa][:, 24:28], stats[pa][:, 24:28], scal[:, 1:2],
                       ALU.mult, reads=(f"stF{pa}", "temp"),
                       writes=(f"stF{pa}",))
                ctx.tt(bass.AP(qn, 0, [[1024, 128], [256, 4], [1, 256]]),
                       bass.AP(qf, 0, [[1024, 128], [256, 4], [1, 256]]),
                       bass.AP(stats[pa], 24, [[32, 128], [1, 4], [0, 256]]),
                       ALU.mult, reads=("qf", f"stF{pa}"), writes=("qn",))
                ctx.tt(bass.AP(kn, 0, [[512, 128], [256, 2], [1, 256]]),
                       bass.AP(kf, 0, [[512, 128], [256, 2], [1, 256]]),
                       bass.AP(stats[pa], 28, [[32, 128], [1, 2], [0, 256]]),
                       ALU.mult, reads=("kf", f"stF{pa}"), writes=("kn",),
                       eng="vector")
                for c in range(8):
                    ctx.tr(PST[1][:, c * 128:(c + 1) * 128],
                           qn[:, c * 128:(c + 1) * 128],
                           reads=("qn",), wkey="psT1")
                ctx.cp(bass.AP(QFM, tt * 128,
                               [[8 * 512, 128], [512, 8], [1, 128]]),
                       bass.AP(PST[1], 0, [[1024, 128], [128, 8], [1, 128]]),
                       reads=("psT1",), writes=(f"QFM:{tt}",))
                for c in range(4):
                    ctx.tr(PST[0][:, c * 128:(c + 1) * 128],
                           kn[:, c * 128:(c + 1) * 128],
                           reads=("kn",), wkey="psT0")
                ctx.cp(bass.AP(KFM, t * 128,
                               [[4 * 2048, 128], [2048, 4], [1, 128]]),
                       bass.AP(PST[0], 0, [[1024, 128], [128, 4], [1, 128]]),
                       reads=("psT0",), writes=(f"KFM:{t}",), eng="scalar")
            nch = qb + 1
            nkc = nch * 4

            def a2_sm1(h, qt):
                kvh = h // 2
                kq = f"QFM:{qt}"
                sp = qt % 2
                for c in range(nch):
                    pb_ = 4 + (c % 2)
                    for cc in range(2):
                        ctx.mm(PS[pb_][:],
                               QFM[:, (2 * h + cc) * 512 + qt * 128:
                                   (2 * h + cc) * 512 + (qt + 1) * 128],
                               KFM[:, (2 * kvh + cc) * 2048 + c * 512:
                                   (2 * kvh + cc) * 2048 + (c + 1) * 512],
                               reads=(kq,) + tuple(
                                   f"KFM:{4 * c + i}" for i in range(4)),
                               wkey=f"ps{pb_}", start=(cc == 0),
                               stop=(cc == 1))
                    if c == qb:
                        ctx.tt(sc[:, c * 512:(c + 1) * 512], PS[pb_][:],
                               cmask[:, qt * 512:(qt + 1) * 512], ALU.add,
                               reads=(f"ps{pb_}", "cmask"),
                               writes=(f"sc:{c}",))
                    else:
                        ctx.cp(sc[:, c * 512:(c + 1) * 512], PS[pb_][:],
                               reads=(f"ps{pb_}",), writes=(f"sc:{c}",),
                               eng="scalar")
                ctx.red(st2[sp][:, 4:5], sc[:, :nch * 512], ALU.max,
                        reads=tuple(f"sc:{c}" for c in range(nch)),
                        writes=(f"stM{sp}",), negate=True)
                ctx.act(Pn[sp][:, :nch * 512], sc[:, :nch * 512], AF.Exp,
                        reads=(f"stM{sp}",) + tuple(
                            f"sc:{c}" for c in range(nch)),
                        writes=(f"Pn{sp}", f"stM{sp}"),
                        bias=st2[sp][:, 4:5], accum=st2[sp][:, 5:6])

            def a2_sm2pt(h, qt):
                sp = qt % 2
                ctx.recip(st2[sp][:, 5:6], st2[sp][:, 5:6],
                          reads=(f"stM{sp}",), writes=(f"stM{sp}",))
                ctx.ts(Pn[sp][:, :nch * 512], Pn[sp][:, :nch * 512],
                       st2[sp][:, 5:6], ALU.mult,
                       reads=(f"Pn{sp}", f"stM{sp}"), writes=(f"Pn{sp}",),
                       eng="vector")
                for kg in range((nkc + 7) // 8):
                    kcs = list(range(kg * 8, min(nkc, kg * 8 + 8)))
                    for kc in kcs:
                        ctx.tr(PST[1][:, (kc % 8) * 128:(kc % 8 + 1) * 128],
                               Pn[sp][:, kc * 128:(kc + 1) * 128],
                               reads=(f"Pn{sp}",), wkey="psT1")
                    nn = len(kcs)
                    ctx.cp(bass.AP(PT, kcs[0] * 512 + qt * 128,
                                   [[16 * 512, 128], [512, nn], [1, 128]]),
                           bass.AP(PST[1], 0,
                                   [[1024, 128], [128, nn], [1, 128]]),
                           reads=("psT1",),
                           writes=tuple(f"PT:{kc}:{qt}" for kc in kcs),
                           eng="scalar")

            pend = None
            for h in range(4):
                kvh = h // 2
                for qt in range(4):
                    a2_sm1(h, qt)
                    if pend is not None:
                        a2_sm2pt(*pend)
                    pend = (h, qt)
                a2_sm2pt(*pend)
                pend = None
                for vc in range(2):
                    for kc in range(nkc):
                        ctx.mm(PS[vc][:],
                               VTM[:, kc * 512 + kvh * 256 + vc * 128:
                                   kc * 512 + kvh * 256 + (vc + 1) * 128],
                               PT[:, kc * 512:(kc + 1) * 512],
                               reads=(f"VTM:{kc}",) + tuple(
                                   f"PT:{kc}:{q_}" for q_ in range(4)),
                               wkey=f"ps{vc}", start=(kc == 0),
                               stop=(kc == nkc - 1))
                    ctx.cp(oall[:, (2 * h + vc) * 512:(2 * h + vc + 1) * 512],
                           PS[vc][:], reads=(f"ps{vc}",),
                           writes=(f"oall:{2 * h + vc}",),
                           eng=("vector" if vc == 0 else "scalar"))
            for qt in range(4):
                t = qb * 4 + qt
                g = b * NT + t
                pa = t % 2
                kx2, ky2 = "xt2s", f"yt2{pa}"
                p.dma("sync", xt2[pa][:], dram_row_ap(x_in, g), writes=(kx2,))
                p.dma("sync", yt2[pa][:], gr_ap(yr, g),
                      reads=(f"yr:{g}",), writes=(ky2,))
                for n in range(2):
                    for c in range(8):
                        ctx.mm(PS[2 + n][:],
                               oall[:, c * 512 + qt * 128:c * 512 + (qt + 1) * 128],
                               wo_s[:, c * 1024 + n * 512:c * 1024 + (n + 1) * 512],
                               reads=(f"oall:{c}", "wo"), wkey=f"ps{2 + n}",
                               start=(c == 0), stop=(c == 7))
                for n in range(2):
                    ctx.stt(t1[:, n * 512:(n + 1) * 512], PS[2 + n][:],
                            scal[:, 0:1], yt2[pa][:, n * 512:(n + 1) * 512],
                            ALU.mult, ALU.add,
                            reads=(f"ps{2 + n}", "gate", ky2), writes=("t1",))
                ctx.tt(xt2[pa][:], t1[:], xt2[pa][:], ALU.add,
                       reads=("t1", kx2), writes=(kx2,), eng="vector")
                wb_store(ctx, xt2[pa][:], kx2, g, xr, yr, scr[pa], scr2[pa],
                         stats[pa], last, out_d, pa)


def wb_store(ctx, x1_ap, kx1, g, xr, yr, scr, scr2, stats, last, out_d,
             pa=0):
    p = ctx.p
    if last:
        p.dma("sync", dram_row_ap(out_d, g), x1_ap, reads=(kx1,),
              writes=(f"out:{g}",))
        p.dma("sync", dram_row_ap(xr, g), x1_ap, reads=(kx1,),
              writes=(f"xr:{g}",))
        return
    p.dma("sync", dram_row_ap(xr, g), x1_ap, reads=(kx1,), writes=(f"xr:{g}",))
    ctx.ln_tile(x1_ap, scr[:], kx1, f"scrA{pa}w", scr, scr2, stats, 10,
                f"stW{pa}")
    p.dma("sync", gr_ap(yr, g), scr[:], reads=(f"scrA{pa}w",),
          writes=(f"yr:{g}",))


def load_shifted(ctx, dst, src_t, g, t, key, dt_in, kpre):
    p = ctx.p
    b = g // NT
    p.dma("sync", dst[:, 0:512],
          bass.AP(src_t, grow(g) * D, [[D, 128], [1, 512]]),
          reads=(f"{kpre}:{g}",), writes=(key,))
    prev = (f"{kpre}guard:{b}",) if t == 0 else (f"{kpre}:{g - 1}",)
    p.dma("sync", dst[:, 512:1024], gr_shift_ap(src_t, g),
          reads=prev + (f"{kpre}:{g}",), writes=(key,))


def pass_ffn_attn(ctx, dt_in, sb, xr, yr, y1r, inv, l):
    p = ctx.p
    PS, PST = ctx.ps, ctx.psT
    pfx = f"F{inv}{l}a"
    N = lambda s: pfx + s

    wqkv = sb(N("wqkv"), [128, 8 * 1536], BF16)
    wo_s = sb(N("wo"), [128, 4 * 1024], BF16)
    fmask = sb(N("fmask"), [128, 8 * 1024], BF16)
    tabb = sb(N("tabb"), [128, 4 * 1024], F32)
    tabl = sb(N("tabl"), [128, 4 * 1024], F32)
    qsb = sb(N("qsb"), [128, 128], F32)
    Kw = [sb(N(f"K{i}"), [128, 4 * 512], BF16) for i in range(2)]
    Vw = [sb(N(f"V{i}"), [128, 4 * 512], BF16) for i in range(2)]
    Qw = sb(N("Q"), [128, 4 * 512], BF16)
    yl = [sb(N(f"yl{i}"), [128, 1024], F32) for i in range(2)]
    hbf = [sb(N(f"hbf{i}"), [128, 1024], BF16) for i in range(2)]
    hT = [sb(N(f"hT{i}"), [128, 1024], BF16) for i in range(2)]
    qf = [sb(N(f"qf{i}"), [128, 512], F32) for i in range(2)]
    kf = [sb(N(f"kf{i}"), [128, 512], F32) for i in range(2)]
    t1r = [sb(N(f"t1r{i}"), [128, 512], F32) for i in range(2)]
    t2r = [sb(N(f"t2r{i}"), [128, 512], F32) for i in range(2)]
    qrot = [sb(N(f"qrot{i}"), [128, 512], BF16) for i in range(2)]
    krot = [sb(N(f"krot{i}"), [128, 512], BF16) for i in range(2)]
    scr = [sb(N(f"scr{i}"), [128, 1024], F32) for i in range(2)]
    stats = [sb(N(f"stats{i}"), [128, 32], F32) for i in range(2)]
    st2 = [sb(N(f"st2{i}"), [128, 8], F32) for i in range(4)]
    sc = [sb(N(f"sc{i}"), [128, 1024], F32) for i in range(4)]
    Pn = [sb(N(f"pn{i}"), [128, 1024], BF16) for i in range(8)]
    PT = sb(N("pt"), [128, 8 * 512], BF16)
    oall = sb(N("oall"), [128, 4 * 512], BF16)
    yl2 = [sb(N(f"yl2{i}"), [128, 1024], F32) for i in range(2)]
    y1o = sb(N("y1o"), [128, 1024], F32)

    for g2 in range(4):
        ap = bass.AP(dt_in["fw_qkv"], l * D * 1536 + g2 * 2 * 128 * 1536,
                     [[1536, 128], [1536 * 128, 2], [1, 1536]])
        dst = bass.AP(wqkv, g2 * 2 * 1536,
                      [[8 * 1536, 128], [1536, 2], [1, 1536]])
        p.dma("gpsimd", dst, ap, writes=(N("wqkv"),))
    ap = bass.AP(dt_in["fw_wo"], l * 512 * D, [[D, 128], [D * 128, 4], [1, D]])
    p.dma("gpsimd", bass.AP(wo_s, 0, [[4 * 1024, 128], [1024, 4], [1, 1024]]),
          ap, writes=(N("wo"),))
    p.dma("sync", bass.AP(fmask, 0, [[8 * 1024, 128], [1024, 8], [1, 1024]]),
          dt_in["c_fmask"].ap().rearrange("(m p) n -> p m n", p=128),
          writes=(N("fmask"),))
    for ti, nm in enumerate(("c_xqA", "c_xqB", "c_xkA", "c_xkB")):
        p.dma("sync", bass.AP(tabb, ti * 1024, [[4096, 128], [64, 16], [1, 64]]),
              dt_in[nm].ap().rearrange("(t p) n -> p t n", p=128),
              writes=(N("tabb"),))
    ctx.bcast(qsb[:, 0:64], dt_in["fw_qs"], 64, (N("qsb"),), offset=l * 64)
    ctx.bcast(qsb[:, 64:128], dt_in["fw_ks"], 64, (N("qsb"),), offset=l * 64)
    for ti in range(4):
        qcol = 0 if ti < 2 else 64
        swap = (ti % 2 == 1)
        out3 = bass.AP(tabl, ti * 1024, [[4096, 128], [64, 16], [32, 2], [1, 32]])
        in3 = bass.AP(tabb, ti * 1024, [[4096, 128], [64, 16], [32, 2], [1, 32]])
        if swap:
            sc3 = bass.AP(qsb, qcol + 32, [[128, 128], [0, 16], [-32, 2], [1, 32]])
        else:
            sc3 = bass.AP(qsb, qcol, [[128, 128], [0, 16], [32, 2], [1, 32]])
        ctx.tt(out3, in3, sc3, ALU.mult, reads=(N("tabb"), N("qsb")),
               writes=(N("tabl"),))

    for b in range(B_LOC):
        p.dma("sync", Kw[1][:],
              bass.AP(dt_in["c_zrow"], 0, [[0, 128], [0, 2], [1, 1024]]),
              writes=(N("K1"),))
        p.dma("sync", Vw[1][:],
              bass.AP(dt_in["c_zrow"], 0, [[0, 128], [0, 2], [1, 1024]]),
              writes=(N("V1"),))
        cur, prv = 0, 1
        for wi in range(4):
            kK, kV = N(f"K{cur}"), N(f"V{cur}")
            pK, pV = N(f"K{prv}"), N(f"V{prv}")
            for tt in range(4):
                t = wi * 4 + tt
                g = b * NT + t
                pa = tt % 2
                kyl = N(f"yl{pa}")
                khb, kht = N(f"hbf{pa}"), N(f"hT{pa}")
                kqf, kkf = N(f"qf{pa}"), N(f"kf{pa}")
                kqr, kkr = N(f"qrot{pa}"), N(f"krot{pa}")
                load_shifted(ctx, yl[pa], yr, g, t, kyl, dt_in, "yr")
                ctx.cp(hbf[pa][:], yl[pa][:], reads=(kyl,), writes=(khb,))
                for c in range(8):
                    ctx.tr(PST[0][:, c * 128:(c + 1) * 128],
                           hbf[pa][:, c * 128:(c + 1) * 128],
                           reads=(khb,), wkey="psT0")
                ctx.cp(hT[pa][:], PST[0][:], reads=("psT0",), writes=(kht,))
                for o3 in range(3):
                    for c in range(8):
                        ctx.mm(PS[o3][:], hT[pa][:, c * 128:(c + 1) * 128],
                               wqkv[:, c * 1536 + o3 * 512:
                                    c * 1536 + (o3 + 1) * 512],
                               reads=(kht, N("wqkv")), wkey=f"ps{o3}",
                               start=(c == 0), stop=(c == 7))
                ctx.cp(qf[pa][:], PS[0][:], reads=("ps0",), writes=(kqf,),
                       eng="scalar")
                ctx.cp(kf[pa][:], PS[1][:], reads=("ps1",), writes=(kkf,),
                       eng="scalar")
                ctx.cp(Vw[cur][:, tt * 512:(tt + 1) * 512], PS[2][:],
                       reads=("ps2",), writes=(kV,))
                ctx.act(scr[pa][:, 0:512], qf[pa][:], AF.Square,
                        reads=(kqf,), writes=(N(f"scrA{pa}"),))
                ctx.act(scr[pa][:, 512:1024], kf[pa][:], AF.Square,
                        reads=(kkf,), writes=(N(f"scrA{pa}"),))
                ctx.red(stats[pa][:, 0:8],
                        bass.AP(scr[pa], 0, [[1024, 128], [64, 8], [1, 64]]),
                        ALU.add, reads=(N(f"scrA{pa}"),),
                        writes=(N(f"stQ{pa}"),))
                ctx.red(stats[pa][:, 8:16],
                        bass.AP(scr[pa], 512, [[1024, 128], [64, 8], [1, 64]]),
                        ALU.add, reads=(N(f"scrA{pa}"),),
                        writes=(N(f"stQ{pa}"),))
                ctx.rsqrt(stats[pa][:, 16:24], stats[pa][:, 0:8],
                          reads=(N(f"stQ{pa}"),), writes=(N(f"stF{pa}"),),
                          minv=1e12)
                ctx.rsqrt(stats[pa][:, 24:32], stats[pa][:, 8:16],
                          reads=(N(f"stQ{pa}"),), writes=(N(f"stF{pa}"),),
                          minv=1e12)
                ctx.tt(bass.AP(qf[pa], 0, [[512, 128], [64, 8], [1, 64]]),
                       bass.AP(qf[pa], 0, [[512, 128], [64, 8], [1, 64]]),
                       bass.AP(stats[pa], 16, [[32, 128], [1, 8], [0, 64]]),
                       ALU.mult, reads=(kqf, N(f"stF{pa}")), writes=(kqf,))
                ctx.tt(bass.AP(kf[pa], 0, [[512, 128], [64, 8], [1, 64]]),
                       bass.AP(kf[pa], 0, [[512, 128], [64, 8], [1, 64]]),
                       bass.AP(stats[pa], 24, [[32, 128], [1, 8], [0, 64]]),
                       ALU.mult, reads=(kkf, N(f"stF{pa}")), writes=(kkf,),
                       eng="vector")
                xrope(ctx, qf[pa], t1r[pa], t2r[pa], qrot[pa], tabl, 0, t,
                      kqf, kqr, N("tabl"), eng="vector", pfx2=N(f"r{pa}"))
                xrope(ctx, kf[pa], t1r[pa], t2r[pa], krot[pa], tabl, 2, t,
                      kkf, kkr, N("tabl"), eng="vector", pfx2=N(f"r{pa}"))
                for c in range(4):
                    ctx.tr(PST[1][:, c * 128:(c + 1) * 128],
                           qrot[pa][:, c * 128:(c + 1) * 128],
                           reads=(kqr,), wkey="psT1")
                ctx.cp(bass.AP(Qw, tt * 128, [[4 * 512, 128], [512, 4], [1, 128]]),
                       bass.AP(PST[1], 0, [[1024, 128], [128, 4], [1, 128]]),
                       reads=("psT1",), writes=(N("Q"),))
                for c in range(4):
                    ctx.tr(PST[0][:, c * 128:(c + 1) * 128],
                           krot[pa][:, c * 128:(c + 1) * 128],
                           reads=(kkr,), wkey="psT0")
                ctx.cp(bass.AP(Kw[cur], tt * 128,
                               [[4 * 512, 128], [512, 4], [1, 128]]),
                       bass.AP(PST[0], 0, [[1024, 128], [128, 4], [1, 128]]),
                       reads=("psT0",), writes=(kK,), eng="scalar")
            def c2_sm1(h, qt):
                hb = (h % 2) * 64
                gp = h // 2
                sp = qt
                pnn = (h % 2) * 4 + qt
                ksc, kpn, kstm = (N(f"sc{sp}"), N(f"pn{pnn}"), N(f"stM{sp}"))
                for ci in range(2):
                    kb = Kw[prv] if ci == 0 else Kw[cur]
                    kkey = pK if ci == 0 else kK
                    pb_ = 4 + ci
                    ctx.mm(PS[pb_][:],
                           Qw[hb:hb + 64, gp * 512 + qt * 128:
                              gp * 512 + (qt + 1) * 128],
                           kb[hb:hb + 64, gp * 512:(gp + 1) * 512],
                           reads=(N("Q"), kkey), wkey=f"ps{pb_}",
                           start=True, stop=True)
                    mrow = (4 if wi == 0 else 0) + qt
                    ctx.tt(sc[sp][:, ci * 512:(ci + 1) * 512], PS[pb_][:],
                           fmask[:, mrow * 1024 + ci * 512:
                                 mrow * 1024 + (ci + 1) * 512],
                           ALU.add, reads=(f"ps{pb_}", N("fmask")),
                           writes=(ksc,))
                ctx.red(st2[sp][:, 4:5], sc[sp][:], ALU.max,
                        reads=(ksc,), writes=(kstm,), negate=True)
                ctx.act(Pn[pnn][:], sc[sp][:], AF.Exp, reads=(ksc, kstm),
                        writes=(kpn, kstm), bias=st2[sp][:, 4:5],
                        accum=st2[sp][:, 5:6])

            def c2_sm2(h, qt):
                sp = qt
                pnn = (h % 2) * 4 + qt
                kpn, kstm = N(f"pn{pnn}"), N(f"stM{sp}")
                ctx.recip(st2[sp][:, 5:6], st2[sp][:, 5:6], reads=(kstm,),
                          writes=(kstm,))
                ctx.ts(Pn[pnn][:], Pn[pnn][:], st2[sp][:, 5:6], ALU.mult,
                       reads=(kpn, kstm), writes=(kpn,),
                       eng="vector")

            def c2_softmax(h):
                for qt in range(4):
                    c2_sm1(h, qt)
                    if ctx.sm_pend is not None:
                        c2_sm2(*ctx.sm_pend)
                    ctx.sm_pend = (h, qt)

            def c2_pv(h):
                hb = (h % 2) * 64
                gp = h // 2
                for qt in range(4):
                    pnn = (h % 2) * 4 + qt
                    kpn = N(f"pn{pnn}")
                    for kc in range(8):
                        ctx.tr(PST[1][:, kc * 128:(kc + 1) * 128],
                               Pn[pnn][:, kc * 128:(kc + 1) * 128],
                               reads=(kpn,), wkey="psT1")
                    ctx.cp(bass.AP(PT, qt * 128,
                                   [[8 * 512, 128], [512, 8], [1, 128]]),
                           bass.AP(PST[1], 0, [[1024, 128], [128, 8], [1, 128]]),
                           reads=("psT1",),
                           writes=tuple(f"{pfx}PT:{kc}:{qt}"
                                        for kc in range(8)), eng="scalar")
                psel = PS[0][0:64, :] if h % 2 == 0 else PS[0][64:128, :]
                for kc in range(8):
                    vb = Vw[prv] if kc < 4 else Vw[cur]
                    vkey = pV if kc < 4 else kV
                    ctx.mm(psel,
                           vb[:, (kc % 4) * 512 + h * 64:
                              (kc % 4) * 512 + (h + 1) * 64],
                           PT[:, kc * 512:(kc + 1) * 512],
                           reads=(vkey,) + tuple(
                               f"{pfx}PT:{kc}:{q_}" for q_ in range(4)),
                           wkey="ps0", start=(kc == 0), stop=(kc == 7),
                           skip=True)
                if h % 2 == 1:
                    ctx.cp(oall[:, gp * 512:(gp + 1) * 512], PS[0][:],
                           reads=("ps0",), writes=(f"{pfx}oall:{gp}",),
                           eng="scalar")

            ctx.sm_pend = None
            c2_softmax(0)
            for h in range(8):
                if h + 1 < 8:
                    c2_softmax(h + 1)
                else:
                    c2_sm2(*ctx.sm_pend)
                    ctx.sm_pend = None
                c2_pv(h)
            if getattr(ctx, "dbg_c1", None) is not None and (b, wi) == (0, 0)                     and inv == 0 and l == 0:
                dbgd = ctx.dbg_c1
                for nm, buf in (("Qw", Qw), ("Kw", Kw[cur]), ("Vw", Vw[cur]),
                                ("oall", oall), ("sc", sc[0]), ("pn", Pn[0]),
                                ("qrot", qrot[0]), ("krot", krot[0]),
                                ("qf", qf[0]), ("kf", kf[0]),
                                ("hbf", hbf[0])):
                    p.dma("sync", dbgd[nm].ap(), buf[:],
                          reads=tuple(ctx.p.writer.keys()), writes=())
            for qt in range(4):
                t = wi * 4 + qt
                g = b * NT + t
                pa = qt % 2
                ky2 = N(f"yl2{pa}")
                p.dma("sync", yl2[pa][:], gr_ap(yr, g),
                      reads=(f"yr:{g}",), writes=(ky2,))
                for n in range(2):
                    for c in range(4):
                        ctx.mm(PS[2 + n][:],
                               oall[:, c * 512 + qt * 128:c * 512 + (qt + 1) * 128],
                               wo_s[:, c * 1024 + n * 512:c * 1024 + (n + 1) * 512],
                               reads=(f"{pfx}oall:{c}", N("wo")),
                               wkey=f"ps{2 + n}", start=(c == 0), stop=(c == 3))
                for n in range(2):
                    ctx.tt(y1o[:, n * 512:(n + 1) * 512], PS[2 + n][:],
                           yl2[pa][:, n * 512:(n + 1) * 512], ALU.add,
                           reads=(f"ps{2 + n}", ky2), writes=(N("y1o"),))
                p.dma("sync", gr_ap(y1r, g), y1o[:], reads=(N("y1o"),),
                      writes=(f"y1r:{g}",))
            cur, prv = prv, cur


def xrope(ctx, src, t1r, t2r, outb, tabl, ti0, t, ksrc, kout, ktab,
          eng="vector", pfx2="r"):
    At = bass.AP(tabl, ti0 * 1024 + t * 64, [[4096, 128], [0, 8], [1, 64]])
    s3 = bass.AP(src, 0, [[512, 128], [64, 8], [1, 64]])
    ctx.tt(bass.AP(t1r, 0, [[512, 128], [64, 8], [1, 64]]), s3, At, ALU.mult,
           reads=(ksrc, ktab), writes=(pfx2 + "t1",), eng=eng)
    o2 = bass.AP(t2r, 0, [[512, 128], [64, 8], [32, 2], [1, 32]])
    prt = bass.AP(src, 32, [[512, 128], [64, 8], [-32, 2], [1, 32]])
    Bt = bass.AP(tabl, (ti0 + 1) * 1024 + t * 64,
                 [[4096, 128], [0, 8], [32, 2], [1, 32]])
    ctx.tt(o2, prt, Bt, ALU.mult, reads=(ksrc, ktab), writes=(pfx2 + "t2",),
           eng=eng)
    ctx.tt(outb[:], t1r[:], t2r[:], ALU.add,
           reads=(pfx2 + "t1", pfx2 + "t2"), writes=(kout,), eng=eng)


def pass_ffn_mlp(ctx, dt_in, sb, xr, yr, y1r, w1b, inv, l, last, out_d):
    p = ctx.p
    PS, PST = ctx.ps, ctx.psT
    endinv = (l == 1)
    pfx = f"F{inv}{l}m"
    N = lambda s: pfx + s

    lng = sb(N("lng"), [128, 1024], F32)
    lnb = sb(N("lnb"), [128, 1024], F32)
    b2b = sb(N("b2b"), [128, 1024], F32)
    b1c = sb(N("b1c"), [128, 2 * NHC], F32)
    h2T = sb(N("h2T"), [128, 8 * 1024], BF16)
    h4 = sb(N("h4"), [128, NHC * 1024], BF16)
    w2s = sb(N("w2s"), [128, NHC * 1024], BF16)
    w1s = [sb(N(f"w1s{i}"), [128, 1024], BF16) for i in range(4)]
    gact = sb(N("gact"), [128, 1024], BF16)
    abf = sb(N("abf"), [128, 1024], BF16)
    h2bf = [sb(N(f"h2bf{i}"), [128, 1024], BF16) for i in range(2)]
    hl = [sb(N(f"hl{i}"), [128, 1024], F32) for i in range(2)]
    h2n = [sb(N(f"h2n{i}"), [128, 1024], F32) for i in range(2)]
    scr = [sb(N(f"scr{i}"), [128, 1024], F32) for i in range(2)]
    scr2 = [sb(N(f"scr2{i}"), [128, 1024], F32) for i in range(2)]
    stats = [sb(N(f"stats{i}"), [128, 32], F32) for i in range(2)]
    y1l = [sb(N(f"y1l{i}"), [128, 1024], F32) for i in range(2)]
    xl = [sb(N(f"xl{i}"), [128, 1024], F32) for i in range(2)]
    yo = sb(N("yo"), [128, 1024], F32)

    ctx.bcast(lng[:], dt_in["fw_lng"], 1024, (N("lng"),), offset=l * D)
    ctx.bcast(lnb[:], dt_in["fw_lnb"], 1024, (N("lnb"),), offset=l * D)
    ctx.bcast(b2b[:], dt_in["fw_b2"], 1024, (N("b2b"),), offset=l * D)
    for half in range(2):
        base = l * 2 * HID + half * HID
        p.dma("gpsimd", scr2[0][0:22, 0:128],
              bass.AP(dt_in["c_zrow"], 0, [[0, 22], [1, 128]]),
              reads=("scrB0",), writes=("scrB0",))
        p.dma("sync", scr2[0][0:21, 0:128],
              bass.AP(dt_in["fw_b1"], base, [[128, 21], [1, 128]]),
              reads=("scrB0",), writes=("scrB0",))
        p.dma("sync", scr2[0][21:22, 0:42],
              bass.AP(dt_in["fw_b1"], base + 21 * 128, [[42, 1], [1, 42]]),
              reads=("scrB0",), writes=("scrB0",))
        ctx.p.op("tensor",
                 lambda q, o=PS[5][:, 0:22], i=scr2[0][0:22, 0:128],
                 idn=ctx.identf: q.transpose(o, i, idn[0:22, 0:22]),
                 reads=("scrB0", "identf"), writes=("ps5",), tag="tr")
        ctx.cp(b1c[:, half * NHC:(half + 1) * NHC], PS[5][:, 0:22],
               reads=("ps5",), writes=(N("b1c"),))

    for blk in range(4):
        b = blk // 2
        if blk == 0:
            for i in range(NHC):
                w = min(128, HID - i * 128)
                if w < 128:
                    p.dma("gpsimd", w2s[:, i * 1024:(i + 1) * 1024],
                          bass.AP(dt_in["c_zrow"], 0, [[0, 128], [1, 1024]]),
                          reads=(N("w2s"),), writes=(N("w2s"),))
                p.dma("gpsimd", w2s[0:w, i * 1024:(i + 1) * 1024],
                      bass.AP(dt_in["fw_w2"], l * HID * D + i * 128 * D,
                              [[D, w], [1, D]]),
                      reads=(N("w2s"),), writes=(N("w2s"),),
                      partial=(w < 128))
        for tt in range(8):
            t = (blk % 2) * 8 + tt
            g = b * NT + t
            pa = tt % 2
            khl = N(f"hl{pa}")
            kh2, kh2b = N(f"h2n{pa}"), N(f"h2bf{pa}")
            load_shifted(ctx, hl[pa], y1r, g, t, khl, dt_in, "y1r")
            ctx.ln_tile(hl[pa][:], h2n[pa][:], khl, kh2, scr[pa], scr2[pa],
                        stats[pa], 0, N(f"stL{pa}"))
            ctx.stt(h2n[pa][:], h2n[pa][:], 1.0, lng[:], ALU.mult, ALU.mult,
                    reads=(kh2, N("lng")), writes=(kh2,))
            ctx.tt(h2bf[pa][:], h2n[pa][:], lnb[:], ALU.add,
                   reads=(kh2, N("lnb")), writes=(kh2b,),
                   eng="vector")
            for c in range(8):
                ctx.tr(PST[0][:, c * 128:(c + 1) * 128],
                       h2bf[pa][:, c * 128:(c + 1) * 128],
                       reads=(kh2b,), wkey="psT0")
            ctx.cp(bass.AP(h2T, tt * 128, [[8 * 1024, 128], [1024, 8], [1, 128]]),
                   bass.AP(PST[0], 0, [[1024, 128], [128, 8], [1, 128]]),
                   reads=("psT0",), writes=(f"{pfx}h2T:{tt}",), eng="scalar")
        for i in range(NHC):
            pa = i % 2
            ka, kg = N(f"w1a{pa}"), N(f"w1g{pa}")
            p.dma("gpsimd", w1s[pa][:],
                  bass.AP(w1b, (l * 2 * NHC + i) * 128 * 1024,
                          [[1024, 128], [1, 1024]]),
                  reads=(f"w1b:{l}:{i}",), writes=(ka,))
            p.dma("gpsimd", w1s[2 + pa][:],
                  bass.AP(w1b, (l * 2 * NHC + NHC + i) * 128 * 1024,
                          [[1024, 128], [1, 1024]]),
                  reads=(f"w1b:{l}:{NHC + i}",), writes=(kg,))
            kpa, kpg = f"ps{4 + pa}", f"ps{pa}"
            for nn in range(2):
                psa, psg = PS[4 + pa], PS[pa]
                for c in range(8):
                    ctx.mm(psa[:], w1s[pa][:, c * 128:(c + 1) * 128],
                           h2T[:, c * 1024 + nn * 512:c * 1024 + (nn + 1) * 512],
                           reads=(ka,) + tuple(f"{pfx}h2T:{t_}"
                                               for t_ in range(8)),
                           wkey=kpa, start=(c == 0), stop=(c == 7))
                for c in range(8):
                    ctx.mm(psg[:], w1s[2 + pa][:, c * 128:(c + 1) * 128],
                           h2T[:, c * 1024 + nn * 512:c * 1024 + (nn + 1) * 512],
                           reads=(kg,) + tuple(f"{pfx}h2T:{t_}"
                                               for t_ in range(8)),
                           wkey=kpg, start=(c == 0), stop=(c == 7))
                ctx.act(gact[:, nn * 512:(nn + 1) * 512], psg[:], AF.Gelu,
                        reads=(kpg, N("b1c")), writes=(N("gact"),),
                        bias=b1c[:, NHC + i:NHC + i + 1])
                ctx.ts(abf[:, nn * 512:(nn + 1) * 512], psa[:],
                       b1c[:, i:i + 1], ALU.add, reads=(kpa, N("b1c")),
                       writes=(N("abf"),))
            ctx.tt(h4[:, i * 1024:(i + 1) * 1024], abf[:], gact[:], ALU.mult,
                   reads=(N("abf"), N("gact")), writes=(f"{pfx}h4:{i}",),
                   eng="vector")
        for tt in range(8):
            t = (blk % 2) * 8 + tt
            g = b * NT + t
            pa = tt % 2
            ky1, kx2 = N(f"y1l{pa}"), N(f"xl{pa}")
            p.dma("sync", y1l[pa][:], gr_ap(y1r, g),
                  reads=(f"y1r:{g}",), writes=(ky1,))
            if endinv:
                p.dma("sync", xl[pa][:], dram_row_ap(xr, g),
                      reads=(f"xr:{g}",), writes=(kx2,))
            pq = (2, 3) if tt % 2 == 0 else (4, 5)
            for n in range(2):
                for i in range(NHC):
                    ctx.mm(PS[pq[n]][:],
                           h4[:, i * 1024 + tt * 128:i * 1024 + (tt + 1) * 128],
                           w2s[:, i * 1024 + n * 512:i * 1024 + (n + 1) * 512],
                           reads=(f"{pfx}h4:{i}", N("w2s")),
                           wkey=f"ps{pq[n]}", start=(i == 0),
                           stop=(i == NHC - 1))
            for n in range(2):
                ctx.tt(yo[:, n * 512:(n + 1) * 512], PS[pq[n]][:],
                       y1l[pa][:, n * 512:(n + 1) * 512], ALU.add,
                       reads=(f"ps{pq[n]}", ky1), writes=(N("yo"),))
            ctx.tt(yo[:], yo[:], b2b[:], ALU.add, reads=(N("yo"), N("b2b")),
                   writes=(N("yo"),), eng="vector")
            if not endinv:
                p.dma("sync", gr_ap(yr, g), yo[:], reads=(N("yo"),),
                      writes=(f"yr:{g}",))
            else:
                ctx.tt(xl[pa][:], yo[:], xl[pa][:], ALU.add,
                       reads=(N("yo"), kx2), writes=(kx2,), eng="vector")
                wb_store(ctx, xl[pa][:], kx2, g, xr, yr, scr[pa], scr2[pa],
                         stats[pa], last, out_d, pa)


def pass_attn_b(ctx, dt_in, sb, xr, yr, wTd, last, out_d):
    p = ctx.p
    PS, PST = ctx.ps, ctx.psT

    wsl = [sb(f"B_w{i}", [128, 8 * 1024], BF16) for i in range(2)]
    cmask = sb("B_cmask", [128, 4 * 512], BF16)
    rtab = sb("B_rtab", [128, 2 * 1024], F32)
    QFM = sb("B_qfm", [128, 8 * 512], BF16)
    KFM = sb("B_kfm", [128, 8 * 2048], BF16)
    VTM = sb("B_vtm", [128, 16 * 1024], BF16)
    yl = [sb(f"B_yl{i}", [128, 1024], F32) for i in range(2)]
    ybf = sb("B_ybf", [128, 1024], BF16)
    yT = sb("B_yT", [128, 1024], BF16)
    t1r = sb("B_t1r", [128, 1024], F32)
    t2r = sb("B_t2r", [128, 1024], F32)
    qrot = sb("B_qrot", [128, 1024], BF16)
    krot = sb("B_krot", [128, 1024], BF16)
    kf = sb("B_kf", [128, 1024], F32)
    st2 = sb("B_st2", [128, 8], F32)
    sc = sb("B_sc", [128, 2048], F32)
    Pn = [sb(f"B_pn{i}", [128, 2048], BF16) for i in range(2)]
    PT = sb("B_ptb", [128, 16 * 512], BF16)
    oall = sb("B_oall", [128, 8 * 512], BF16)
    xt2 = [sb(f"B_xt2{i}", [128, 1024], F32) for i in range(2)]
    scr = sb("B_scr", [128, 1024], F32)
    scr2 = t2r
    stats = sb("B_stats", [128, 32], F32)

    p.dma("sync", bass.AP(cmask, 0, [[2048, 128], [512, 4], [1, 512]]),
          dt_in["c_cmask"].ap().rearrange("(m p) n -> p m n", p=128),
          writes=("cmask",))
    for ti, nm in enumerate(("c_ropeA", "c_ropeB")):
        p.dma("sync", bass.AP(rtab, ti * 1024, [[2048, 128], [64, 16], [1, 64]]),
              dt_in[nm].ap().rearrange("(t p) n -> p t n", p=128),
              writes=("Brtab",))

    wnames = ("bw_q", "bw_k", "bw_v", "bw_o")

    def wTload(slot, wi, key):
        nm = wnames[wi]
        for g2 in range(2):
            ap = bass.AP(wTd, wi * D * D + g2 * 4 * 128 * D,
                         [[D, 128], [D * 128, 4], [1, D]])
            dst = bass.AP(wsl[slot], g2 * 4 * 1024,
                          [[8 * 1024, 128], [1024, 4], [1, 1024]])
            p.dma("gpsimd", dst, ap,
                  reads=tuple(f"wTd:{nm}:{j}" for j in range(8)),
                  writes=(key,))

    for b in range(B_LOC):
        wTload(0, 1, "Bw0")
        wTload(1, 2, "Bw1")
        for t in range(NT):
            g = b * NT + t
            pa = t % 2
            kyl = f"Byl{pa}"
            p.dma("sync", yl[pa][:], gr_ap(yr, g), reads=(f"yr:{g}",),
                  writes=(kyl,))
            ctx.cp(ybf[:], yl[pa][:], reads=(kyl,), writes=("Bybf",))
            for c in range(8):
                ctx.tr(PST[0][:, c * 128:(c + 1) * 128],
                       ybf[:, c * 128:(c + 1) * 128], reads=("Bybf",),
                       wkey="psT0")
            ctx.cp(yT[:], PST[0][:], reads=("psT0",), writes=("ByT",))
            for n in range(2):
                for c in range(8):
                    ctx.mm(PS[n][:], yT[:, c * 128:(c + 1) * 128],
                           wsl[0][:, c * 1024 + n * 512:c * 1024 + (n + 1) * 512],
                           reads=("ByT", "Bw0"), wkey=f"ps{n}",
                           start=(c == 0), stop=(c == 7))
            for n in range(2):
                for c in range(8):
                    ctx.mm(PS[2 + n][:], yT[:, c * 128:(c + 1) * 128],
                           wsl[1][:, c * 1024 + n * 512:c * 1024 + (n + 1) * 512],
                           reads=("ByT", "Bw1"), wkey=f"ps{2 + n}",
                           start=(c == 0), stop=(c == 7))
            ctx.cp(kf[:, 0:512], PS[0][:], reads=("ps0",), writes=("Bkf",),
                   eng="scalar")
            ctx.cp(kf[:, 512:1024], PS[1][:], reads=("ps1",), writes=("Bkf",),
                   eng="scalar")
            brope(ctx, kf, "Bkf", t1r, t2r, krot, rtab, t, "Bkrot")
            ctx.cp(VTM[:, t * 1024:t * 1024 + 512], PS[2][:], reads=("ps2",),
                   writes=(f"BVTM:{t}:0",))
            ctx.cp(VTM[:, t * 1024 + 512:(t + 1) * 1024], PS[3][:],
                   reads=("ps3",), writes=(f"BVTM:{t}:1",), eng="scalar")
            for c in range(8):
                ctx.tr(PST[1][:, c * 128:(c + 1) * 128],
                       krot[:, c * 128:(c + 1) * 128], reads=("Bkrot",),
                       wkey="psT1")
            ctx.cp(bass.AP(KFM, t * 128, [[8 * 2048, 128], [2048, 8], [1, 128]]),
                   bass.AP(PST[1], 0, [[1024, 128], [128, 8], [1, 128]]),
                   reads=("psT1",), writes=(f"BKFM:{t}",))
        wTload(0, 0, "Bw0")
        wTload(1, 3, "Bw1")
        for qb in range(4):
            for tt in range(4):
                t = qb * 4 + tt
                g = b * NT + t
                pa = t % 2
                kyl = f"Byl{pa}"
                p.dma("sync", yl[pa][:], gr_ap(yr, g),
                      reads=(f"yr:{g}",), writes=(kyl,))
                ctx.cp(ybf[:], yl[pa][:], reads=(kyl,), writes=("Bybf",))
                for c in range(8):
                    ctx.tr(PST[0][:, c * 128:(c + 1) * 128],
                           ybf[:, c * 128:(c + 1) * 128], reads=("Bybf",),
                           wkey="psT0")
                ctx.cp(yT[:], PST[0][:], reads=("psT0",), writes=("ByT",))
                for n in range(2):
                    for c in range(8):
                        ctx.mm(PS[n][:], yT[:, c * 128:(c + 1) * 128],
                               wsl[0][:, c * 1024 + n * 512:
                                    c * 1024 + (n + 1) * 512],
                               reads=("ByT", "Bw0"), wkey=f"ps{n}",
                               start=(c == 0), stop=(c == 7))
                ctx.cp(kf[:, 0:512], PS[0][:], reads=("ps0",),
                       writes=("Bkf",), eng="scalar")
                ctx.cp(kf[:, 512:1024], PS[1][:], reads=("ps1",),
                       writes=("Bkf",), eng="scalar")
                brope(ctx, kf, "Bkf", t1r, t2r, qrot, rtab, t, "Bqrot")
                for c in range(8):
                    ctx.tr(PST[1][:, c * 128:(c + 1) * 128],
                           qrot[:, c * 128:(c + 1) * 128], reads=("Bqrot",),
                           wkey="psT1")
                ctx.cp(bass.AP(QFM, tt * 128,
                               [[8 * 512, 128], [512, 8], [1, 128]]),
                       bass.AP(PST[1], 0, [[1024, 128], [128, 8], [1, 128]]),
                       reads=("psT1",), writes=(f"BQFM:{tt}",))
            nch = qb + 1
            nkc = nch * 4

            def b2_sm1(gp, h2, qt):
                hb = h2 * 64
                sp = qt % 2
                for c in range(nch):
                    pb_ = 4 + (c % 2)
                    ctx.mm(PS[pb_][:],
                           QFM[hb:hb + 64, gp * 512 + qt * 128:
                               gp * 512 + (qt + 1) * 128],
                           KFM[hb:hb + 64, gp * 2048 + c * 512:
                               gp * 2048 + (c + 1) * 512],
                           reads=(f"BQFM:{qt}",) + tuple(
                               f"BKFM:{4 * c + i}" for i in range(4)),
                           wkey=f"ps{pb_}", start=True, stop=True)
                    if c == qb:
                        ctx.tt(sc[:, c * 512:(c + 1) * 512],
                               PS[pb_][:],
                               cmask[:, qt * 512:(qt + 1) * 512],
                               ALU.add, reads=(f"ps{pb_}", "cmask"),
                               writes=(f"Bsc:{c}",))
                    else:
                        ctx.cp(sc[:, c * 512:(c + 1) * 512],
                               PS[pb_][:], reads=(f"ps{pb_}",),
                               writes=(f"Bsc:{c}",), eng="scalar")
                ctx.red(st2[:, 4 + sp:5 + sp], sc[:, :nch * 512], ALU.max,
                        reads=tuple(f"Bsc:{c}" for c in range(nch)),
                        writes=(f"BstM{sp}",), negate=True)
                ctx.act(Pn[sp][:, :nch * 512], sc[:, :nch * 512], AF.Exp,
                        reads=(f"BstM{sp}",) + tuple(
                            f"Bsc:{c}" for c in range(nch)),
                        writes=(f"BPn{sp}", f"BstM{sp}"),
                        bias=st2[:, 4 + sp:5 + sp], accum=st2[:, 6 + sp:7 + sp])

            def b2_sm2pt(gp, h2, qt):
                sp = qt % 2
                ctx.recip(st2[:, 6 + sp:7 + sp], st2[:, 6 + sp:7 + sp],
                          reads=(f"BstM{sp}",), writes=(f"BstM{sp}",))
                ctx.ts(Pn[sp][:, :nch * 512], Pn[sp][:, :nch * 512],
                       st2[:, 6 + sp:7 + sp], ALU.mult,
                       reads=(f"BPn{sp}", f"BstM{sp}"),
                       writes=(f"BPn{sp}",), eng="vector")
                for kg in range((nkc + 7) // 8):
                    kcs = list(range(kg * 8, min(nkc, kg * 8 + 8)))
                    for kc in kcs:
                        ctx.tr(PST[1][:, (kc % 8) * 128:(kc % 8 + 1) * 128],
                               Pn[sp][:, kc * 128:(kc + 1) * 128],
                               reads=(f"BPn{sp}",), wkey="psT1")
                    nn = len(kcs)
                    ctx.cp(bass.AP(PT, kcs[0] * 512 + qt * 128,
                                   [[16 * 512, 128], [512, nn], [1, 128]]),
                           bass.AP(PST[1], 0,
                                   [[1024, 128], [128, nn], [1, 128]]),
                           reads=("psT1",),
                           writes=tuple(f"BPT:{kc}:{qt}" for kc in kcs),
                           eng="scalar")

            for gp in range(8):
                for h2 in range(2):
                    h = gp * 2 + h2
                    pend = None
                    for qt in range(4):
                        b2_sm1(gp, h2, qt)
                        if pend is not None:
                            b2_sm2pt(*pend)
                        pend = (gp, h2, qt)
                    b2_sm2pt(*pend)
                    psel = PS[0][0:64, :] if h2 == 0 else PS[0][64:128, :]
                    for kc in range(nkc):
                        ctx.mm(psel,
                               VTM[:, kc * 1024 + h * 64:
                                   kc * 1024 + (h + 1) * 64],
                               PT[:, kc * 512:(kc + 1) * 512],
                               reads=(f"BVTM:{kc}:{h // 8}",) + tuple(
                                   f"BPT:{kc}:{q_}" for q_ in range(4)),
                               wkey="ps0", start=(kc == 0),
                               stop=(kc == nkc - 1), skip=True)
                ctx.cp(oall[:, gp * 512:(gp + 1) * 512], PS[0][:],
                       reads=("ps0",), writes=(f"Boall:{gp}",))
            for qt in range(4):
                t = qb * 4 + qt
                g = b * NT + t
                pa = t % 2
                kx2 = f"Bxt2{pa}"
                p.dma("sync", xt2[pa][:], dram_row_ap(xr, g),
                      reads=(f"xr:{g}",), writes=(kx2,))
                for n in range(2):
                    for c in range(8):
                        ctx.mm(PS[2 + n][:],
                               oall[:, c * 512 + qt * 128:c * 512 + (qt + 1) * 128],
                               wsl[1][:, c * 1024 + n * 512:
                                    c * 1024 + (n + 1) * 512],
                               reads=(f"Boall:{c}", "Bw1"), wkey=f"ps{2 + n}",
                               start=(c == 0), stop=(c == 7))
                for n in range(2):
                    ctx.tt(xt2[pa][:, n * 512:(n + 1) * 512], PS[2 + n][:],
                           xt2[pa][:, n * 512:(n + 1) * 512], ALU.add,
                           reads=(f"ps{2 + n}", kx2), writes=(kx2,))
                wb_store(ctx, xt2[pa][:], kx2, g, xr, yr, scr, scr2, stats,
                         last, out_d)


def brope(ctx, src, ksrc, t1r, t2r, outb, rtab, t, kout):
    At = bass.AP(rtab, t * 64, [[2048, 128], [0, 16], [1, 64]])
    s3 = bass.AP(src, 0, [[1024, 128], [64, 16], [1, 64]])
    ctx.tt(bass.AP(t1r, 0, [[1024, 128], [64, 16], [1, 64]]), s3, At,
           ALU.mult, reads=(ksrc, "Brtab"), writes=("Brt1",))
    oe = bass.AP(t2r, 0, [[1024, 128], [64, 16], [2, 32]])
    pe = bass.AP(src, 1, [[1024, 128], [64, 16], [2, 32]])
    be = bass.AP(rtab, 1024 + t * 64, [[2048, 128], [0, 16], [2, 32]])
    ctx.tt(oe, pe, be, ALU.mult, reads=(ksrc, "Brtab"), writes=("Brt2",),
           eng="vector")
    oo = bass.AP(t2r, 1, [[1024, 128], [64, 16], [2, 32]])
    po = bass.AP(src, 0, [[1024, 128], [64, 16], [2, 32]])
    bo = bass.AP(rtab, 1024 + t * 64 + 1, [[2048, 128], [0, 16], [2, 32]])
    ctx.tt(oo, po, bo, ALU.mult, reads=(ksrc, "Brtab"), writes=("Brt2",),
           eng="vector")
    ctx.tt(outb[:], t1r[:], t2r[:], ALU.add, reads=("Brt1", "Brt2"),
           writes=(kout,), eng="vector")


def kernel(**inputs):
    consts = host_consts()
    x = np.ascontiguousarray(inputs["L0_attn_a_x"], dtype=np.float32)
    base = {
        "aw_normw": inputs["L0_attn_a_norm_w"],
        "aw_q": inputs["L0_attn_a_Wq"], "aw_k": inputs["L0_attn_a_Wk"],
        "aw_v": inputs["L0_attn_a_Wv"], "aw_o": inputs["L0_attn_a_Wo"],
        "aw_gate": np.reshape(inputs["L0_attn_a_gate"], [1]),
        "aw_lt": np.reshape(inputs["L0_attn_a_log_temperature"], [1]),
        "bw_q": inputs["L0_attn_b_Wq"], "bw_k": inputs["L0_attn_b_Wk"],
        "bw_v": inputs["L0_attn_b_Wv"], "bw_o": inputs["L0_attn_b_Wo"],
        "fw_qkv": inputs["L0_ffn_wqkv"], "fw_qs": inputs["L0_ffn_q_scale"],
        "fw_ks": inputs["L0_ffn_k_scale"], "fw_wo": inputs["L0_ffn_wo"],
        "fw_lng": inputs["L0_ffn_ln_g"], "fw_lnb": inputs["L0_ffn_ln_b"],
        "fw_w1": inputs["L0_ffn_w1"], "fw_b1": inputs["L0_ffn_b1"],
        "fw_w2": inputs["L0_ffn_w2"], "fw_b2": inputs["L0_ffn_b2"],
        "c_ident": consts["ident"], "c_cmask": consts["cmask"],
        "c_fmask": consts["fmask"], "c_ropeA": consts["ropeA"],
        "c_ropeB": consts["ropeB"], "c_xqA": consts["xqA"],
        "c_xqB": consts["xqB"], "c_xkA": consts["xkA"],
        "c_xkB": consts["xkB"], "c_zrow": consts["zrow"],
        "c_eps": consts["eps"],
    }
    base = {k: np.ascontiguousarray(v) for k, v in base.items()}
    nc = build(PASSES)
    in_maps = []
    for c in range(8):
        m = dict(base)
        m["x"] = np.ascontiguousarray(x[2 * c:2 * c + 2].reshape(B_LOC * S, D))
        in_maps.append(m)
    res = run_bass_kernel_spmd(nc, in_maps, core_ids=list(range(8)))
    if PASSES == "A1":
        kernel.dbg = [{k: np.asarray(v) for k, v in res.results[c].items()}
                      for c in range(8)]
        return np.zeros((16, S, D), np.float32)
    if PASSES in ("F1", "F1m"):
        kernel.dbg = [
            {k: np.asarray(res.results[c][f"dbg_{k}"])
             for k in ("xr", "yr", "y1r")} for c in range(8)]
        if PASSES == "F1":
            for c in range(8):
                for k in list(res.results[c].keys()):
                    if k.startswith("dbgC_"):
                        kernel.dbg[c][k] = np.asarray(res.results[c][k])
        return np.zeros((16, S, D), np.float32)
    outs = [np.asarray(res.results[c]["out"]).reshape(B_LOC, S, D)
            for c in range(8)]
    full = np.concatenate(outs, axis=0).astype(np.float32)
    if PASSES != "full":
        kernel.dbg = [
            {k: np.asarray(res.results[c][f"dbg_{k}"]) for k in ("xr", "yr")}
            for c in range(8)]
    return full


# version 10
# speedup vs baseline: 1.2755x; 1.0867x over previous
import numpy as np
import ml_dtypes
from contextlib import ExitStack
import concourse.bass as bass
import concourse.mybir as mybir
from concourse.bass_utils import run_bass_kernel_spmd

F32 = mybir.dt.float32
BF16 = mybir.dt.bfloat16
AF = mybir.ActivationFunctionType
ALU = mybir.AluOpType
AX = mybir.AxisListType

NEG = -1.0e30
B_LOC = 2
S = 2048
D = 1024
NT = 16
HID = 2730
NHC = 22

PASSES = "full"
FORCE_SYNC = False

ENGINES = ("sync", "tensor", "vector", "scalar", "gpsimd")
DMA_POOL = {"sync": 32, "tensor": 2, "vector": 2, "scalar": 8, "gpsimd": 24}


class Prog:
    def __init__(self):
        self.ops = {e: [] for e in ENGINES}
        self.writer = {}
        self.readers = {}
        self.fence_deps = ()

    def op(self, eng, fn, reads=(), writes=(), dma=False, tag="", accum=False):
        deps = set(self.fence_deps)
        for k in reads:
            for w in self.writer.get(k, {}).values():
                deps.add(w)
        for k in writes:
            for w in self.writer.get(k, {}).values():
                deps.add(w)
            for r in self.readers.get(k, ()):
                deps.add(r)
        idx = len(self.ops[eng])
        me = (eng, idx)
        deps.discard(me)
        self.ops[eng].append([fn, dma, deps, False, tag, accum])
        for k in writes:
            self.writer.setdefault(k, {})[eng] = me
            self.readers[k] = []
        for k in reads:
            self.readers.setdefault(k, []).append(me)

    def dma(self, queue, out, in_, reads=(), writes=(), tag="", partial=False,
            **kw):
        self.op(queue, (lambda q, o=out, i=in_, kw=kw: q.dma_start(o, i, **kw)),
                reads=reads, writes=writes, dma=True, tag=tag)

    def fence(self):
        self.fence_deps = tuple(
            (e, len(self.ops[e]) - 1) for e in ENGINES if self.ops[e])

    def finalize(self):
        need = {e: set() for e in ENGINES}
        for e in ENGINES:
            for i, o in enumerate(self.ops[e]):
                for (de, di) in o[2]:
                    dop = self.ops[de][di]
                    if de != e or dop[1] or dop[5] or FORCE_SYNC:
                        need[de].add(di)
        self.mnum = {}
        self.dma_final = {}
        for e in ENGINES:
            c = 0
            d = 0
            K = DMA_POOL[e]
            for i, o in enumerate(self.ops[e]):
                if o[1]:
                    s = d % K
                    r = d // K + 1
                    self.mnum[(e, i)] = ("s", s, r)
                    self.dma_final[(e, s)] = r
                    d += 1
                elif i in need[e]:
                    c += 1
                    o[3] = True
                    self.mnum[(e, i)] = ("c", c)

    def emit_engine(self, q, e, sems):
        wm = {}
        for i, (fn, dma, deps, ms, tag, accum) in enumerate(self.ops[e]):
            req = {}
            for (de, di) in deps:
                dop = self.ops[de][di]
                if de == e and not dop[1] and not dop[5] and not FORCE_SYNC:
                    continue
                mn = self.mnum[(de, di)]
                if mn[0] == "c":
                    kk = (de, "c")
                    if mn[1] > req.get(kk, 0):
                        req[kk] = mn[1]
                else:
                    kk = (de, "s", mn[1])
                    if mn[2] > req.get(kk, 0):
                        req[kk] = mn[2]
            for kk in sorted(req):
                val = req[kk]
                if wm.get(kk, 0) >= val:
                    continue
                if kk[1] == "c":
                    q.wait_ge(sems[kk], val)
                else:
                    q.wait_ge(sems[kk], val * 16)
                wm[kk] = val
            if dma:
                kind, s, r = self.mnum[(e, i)]
                kk = (e, "s", s)
                if r > 1 and wm.get(kk, 0) < r - 1:
                    q.wait_ge(sems[kk], 16 * (r - 1))
                    wm[kk] = r - 1
            inst = fn(q)
            if dma:
                kind, s, r = self.mnum[(e, i)]
                inst.then_inc(sems[(e, "s", s)], 16)
            elif ms:
                inst.then_inc(sems[(e, "c")], 1)
        if e == "sync":
            for (qe, s), r in sorted(self.dma_final.items()):
                kk = (qe, "s", s)
                if wm.get(kk, 0) < r:
                    q.wait_ge(sems[kk], 16 * r)


def host_consts():
    c = {}
    c["ident"] = np.eye(128, dtype=ml_dtypes.bfloat16)
    cm = np.zeros((4, 128, 512), dtype=np.float32)
    for d_ in range(4):
        i = np.arange(128)[:, None]
        j = np.arange(512)[None, :]
        cm[d_] = np.where(j <= d_ * 128 + i, 0.0, NEG)
    c["cmask"] = cm.astype(ml_dtypes.bfloat16).reshape(4 * 128, 512)
    fm = np.zeros((2, 4, 128, 1024), dtype=np.float32)
    for first in range(2):
        for jt in range(4):
            pp = (128 * jt + np.arange(128))[:, None]
            m = np.arange(1024)[None, :]
            ok = (m >= pp) & (m <= pp + 512)
            if first:
                ok &= m >= 512
            fm[first, jt] = np.where(ok, 0.0, NEG)
    c["fmask"] = fm.astype(ml_dtypes.bfloat16).reshape(8 * 128, 1024)
    hd = 64
    pos = np.arange(S, dtype=np.float64)
    inv = 1.0 / (10000.0 ** (np.arange(0, hd, 2, dtype=np.float64) / hd))
    fr = np.concatenate([np.outer(pos, inv), np.outer(pos, inv)], -1)
    cosb, sinb = np.cos(fr), np.sin(fr)
    Bt = sinb.copy()
    Bt[:, 0::2] = -sinb[:, 0::2]
    c["ropeA"] = cosb.astype(np.float32)
    c["ropeB"] = Bt.astype(np.float32)
    half = np.arange(0, hd, 2, dtype=np.float64)
    invf = 1.0 / (10000.0 ** (half / hd))
    fr2 = np.outer(pos, invf)
    fr2 = np.concatenate([fr2, fr2], -1)
    base = (half + 0.4 * hd) / (1.4 * hd)
    power = (pos - S // 2) / 256.0
    scale = base[None, :] ** power[:, None]
    scale = np.concatenate([scale, scale], -1)
    cosx, sinx = np.cos(fr2), np.sin(fr2)
    sgn = np.ones(hd)
    sgn[:32] = -1.0
    c["xqA"] = (cosx * scale * 8.0).astype(np.float32)
    c["xqB"] = (sgn[None, :] * sinx * scale * 8.0).astype(np.float32)
    c["xkA"] = (cosx / scale).astype(np.float32)
    c["xkB"] = (sgn[None, :] * sinx / scale).astype(np.float32)
    c["zrow"] = np.zeros((1, 1024), dtype=ml_dtypes.bfloat16)
    c["eps"] = np.array([1e-5, 1e-6], dtype=np.float32)
    return c


def _is_small(out):
    try:
        return out.free_size() <= 64
    except Exception:
        return False


class Ctx:
    def __init__(self, nc, prog):
        self.nc = nc
        self.p = prog
        self.ident = None
        self.ps = []
        self.psT = []
        self.pro_steps = []

    def tbank(self):
        self._tb = (getattr(self, "_tb", -1) + 1) % 3
        return self._tb

    def step_prologue(self, n=1):
        for _ in range(n):
            if self.pro_steps:
                self.pro_steps.pop(0)()

    def mm(self, ps_ap, lhsT_ap, rhs_ap, reads, wkey, start, stop, skip=False):
        self.p.op("tensor",
                  lambda q, o=ps_ap, a=lhsT_ap, b=rhs_ap, s1=start, s2=stop,
                  sk=skip: q.matmul(o, a, b, start=s1, stop=s2,
                                    skip_group_check=sk),
                  reads=reads, writes=(wkey,), tag="mm")

    def tr(self, ps_ap, in_ap, reads, wkey):
        self.p.op("tensor",
                  lambda q, o=ps_ap, i=in_ap, idn=self.ident:
                  q.transpose(o, i, idn[:]),
                  reads=reads + ("ident",), writes=(wkey,), tag="tr")

    def act(self, out, in_, func, reads, writes, bias=None, scale=None,
            accum=None):
        def f(q, o=out, i=in_, fu=func, b=bias, s=scale, a=accum):
            kw = {}
            if b is not None:
                kw["bias"] = b
            if s is not None:
                kw["scale"] = s
            if a is not None:
                kw["accum_out"] = a
            return q.activation(o, i, fu, **kw)
        self.p.op("scalar", f, reads=reads, writes=writes, tag="act",
                  accum=(accum is not None) or _is_small(out))

    def tt(self, out, a, b, op, reads, writes, eng="vector"):
        self.p.op(eng,
                  lambda q, o=out, x=a, y=b, p_=op: q.tensor_tensor(o, x, y, p_),
                  reads=reads, writes=writes, tag="tt", accum=_is_small(out))

    def ts(self, out, a, s1, op0, reads, writes, s2=None, op1=None, accum=None,
           eng="vector"):
        def f(q, o=out, x=a, s1=s1, s2=s2, o0=op0, o1=op1, ac=accum):
            kw = {}
            if ac is not None:
                kw["accum_out"] = ac
            if o1 is None:
                return q.tensor_scalar(o, x, s1, s2, o0, **kw)
            return q.tensor_scalar(o, x, s1, s2, o0, o1, **kw)
        self.p.op(eng, f, reads=reads, writes=writes, tag="ts",
                  accum=(accum is not None) or _is_small(out))

    def ttr(self, out, a, b, op0, op1, accum, reads, writes, scale=1.0,
            seed=0.0):
        self.p.op("vector",
                  lambda q, o=out, x=a, y=b, s=scale, sd=seed, o0=op0, o1=op1,
                  ac=accum: q.tensor_tensor_reduce(o, x, y, s, sd, o0, o1, ac),
                  reads=reads, writes=writes, tag="ttr")

    def stt(self, out, a, s, b, op0, op1, reads, writes, eng="vector"):
        self.p.op(eng,
                  lambda q, o=out, x=a, sc=s, y=b, o0=op0, o1=op1:
                  q.scalar_tensor_tensor(o, x, sc, y, o0, o1),
                  reads=reads, writes=writes, tag="stt", accum=_is_small(out))

    def cp(self, out, in_, reads, writes, eng="vector"):
        if eng == "scalar":
            self.p.op(eng, lambda q, o=out, i=in_: q.copy(o, i),
                      reads=reads, writes=writes, tag="cp",
                      accum=_is_small(out))
        else:
            self.p.op(eng, lambda q, o=out, i=in_: q.tensor_copy(o, i),
                      reads=reads, writes=writes, tag="cp",
                      accum=_is_small(out))

    def red(self, out, in_, op, reads, writes, negate=False):
        self.p.op("vector",
                  lambda q, o=out, i=in_, p_=op, n=negate:
                  q.tensor_reduce(o, i, axis=AX.X, op=p_, negate=n),
                  reads=reads, writes=writes, tag="red")

    def recip(self, out, in_, reads, writes):
        self.p.op("vector", lambda q, o=out, i=in_: q.reciprocal(o, i),
                  reads=reads, writes=writes, tag="recip",
                  accum=_is_small(out))

    def rsqrt(self, out, in_, reads, writes, scale=None, bias=None, minv=None):
        self.act(out, in_, AF.Ln, reads, writes, bias=bias, scale=scale)
        self.act(out, out, AF.Exp, writes, writes, scale=-0.5)
        if minv is not None:
            self.ts(out, out, minv, ALU.min, writes, writes)

    def bcast(self, dst, src_t, n, writes, offset=0, queue="sync"):
        ap = bass.AP(src_t, offset, [[0, 128], [1, n]])
        self.p.dma(queue, dst, ap, writes=writes)

    def ln_tile(self, x_ap, out_ap, kx, kout, scr, scr2, stats, sl, kst,
                ksc=None):
        ksc = ksc or kst
        st = lambda i: stats[:, sl + i: sl + i + 1]
        self.act(scr[:], x_ap, AF.Square, reads=(kx,), writes=(ksc + "sA", kst),
                 accum=st(0))
        self.ts(scr2[:], x_ap, 0.0, ALU.add, reads=(kx,), writes=(ksc + "sB", kst),
                s2=0.0, op1=ALU.add, accum=st(1))
        self.ts(st(2), st(1), 1.0 / D, ALU.mult, reads=(kst,), writes=(kst,))
        self.tt(st(5), st(2), st(2), ALU.mult, reads=(kst,), writes=(kst,))
        self.stt(st(3), st(0), 1.0 / D, st(5), ALU.mult, ALU.subtract,
                 reads=(kst,), writes=(kst,))
        self.rsqrt(st(4), st(3), reads=(kst, "epsc"), writes=(kst,),
                   bias=self.eps5)
        self.ts(out_ap, x_ap, st(2), ALU.subtract, reads=(kx, kst),
                writes=(kout,), s2=st(4), op1=ALU.mult)


def dram_row_ap(t, g, width=D):
    return bass.AP(t, g * 128 * width, [[width, 128], [1, width]])


def grow(g):
    return (g // NT) * (S + 128) + 128 + (g % NT) * 128


def gr_ap(t, g, width=D):
    return bass.AP(t, grow(g) * width, [[width, 128], [1, width]])


def gr_shift_ap(t, g):
    return bass.AP(t, (grow(g) - 1) * D + 512, [[D, 128], [1, 512]])


def build(passes=PASSES):
    nc = bass.Bass("TRN2", target_bir_lowering=False)
    prog = Prog()
    ctx = Ctx(nc, prog)
    p = prog

    dt_in = {}

    def din(name, shape, dtype=F32):
        dt_in[name] = nc.dram_tensor(name, list(shape), dtype,
                                     kind="ExternalInput")
        return dt_in[name]

    x_in = din("x", [B_LOC * S, D])
    din("aw_normw", [D]); din("aw_q", [D, D]); din("aw_k", [D, 512])
    din("aw_v", [D, 512]); din("aw_o", [D, D]); din("aw_gate", [1])
    din("aw_lt", [1])
    din("bw_q", [D, D]); din("bw_k", [D, D]); din("bw_v", [D, D])
    din("bw_o", [D, D])
    din("fw_qkv", [2, D, 1536]); din("fw_qs", [2, 64]); din("fw_ks", [2, 64])
    din("fw_wo", [2, 512, D]); din("fw_lng", [2, D]); din("fw_lnb", [2, D])
    din("fw_w1", [2, D, 2 * HID]); din("fw_b1", [2, 2 * HID])
    din("fw_w2", [2, HID, D]); din("fw_b2", [2, D])
    din("c_ident", [128, 128], BF16)
    din("c_cmask", [4 * 128, 512], BF16)
    din("c_fmask", [8 * 128, 1024], BF16)
    for nm in ("c_ropeA", "c_ropeB", "c_xqA", "c_xqB", "c_xkA", "c_xkB"):
        din(nm, [S, 64])
    din("c_zrow", [1, 1024], BF16)
    din("c_eps", [2])

    out_d = nc.dram_tensor("out", [B_LOC * S, D], F32, kind="ExternalOutput")
    dbg = {}
    if passes == "F1":
        ctx.dbg_c1 = {}
        for nm, shape, dt_ in (("Qw", [128, 4 * 512], BF16),
                               ("Kw", [128, 4 * 512], BF16),
                               ("Vw", [128, 4 * 512], BF16),
                               ("oall", [128, 4 * 512], BF16),
                               ("sc", [128, 1024], F32),
                               ("pn", [128, 1024], BF16),
                               ("qrot", [128, 512], BF16),
                               ("krot", [128, 512], BF16),
                               ("qf", [128, 512], F32),
                               ("kf", [128, 512], F32),
                               ("hbf", [128, 1024], BF16),
                               ("yl0", [128, 1024], F32),
                               ("qf_raw", [128, 512], F32),
                               ("kf_raw", [128, 512], F32),
                               ("qf_n", [128, 512], F32),
                               ("qrot0", [128, 512], BF16),
                               ("fac0", [128, 32], F32),
                               ("hbf0", [128, 1024], BF16),
                               ("hT0", [128, 1024], BF16)):
            ctx.dbg_c1[nm] = nc.dram_tensor("dbgC_" + nm, shape, dt_,
                                            kind="ExternalOutput")
    if passes == "A1":
        ctx.dbg_a1 = {}
        for nm, shape, dt_ in (("kfm", [128, 4 * 2048], BF16),
                               ("vtm", [128, 16 * 512], BF16),
                               ("qfm", [128, 8 * 512], BF16),
                               ("qn", [128, 1024], BF16),
                               ("kn", [128, 512], BF16),
                               ("qf", [128, 1024], F32),
                               ("hbf", [128, 1024], BF16),
                               ("sc", [128, 2048], F32),
                               ("pn", [128, 2048], BF16),
                               ("oall", [128, 8 * 512], BF16)):
            ctx.dbg_a1[nm] = nc.dram_tensor("dbgA_" + nm, shape, dt_,
                                            kind="ExternalOutput")
    if passes != "full":
        dbg["xr"] = nc.dram_tensor("dbg_xr", [B_LOC * S, D], F32,
                                   kind="ExternalOutput")
        dbg["yr"] = nc.dram_tensor("dbg_yr", [B_LOC * S, D], F32,
                                   kind="ExternalOutput")
        dbg["y1r"] = nc.dram_tensor("dbg_y1r", [B_LOC * S, D], F32,
                                    kind="ExternalOutput")

    xr = nc.dram_tensor("xr", [B_LOC * S, D], F32)
    yr = nc.dram_tensor("yr", [B_LOC * (S + 128), D], F32)
    y1r = nc.dram_tensor("y1r", [B_LOC * (S + 128), D], F32)
    wTd = nc.dram_tensor("wTd", [4, D, D], BF16)
    w1b = nc.dram_tensor("w1b", [2, 2 * NHC, 128, 1024], BF16)

    es = ExitStack()
    ident = es.enter_context(nc.sbuf_tensor("ident", [128, 128], BF16))
    identf = es.enter_context(nc.sbuf_tensor("identf", [128, 128], F32))
    scal = es.enter_context(nc.sbuf_tensor("scal", [128, 8], F32))
    ctx.ident = ident
    ctx.ps = [es.enter_context(nc.psum_tensor(f"ps{i}", [128, 512], F32))
              for i in range(5)]
    ctx.psT = [es.enter_context(nc.psum_tensor(f"psT{i}", [128, 1024], BF16))
               for i in range(3)]

    p.dma("sync", ident[:], dt_in["c_ident"].ap(), writes=("ident",))
    ctx.cp(identf[:], ident[:], reads=("ident",), writes=("identf",))
    ctx.identf = identf
    ctx.bcast(scal[:, 0:1], dt_in["aw_gate"], 1, ("gate",))
    ctx.bcast(scal[:, 1:2], dt_in["aw_lt"], 1, ("temp",))
    ctx.act(scal[:, 1:2], scal[:, 1:2], AF.Exp, reads=("temp",),
            writes=("temp",))
    p.dma("sync", scal[:, 2:4],
          bass.AP(dt_in["c_eps"], 0, [[0, 128], [1, 2]]), writes=("epsc",))
    ctx.eps5 = scal[:, 2:3]
    ctx.eps6 = scal[:, 3:4]

    for tns, knm in ((yr, "yr"), (y1r, "y1r")):
        for b in range(B_LOC):
            p.dma("gpsimd",
                  bass.AP(tns, b * (S + 128) * D, [[D, 128], [1, D]]),
                  bass.AP(dt_in["c_zrow"], 0, [[0, 128], [1, 1024]]),
                  writes=(f"{knm}guard:{b}",))

    make_prologue(ctx, dt_in, wTd, w1b, es)

    def scoped(fn, *args, **kw):
        st = ExitStack()
        sb = lambda name, shape, dtype: st.enter_context(
            nc.sbuf_tensor(name, shape, dtype))
        fn(ctx, dt_in, sb, *args, **kw)
        st.close()
        prog.fence()

    ncut = {"A": 1, "A1": 1, "F1": 2, "F1m": 3, "AF0": 5, "AF0B": 6,
            "full": 10}[passes]
    seq = [
        lambda last: scoped(pass_attn_a, x_in, xr, yr, scal, last=last,
                            out_d=out_d),
        lambda last: scoped(pass_ffn_attn, xr, yr, y1r, inv=0, l=0),
        lambda last: scoped(pass_ffn_mlp, xr, yr, y1r, w1b, inv=0, l=0,
                            last=last, out_d=out_d),
        lambda last: scoped(pass_ffn_attn, xr, yr, y1r, inv=0, l=1),
        lambda last: scoped(pass_ffn_mlp, xr, yr, y1r, w1b, inv=0, l=1,
                            last=last, out_d=out_d),
        lambda last: scoped(pass_attn_b, xr, yr, wTd, last=last, out_d=out_d),
        lambda last: scoped(pass_ffn_attn, xr, yr, y1r, inv=1, l=0),
        lambda last: scoped(pass_ffn_mlp, xr, yr, y1r, w1b, inv=1, l=0,
                            last=last, out_d=out_d),
        lambda last: scoped(pass_ffn_attn, xr, yr, y1r, inv=1, l=1),
        lambda last: scoped(pass_ffn_mlp, xr, yr, y1r, w1b, inv=1, l=1,
                            last=last, out_d=out_d),
    ]
    for i in range(ncut):
        seq[i](i == ncut - 1)

    if passes != "full" and passes != "A1":
        for g in range(B_LOC * NT):
            p.dma("sync", dram_row_ap(dbg["xr"], g), dram_row_ap(xr, g),
                  reads=(f"xr:{g}",))
            p.dma("sync", dram_row_ap(dbg["yr"], g), gr_ap(yr, g),
                  reads=(f"yr:{g}",))
            if passes in ("F1", "F1m"):
                p.dma("sync", dram_row_ap(dbg["y1r"], g),
                      gr_ap(y1r, g), reads=(f"y1r:{g}",))

    prog.finalize()
    sems = {}
    for e in ENGINES:
        sems[(e, "c")] = es.enter_context(nc.semaphore(f"s_{e}_c"))
        for s in range(DMA_POOL[e]):
            sems[(e, "s", s)] = es.enter_context(
                nc.semaphore(f"s_{e}_d{s}"))
    with nc.Block() as blk:
        for e in ENGINES:
            if prog.ops[e] or e == "sync":
                getattr(blk, e)(lambda q, ee=e: prog.emit_engine(q, ee, sems))
    es.close()
    return nc


def make_prologue(ctx, dt_in, wTd, w1b, es):
    nc = ctx.nc
    p = ctx.p
    stg = es.enter_context(nc.sbuf_tensor("pro_stg", [128, 1024], BF16))
    stg2 = es.enter_context(nc.sbuf_tensor("pro_stg2", [128, 1024], BF16))
    steps = ctx.pro_steps

    def w1_step(l, hc):
        def go():
            half, hcl = divmod(hc, NHC)
            col0 = half * HID + hcl * 128
            w = min(128, HID - hcl * 128)
            if w < 128:
                p.dma("gpsimd", stg[:],
                      bass.AP(dt_in["c_zrow"], 0, [[0, 128], [1, 1024]]),
                      reads=("pstg",), writes=("pstg",))
            ap = bass.AP(dt_in["fw_w1"], l * D * 2 * HID + col0,
                         [[2 * HID, 128], [2 * HID * 128, 8], [1, w]])
            dst = bass.AP(stg, 0, [[1024, 128], [128, 8], [1, w]])
            p.dma("gpsimd", dst, ap, reads=("pstg",), writes=("pstg",))
            p.dma("sync",
                  bass.AP(w1b, (l * 2 * NHC + hc) * 128 * 1024,
                          [[1024, 128], [1, 1024]]),
                  stg[:], reads=("pstg",), writes=(f"w1b:{l}:{hc}",))
        return go

    for l in range(2):
        for hc in range(2 * NHC):
            steps.append(w1_step(l, hc))

    def wt_step(wi, nm, i):
        def go():
            ap = bass.AP(dt_in[nm], i * 128 * D, [[D, 128], [1, 1024]])
            p.dma("gpsimd", stg2[:], ap, reads=("pstg2",), writes=("pstg2",))
            if nm == "bw_q":
                ctx.ts(stg2[:], stg2[:], 0.125, ALU.mult, reads=("pstg2",),
                       writes=("pstg2",))
            for jh in range(2):
                pk = f"psT{jh}"
                for j4 in range(4):
                    j = jh * 4 + j4
                    ctx.tr(ctx.psT[jh][:, j4 * 128:(j4 + 1) * 128],
                           stg2[:, j * 128:(j + 1) * 128],
                           reads=("pstg2",), wkey=pk)
            ctx.cp(stg[:, 0:512], ctx.psT[0][:, 0:512], reads=("psT0",),
                   writes=("pstg",))
            ctx.cp(stg[:, 512:1024], ctx.psT[1][:, 0:512], reads=("psT1",),
                   writes=("pstg2e",))
            p.dma("sync",
                  bass.AP(wTd, wi * D * D + i * 128,
                          [[D, 128], [D * 128, 4], [1, 128]]),
                  bass.AP(stg, 0, [[1024, 128], [128, 4], [1, 128]]),
                  reads=("pstg",), writes=(f"wTd:{nm}:{i}",))
            p.dma("sync",
                  bass.AP(wTd, wi * D * D + 4 * 128 * D + i * 128,
                          [[D, 128], [D * 128, 4], [1, 128]]),
                  bass.AP(stg, 512, [[1024, 128], [128, 4], [1, 128]]),
                  reads=("pstg2e",), writes=(f"wTd:{nm}:{i}",))
        return go

    for wi, nm in enumerate(["bw_q", "bw_k", "bw_v", "bw_o"]):
        for i in range(8):
            steps.append(wt_step(wi, nm, i))


def pass_attn_a(ctx, dt_in, sb, x_in, xr, yr, scal, last, out_d):
    p = ctx.p
    PS, PST = ctx.ps, ctx.psT

    wq_s = sb("A_wq", [128, 8 * 1024], BF16)
    wk_s = sb("A_wk", [128, 8 * 512], BF16)
    wv_s = sb("A_wv", [128, 8 * 512], BF16)
    wo_s = sb("A_wo", [128, 8 * 1024], BF16)
    normw = sb("A_normw", [128, 1024], F32)
    cmask = sb("A_cmask", [128, 4 * 512], BF16)
    QFM = sb("A_qfm", [128, 8 * 512], BF16)
    KFM = sb("A_kfm", [128, 4 * 2048], BF16)
    VTM = sb("A_vtm", [128, 16 * 512], BF16)
    xin = [sb(f"A_xin{i}", [128, 1024], F32) for i in range(2)]
    ybuf = [sb(f"A_y{i}", [128, 1024], F32) for i in range(2)]
    hbf = [sb(f"A_hbf{i}", [128, 1024], BF16) for i in range(2)]
    hT = sb("A_hT", [128, 1024], BF16)
    qf = sb("A_qf", [128, 1024], F32)
    kf = sb("A_kf", [128, 512], F32)
    qn = sb("A_qn", [128, 1024], BF16)
    kn = sb("A_kn", [128, 512], BF16)
    scr = [sb(f"A_scr{i}", [128, 1024], F32) for i in range(2)]
    scr2 = [sb(f"A_scr2{i}", [128, 1024], F32) for i in range(2)]
    stats = [sb(f"A_stats{i}", [128, 32], F32) for i in range(2)]
    st2 = [sb(f"A_st2{i}", [128, 8], F32) for i in range(2)]
    sc = sb("A_sc", [128, 2048], F32)
    Pn = [sb(f"A_pn{i}", [128, 2048], BF16) for i in range(2)]
    PT = sb("A_ptb", [128, 16 * 512], BF16)
    oall = sb("A_oall", [128, 8 * 512], BF16)
    xt2 = [sb("A_xt2", [128, 1024], F32)] * 2
    yt2 = [sb(f"A_yt2{i}", [128, 1024], F32) for i in range(2)]
    t1 = sb("A_t1", [128, 1024], F32)

    def wload(dst, name, cols, key):
        t = dt_in[name]
        for g2 in range(2):
            ap = bass.AP(t, g2 * 4 * 128 * cols,
                         [[cols, 128], [cols * 128, 4], [1, cols]])
            dstap = bass.AP(dst, g2 * 4 * cols,
                            [[8 * cols, 128], [cols, 4], [1, cols]])
            p.dma("gpsimd", dstap, ap, writes=(key,))

    wload(wq_s, "aw_q", 1024, "wq")
    wload(wk_s, "aw_k", 512, "wk")
    wload(wv_s, "aw_v", 512, "wv")
    wload(wo_s, "aw_o", 1024, "wo")
    ctx.bcast(normw[:], dt_in["aw_normw"], 1024, ("normw",))
    p.dma("sync", bass.AP(cmask, 0, [[2048, 128], [512, 4], [1, 512]]),
          dt_in["c_cmask"].ap().rearrange("(m p) n -> p m n", p=128),
          writes=("cmask",))

    for b in range(B_LOC):
        for qb in range(4):
            if getattr(ctx, "dbg_a1", None) is not None and (b, qb) == (0, 1):
                dbgd = ctx.dbg_a1
                for nm, buf in (("kfm", KFM), ("vtm", VTM), ("qfm", QFM),
                                ("qn", qn), ("kn", kn), ("qf", qf),
                                ("hbf", hbf[1]), ("sc", sc), ("pn", Pn),
                                ("oall", oall)):
                    p.dma("sync", dbgd[nm].ap(), buf[:],
                          reads=tuple(ctx.p.writer.keys()), writes=())
                return
            for tt in range(4):
                t = qb * 4 + tt
                g = b * NT + t
                pa = t % 2
                ctx.step_prologue(4)
                kx, ky = f"xin{pa}", f"ybuf{pa}"
                p.dma("sync", xin[pa][:], dram_row_ap(x_in, g), writes=(kx,))
                ctx.ln_tile(xin[pa][:], ybuf[pa][:], kx, ky, scr[pa],
                            scr2[pa], stats[pa], 0, f"stL{pa}")
                p.dma("sync", gr_ap(yr, g), ybuf[pa][:], reads=(ky,),
                      writes=(f"yr:{g}",))
                ctx.act(scr[pa][:], ybuf[pa][:], AF.Square, reads=(ky,),
                        writes=(f"scrA{pa}", f"stR{pa}"), scale=0.03125,
                        accum=stats[pa][:, 8:9])
                ctx.rsqrt(stats[pa][:, 9:10], stats[pa][:, 8:9],
                          reads=(f"stR{pa}", "epsc"), writes=(f"stR{pa}",),
                          bias=ctx.eps6)
                kh = f"hbf{pa}"
                ctx.stt(hbf[pa][:], ybuf[pa][:], stats[pa][:, 9:10], normw[:],
                        ALU.mult, ALU.mult, reads=(ky, f"stR{pa}", "normw"),
                        writes=(kh,))
                tb = ctx.tbank()
                for c in range(8):
                    ctx.tr(PST[tb][:, c * 128:(c + 1) * 128],
                           hbf[pa][:, c * 128:(c + 1) * 128],
                           reads=(kh,), wkey=f"psT{tb}")
                ctx.cp(hT[:], PST[tb][:], reads=(f"psT{tb}",), writes=("hT",))
                for n in range(2):
                    for c in range(8):
                        ctx.mm(PS[n][:], hT[:, c * 128:(c + 1) * 128],
                               wq_s[:, c * 1024 + n * 512:c * 1024 + (n + 1) * 512],
                               reads=("hT", "wq"), wkey=f"ps{n}",
                               start=(c == 0), stop=(c == 7))
                for c in range(8):
                    ctx.mm(PS[2][:], hT[:, c * 128:(c + 1) * 128],
                           wk_s[:, c * 512:(c + 1) * 512],
                           reads=("hT", "wk"), wkey="ps2",
                           start=(c == 0), stop=(c == 7))
                for c in range(8):
                    ctx.mm(PS[3][:], hT[:, c * 128:(c + 1) * 128],
                           wv_s[:, c * 512:(c + 1) * 512],
                           reads=("hT", "wv"), wkey="ps3",
                           start=(c == 0), stop=(c == 7))
                ctx.cp(qf[:, 0:512], PS[0][:], reads=("ps0",), writes=("qf",),
                       eng="scalar")
                ctx.cp(qf[:, 512:1024], PS[1][:], reads=("ps1",),
                       writes=("qf",), eng="scalar")
                ctx.cp(kf[:], PS[2][:], reads=("ps2",), writes=("kf",),
                       eng="scalar")
                ctx.cp(VTM[:, t * 512:(t + 1) * 512], PS[3][:], reads=("ps3",),
                       writes=(f"VTM:{t}",))
                ctx.act(scr[pa][:], qf[:], AF.Square, reads=("qf",),
                        writes=(f"scrA{pa}",))
                ctx.act(scr2[pa][:, 0:512], kf[:], AF.Square, reads=("kf",),
                        writes=(f"scrB{pa}",))
                ctx.red(stats[pa][:, 16:20],
                        bass.AP(scr[pa], 0, [[1024, 128], [256, 4], [1, 256]]),
                        ALU.add, reads=(f"scrA{pa}",), writes=(f"stQ{pa}",))
                ctx.red(stats[pa][:, 20:22],
                        bass.AP(scr2[pa], 0, [[1024, 128], [256, 2], [1, 256]]),
                        ALU.add, reads=(f"scrB{pa}",), writes=(f"stQ{pa}",))
                ctx.rsqrt(stats[pa][:, 24:28], stats[pa][:, 16:20],
                          reads=(f"stQ{pa}",), writes=(f"stF{pa}",), minv=1e12)
                ctx.rsqrt(stats[pa][:, 28:30], stats[pa][:, 20:22],
                          reads=(f"stQ{pa}",), writes=(f"stF{pa}",), minv=1e12)
                ctx.ts(stats[pa][:, 24:28], stats[pa][:, 24:28], scal[:, 1:2],
                       ALU.mult, reads=(f"stF{pa}", "temp"),
                       writes=(f"stF{pa}",))
                ctx.tt(bass.AP(qn, 0, [[1024, 128], [256, 4], [1, 256]]),
                       bass.AP(qf, 0, [[1024, 128], [256, 4], [1, 256]]),
                       bass.AP(stats[pa], 24, [[32, 128], [1, 4], [0, 256]]),
                       ALU.mult, reads=("qf", f"stF{pa}"), writes=("qn",))
                ctx.tt(bass.AP(kn, 0, [[512, 128], [256, 2], [1, 256]]),
                       bass.AP(kf, 0, [[512, 128], [256, 2], [1, 256]]),
                       bass.AP(stats[pa], 28, [[32, 128], [1, 2], [0, 256]]),
                       ALU.mult, reads=("kf", f"stF{pa}"), writes=("kn",),
                       eng="vector")
                tb = ctx.tbank()
                for c in range(8):
                    ctx.tr(PST[tb][:, c * 128:(c + 1) * 128],
                           qn[:, c * 128:(c + 1) * 128],
                           reads=("qn",), wkey=f"psT{tb}")
                ctx.cp(bass.AP(QFM, tt * 128,
                               [[8 * 512, 128], [512, 8], [1, 128]]),
                       bass.AP(PST[tb], 0, [[1024, 128], [128, 8], [1, 128]]),
                       reads=(f"psT{tb}",), writes=(f"QFM:{tt}",))
                tb = ctx.tbank()
                for c in range(4):
                    ctx.tr(PST[tb][:, c * 128:(c + 1) * 128],
                           kn[:, c * 128:(c + 1) * 128],
                           reads=("kn",), wkey=f"psT{tb}")
                ctx.cp(bass.AP(KFM, t * 128,
                               [[4 * 2048, 128], [2048, 4], [1, 128]]),
                       bass.AP(PST[tb], 0, [[1024, 128], [128, 4], [1, 128]]),
                       reads=(f"psT{tb}",), writes=(f"KFM:{t}",), eng="scalar")
            nch = qb + 1
            nkc = nch * 4

            def a2_sm1(h, qt):
                kvh = h // 2
                kq = f"QFM:{qt}"
                sp = qt % 2
                for c in range(nch):
                    pb_ = 2 + ((h * nch + c) % 3)
                    for cc in range(2):
                        ctx.mm(PS[pb_][:],
                               QFM[:, (2 * h + cc) * 512 + qt * 128:
                                   (2 * h + cc) * 512 + (qt + 1) * 128],
                               KFM[:, (2 * kvh + cc) * 2048 + c * 512:
                                   (2 * kvh + cc) * 2048 + (c + 1) * 512],
                               reads=(kq,) + tuple(
                                   f"KFM:{4 * c + i}" for i in range(4)),
                               wkey=f"ps{pb_}", start=(cc == 0),
                               stop=(cc == 1))
                    if c == qb:
                        ctx.tt(sc[:, c * 512:(c + 1) * 512], PS[pb_][:],
                               cmask[:, qt * 512:(qt + 1) * 512], ALU.add,
                               reads=(f"ps{pb_}", "cmask"),
                               writes=(f"sc:{c}",))
                    else:
                        ctx.cp(sc[:, c * 512:(c + 1) * 512], PS[pb_][:],
                               reads=(f"ps{pb_}",), writes=(f"sc:{c}",),
                               eng="scalar")
                ctx.red(st2[sp][:, 4:5], sc[:, :nch * 512], ALU.max,
                        reads=tuple(f"sc:{c}" for c in range(nch)),
                        writes=(f"stM{sp}",), negate=True)
                ctx.act(Pn[sp][:, :nch * 512], sc[:, :nch * 512], AF.Exp,
                        reads=(f"stM{sp}",) + tuple(
                            f"sc:{c}" for c in range(nch)),
                        writes=(f"Pn{sp}", f"stM{sp}"),
                        bias=st2[sp][:, 4:5], accum=st2[sp][:, 5:6])

            def a2_sm2pt(h, qt):
                sp = qt % 2
                ctx.recip(st2[sp][:, 5:6], st2[sp][:, 5:6],
                          reads=(f"stM{sp}",), writes=(f"stM{sp}",))
                ctx.ts(Pn[sp][:, :nch * 512], Pn[sp][:, :nch * 512],
                       st2[sp][:, 5:6], ALU.mult,
                       reads=(f"Pn{sp}", f"stM{sp}"), writes=(f"Pn{sp}",),
                       eng="vector")
                for kg in range((nkc + 7) // 8):
                    kcs = list(range(kg * 8, min(nkc, kg * 8 + 8)))
                    tb = ctx.tbank()
                    for kc in kcs:
                        ctx.tr(PST[tb][:, (kc % 8) * 128:(kc % 8 + 1) * 128],
                               Pn[sp][:, kc * 128:(kc + 1) * 128],
                               reads=(f"Pn{sp}",), wkey=f"psT{tb}")
                    nn = len(kcs)
                    ctx.cp(bass.AP(PT, kcs[0] * 512 + qt * 128,
                                   [[16 * 512, 128], [512, nn], [1, 128]]),
                           bass.AP(PST[tb], 0,
                                   [[1024, 128], [128, nn], [1, 128]]),
                           reads=(f"psT{tb}",),
                           writes=tuple(f"PT:{kc}:{qt}" for kc in kcs),
                           eng=("scalar" if qt % 2 else "vector"))

            pend = None
            for h in range(4):
                kvh = h // 2
                for qt in range(4):
                    a2_sm1(h, qt)
                    if pend is not None:
                        a2_sm2pt(*pend)
                    pend = (h, qt)
                a2_sm2pt(*pend)
                pend = None
                for vc in range(2):
                    for kc in range(nkc):
                        ctx.mm(PS[vc][:],
                               VTM[:, kc * 512 + kvh * 256 + vc * 128:
                                   kc * 512 + kvh * 256 + (vc + 1) * 128],
                               PT[:, kc * 512:(kc + 1) * 512],
                               reads=(f"VTM:{kc}",) + tuple(
                                   f"PT:{kc}:{q_}" for q_ in range(4)),
                               wkey=f"ps{vc}", start=(kc == 0),
                               stop=(kc == nkc - 1))
                    ctx.cp(oall[:, (2 * h + vc) * 512:(2 * h + vc + 1) * 512],
                           PS[vc][:], reads=(f"ps{vc}",),
                           writes=(f"oall:{2 * h + vc}",),
                           eng=("vector" if vc == 0 else "scalar"))
            for qt in range(4):
                t = qb * 4 + qt
                g = b * NT + t
                pa = t % 2
                kx2, ky2 = "xt2s", f"yt2{pa}"
                p.dma("sync", xt2[pa][:], dram_row_ap(x_in, g), writes=(kx2,))
                p.dma("sync", yt2[pa][:], gr_ap(yr, g),
                      reads=(f"yr:{g}",), writes=(ky2,))
                for n in range(2):
                    for c in range(8):
                        ctx.mm(PS[2 + n][:],
                               oall[:, c * 512 + qt * 128:c * 512 + (qt + 1) * 128],
                               wo_s[:, c * 1024 + n * 512:c * 1024 + (n + 1) * 512],
                               reads=(f"oall:{c}", "wo"), wkey=f"ps{2 + n}",
                               start=(c == 0), stop=(c == 7))
                for n in range(2):
                    ctx.stt(t1[:, n * 512:(n + 1) * 512], PS[2 + n][:],
                            scal[:, 0:1], yt2[pa][:, n * 512:(n + 1) * 512],
                            ALU.mult, ALU.add,
                            reads=(f"ps{2 + n}", "gate", ky2), writes=("t1",))
                ctx.tt(xt2[pa][:], t1[:], xt2[pa][:], ALU.add,
                       reads=("t1", kx2), writes=(kx2,), eng="vector")
                wb_store(ctx, xt2[pa][:], kx2, g, xr, yr, scr[pa], scr2[pa],
                         stats[pa], last, out_d, pa)


def wb_store(ctx, x1_ap, kx1, g, xr, yr, scr, scr2, stats, last, out_d,
             pa=0):
    p = ctx.p
    if last:
        p.dma("sync", dram_row_ap(out_d, g), x1_ap, reads=(kx1,),
              writes=(f"out:{g}",))
        p.dma("sync", dram_row_ap(xr, g), x1_ap, reads=(kx1,),
              writes=(f"xr:{g}",))
        return
    p.dma("sync", dram_row_ap(xr, g), x1_ap, reads=(kx1,), writes=(f"xr:{g}",))
    ctx.ln_tile(x1_ap, scr[:], kx1, f"scrA{pa}w", scr, scr2, stats, 10,
                f"stW{pa}")
    p.dma("sync", gr_ap(yr, g), scr[:], reads=(f"scrA{pa}w",),
          writes=(f"yr:{g}",))


def load_shifted(ctx, dst, src_t, g, t, key, dt_in, kpre):
    p = ctx.p
    b = g // NT
    p.dma("sync", dst[:, 0:512],
          bass.AP(src_t, grow(g) * D, [[D, 128], [1, 512]]),
          reads=(f"{kpre}:{g}",), writes=(key,))
    prev = (f"{kpre}guard:{b}",) if t == 0 else (f"{kpre}:{g - 1}",)
    p.dma("sync", dst[:, 512:1024], gr_shift_ap(src_t, g),
          reads=prev + (f"{kpre}:{g}",), writes=(key,))


def pass_ffn_attn(ctx, dt_in, sb, xr, yr, y1r, inv, l):
    p = ctx.p
    PS, PST = ctx.ps, ctx.psT
    pfx = f"F{inv}{l}a"
    N = lambda s: pfx + s

    wqkv = sb(N("wqkv"), [128, 8 * 1536], BF16)
    wo_s = sb(N("wo"), [128, 4 * 1024], BF16)
    fmask = sb(N("fmask"), [128, 8 * 1024], BF16)
    tabl = sb(N("tabl"), [128, 4 * 1024], F32)
    qsb = sb(N("qsb"), [128, 128], F32)
    Kw = [sb(N(f"K{i}"), [128, 4 * 512], BF16) for i in range(2)]
    Vw = [sb(N(f"V{i}"), [128, 4 * 512], BF16) for i in range(2)]
    Qw = sb(N("Q"), [128, 4 * 512], BF16)
    yl = [sb(N(f"yl{i}"), [128, 1024], F32) for i in range(4)]
    hbf = [sb(N(f"hbf{i}"), [128, 1024], BF16) for i in range(2)]
    hT = [sb(N(f"hT{i}"), [128, 1024], BF16) for i in range(4)]
    qf = [sb(N(f"qf{i}"), [128, 512], F32) for i in range(4)]
    kf = [sb(N(f"kf{i}"), [128, 512], F32) for i in range(2)]
    t1r = [sb(N(f"t1r{i}"), [128, 512], F32) for i in range(2)]
    t2r = [sb(N(f"t2r{i}"), [128, 512], F32) for i in range(2)]
    qrot = [sb(N(f"qrot{i}"), [128, 512], BF16) for i in range(2)]
    krot = [sb(N(f"krot{i}"), [128, 512], BF16) for i in range(2)]
    scr = [sb(N(f"scr{i}"), [128, 1024], F32) for i in range(4)]
    stats = [sb(N(f"stats{i}"), [128, 32], F32) for i in range(4)]
    st2 = [sb(N(f"st2{i}"), [128, 8], F32) for i in range(4)]
    sc = [sb(N(f"sc{i}"), [128, 1024], F32) for i in range(2)] * 2
    Pn = [sb(N(f"pn{i}"), [128, 1024], BF16) for i in range(8)]
    PT = sb(N("pt"), [128, 8 * 512], BF16)
    oall = sb(N("oall"), [128, 4 * 512], BF16)
    yl2 = [sb(N(f"yl2{i}"), [128, 1024], F32) for i in range(2)]
    y1o = sb(N("y1o"), [128, 1024], F32)

    for g2 in range(4):
        ap = bass.AP(dt_in["fw_qkv"], l * D * 1536 + g2 * 2 * 128 * 1536,
                     [[1536, 128], [1536 * 128, 2], [1, 1536]])
        dst = bass.AP(wqkv, g2 * 2 * 1536,
                      [[8 * 1536, 128], [1536, 2], [1, 1536]])
        p.dma("gpsimd", dst, ap, writes=(N("wqkv"),))
    ap = bass.AP(dt_in["fw_wo"], l * 512 * D, [[D, 128], [D * 128, 4], [1, D]])
    p.dma("gpsimd", bass.AP(wo_s, 0, [[4 * 1024, 128], [1024, 4], [1, 1024]]),
          ap, writes=(N("wo"),))
    p.dma("sync", bass.AP(fmask, 0, [[8 * 1024, 128], [1024, 8], [1, 1024]]),
          dt_in["c_fmask"].ap().rearrange("(m p) n -> p m n", p=128),
          writes=(N("fmask"),))
    ctx.bcast(qsb[:, 0:64], dt_in["fw_qs"], 64, (N("qsb"),), offset=l * 64)
    ctx.bcast(qsb[:, 64:128], dt_in["fw_ks"], 64, (N("qsb"),), offset=l * 64)
    for ti, nm in enumerate(("c_xqA", "c_xqB", "c_xkA", "c_xkB")):
        p.dma("sync", bass.AP(scr[3], 0, [[1024, 128], [64, 16], [1, 64]]),
              dt_in[nm].ap().rearrange("(t p) n -> p t n", p=128),
              writes=(N("tabb"),))
        qcol = 0 if ti < 2 else 64
        swap = (ti % 2 == 1)
        out3 = bass.AP(tabl, ti * 1024, [[4096, 128], [64, 16], [32, 2], [1, 32]])
        in3 = bass.AP(scr[3], 0, [[1024, 128], [64, 16], [32, 2], [1, 32]])
        if swap:
            sc3 = bass.AP(qsb, qcol + 32, [[128, 128], [0, 16], [-32, 2], [1, 32]])
        else:
            sc3 = bass.AP(qsb, qcol, [[128, 128], [0, 16], [32, 2], [1, 32]])
        ctx.tt(out3, in3, sc3, ALU.mult, reads=(N("tabb"), N("qsb")),
               writes=(N("tabl"),))

    for b in range(B_LOC):
        p.dma("sync", Kw[1][:],
              bass.AP(dt_in["c_zrow"], 0, [[0, 128], [0, 2], [1, 1024]]),
              writes=(N("K1"),))
        p.dma("sync", Vw[1][:],
              bass.AP(dt_in["c_zrow"], 0, [[0, 128], [0, 2], [1, 1024]]),
              writes=(N("V1"),))
        cur, prv = 0, 1
        for wi in range(4):
            kK, kV = N(f"K{cur}"), N(f"V{cur}")
            pK, pV = N(f"K{prv}"), N(f"V{prv}")
            for tt in range(4):
                t = wi * 4 + tt
                g = b * NT + t
                pa = tt
                kyl = N(f"yl{pa}")
                khb, kht = N(f"hbf{pa % 2}"), N(f"hT{pa}")
                kqf, kkf = N(f"qf{pa}"), N(f"kf{pa % 2}")
                kqr, kkr = N(f"qrot{pa % 2}"), N(f"krot{pa % 2}")
                load_shifted(ctx, yl[pa], yr, g, t, kyl, dt_in, "yr")
                ctx.cp(hbf[pa % 2][:], yl[pa][:], reads=(kyl,), writes=(khb,))
                tb = ctx.tbank()
                for c in range(8):
                    ctx.tr(PST[tb][:, c * 128:(c + 1) * 128],
                           hbf[pa % 2][:, c * 128:(c + 1) * 128],
                           reads=(khb,), wkey=f"psT{tb}")
                ctx.cp(hT[pa][:], PST[tb][:], reads=(f"psT{tb}",), writes=(kht,))
                for o3 in range(3):
                    for c in range(8):
                        ctx.mm(PS[o3][:], hT[pa][:, c * 128:(c + 1) * 128],
                               wqkv[:, c * 1536 + o3 * 512:
                                    c * 1536 + (o3 + 1) * 512],
                               reads=(kht, N("wqkv")), wkey=f"ps{o3}",
                               start=(c == 0), stop=(c == 7))
                ctx.cp(qf[pa][:], PS[0][:], reads=("ps0",), writes=(kqf,),
                       eng="scalar")
                ctx.cp(kf[pa % 2][:], PS[1][:], reads=("ps1",),
                       writes=(kkf,), eng="scalar")
                ctx.cp(Vw[cur][:, tt * 512:(tt + 1) * 512], PS[2][:],
                       reads=("ps2",), writes=(kV,))
                ctx.act(scr[pa][:, 0:512], qf[pa][:], AF.Square,
                        reads=(kqf, N("tabb")), writes=(N(f"scrA{pa}"),))
                ctx.act(scr[pa][:, 512:1024], kf[pa % 2][:], AF.Square,
                        reads=(kkf,), writes=(N(f"scrA{pa}"),))
                ctx.red(stats[pa][:, 0:8],
                        bass.AP(scr[pa], 0, [[1024, 128], [64, 8], [1, 64]]),
                        ALU.add, reads=(N(f"scrA{pa}"),),
                        writes=(N(f"stQ{pa}"),))
                ctx.red(stats[pa][:, 8:16],
                        bass.AP(scr[pa], 512, [[1024, 128], [64, 8], [1, 64]]),
                        ALU.add, reads=(N(f"scrA{pa}"),),
                        writes=(N(f"stQ{pa}"),))
                ctx.rsqrt(stats[pa][:, 16:24], stats[pa][:, 0:8],
                          reads=(N(f"stQ{pa}"),), writes=(N(f"stF{pa}"),),
                          minv=1e12)
                ctx.rsqrt(stats[pa][:, 24:32], stats[pa][:, 8:16],
                          reads=(N(f"stQ{pa}"),), writes=(N(f"stF{pa}"),),
                          minv=1e12)
                ctx.tt(bass.AP(qf[pa], 0, [[512, 128], [64, 8], [1, 64]]),
                       bass.AP(qf[pa], 0, [[512, 128], [64, 8], [1, 64]]),
                       bass.AP(stats[pa], 16, [[32, 128], [1, 8], [0, 64]]),
                       ALU.mult, reads=(kqf, N(f"stF{pa}")), writes=(kqf,))
                ctx.tt(bass.AP(kf[pa % 2], 0, [[512, 128], [64, 8], [1, 64]]),
                       bass.AP(kf[pa % 2], 0, [[512, 128], [64, 8], [1, 64]]),
                       bass.AP(stats[pa], 24, [[32, 128], [1, 8], [0, 64]]),
                       ALU.mult, reads=(kkf, N(f"stF{pa}")), writes=(kkf,),
                       eng="vector")
                xrope(ctx, qf[pa], t1r[pa % 2], t2r[pa % 2], qrot[pa % 2],
                      tabl, 0, t, kqf, kqr, N("tabl"), eng="vector",
                      pfx2=N(f"r{pa % 2}"))
                xrope(ctx, kf[pa % 2], t1r[pa % 2], t2r[pa % 2],
                      krot[pa % 2], tabl, 2, t, kkf, kkr, N("tabl"),
                      eng="vector", pfx2=N(f"r{pa % 2}"))
                tb = ctx.tbank()
                for c in range(4):
                    ctx.tr(PST[tb][:, c * 128:(c + 1) * 128],
                           qrot[pa % 2][:, c * 128:(c + 1) * 128],
                           reads=(kqr,), wkey=f"psT{tb}")
                ctx.cp(bass.AP(Qw, tt * 128, [[4 * 512, 128], [512, 4], [1, 128]]),
                       bass.AP(PST[tb], 0, [[1024, 128], [128, 4], [1, 128]]),
                       reads=(f"psT{tb}",), writes=(N("Q"),))
                tb = ctx.tbank()
                for c in range(4):
                    ctx.tr(PST[tb][:, c * 128:(c + 1) * 128],
                           krot[pa % 2][:, c * 128:(c + 1) * 128],
                           reads=(kkr,), wkey=f"psT{tb}")
                ctx.cp(bass.AP(Kw[cur], tt * 128,
                               [[4 * 512, 128], [512, 4], [1, 128]]),
                       bass.AP(PST[tb], 0, [[1024, 128], [128, 4], [1, 128]]),
                       reads=(f"psT{tb}",), writes=(kK,), eng="scalar")
            def c2_sm1(h, qt):
                hb = (h % 2) * 64
                gp = h // 2
                sp = qt
                sci = qt % 2
                pnn = (h % 2) * 4 + qt
                ksc, kpn, kstm = (N(f"sc{sci}"), N(f"pn{pnn}"), N(f"stM{sp}"))
                for ci in range(2):
                    kb = Kw[prv] if ci == 0 else Kw[cur]
                    kkey = pK if ci == 0 else kK
                    pb_ = 2 + (((h * 4 + qt) * 2 + ci) % 3)
                    ctx.mm(PS[pb_][:],
                           Qw[hb:hb + 64, gp * 512 + qt * 128:
                              gp * 512 + (qt + 1) * 128],
                           kb[hb:hb + 64, gp * 512:(gp + 1) * 512],
                           reads=(N("Q"), kkey), wkey=f"ps{pb_}",
                           start=True, stop=True)
                    mrow = (4 if wi == 0 else 0) + qt
                    ctx.tt(sc[sp][:, ci * 512:(ci + 1) * 512], PS[pb_][:],
                           fmask[:, mrow * 1024 + ci * 512:
                                 mrow * 1024 + (ci + 1) * 512],
                           ALU.add, reads=(f"ps{pb_}", N("fmask")),
                           writes=(ksc,))
                ctx.red(st2[sp][:, 4:5], sc[sp][:], ALU.max,
                        reads=(ksc,), writes=(kstm,), negate=True)
                ctx.act(Pn[pnn][:], sc[sp][:], AF.Exp, reads=(ksc, kstm),
                        writes=(kpn, kstm), bias=st2[sp][:, 4:5],
                        accum=st2[sp][:, 5:6])

            def c2_sm2(h, qt):
                sp = qt
                pnn = (h % 2) * 4 + qt
                kpn, kstm = N(f"pn{pnn}"), N(f"stM{sp}")
                ctx.recip(st2[sp][:, 5:6], st2[sp][:, 5:6], reads=(kstm,),
                          writes=(kstm,))
                ctx.ts(Pn[pnn][:], Pn[pnn][:], st2[sp][:, 5:6], ALU.mult,
                       reads=(kpn, kstm), writes=(kpn,),
                       eng="vector")

            def c2_softmax(h):
                for qt in range(4):
                    c2_sm1(h, qt)
                    if ctx.sm_pend is not None:
                        c2_sm2(*ctx.sm_pend)
                    ctx.sm_pend = (h, qt)

            def c2_pv(h):
                hb = (h % 2) * 64
                gp = h // 2
                for qt in range(4):
                    pnn = (h % 2) * 4 + qt
                    kpn = N(f"pn{pnn}")
                    tb = ctx.tbank()
                    for kc in range(8):
                        ctx.tr(PST[tb][:, kc * 128:(kc + 1) * 128],
                               Pn[pnn][:, kc * 128:(kc + 1) * 128],
                               reads=(kpn,), wkey=f"psT{tb}")
                    ctx.cp(bass.AP(PT, qt * 128,
                                   [[8 * 512, 128], [512, 8], [1, 128]]),
                           bass.AP(PST[tb], 0, [[1024, 128], [128, 8], [1, 128]]),
                           reads=(f"psT{tb}",),
                           writes=tuple(f"{pfx}PT:{kc}:{qt}"
                                        for kc in range(8)),
                           eng=("scalar" if qt % 2 else "vector"))
                psel = PS[0][0:64, :] if h % 2 == 0 else PS[0][64:128, :]
                for kc in range(8):
                    vb = Vw[prv] if kc < 4 else Vw[cur]
                    vkey = pV if kc < 4 else kV
                    ctx.mm(psel,
                           vb[:, (kc % 4) * 512 + h * 64:
                              (kc % 4) * 512 + (h + 1) * 64],
                           PT[:, kc * 512:(kc + 1) * 512],
                           reads=(vkey,) + tuple(
                               f"{pfx}PT:{kc}:{q_}" for q_ in range(4)),
                           wkey="ps0", start=(kc == 0), stop=(kc == 7),
                           skip=True)
                if h % 2 == 1:
                    ctx.cp(oall[:, gp * 512:(gp + 1) * 512], PS[0][:],
                           reads=("ps0",), writes=(f"{pfx}oall:{gp}",),
                           eng="scalar")

            ctx.sm_pend = None
            c2_softmax(0)
            for h in range(8):
                if h + 1 < 8:
                    c2_softmax(h + 1)
                else:
                    c2_sm2(*ctx.sm_pend)
                    ctx.sm_pend = None
                c2_pv(h)
            if getattr(ctx, "dbg_c1", None) is not None and (b, wi) == (0, 0)                     and inv == 0 and l == 0:
                dbgd = ctx.dbg_c1
                for nm, buf in (("Qw", Qw), ("Kw", Kw[cur]), ("Vw", Vw[cur]),
                                ("oall", oall), ("sc", sc[0]), ("pn", Pn[0]),
                                ("qrot", qrot[0]), ("krot", krot[0]),
                                ("qf", qf[0]), ("kf", kf[0]),
                                ("hbf", hbf[0])):
                    p.dma("sync", dbgd[nm].ap(), buf[:],
                          reads=tuple(ctx.p.writer.keys()), writes=())
            for qt in range(4):
                t = wi * 4 + qt
                g = b * NT + t
                pa = qt % 2
                ky2 = N(f"yl2{pa}")
                p.dma("sync", yl2[pa][:], gr_ap(yr, g),
                      reads=(f"yr:{g}",), writes=(ky2,))
                for n in range(2):
                    for c in range(4):
                        ctx.mm(PS[2 + n][:],
                               oall[:, c * 512 + qt * 128:c * 512 + (qt + 1) * 128],
                               wo_s[:, c * 1024 + n * 512:c * 1024 + (n + 1) * 512],
                               reads=(f"{pfx}oall:{c}", N("wo")),
                               wkey=f"ps{2 + n}", start=(c == 0), stop=(c == 3))
                for n in range(2):
                    ctx.tt(y1o[:, n * 512:(n + 1) * 512], PS[2 + n][:],
                           yl2[pa][:, n * 512:(n + 1) * 512], ALU.add,
                           reads=(f"ps{2 + n}", ky2), writes=(N("y1o"),))
                p.dma("sync", gr_ap(y1r, g), y1o[:], reads=(N("y1o"),),
                      writes=(f"y1r:{g}",))
            cur, prv = prv, cur


def xrope(ctx, src, t1r, t2r, outb, tabl, ti0, t, ksrc, kout, ktab,
          eng="vector", pfx2="r"):
    At = bass.AP(tabl, ti0 * 1024 + t * 64, [[4096, 128], [0, 8], [1, 64]])
    s3 = bass.AP(src, 0, [[512, 128], [64, 8], [1, 64]])
    ctx.tt(bass.AP(t1r, 0, [[512, 128], [64, 8], [1, 64]]), s3, At, ALU.mult,
           reads=(ksrc, ktab), writes=(pfx2 + "t1",), eng=eng)
    o2 = bass.AP(t2r, 0, [[512, 128], [64, 8], [32, 2], [1, 32]])
    prt = bass.AP(src, 32, [[512, 128], [64, 8], [-32, 2], [1, 32]])
    Bt = bass.AP(tabl, (ti0 + 1) * 1024 + t * 64,
                 [[4096, 128], [0, 8], [32, 2], [1, 32]])
    ctx.tt(o2, prt, Bt, ALU.mult, reads=(ksrc, ktab), writes=(pfx2 + "t2",),
           eng=eng)
    ctx.tt(outb[:], t1r[:], t2r[:], ALU.add,
           reads=(pfx2 + "t1", pfx2 + "t2"), writes=(kout,), eng=eng)


def pass_ffn_mlp(ctx, dt_in, sb, xr, yr, y1r, w1b, inv, l, last, out_d):
    p = ctx.p
    PS, PST = ctx.ps, ctx.psT
    endinv = (l == 1)
    pfx = f"F{inv}{l}m"
    N = lambda s: pfx + s

    lng = sb(N("lng"), [128, 1024], F32)
    lnb = sb(N("lnb"), [128, 1024], F32)
    b2b = sb(N("b2b"), [128, 1024], F32)
    b1c = sb(N("b1c"), [128, 2 * NHC], F32)
    h2T = sb(N("h2T"), [128, 8 * 1024], BF16)
    h4 = sb(N("h4"), [128, NHC * 1024], BF16)
    w2s = sb(N("w2s"), [128, NHC * 1024], BF16)
    w1s = [sb(N(f"w1s{i}"), [128, 1024], BF16) for i in range(4)]
    gact = sb(N("gact"), [128, 1024], BF16)
    abf = sb(N("abf"), [128, 1024], BF16)
    h2bf = [sb(N(f"h2bf{i}"), [128, 1024], BF16) for i in range(2)]
    hl = [sb(N(f"hl{i}"), [128, 1024], F32) for i in range(4)]
    h2n = [sb(N(f"h2n{i}"), [128, 1024], F32) for i in range(4)]
    scr = [sb(N(f"scr{i}"), [128, 1024], F32) for i in range(2)]
    scr2 = [sb(N(f"scr2{i}"), [128, 1024], F32) for i in range(2)]
    stats = [sb(N(f"stats{i}"), [128, 32], F32) for i in range(4)]
    y1l = [sb(N(f"y1l{i}"), [128, 1024], F32) for i in range(2)]
    xl = [sb(N(f"xl{i}"), [128, 1024], F32) for i in range(2)]
    yo = sb(N("yo"), [128, 1024], F32)

    ctx.bcast(lng[:], dt_in["fw_lng"], 1024, (N("lng"),), offset=l * D)
    ctx.bcast(lnb[:], dt_in["fw_lnb"], 1024, (N("lnb"),), offset=l * D)
    ctx.bcast(b2b[:], dt_in["fw_b2"], 1024, (N("b2b"),), offset=l * D)
    for half in range(2):
        base = l * 2 * HID + half * HID
        p.dma("gpsimd", scr2[0][0:22, 0:128],
              bass.AP(dt_in["c_zrow"], 0, [[0, 22], [1, 128]]),
              reads=("scrB0",), writes=("scrB0",))
        p.dma("sync", scr2[0][0:21, 0:128],
              bass.AP(dt_in["fw_b1"], base, [[128, 21], [1, 128]]),
              reads=("scrB0",), writes=("scrB0",))
        p.dma("sync", scr2[0][21:22, 0:42],
              bass.AP(dt_in["fw_b1"], base + 21 * 128, [[42, 1], [1, 42]]),
              reads=("scrB0",), writes=("scrB0",))
        ctx.p.op("tensor",
                 lambda q, o=PS[4][:, 0:22], i=scr2[0][0:22, 0:128],
                 idn=ctx.identf: q.transpose(o, i, idn[0:22, 0:22]),
                 reads=("scrB0", "identf"), writes=("ps4",), tag="tr")
        ctx.cp(b1c[:, half * NHC:(half + 1) * NHC], PS[4][:, 0:22],
               reads=("ps4",), writes=(N("b1c"),))

    for blk in range(4):
        b = blk // 2
        if blk == 0:
            for i in range(NHC):
                w = min(128, HID - i * 128)
                if w < 128:
                    p.dma("gpsimd", w2s[:, i * 1024:(i + 1) * 1024],
                          bass.AP(dt_in["c_zrow"], 0, [[0, 128], [1, 1024]]),
                          reads=(N("w2s"),), writes=(N("w2s"),))
                p.dma("gpsimd", w2s[0:w, i * 1024:(i + 1) * 1024],
                      bass.AP(dt_in["fw_w2"], l * HID * D + i * 128 * D,
                              [[D, w], [1, D]]),
                      reads=(N("w2s"),), writes=(N("w2s"),),
                      partial=(w < 128))
        for tt in range(8):
            t = (blk % 2) * 8 + tt
            g = b * NT + t
            pa = tt % 4
            khl = N(f"hl{pa}")
            kh2, kh2b = N(f"h2n{pa}"), N(f"h2bf{pa % 2}")
            load_shifted(ctx, hl[pa], y1r, g, t, khl, dt_in, "y1r")
            ctx.ln_tile(hl[pa][:], h2n[pa][:], khl, kh2, scr[pa % 2],
                        scr2[pa % 2], stats[pa], 0, N(f"stL{pa}"),
                        ksc=N(f"sl{pa % 2}"))
            ctx.stt(h2n[pa][:], h2n[pa][:], 1.0, lng[:], ALU.mult, ALU.mult,
                    reads=(kh2, N("lng")), writes=(kh2,))
            ctx.tt(h2bf[pa % 2][:], h2n[pa][:], lnb[:], ALU.add,
                   reads=(kh2, N("lnb")), writes=(kh2b,),
                   eng="vector")
            tb = ctx.tbank()
            for c in range(8):
                ctx.tr(PST[tb][:, c * 128:(c + 1) * 128],
                       h2bf[pa % 2][:, c * 128:(c + 1) * 128],
                       reads=(kh2b,), wkey=f"psT{tb}")
            ctx.cp(bass.AP(h2T, tt * 128, [[8 * 1024, 128], [1024, 8], [1, 128]]),
                   bass.AP(PST[tb], 0, [[1024, 128], [128, 8], [1, 128]]),
                   reads=(f"psT{tb}",), writes=(f"{pfx}h2T:{tt}",), eng="scalar")
        for i in range(NHC):
            pa = i % 2
            ka, kg = N(f"w1a{pa}"), N(f"w1g{pa}")
            p.dma("gpsimd", w1s[pa][:],
                  bass.AP(w1b, (l * 2 * NHC + i) * 128 * 1024,
                          [[1024, 128], [1, 1024]]),
                  reads=(f"w1b:{l}:{i}",), writes=(ka,))
            p.dma("gpsimd", w1s[2 + pa][:],
                  bass.AP(w1b, (l * 2 * NHC + NHC + i) * 128 * 1024,
                          [[1024, 128], [1, 1024]]),
                  reads=(f"w1b:{l}:{NHC + i}",), writes=(kg,))
            kpa, kpg = f"ps{pa}", f"ps{2 + pa}"
            for nn in range(2):
                psa, psg = PS[pa], PS[2 + pa]
                for c in range(8):
                    ctx.mm(psa[:], w1s[pa][:, c * 128:(c + 1) * 128],
                           h2T[:, c * 1024 + nn * 512:c * 1024 + (nn + 1) * 512],
                           reads=(ka,) + tuple(f"{pfx}h2T:{t_}"
                                               for t_ in range(8)),
                           wkey=kpa, start=(c == 0), stop=(c == 7))
                for c in range(8):
                    ctx.mm(psg[:], w1s[2 + pa][:, c * 128:(c + 1) * 128],
                           h2T[:, c * 1024 + nn * 512:c * 1024 + (nn + 1) * 512],
                           reads=(kg,) + tuple(f"{pfx}h2T:{t_}"
                                               for t_ in range(8)),
                           wkey=kpg, start=(c == 0), stop=(c == 7))
                ctx.act(gact[:, nn * 512:(nn + 1) * 512], psg[:], AF.Gelu,
                        reads=(kpg, N("b1c")), writes=(N("gact"),),
                        bias=b1c[:, NHC + i:NHC + i + 1])
                ctx.ts(abf[:, nn * 512:(nn + 1) * 512], psa[:],
                       b1c[:, i:i + 1], ALU.add, reads=(kpa, N("b1c")),
                       writes=(N("abf"),))
            ctx.tt(h4[:, i * 1024:(i + 1) * 1024], abf[:], gact[:], ALU.mult,
                   reads=(N("abf"), N("gact")), writes=(f"{pfx}h4:{i}",),
                   eng="vector")
        for tt in range(8):
            t = (blk % 2) * 8 + tt
            g = b * NT + t
            pa = tt % 2
            ky1, kx2 = N(f"y1l{pa}"), N(f"xl{pa}")
            p.dma("sync", y1l[pa][:], gr_ap(y1r, g),
                  reads=(f"y1r:{g}",), writes=(ky1,))
            if endinv:
                p.dma("sync", xl[pa][:], dram_row_ap(xr, g),
                      reads=(f"xr:{g}",), writes=(kx2,))
            pq = (4, 0) if tt % 2 == 0 else (1, 2)
            for n in range(2):
                for i in range(NHC):
                    ctx.mm(PS[pq[n]][:],
                           h4[:, i * 1024 + tt * 128:i * 1024 + (tt + 1) * 128],
                           w2s[:, i * 1024 + n * 512:i * 1024 + (n + 1) * 512],
                           reads=(f"{pfx}h4:{i}", N("w2s")),
                           wkey=f"ps{pq[n]}", start=(i == 0),
                           stop=(i == NHC - 1))
            for n in range(2):
                ctx.tt(yo[:, n * 512:(n + 1) * 512], PS[pq[n]][:],
                       y1l[pa][:, n * 512:(n + 1) * 512], ALU.add,
                       reads=(f"ps{pq[n]}", ky1), writes=(N("yo"),))
            ctx.tt(yo[:], yo[:], b2b[:], ALU.add, reads=(N("yo"), N("b2b")),
                   writes=(N("yo"),), eng="vector")
            if not endinv:
                p.dma("sync", gr_ap(yr, g), yo[:], reads=(N("yo"),),
                      writes=(f"yr:{g}",))
            else:
                ctx.tt(xl[pa][:], yo[:], xl[pa][:], ALU.add,
                       reads=(N("yo"), kx2), writes=(kx2,), eng="vector")
                wb_store(ctx, xl[pa][:], kx2, g, xr, yr, scr[pa % 2],
                         scr2[pa % 2], stats[pa], last, out_d, pa % 2)


def pass_attn_b(ctx, dt_in, sb, xr, yr, wTd, last, out_d):
    p = ctx.p
    PS, PST = ctx.ps, ctx.psT

    wsl = [sb(f"B_w{i}", [128, 8 * 1024], BF16) for i in range(2)]
    cmask = sb("B_cmask", [128, 4 * 512], BF16)
    rtab = sb("B_rtab", [128, 2 * 1024], F32)
    QFM = sb("B_qfm", [128, 8 * 512], BF16)
    KFM = sb("B_kfm", [128, 8 * 2048], BF16)
    VTM = sb("B_vtm", [128, 16 * 1024], BF16)
    yl = [sb(f"B_yl{i}", [128, 1024], F32) for i in range(2)]
    ybf = sb("B_ybf", [128, 1024], BF16)
    yT = sb("B_yT", [128, 1024], BF16)
    t1r = sb("B_t1r", [128, 1024], F32)
    t2r = sb("B_t2r", [128, 1024], F32)
    qrot = sb("B_qrot", [128, 1024], BF16)
    krot = sb("B_krot", [128, 1024], BF16)
    kf = sb("B_kf", [128, 1024], F32)
    st2 = sb("B_st2", [128, 8], F32)
    sc = sb("B_sc", [128, 2048], F32)
    Pn = [sb(f"B_pn{i}", [128, 2048], BF16) for i in range(2)]
    PT = sb("B_ptb", [128, 16 * 512], BF16)
    oall = sb("B_oall", [128, 8 * 512], BF16)
    xt2 = [sb(f"B_xt2{i}", [128, 1024], F32) for i in range(2)]
    scr = sb("B_scr", [128, 1024], F32)
    scr2 = t2r
    stats = sb("B_stats", [128, 32], F32)

    p.dma("sync", bass.AP(cmask, 0, [[2048, 128], [512, 4], [1, 512]]),
          dt_in["c_cmask"].ap().rearrange("(m p) n -> p m n", p=128),
          writes=("cmask",))
    for ti, nm in enumerate(("c_ropeA", "c_ropeB")):
        p.dma("sync", bass.AP(rtab, ti * 1024, [[2048, 128], [64, 16], [1, 64]]),
              dt_in[nm].ap().rearrange("(t p) n -> p t n", p=128),
              writes=("Brtab",))

    wnames = ("bw_q", "bw_k", "bw_v", "bw_o")

    def wTload(slot, wi, key):
        nm = wnames[wi]
        for g2 in range(2):
            ap = bass.AP(wTd, wi * D * D + g2 * 4 * 128 * D,
                         [[D, 128], [D * 128, 4], [1, D]])
            dst = bass.AP(wsl[slot], g2 * 4 * 1024,
                          [[8 * 1024, 128], [1024, 4], [1, 1024]])
            p.dma("gpsimd", dst, ap,
                  reads=tuple(f"wTd:{nm}:{j}" for j in range(8)),
                  writes=(key,))

    for b in range(B_LOC):
        wTload(0, 1, "Bw0")
        wTload(1, 2, "Bw1")
        for t in range(NT):
            g = b * NT + t
            pa = t % 2
            kyl = f"Byl{pa}"
            p.dma("sync", yl[pa][:], gr_ap(yr, g), reads=(f"yr:{g}",),
                  writes=(kyl,))
            ctx.cp(ybf[:], yl[pa][:], reads=(kyl,), writes=("Bybf",))
            tb = ctx.tbank()
            for c in range(8):
                ctx.tr(PST[tb][:, c * 128:(c + 1) * 128],
                       ybf[:, c * 128:(c + 1) * 128], reads=("Bybf",),
                       wkey=f"psT{tb}")
            ctx.cp(yT[:], PST[tb][:], reads=(f"psT{tb}",), writes=("ByT",))
            for n in range(2):
                for c in range(8):
                    ctx.mm(PS[n][:], yT[:, c * 128:(c + 1) * 128],
                           wsl[0][:, c * 1024 + n * 512:c * 1024 + (n + 1) * 512],
                           reads=("ByT", "Bw0"), wkey=f"ps{n}",
                           start=(c == 0), stop=(c == 7))
            for n in range(2):
                for c in range(8):
                    ctx.mm(PS[2 + n][:], yT[:, c * 128:(c + 1) * 128],
                           wsl[1][:, c * 1024 + n * 512:c * 1024 + (n + 1) * 512],
                           reads=("ByT", "Bw1"), wkey=f"ps{2 + n}",
                           start=(c == 0), stop=(c == 7))
            ctx.cp(kf[:, 0:512], PS[0][:], reads=("ps0",), writes=("Bkf",),
                   eng="scalar")
            ctx.cp(kf[:, 512:1024], PS[1][:], reads=("ps1",), writes=("Bkf",),
                   eng="scalar")
            brope(ctx, kf, "Bkf", t1r, t2r, krot, rtab, t, "Bkrot")
            ctx.cp(VTM[:, t * 1024:t * 1024 + 512], PS[2][:], reads=("ps2",),
                   writes=(f"BVTM:{t}:0",))
            ctx.cp(VTM[:, t * 1024 + 512:(t + 1) * 1024], PS[3][:],
                   reads=("ps3",), writes=(f"BVTM:{t}:1",), eng="scalar")
            tb = ctx.tbank()
            for c in range(8):
                ctx.tr(PST[tb][:, c * 128:(c + 1) * 128],
                       krot[:, c * 128:(c + 1) * 128], reads=("Bkrot",),
                       wkey=f"psT{tb}")
            ctx.cp(bass.AP(KFM, t * 128, [[8 * 2048, 128], [2048, 8], [1, 128]]),
                   bass.AP(PST[tb], 0, [[1024, 128], [128, 8], [1, 128]]),
                   reads=(f"psT{tb}",), writes=(f"BKFM:{t}",))
        wTload(0, 0, "Bw0")
        wTload(1, 3, "Bw1")
        for qb in range(4):
            for tt in range(4):
                t = qb * 4 + tt
                g = b * NT + t
                pa = t % 2
                kyl = f"Byl{pa}"
                p.dma("sync", yl[pa][:], gr_ap(yr, g),
                      reads=(f"yr:{g}",), writes=(kyl,))
                ctx.cp(ybf[:], yl[pa][:], reads=(kyl,), writes=("Bybf",))
                tb = ctx.tbank()
                for c in range(8):
                    ctx.tr(PST[tb][:, c * 128:(c + 1) * 128],
                           ybf[:, c * 128:(c + 1) * 128], reads=("Bybf",),
                           wkey=f"psT{tb}")
                ctx.cp(yT[:], PST[tb][:], reads=(f"psT{tb}",), writes=("ByT",))
                for n in range(2):
                    for c in range(8):
                        ctx.mm(PS[n][:], yT[:, c * 128:(c + 1) * 128],
                               wsl[0][:, c * 1024 + n * 512:
                                    c * 1024 + (n + 1) * 512],
                               reads=("ByT", "Bw0"), wkey=f"ps{n}",
                               start=(c == 0), stop=(c == 7))
                ctx.cp(kf[:, 0:512], PS[0][:], reads=("ps0",),
                       writes=("Bkf",), eng="scalar")
                ctx.cp(kf[:, 512:1024], PS[1][:], reads=("ps1",),
                       writes=("Bkf",), eng="scalar")
                brope(ctx, kf, "Bkf", t1r, t2r, qrot, rtab, t, "Bqrot")
                tb = ctx.tbank()
                for c in range(8):
                    ctx.tr(PST[tb][:, c * 128:(c + 1) * 128],
                           qrot[:, c * 128:(c + 1) * 128], reads=("Bqrot",),
                           wkey=f"psT{tb}")
                ctx.cp(bass.AP(QFM, tt * 128,
                               [[8 * 512, 128], [512, 8], [1, 128]]),
                       bass.AP(PST[tb], 0, [[1024, 128], [128, 8], [1, 128]]),
                       reads=(f"psT{tb}",), writes=(f"BQFM:{tt}",))
            nch = qb + 1
            nkc = nch * 4

            def b2_sm1(gp, h2, qt):
                hb = h2 * 64
                sp = qt % 2
                for c in range(nch):
                    pb_ = 2 + ((((gp * 2 + h2) * 4 + qt) * nch + c) % 3)
                    ctx.mm(PS[pb_][:],
                           QFM[hb:hb + 64, gp * 512 + qt * 128:
                               gp * 512 + (qt + 1) * 128],
                           KFM[hb:hb + 64, gp * 2048 + c * 512:
                               gp * 2048 + (c + 1) * 512],
                           reads=(f"BQFM:{qt}",) + tuple(
                               f"BKFM:{4 * c + i}" for i in range(4)),
                           wkey=f"ps{pb_}", start=True, stop=True)
                    if c == qb:
                        ctx.tt(sc[:, c * 512:(c + 1) * 512],
                               PS[pb_][:],
                               cmask[:, qt * 512:(qt + 1) * 512],
                               ALU.add, reads=(f"ps{pb_}", "cmask"),
                               writes=(f"Bsc:{c}",))
                    else:
                        ctx.cp(sc[:, c * 512:(c + 1) * 512],
                               PS[pb_][:], reads=(f"ps{pb_}",),
                               writes=(f"Bsc:{c}",), eng="scalar")
                ctx.red(st2[:, 4 + sp:5 + sp], sc[:, :nch * 512], ALU.max,
                        reads=tuple(f"Bsc:{c}" for c in range(nch)),
                        writes=(f"BstM{sp}",), negate=True)
                ctx.act(Pn[sp][:, :nch * 512], sc[:, :nch * 512], AF.Exp,
                        reads=(f"BstM{sp}",) + tuple(
                            f"Bsc:{c}" for c in range(nch)),
                        writes=(f"BPn{sp}", f"BstM{sp}"),
                        bias=st2[:, 4 + sp:5 + sp], accum=st2[:, 6 + sp:7 + sp])

            def b2_sm2pt(gp, h2, qt):
                sp = qt % 2
                ctx.recip(st2[:, 6 + sp:7 + sp], st2[:, 6 + sp:7 + sp],
                          reads=(f"BstM{sp}",), writes=(f"BstM{sp}",))
                ctx.ts(Pn[sp][:, :nch * 512], Pn[sp][:, :nch * 512],
                       st2[:, 6 + sp:7 + sp], ALU.mult,
                       reads=(f"BPn{sp}", f"BstM{sp}"),
                       writes=(f"BPn{sp}",), eng="vector")
                for kg in range((nkc + 7) // 8):
                    kcs = list(range(kg * 8, min(nkc, kg * 8 + 8)))
                    tb = ctx.tbank()
                    for kc in kcs:
                        ctx.tr(PST[tb][:, (kc % 8) * 128:(kc % 8 + 1) * 128],
                               Pn[sp][:, kc * 128:(kc + 1) * 128],
                               reads=(f"BPn{sp}",), wkey=f"psT{tb}")
                    nn = len(kcs)
                    ctx.cp(bass.AP(PT, kcs[0] * 512 + qt * 128,
                                   [[16 * 512, 128], [512, nn], [1, 128]]),
                           bass.AP(PST[tb], 0,
                                   [[1024, 128], [128, nn], [1, 128]]),
                           reads=(f"psT{tb}",),
                           writes=tuple(f"BPT:{kc}:{qt}" for kc in kcs),
                           eng=("scalar" if qt % 2 else "vector"))

            for gp in range(8):
                for h2 in range(2):
                    h = gp * 2 + h2
                    pend = None
                    for qt in range(4):
                        b2_sm1(gp, h2, qt)
                        if pend is not None:
                            b2_sm2pt(*pend)
                        pend = (gp, h2, qt)
                    b2_sm2pt(*pend)
                    psel = PS[0][0:64, :] if h2 == 0 else PS[0][64:128, :]
                    for kc in range(nkc):
                        ctx.mm(psel,
                               VTM[:, kc * 1024 + h * 64:
                                   kc * 1024 + (h + 1) * 64],
                               PT[:, kc * 512:(kc + 1) * 512],
                               reads=(f"BVTM:{kc}:{h // 8}",) + tuple(
                                   f"BPT:{kc}:{q_}" for q_ in range(4)),
                               wkey="ps0", start=(kc == 0),
                               stop=(kc == nkc - 1), skip=True)
                ctx.cp(oall[:, gp * 512:(gp + 1) * 512], PS[0][:],
                       reads=("ps0",), writes=(f"Boall:{gp}",))
            for qt in range(4):
                t = qb * 4 + qt
                g = b * NT + t
                pa = t % 2
                kx2 = f"Bxt2{pa}"
                p.dma("sync", xt2[pa][:], dram_row_ap(xr, g),
                      reads=(f"xr:{g}",), writes=(kx2,))
                for n in range(2):
                    for c in range(8):
                        ctx.mm(PS[2 + n][:],
                               oall[:, c * 512 + qt * 128:c * 512 + (qt + 1) * 128],
                               wsl[1][:, c * 1024 + n * 512:
                                    c * 1024 + (n + 1) * 512],
                               reads=(f"Boall:{c}", "Bw1"), wkey=f"ps{2 + n}",
                               start=(c == 0), stop=(c == 7))
                for n in range(2):
                    ctx.tt(xt2[pa][:, n * 512:(n + 1) * 512], PS[2 + n][:],
                           xt2[pa][:, n * 512:(n + 1) * 512], ALU.add,
                           reads=(f"ps{2 + n}", kx2), writes=(kx2,))
                wb_store(ctx, xt2[pa][:], kx2, g, xr, yr, scr, scr2, stats,
                         last, out_d)


def brope(ctx, src, ksrc, t1r, t2r, outb, rtab, t, kout):
    At = bass.AP(rtab, t * 64, [[2048, 128], [0, 16], [1, 64]])
    s3 = bass.AP(src, 0, [[1024, 128], [64, 16], [1, 64]])
    ctx.tt(bass.AP(t1r, 0, [[1024, 128], [64, 16], [1, 64]]), s3, At,
           ALU.mult, reads=(ksrc, "Brtab"), writes=("Brt1",))
    oe = bass.AP(t2r, 0, [[1024, 128], [64, 16], [2, 32]])
    pe = bass.AP(src, 1, [[1024, 128], [64, 16], [2, 32]])
    be = bass.AP(rtab, 1024 + t * 64, [[2048, 128], [0, 16], [2, 32]])
    ctx.tt(oe, pe, be, ALU.mult, reads=(ksrc, "Brtab"), writes=("Brt2",),
           eng="vector")
    oo = bass.AP(t2r, 1, [[1024, 128], [64, 16], [2, 32]])
    po = bass.AP(src, 0, [[1024, 128], [64, 16], [2, 32]])
    bo = bass.AP(rtab, 1024 + t * 64 + 1, [[2048, 128], [0, 16], [2, 32]])
    ctx.tt(oo, po, bo, ALU.mult, reads=(ksrc, "Brtab"), writes=("Brt2",),
           eng="vector")
    ctx.tt(outb[:], t1r[:], t2r[:], ALU.add, reads=("Brt1", "Brt2"),
           writes=(kout,), eng="vector")


def kernel(**inputs):
    consts = host_consts()
    x = np.ascontiguousarray(inputs["L0_attn_a_x"], dtype=np.float32)
    base = {
        "aw_normw": inputs["L0_attn_a_norm_w"],
        "aw_q": inputs["L0_attn_a_Wq"], "aw_k": inputs["L0_attn_a_Wk"],
        "aw_v": inputs["L0_attn_a_Wv"], "aw_o": inputs["L0_attn_a_Wo"],
        "aw_gate": np.reshape(inputs["L0_attn_a_gate"], [1]),
        "aw_lt": np.reshape(inputs["L0_attn_a_log_temperature"], [1]),
        "bw_q": inputs["L0_attn_b_Wq"], "bw_k": inputs["L0_attn_b_Wk"],
        "bw_v": inputs["L0_attn_b_Wv"], "bw_o": inputs["L0_attn_b_Wo"],
        "fw_qkv": inputs["L0_ffn_wqkv"], "fw_qs": inputs["L0_ffn_q_scale"],
        "fw_ks": inputs["L0_ffn_k_scale"], "fw_wo": inputs["L0_ffn_wo"],
        "fw_lng": inputs["L0_ffn_ln_g"], "fw_lnb": inputs["L0_ffn_ln_b"],
        "fw_w1": inputs["L0_ffn_w1"], "fw_b1": inputs["L0_ffn_b1"],
        "fw_w2": inputs["L0_ffn_w2"], "fw_b2": inputs["L0_ffn_b2"],
        "c_ident": consts["ident"], "c_cmask": consts["cmask"],
        "c_fmask": consts["fmask"], "c_ropeA": consts["ropeA"],
        "c_ropeB": consts["ropeB"], "c_xqA": consts["xqA"],
        "c_xqB": consts["xqB"], "c_xkA": consts["xkA"],
        "c_xkB": consts["xkB"], "c_zrow": consts["zrow"],
        "c_eps": consts["eps"],
    }
    base = {k: np.ascontiguousarray(v) for k, v in base.items()}
    nc = build(PASSES)
    in_maps = []
    for c in range(8):
        m = dict(base)
        m["x"] = np.ascontiguousarray(x[2 * c:2 * c + 2].reshape(B_LOC * S, D))
        in_maps.append(m)
    res = run_bass_kernel_spmd(nc, in_maps, core_ids=list(range(8)))
    if PASSES == "A1":
        kernel.dbg = [{k: np.asarray(v) for k, v in res.results[c].items()}
                      for c in range(8)]
        return np.zeros((16, S, D), np.float32)
    if PASSES in ("F1", "F1m"):
        kernel.dbg = [
            {k: np.asarray(res.results[c][f"dbg_{k}"])
             for k in ("xr", "yr", "y1r")} for c in range(8)]
        if PASSES == "F1":
            for c in range(8):
                for k in list(res.results[c].keys()):
                    if k.startswith("dbgC_"):
                        kernel.dbg[c][k] = np.asarray(res.results[c][k])
        return np.zeros((16, S, D), np.float32)
    outs = [np.asarray(res.results[c]["out"]).reshape(B_LOC, S, D)
            for c in range(8)]
    full = np.concatenate(outs, axis=0).astype(np.float32)
    if PASSES != "full":
        kernel.dbg = [
            {k: np.asarray(res.results[c][f"dbg_{k}"]) for k in ("xr", "yr")}
            for c in range(8)]
    return full


# version 11
# speedup vs baseline: 1.3855x; 1.0862x over previous
import numpy as np
import ml_dtypes
from contextlib import ExitStack
import concourse.bass as bass
import concourse.mybir as mybir
from concourse.bass_utils import run_bass_kernel_spmd

F32 = mybir.dt.float32
BF16 = mybir.dt.bfloat16
AF = mybir.ActivationFunctionType
ALU = mybir.AluOpType
AX = mybir.AxisListType

NEG = -1.0e30
B_LOC = 2
S = 2048
D = 1024
NT = 16
HID = 2730
NHC = 22

PASSES = "full"
FORCE_SYNC = False

ENGINES = ("sync", "tensor", "vector", "scalar", "gpsimd")
DMA_POOL = {"sync": 32, "tensor": 2, "vector": 2, "scalar": 8, "gpsimd": 24}


class Prog:
    def __init__(self):
        self.ops = {e: [] for e in ENGINES}
        self.writer = {}
        self.readers = {}
        self.fence_deps = ()

    def op(self, eng, fn, reads=(), writes=(), dma=False, tag="", accum=False):
        deps = set(self.fence_deps)
        for k in reads:
            for w in self.writer.get(k, {}).values():
                deps.add(w)
        for k in writes:
            for w in self.writer.get(k, {}).values():
                deps.add(w)
            for r in self.readers.get(k, ()):
                deps.add(r)
        idx = len(self.ops[eng])
        me = (eng, idx)
        deps.discard(me)
        self.ops[eng].append([fn, dma, deps, False, tag, accum])
        for k in writes:
            self.writer.setdefault(k, {})[eng] = me
            self.readers[k] = []
        for k in reads:
            self.readers.setdefault(k, []).append(me)

    def dma(self, queue, out, in_, reads=(), writes=(), tag="", partial=False,
            **kw):
        self.op(queue, (lambda q, o=out, i=in_, kw=kw: q.dma_start(o, i, **kw)),
                reads=reads, writes=writes, dma=True, tag=tag)

    def fence(self):
        self.fence_deps = tuple(
            (e, len(self.ops[e]) - 1) for e in ENGINES if self.ops[e])

    def finalize(self):
        need = {e: set() for e in ENGINES}
        for e in ENGINES:
            for i, o in enumerate(self.ops[e]):
                for (de, di) in o[2]:
                    dop = self.ops[de][di]
                    if de != e or dop[1] or dop[5] or FORCE_SYNC:
                        need[de].add(di)
        self.mnum = {}
        self.dma_final = {}
        for e in ENGINES:
            c = 0
            d = 0
            K = DMA_POOL[e]
            for i, o in enumerate(self.ops[e]):
                if o[1]:
                    s = d % K
                    r = d // K + 1
                    self.mnum[(e, i)] = ("s", s, r)
                    self.dma_final[(e, s)] = r
                    d += 1
                elif i in need[e]:
                    c += 1
                    o[3] = True
                    self.mnum[(e, i)] = ("c", c)

    def emit_engine(self, q, e, sems):
        wm = {}
        for i, (fn, dma, deps, ms, tag, accum) in enumerate(self.ops[e]):
            req = {}
            for (de, di) in deps:
                dop = self.ops[de][di]
                if de == e and not dop[1] and not dop[5] and not FORCE_SYNC:
                    continue
                mn = self.mnum[(de, di)]
                if mn[0] == "c":
                    kk = (de, "c")
                    if mn[1] > req.get(kk, 0):
                        req[kk] = mn[1]
                else:
                    kk = (de, "s", mn[1])
                    if mn[2] > req.get(kk, 0):
                        req[kk] = mn[2]
            for kk in sorted(req):
                val = req[kk]
                if wm.get(kk, 0) >= val:
                    continue
                if kk[1] == "c":
                    q.wait_ge(sems[kk], val)
                else:
                    q.wait_ge(sems[kk], val * 16)
                wm[kk] = val
            if dma:
                kind, s, r = self.mnum[(e, i)]
                kk = (e, "s", s)
                if r > 1 and wm.get(kk, 0) < r - 1:
                    q.wait_ge(sems[kk], 16 * (r - 1))
                    wm[kk] = r - 1
            inst = fn(q)
            if dma:
                kind, s, r = self.mnum[(e, i)]
                inst.then_inc(sems[(e, "s", s)], 16)
            elif ms:
                inst.then_inc(sems[(e, "c")], 1)
        if e == "sync":
            for (qe, s), r in sorted(self.dma_final.items()):
                kk = (qe, "s", s)
                if wm.get(kk, 0) < r:
                    q.wait_ge(sems[kk], 16 * r)


def host_consts():
    c = {}
    c["ident"] = np.eye(128, dtype=ml_dtypes.bfloat16)
    cm = np.zeros((4, 128, 512), dtype=np.float32)
    for d_ in range(4):
        i = np.arange(128)[:, None]
        j = np.arange(512)[None, :]
        cm[d_] = np.where(j <= d_ * 128 + i, 0.0, NEG)
    c["cmask"] = cm.astype(ml_dtypes.bfloat16).reshape(4 * 128, 512)
    fm = np.zeros((5, 128, 640), dtype=np.float32)
    pp = np.arange(128)[:, None]
    m = np.arange(640)[None, :]
    band = (m >= pp) & (m <= pp + 512)
    for jt in range(4):
        fm[jt] = np.where(band & (m >= 512 - 128 * jt), 0.0, NEG)
    fm[4] = np.where(band, 0.0, NEG)
    c["fmask"] = fm.astype(ml_dtypes.bfloat16).reshape(5 * 128, 640)
    hd = 64
    pos = np.arange(S, dtype=np.float64)
    inv = 1.0 / (10000.0 ** (np.arange(0, hd, 2, dtype=np.float64) / hd))
    fr = np.concatenate([np.outer(pos, inv), np.outer(pos, inv)], -1)
    cosb, sinb = np.cos(fr), np.sin(fr)
    Bt = sinb.copy()
    Bt[:, 0::2] = -sinb[:, 0::2]
    c["ropeA"] = cosb.astype(np.float32)
    c["ropeB"] = Bt.astype(np.float32)
    half = np.arange(0, hd, 2, dtype=np.float64)
    invf = 1.0 / (10000.0 ** (half / hd))
    fr2 = np.outer(pos, invf)
    fr2 = np.concatenate([fr2, fr2], -1)
    base = (half + 0.4 * hd) / (1.4 * hd)
    power = (pos - S // 2) / 256.0
    scale = base[None, :] ** power[:, None]
    scale = np.concatenate([scale, scale], -1)
    cosx, sinx = np.cos(fr2), np.sin(fr2)
    sgn = np.ones(hd)
    sgn[:32] = -1.0
    c["xqA"] = (cosx * scale * 8.0).astype(np.float32)
    c["xqB"] = (sgn[None, :] * sinx * scale * 8.0).astype(np.float32)
    c["xkA"] = (cosx / scale).astype(np.float32)
    c["xkB"] = (sgn[None, :] * sinx / scale).astype(np.float32)
    c["zrow"] = np.zeros((1, 1024), dtype=ml_dtypes.bfloat16)
    c["eps"] = np.array([1e-5, 1e-6], dtype=np.float32)
    return c


def _is_small(out):
    try:
        return out.free_size() <= 64
    except Exception:
        return False


class Ctx:
    def __init__(self, nc, prog):
        self.nc = nc
        self.p = prog
        self.ident = None
        self.ps = []
        self.psT = []
        self.pro_steps = []

    def tbank(self):
        self._tb = (getattr(self, "_tb", -1) + 1) % 3
        return self._tb

    def step_prologue(self, n=1):
        for _ in range(n):
            if self.pro_steps:
                self.pro_steps.pop(0)()

    def mm(self, ps_ap, lhsT_ap, rhs_ap, reads, wkey, start, stop, skip=False):
        self.p.op("tensor",
                  lambda q, o=ps_ap, a=lhsT_ap, b=rhs_ap, s1=start, s2=stop,
                  sk=skip: q.matmul(o, a, b, start=s1, stop=s2,
                                    skip_group_check=sk),
                  reads=reads, writes=(wkey,), tag="mm")

    def tr(self, ps_ap, in_ap, reads, wkey):
        self.p.op("tensor",
                  lambda q, o=ps_ap, i=in_ap, idn=self.ident:
                  q.transpose(o, i, idn[:]),
                  reads=reads + ("ident",), writes=(wkey,), tag="tr")

    def act(self, out, in_, func, reads, writes, bias=None, scale=None,
            accum=None):
        def f(q, o=out, i=in_, fu=func, b=bias, s=scale, a=accum):
            kw = {}
            if b is not None:
                kw["bias"] = b
            if s is not None:
                kw["scale"] = s
            if a is not None:
                kw["accum_out"] = a
            return q.activation(o, i, fu, **kw)
        self.p.op("scalar", f, reads=reads, writes=writes, tag="act",
                  accum=(accum is not None) or _is_small(out))

    def tt(self, out, a, b, op, reads, writes, eng="vector"):
        self.p.op(eng,
                  lambda q, o=out, x=a, y=b, p_=op: q.tensor_tensor(o, x, y, p_),
                  reads=reads, writes=writes, tag="tt", accum=_is_small(out))

    def ts(self, out, a, s1, op0, reads, writes, s2=None, op1=None, accum=None,
           eng="vector"):
        def f(q, o=out, x=a, s1=s1, s2=s2, o0=op0, o1=op1, ac=accum):
            kw = {}
            if ac is not None:
                kw["accum_out"] = ac
            if o1 is None:
                return q.tensor_scalar(o, x, s1, s2, o0, **kw)
            return q.tensor_scalar(o, x, s1, s2, o0, o1, **kw)
        self.p.op(eng, f, reads=reads, writes=writes, tag="ts",
                  accum=(accum is not None) or _is_small(out))

    def ttr(self, out, a, b, op0, op1, accum, reads, writes, scale=1.0,
            seed=0.0):
        self.p.op("vector",
                  lambda q, o=out, x=a, y=b, s=scale, sd=seed, o0=op0, o1=op1,
                  ac=accum: q.tensor_tensor_reduce(o, x, y, s, sd, o0, o1, ac),
                  reads=reads, writes=writes, tag="ttr")

    def stt(self, out, a, s, b, op0, op1, reads, writes, eng="vector"):
        self.p.op(eng,
                  lambda q, o=out, x=a, sc=s, y=b, o0=op0, o1=op1:
                  q.scalar_tensor_tensor(o, x, sc, y, o0, o1),
                  reads=reads, writes=writes, tag="stt", accum=_is_small(out))

    def cp(self, out, in_, reads, writes, eng="vector"):
        if eng == "scalar":
            self.p.op(eng, lambda q, o=out, i=in_: q.copy(o, i),
                      reads=reads, writes=writes, tag="cp",
                      accum=_is_small(out))
        else:
            self.p.op(eng, lambda q, o=out, i=in_: q.tensor_copy(o, i),
                      reads=reads, writes=writes, tag="cp",
                      accum=_is_small(out))

    def red(self, out, in_, op, reads, writes, negate=False):
        self.p.op("vector",
                  lambda q, o=out, i=in_, p_=op, n=negate:
                  q.tensor_reduce(o, i, axis=AX.X, op=p_, negate=n),
                  reads=reads, writes=writes, tag="red")

    def recip(self, out, in_, reads, writes):
        self.p.op("vector", lambda q, o=out, i=in_: q.reciprocal(o, i),
                  reads=reads, writes=writes, tag="recip",
                  accum=_is_small(out))

    def rsqrt(self, out, in_, reads, writes, scale=None, bias=None, minv=None):
        self.act(out, in_, AF.Ln, reads, writes, bias=bias, scale=scale)
        self.act(out, out, AF.Exp, writes, writes, scale=-0.5)
        if minv is not None:
            self.ts(out, out, minv, ALU.min, writes, writes)

    def bcast(self, dst, src_t, n, writes, offset=0, queue="sync"):
        ap = bass.AP(src_t, offset, [[0, 128], [1, n]])
        self.p.dma(queue, dst, ap, writes=writes)

    def ln_tile(self, x_ap, out_ap, kx, kout, scr, scr2, stats, sl, kst,
                ksc=None):
        ksc = ksc or kst
        st = lambda i: stats[:, sl + i: sl + i + 1]
        self.act(scr[:], x_ap, AF.Square, reads=(kx,), writes=(ksc + "sA", kst),
                 accum=st(0))
        self.ts(scr2[:], x_ap, 0.0, ALU.add, reads=(kx,), writes=(ksc + "sB", kst),
                s2=0.0, op1=ALU.add, accum=st(1))
        self.ts(st(2), st(1), 1.0 / D, ALU.mult, reads=(kst,), writes=(kst,))
        self.tt(st(5), st(2), st(2), ALU.mult, reads=(kst,), writes=(kst,))
        self.stt(st(3), st(0), 1.0 / D, st(5), ALU.mult, ALU.subtract,
                 reads=(kst,), writes=(kst,))
        self.rsqrt(st(4), st(3), reads=(kst, "epsc"), writes=(kst,),
                   bias=self.eps5)
        self.ts(out_ap, x_ap, st(2), ALU.subtract, reads=(kx, kst),
                writes=(kout,), s2=st(4), op1=ALU.mult)


def dram_row_ap(t, g, width=D):
    return bass.AP(t, g * 128 * width, [[width, 128], [1, width]])


def grow(g):
    return (g // NT) * (S + 128) + 128 + (g % NT) * 128


def gr_ap(t, g, width=D):
    return bass.AP(t, grow(g) * width, [[width, 128], [1, width]])


def gr_shift_ap(t, g):
    return bass.AP(t, (grow(g) - 1) * D + 512, [[D, 128], [1, 512]])


def build(passes=PASSES):
    nc = bass.Bass("TRN2", target_bir_lowering=False)
    prog = Prog()
    ctx = Ctx(nc, prog)
    p = prog

    dt_in = {}

    def din(name, shape, dtype=F32):
        dt_in[name] = nc.dram_tensor(name, list(shape), dtype,
                                     kind="ExternalInput")
        return dt_in[name]

    x_in = din("x", [B_LOC * S, D])
    din("aw_normw", [D]); din("aw_q", [D, D]); din("aw_k", [D, 512])
    din("aw_v", [D, 512]); din("aw_o", [D, D]); din("aw_gate", [1])
    din("aw_lt", [1])
    din("bw_q", [D, D]); din("bw_k", [D, D]); din("bw_v", [D, D])
    din("bw_o", [D, D])
    din("fw_qkv", [2, D, 1536]); din("fw_qs", [2, 64]); din("fw_ks", [2, 64])
    din("fw_wo", [2, 512, D]); din("fw_lng", [2, D]); din("fw_lnb", [2, D])
    din("fw_w1", [2, D, 2 * HID]); din("fw_b1", [2, 2 * HID])
    din("fw_w2", [2, HID, D]); din("fw_b2", [2, D])
    din("c_ident", [128, 128], BF16)
    din("c_cmask", [4 * 128, 512], BF16)
    din("c_fmask", [5 * 128, 640], BF16)
    for nm in ("c_ropeA", "c_ropeB", "c_xqA", "c_xqB", "c_xkA", "c_xkB"):
        din(nm, [S, 64])
    din("c_zrow", [1, 1024], BF16)
    din("c_eps", [2])

    out_d = nc.dram_tensor("out", [B_LOC * S, D], F32, kind="ExternalOutput")
    dbg = {}
    if passes == "F1":
        ctx.dbg_c1 = {}
        for nm, shape, dt_ in (("Qw", [128, 4 * 512], BF16),
                               ("Kw", [128, 4 * 512], BF16),
                               ("Vw", [128, 4 * 512], BF16),
                               ("oall", [128, 4 * 512], BF16),
                               ("sc", [128, 1024], F32),
                               ("pn", [128, 1024], BF16),
                               ("qrot", [128, 512], BF16),
                               ("krot", [128, 512], BF16),
                               ("qf", [128, 512], F32),
                               ("kf", [128, 512], F32),
                               ("hbf", [128, 1024], BF16),
                               ("yl0", [128, 1024], F32),
                               ("qf_raw", [128, 512], F32),
                               ("kf_raw", [128, 512], F32),
                               ("qf_n", [128, 512], F32),
                               ("qrot0", [128, 512], BF16),
                               ("fac0", [128, 32], F32),
                               ("hbf0", [128, 1024], BF16),
                               ("hT0", [128, 1024], BF16)):
            ctx.dbg_c1[nm] = nc.dram_tensor("dbgC_" + nm, shape, dt_,
                                            kind="ExternalOutput")
    if passes == "A1":
        ctx.dbg_a1 = {}
        for nm, shape, dt_ in (("kfm", [128, 4 * 2048], BF16),
                               ("vtm", [128, 16 * 512], BF16),
                               ("qfm", [128, 8 * 512], BF16),
                               ("qn", [128, 1024], BF16),
                               ("kn", [128, 512], BF16),
                               ("qf", [128, 1024], F32),
                               ("hbf", [128, 1024], BF16),
                               ("sc", [128, 2048], F32),
                               ("pn", [128, 2048], BF16),
                               ("oall", [128, 8 * 512], BF16)):
            ctx.dbg_a1[nm] = nc.dram_tensor("dbgA_" + nm, shape, dt_,
                                            kind="ExternalOutput")
    if passes != "full":
        dbg["xr"] = nc.dram_tensor("dbg_xr", [B_LOC * S, D], F32,
                                   kind="ExternalOutput")
        dbg["yr"] = nc.dram_tensor("dbg_yr", [B_LOC * S, D], F32,
                                   kind="ExternalOutput")
        dbg["y1r"] = nc.dram_tensor("dbg_y1r", [B_LOC * S, D], F32,
                                    kind="ExternalOutput")

    xr = nc.dram_tensor("xr", [B_LOC * S, D], F32)
    yr = nc.dram_tensor("yr", [B_LOC * (S + 128), D], F32)
    y1r = nc.dram_tensor("y1r", [B_LOC * (S + 128), D], F32)
    wTd = nc.dram_tensor("wTd", [4, D, D], BF16)
    w1b = nc.dram_tensor("w1b", [2, 2 * NHC, 128, 1024], BF16)

    es = ExitStack()
    ident = es.enter_context(nc.sbuf_tensor("ident", [128, 128], BF16))
    identf = es.enter_context(nc.sbuf_tensor("identf", [128, 128], F32))
    scal = es.enter_context(nc.sbuf_tensor("scal", [128, 8], F32))
    ctx.ident = ident
    ctx.ps = [es.enter_context(nc.psum_tensor(f"ps{i}", [128, 512], F32))
              for i in range(5)]
    ctx.psT = [es.enter_context(nc.psum_tensor(f"psT{i}", [128, 1024], BF16))
               for i in range(3)]

    p.dma("sync", ident[:], dt_in["c_ident"].ap(), writes=("ident",))
    ctx.cp(identf[:], ident[:], reads=("ident",), writes=("identf",))
    ctx.identf = identf
    ctx.bcast(scal[:, 0:1], dt_in["aw_gate"], 1, ("gate",))
    ctx.bcast(scal[:, 1:2], dt_in["aw_lt"], 1, ("temp",))
    ctx.act(scal[:, 1:2], scal[:, 1:2], AF.Exp, reads=("temp",),
            writes=("temp",))
    p.dma("sync", scal[:, 2:4],
          bass.AP(dt_in["c_eps"], 0, [[0, 128], [1, 2]]), writes=("epsc",))
    ctx.eps5 = scal[:, 2:3]
    ctx.eps6 = scal[:, 3:4]

    for tns, knm in ((yr, "yr"), (y1r, "y1r")):
        for b in range(B_LOC):
            p.dma("gpsimd",
                  bass.AP(tns, b * (S + 128) * D, [[D, 128], [1, D]]),
                  bass.AP(dt_in["c_zrow"], 0, [[0, 128], [1, 1024]]),
                  writes=(f"{knm}guard:{b}",))

    make_prologue(ctx, dt_in, wTd, w1b, es)

    def scoped(fn, *args, **kw):
        st = ExitStack()
        sb = lambda name, shape, dtype: st.enter_context(
            nc.sbuf_tensor(name, shape, dtype))
        fn(ctx, dt_in, sb, *args, **kw)
        st.close()
        prog.fence()

    ncut = {"A": 1, "A1": 1, "F1": 2, "F1m": 3, "AF0": 5, "AF0B": 6,
            "full": 10}[passes]
    seq = [
        lambda last: scoped(pass_attn_a, x_in, xr, yr, scal, last=last,
                            out_d=out_d),
        lambda last: scoped(pass_ffn_attn, xr, yr, y1r, inv=0, l=0),
        lambda last: scoped(pass_ffn_mlp, xr, yr, y1r, w1b, inv=0, l=0,
                            last=last, out_d=out_d),
        lambda last: scoped(pass_ffn_attn, xr, yr, y1r, inv=0, l=1),
        lambda last: scoped(pass_ffn_mlp, xr, yr, y1r, w1b, inv=0, l=1,
                            last=last, out_d=out_d),
        lambda last: scoped(pass_attn_b, xr, yr, wTd, last=last, out_d=out_d),
        lambda last: scoped(pass_ffn_attn, xr, yr, y1r, inv=1, l=0),
        lambda last: scoped(pass_ffn_mlp, xr, yr, y1r, w1b, inv=1, l=0,
                            last=last, out_d=out_d),
        lambda last: scoped(pass_ffn_attn, xr, yr, y1r, inv=1, l=1),
        lambda last: scoped(pass_ffn_mlp, xr, yr, y1r, w1b, inv=1, l=1,
                            last=last, out_d=out_d),
    ]
    for i in range(ncut):
        seq[i](i == ncut - 1)

    if passes != "full" and passes != "A1":
        for g in range(B_LOC * NT):
            p.dma("sync", dram_row_ap(dbg["xr"], g), dram_row_ap(xr, g),
                  reads=(f"xr:{g}",))
            p.dma("sync", dram_row_ap(dbg["yr"], g), gr_ap(yr, g),
                  reads=(f"yr:{g}",))
            if passes in ("F1", "F1m"):
                p.dma("sync", dram_row_ap(dbg["y1r"], g),
                      gr_ap(y1r, g), reads=(f"y1r:{g}",))

    prog.finalize()
    sems = {}
    for e in ENGINES:
        sems[(e, "c")] = es.enter_context(nc.semaphore(f"s_{e}_c"))
        for s in range(DMA_POOL[e]):
            sems[(e, "s", s)] = es.enter_context(
                nc.semaphore(f"s_{e}_d{s}"))
    with nc.Block() as blk:
        for e in ENGINES:
            if prog.ops[e] or e == "sync":
                getattr(blk, e)(lambda q, ee=e: prog.emit_engine(q, ee, sems))
    es.close()
    return nc


def make_prologue(ctx, dt_in, wTd, w1b, es):
    nc = ctx.nc
    p = ctx.p
    stg = es.enter_context(nc.sbuf_tensor("pro_stg", [128, 1024], BF16))
    stg2 = es.enter_context(nc.sbuf_tensor("pro_stg2", [128, 1024], BF16))
    steps = ctx.pro_steps

    def w1_step(l, hc):
        def go():
            half, hcl = divmod(hc, NHC)
            col0 = half * HID + hcl * 128
            w = min(128, HID - hcl * 128)
            if w < 128:
                p.dma("gpsimd", stg[:],
                      bass.AP(dt_in["c_zrow"], 0, [[0, 128], [1, 1024]]),
                      reads=("pstg",), writes=("pstg",))
            ap = bass.AP(dt_in["fw_w1"], l * D * 2 * HID + col0,
                         [[2 * HID, 128], [2 * HID * 128, 8], [1, w]])
            dst = bass.AP(stg, 0, [[1024, 128], [128, 8], [1, w]])
            p.dma("gpsimd", dst, ap, reads=("pstg",), writes=("pstg",))
            p.dma("sync",
                  bass.AP(w1b, (l * 2 * NHC + hc) * 128 * 1024,
                          [[1024, 128], [1, 1024]]),
                  stg[:], reads=("pstg",), writes=(f"w1b:{l}:{hc}",))
        return go

    for l in range(2):
        for hc in range(2 * NHC):
            steps.append(w1_step(l, hc))

    def wt_step(wi, nm, i):
        def go():
            ap = bass.AP(dt_in[nm], i * 128 * D, [[D, 128], [1, 1024]])
            p.dma("gpsimd", stg2[:], ap, reads=("pstg2",), writes=("pstg2",))
            if nm == "bw_q":
                ctx.ts(stg2[:], stg2[:], 0.125, ALU.mult, reads=("pstg2",),
                       writes=("pstg2",))
            for jh in range(2):
                pk = f"psT{jh}"
                for j4 in range(4):
                    j = jh * 4 + j4
                    ctx.tr(ctx.psT[jh][:, j4 * 128:(j4 + 1) * 128],
                           stg2[:, j * 128:(j + 1) * 128],
                           reads=("pstg2",), wkey=pk)
            ctx.cp(stg[:, 0:512], ctx.psT[0][:, 0:512], reads=("psT0",),
                   writes=("pstg",))
            ctx.cp(stg[:, 512:1024], ctx.psT[1][:, 0:512], reads=("psT1",),
                   writes=("pstg2e",))
            p.dma("sync",
                  bass.AP(wTd, wi * D * D + i * 128,
                          [[D, 128], [D * 128, 4], [1, 128]]),
                  bass.AP(stg, 0, [[1024, 128], [128, 4], [1, 128]]),
                  reads=("pstg",), writes=(f"wTd:{nm}:{i}",))
            p.dma("sync",
                  bass.AP(wTd, wi * D * D + 4 * 128 * D + i * 128,
                          [[D, 128], [D * 128, 4], [1, 128]]),
                  bass.AP(stg, 512, [[1024, 128], [128, 4], [1, 128]]),
                  reads=("pstg2e",), writes=(f"wTd:{nm}:{i}",))
        return go

    for wi, nm in enumerate(["bw_q", "bw_k", "bw_v", "bw_o"]):
        for i in range(8):
            steps.append(wt_step(wi, nm, i))


def pass_attn_a(ctx, dt_in, sb, x_in, xr, yr, scal, last, out_d):
    p = ctx.p
    PS, PST = ctx.ps, ctx.psT

    wq_s = sb("A_wq", [128, 8 * 1024], BF16)
    wk_s = sb("A_wk", [128, 8 * 512], BF16)
    wv_s = sb("A_wv", [128, 8 * 512], BF16)
    wo_s = sb("A_wo", [128, 8 * 1024], BF16)
    normw = sb("A_normw", [128, 1024], F32)
    cmask = sb("A_cmask", [128, 4 * 512], BF16)
    QFM = sb("A_qfm", [128, 8 * 512], BF16)
    KFM = sb("A_kfm", [128, 4 * 2048], BF16)
    VTM = sb("A_vtm", [128, 16 * 512], BF16)
    xin = [sb(f"A_xin{i}", [128, 1024], F32) for i in range(2)]
    ybuf = [sb(f"A_y{i}", [128, 1024], F32) for i in range(2)]
    hbf = [sb(f"A_hbf{i}", [128, 1024], BF16) for i in range(2)]
    hT = sb("A_hT", [128, 1024], BF16)
    qf = sb("A_qf", [128, 1024], F32)
    kf = sb("A_kf", [128, 512], F32)
    qn = sb("A_qn", [128, 1024], BF16)
    kn = sb("A_kn", [128, 512], BF16)
    scr = [sb(f"A_scr{i}", [128, 1024], F32) for i in range(2)]
    scr2 = [sb(f"A_scr2{i}", [128, 1024], F32) for i in range(2)]
    stats = [sb(f"A_stats{i}", [128, 32], F32) for i in range(2)]
    st2 = [sb(f"A_st2{i}", [128, 8], F32) for i in range(2)]
    sc = sb("A_sc", [128, 2048], F32)
    Pn = [sb(f"A_pn{i}", [128, 2048], BF16) for i in range(2)]
    PT = sb("A_ptb", [128, 16 * 512], BF16)
    oall = sb("A_oall", [128, 8 * 512], BF16)
    xt2 = [sb("A_xt2", [128, 1024], F32)] * 2
    yt2 = [sb(f"A_yt2{i}", [128, 1024], F32) for i in range(2)]
    t1 = sb("A_t1", [128, 1024], F32)

    def wload(dst, name, cols, key):
        t = dt_in[name]
        for g2 in range(2):
            ap = bass.AP(t, g2 * 4 * 128 * cols,
                         [[cols, 128], [cols * 128, 4], [1, cols]])
            dstap = bass.AP(dst, g2 * 4 * cols,
                            [[8 * cols, 128], [cols, 4], [1, cols]])
            p.dma("gpsimd", dstap, ap, writes=(key,))

    wload(wq_s, "aw_q", 1024, "wq")
    wload(wk_s, "aw_k", 512, "wk")
    wload(wv_s, "aw_v", 512, "wv")
    wload(wo_s, "aw_o", 1024, "wo")
    ctx.bcast(normw[:], dt_in["aw_normw"], 1024, ("normw",))
    p.dma("sync", bass.AP(cmask, 0, [[2048, 128], [512, 4], [1, 512]]),
          dt_in["c_cmask"].ap().rearrange("(m p) n -> p m n", p=128),
          writes=("cmask",))

    for b in range(B_LOC):
        for qb in range(4):
            if getattr(ctx, "dbg_a1", None) is not None and (b, qb) == (0, 1):
                dbgd = ctx.dbg_a1
                for nm, buf in (("kfm", KFM), ("vtm", VTM), ("qfm", QFM),
                                ("qn", qn), ("kn", kn), ("qf", qf),
                                ("hbf", hbf[1]), ("sc", sc), ("pn", Pn),
                                ("oall", oall)):
                    p.dma("sync", dbgd[nm].ap(), buf[:],
                          reads=tuple(ctx.p.writer.keys()), writes=())
                return
            for tt in range(4):
                t = qb * 4 + tt
                g = b * NT + t
                pa = t % 2
                ctx.step_prologue(4)
                kx, ky = f"xin{pa}", f"ybuf{pa}"
                p.dma("sync", xin[pa][:], dram_row_ap(x_in, g), writes=(kx,))
                ctx.ln_tile(xin[pa][:], ybuf[pa][:], kx, ky, scr[pa],
                            scr2[pa], stats[pa], 0, f"stL{pa}")
                p.dma("sync", gr_ap(yr, g), ybuf[pa][:], reads=(ky,),
                      writes=(f"yr:{g}",))
                ctx.act(scr[pa][:], ybuf[pa][:], AF.Square, reads=(ky,),
                        writes=(f"scrA{pa}", f"stR{pa}"), scale=0.03125,
                        accum=stats[pa][:, 8:9])
                ctx.rsqrt(stats[pa][:, 9:10], stats[pa][:, 8:9],
                          reads=(f"stR{pa}", "epsc"), writes=(f"stR{pa}",),
                          bias=ctx.eps6)
                kh = f"hbf{pa}"
                ctx.stt(hbf[pa][:], ybuf[pa][:], stats[pa][:, 9:10], normw[:],
                        ALU.mult, ALU.mult, reads=(ky, f"stR{pa}", "normw"),
                        writes=(kh,))
                tb = ctx.tbank()
                for c in range(8):
                    ctx.tr(PST[tb][:, c * 128:(c + 1) * 128],
                           hbf[pa][:, c * 128:(c + 1) * 128],
                           reads=(kh,), wkey=f"psT{tb}")
                ctx.cp(hT[:], PST[tb][:], reads=(f"psT{tb}",), writes=("hT",))
                for n in range(2):
                    for c in range(8):
                        ctx.mm(PS[n][:], hT[:, c * 128:(c + 1) * 128],
                               wq_s[:, c * 1024 + n * 512:c * 1024 + (n + 1) * 512],
                               reads=("hT", "wq"), wkey=f"ps{n}",
                               start=(c == 0), stop=(c == 7))
                for c in range(8):
                    ctx.mm(PS[2][:], hT[:, c * 128:(c + 1) * 128],
                           wk_s[:, c * 512:(c + 1) * 512],
                           reads=("hT", "wk"), wkey="ps2",
                           start=(c == 0), stop=(c == 7))
                for c in range(8):
                    ctx.mm(PS[3][:], hT[:, c * 128:(c + 1) * 128],
                           wv_s[:, c * 512:(c + 1) * 512],
                           reads=("hT", "wv"), wkey="ps3",
                           start=(c == 0), stop=(c == 7))
                ctx.cp(qf[:, 0:512], PS[0][:], reads=("ps0",), writes=("qf",),
                       eng="scalar")
                ctx.cp(qf[:, 512:1024], PS[1][:], reads=("ps1",),
                       writes=("qf",), eng="scalar")
                ctx.cp(kf[:], PS[2][:], reads=("ps2",), writes=("kf",),
                       eng="scalar")
                ctx.cp(VTM[:, t * 512:(t + 1) * 512], PS[3][:], reads=("ps3",),
                       writes=(f"VTM:{t}",))
                ctx.act(scr[pa][:], qf[:], AF.Square, reads=("qf",),
                        writes=(f"scrA{pa}",))
                ctx.act(scr2[pa][:, 0:512], kf[:], AF.Square, reads=("kf",),
                        writes=(f"scrB{pa}",))
                ctx.red(stats[pa][:, 16:20],
                        bass.AP(scr[pa], 0, [[1024, 128], [256, 4], [1, 256]]),
                        ALU.add, reads=(f"scrA{pa}",), writes=(f"stQ{pa}",))
                ctx.red(stats[pa][:, 20:22],
                        bass.AP(scr2[pa], 0, [[1024, 128], [256, 2], [1, 256]]),
                        ALU.add, reads=(f"scrB{pa}",), writes=(f"stQ{pa}",))
                ctx.rsqrt(stats[pa][:, 24:28], stats[pa][:, 16:20],
                          reads=(f"stQ{pa}",), writes=(f"stF{pa}",), minv=1e12)
                ctx.rsqrt(stats[pa][:, 28:30], stats[pa][:, 20:22],
                          reads=(f"stQ{pa}",), writes=(f"stF{pa}",), minv=1e12)
                ctx.ts(stats[pa][:, 24:28], stats[pa][:, 24:28], scal[:, 1:2],
                       ALU.mult, reads=(f"stF{pa}", "temp"),
                       writes=(f"stF{pa}",))
                ctx.tt(bass.AP(qn, 0, [[1024, 128], [256, 4], [1, 256]]),
                       bass.AP(qf, 0, [[1024, 128], [256, 4], [1, 256]]),
                       bass.AP(stats[pa], 24, [[32, 128], [1, 4], [0, 256]]),
                       ALU.mult, reads=("qf", f"stF{pa}"), writes=("qn",))
                ctx.tt(bass.AP(kn, 0, [[512, 128], [256, 2], [1, 256]]),
                       bass.AP(kf, 0, [[512, 128], [256, 2], [1, 256]]),
                       bass.AP(stats[pa], 28, [[32, 128], [1, 2], [0, 256]]),
                       ALU.mult, reads=("kf", f"stF{pa}"), writes=("kn",),
                       eng="vector")
                tb = ctx.tbank()
                for c in range(8):
                    ctx.tr(PST[tb][:, c * 128:(c + 1) * 128],
                           qn[:, c * 128:(c + 1) * 128],
                           reads=("qn",), wkey=f"psT{tb}")
                ctx.cp(bass.AP(QFM, tt * 128,
                               [[8 * 512, 128], [512, 8], [1, 128]]),
                       bass.AP(PST[tb], 0, [[1024, 128], [128, 8], [1, 128]]),
                       reads=(f"psT{tb}",), writes=(f"QFM:{tt}",))
                tb = ctx.tbank()
                for c in range(4):
                    ctx.tr(PST[tb][:, c * 128:(c + 1) * 128],
                           kn[:, c * 128:(c + 1) * 128],
                           reads=("kn",), wkey=f"psT{tb}")
                ctx.cp(bass.AP(KFM, t * 128,
                               [[4 * 2048, 128], [2048, 4], [1, 128]]),
                       bass.AP(PST[tb], 0, [[1024, 128], [128, 4], [1, 128]]),
                       reads=(f"psT{tb}",), writes=(f"KFM:{t}",), eng="scalar")
            nch = qb + 1
            nkc = nch * 4

            def a2_sm1(h, qt):
                kvh = h // 2
                kq = f"QFM:{qt}"
                sp = qt % 2
                for c in range(nch):
                    pb_ = 2 + ((h * nch + c) % 3)
                    for cc in range(2):
                        ctx.mm(PS[pb_][:],
                               QFM[:, (2 * h + cc) * 512 + qt * 128:
                                   (2 * h + cc) * 512 + (qt + 1) * 128],
                               KFM[:, (2 * kvh + cc) * 2048 + c * 512:
                                   (2 * kvh + cc) * 2048 + (c + 1) * 512],
                               reads=(kq,) + tuple(
                                   f"KFM:{4 * c + i}" for i in range(4)),
                               wkey=f"ps{pb_}", start=(cc == 0),
                               stop=(cc == 1))
                    if c == qb:
                        ctx.tt(sc[:, c * 512:(c + 1) * 512], PS[pb_][:],
                               cmask[:, qt * 512:(qt + 1) * 512], ALU.add,
                               reads=(f"ps{pb_}", "cmask"),
                               writes=(f"sc:{c}",))
                    else:
                        ctx.cp(sc[:, c * 512:(c + 1) * 512], PS[pb_][:],
                               reads=(f"ps{pb_}",), writes=(f"sc:{c}",),
                               eng="scalar")
                ctx.red(st2[sp][:, 4:5], sc[:, :nch * 512], ALU.max,
                        reads=tuple(f"sc:{c}" for c in range(nch)),
                        writes=(f"stM{sp}",), negate=True)
                ctx.act(Pn[sp][:, :nch * 512], sc[:, :nch * 512], AF.Exp,
                        reads=(f"stM{sp}",) + tuple(
                            f"sc:{c}" for c in range(nch)),
                        writes=(f"Pn{sp}", f"stM{sp}"),
                        bias=st2[sp][:, 4:5], accum=st2[sp][:, 5:6])

            def a2_sm2pt(h, qt):
                sp = qt % 2
                ctx.recip(st2[sp][:, 5:6], st2[sp][:, 5:6],
                          reads=(f"stM{sp}",), writes=(f"stM{sp}",))
                ctx.ts(Pn[sp][:, :nch * 512], Pn[sp][:, :nch * 512],
                       st2[sp][:, 5:6], ALU.mult,
                       reads=(f"Pn{sp}", f"stM{sp}"), writes=(f"Pn{sp}",),
                       eng="vector")
                for kg in range((nkc + 7) // 8):
                    kcs = list(range(kg * 8, min(nkc, kg * 8 + 8)))
                    tb = ctx.tbank()
                    for kc in kcs:
                        ctx.tr(PST[tb][:, (kc % 8) * 128:(kc % 8 + 1) * 128],
                               Pn[sp][:, kc * 128:(kc + 1) * 128],
                               reads=(f"Pn{sp}",), wkey=f"psT{tb}")
                    nn = len(kcs)
                    ctx.cp(bass.AP(PT, kcs[0] * 512 + qt * 128,
                                   [[16 * 512, 128], [512, nn], [1, 128]]),
                           bass.AP(PST[tb], 0,
                                   [[1024, 128], [128, nn], [1, 128]]),
                           reads=(f"psT{tb}",),
                           writes=tuple(f"PT:{kc}:{qt}" for kc in kcs),
                           eng=("scalar" if qt % 2 else "vector"))

            pend = None
            for h in range(4):
                kvh = h // 2
                for qt in range(4):
                    a2_sm1(h, qt)
                    if pend is not None:
                        a2_sm2pt(*pend)
                    pend = (h, qt)
                a2_sm2pt(*pend)
                pend = None
                for vc in range(2):
                    for kc in range(nkc):
                        ctx.mm(PS[vc][:],
                               VTM[:, kc * 512 + kvh * 256 + vc * 128:
                                   kc * 512 + kvh * 256 + (vc + 1) * 128],
                               PT[:, kc * 512:(kc + 1) * 512],
                               reads=(f"VTM:{kc}",) + tuple(
                                   f"PT:{kc}:{q_}" for q_ in range(4)),
                               wkey=f"ps{vc}", start=(kc == 0),
                               stop=(kc == nkc - 1))
                    ctx.cp(oall[:, (2 * h + vc) * 512:(2 * h + vc + 1) * 512],
                           PS[vc][:], reads=(f"ps{vc}",),
                           writes=(f"oall:{2 * h + vc}",),
                           eng=("vector" if vc == 0 else "scalar"))
            for qt in range(4):
                t = qb * 4 + qt
                g = b * NT + t
                pa = t % 2
                kx2, ky2 = "xt2s", f"yt2{pa}"
                p.dma("sync", xt2[pa][:], dram_row_ap(x_in, g), writes=(kx2,))
                p.dma("sync", yt2[pa][:], gr_ap(yr, g),
                      reads=(f"yr:{g}",), writes=(ky2,))
                for n in range(2):
                    for c in range(8):
                        ctx.mm(PS[2 + n][:],
                               oall[:, c * 512 + qt * 128:c * 512 + (qt + 1) * 128],
                               wo_s[:, c * 1024 + n * 512:c * 1024 + (n + 1) * 512],
                               reads=(f"oall:{c}", "wo"), wkey=f"ps{2 + n}",
                               start=(c == 0), stop=(c == 7))
                for n in range(2):
                    ctx.stt(t1[:, n * 512:(n + 1) * 512], PS[2 + n][:],
                            scal[:, 0:1], yt2[pa][:, n * 512:(n + 1) * 512],
                            ALU.mult, ALU.add,
                            reads=(f"ps{2 + n}", "gate", ky2), writes=("t1",))
                ctx.tt(xt2[pa][:], t1[:], xt2[pa][:], ALU.add,
                       reads=("t1", kx2), writes=(kx2,), eng="vector")
                wb_store(ctx, xt2[pa][:], kx2, g, xr, yr, scr[pa], scr2[pa],
                         stats[pa], last, out_d, pa)


def wb_store(ctx, x1_ap, kx1, g, xr, yr, scr, scr2, stats, last, out_d,
             pa=0):
    p = ctx.p
    if last:
        p.dma("sync", dram_row_ap(out_d, g), x1_ap, reads=(kx1,),
              writes=(f"out:{g}",))
        p.dma("sync", dram_row_ap(xr, g), x1_ap, reads=(kx1,),
              writes=(f"xr:{g}",))
        return
    p.dma("sync", dram_row_ap(xr, g), x1_ap, reads=(kx1,), writes=(f"xr:{g}",))
    ctx.ln_tile(x1_ap, scr[:], kx1, f"scrA{pa}w", scr, scr2, stats, 10,
                f"stW{pa}")
    p.dma("sync", gr_ap(yr, g), scr[:], reads=(f"scrA{pa}w",),
          writes=(f"yr:{g}",))


def load_shifted(ctx, dst, src_t, g, t, key, dt_in, kpre):
    p = ctx.p
    b = g // NT
    p.dma("sync", dst[:, 0:512],
          bass.AP(src_t, grow(g) * D, [[D, 128], [1, 512]]),
          reads=(f"{kpre}:{g}",), writes=(key,))
    prev = (f"{kpre}guard:{b}",) if t == 0 else (f"{kpre}:{g - 1}",)
    p.dma("sync", dst[:, 512:1024], gr_shift_ap(src_t, g),
          reads=prev + (f"{kpre}:{g}",), writes=(key,))


def pass_ffn_attn(ctx, dt_in, sb, xr, yr, y1r, inv, l):
    p = ctx.p
    PS, PST = ctx.ps, ctx.psT
    pfx = f"F{inv}{l}a"
    N = lambda s: pfx + s

    wqkv = sb(N("wqkv"), [128, 8 * 1536], BF16)
    wo_s = sb(N("wo"), [128, 4 * 1024], BF16)
    fmask = sb(N("fmask"), [128, 5 * 640], BF16)
    tabl = sb(N("tabl"), [128, 4 * 1024], F32)
    qsb = sb(N("qsb"), [128, 128], F32)
    Kw = [sb(N(f"K{i}"), [128, 4 * 512], BF16) for i in range(2)]
    Vw = [sb(N(f"V{i}"), [128, 4 * 512], BF16) for i in range(2)]
    Qw = sb(N("Q"), [128, 4 * 512], BF16)
    yl = [sb(N(f"yl{i}"), [128, 1024], F32) for i in range(4)]
    hbf = [sb(N(f"hbf{i}"), [128, 1024], BF16) for i in range(2)]
    hT = [sb(N(f"hT{i}"), [128, 1024], BF16) for i in range(4)]
    qf = [sb(N(f"qf{i}"), [128, 512], F32) for i in range(4)]
    kf = [sb(N(f"kf{i}"), [128, 512], F32) for i in range(2)]
    t1r = [sb(N(f"t1r{i}"), [128, 512], F32) for i in range(2)]
    t2r = [sb(N(f"t2r{i}"), [128, 512], F32) for i in range(2)]
    qrot = [sb(N(f"qrot{i}"), [128, 512], BF16) for i in range(2)]
    krot = [sb(N(f"krot{i}"), [128, 512], BF16) for i in range(2)]
    scr = [sb(N(f"scr{i}"), [128, 1024], F32) for i in range(4)]
    stats = [sb(N(f"stats{i}"), [128, 32], F32) for i in range(4)]
    st2 = [sb(N(f"st2{i}"), [128, 8], F32) for i in range(4)]
    sc = [sb(N(f"sc{i}"), [128, 1024], F32) for i in range(2)] * 2
    Pn = [sb(N(f"pn{i}"), [128, 1024], BF16) for i in range(8)]
    PT = sb(N("pt"), [128, 8 * 512], BF16)
    oall = sb(N("oall"), [128, 4 * 512], BF16)
    yl2 = [sb(N(f"yl2{i}"), [128, 1024], F32) for i in range(2)]
    y1o = sb(N("y1o"), [128, 1024], F32)

    for g2 in range(4):
        ap = bass.AP(dt_in["fw_qkv"], l * D * 1536 + g2 * 2 * 128 * 1536,
                     [[1536, 128], [1536 * 128, 2], [1, 1536]])
        dst = bass.AP(wqkv, g2 * 2 * 1536,
                      [[8 * 1536, 128], [1536, 2], [1, 1536]])
        p.dma("gpsimd", dst, ap, writes=(N("wqkv"),))
    ap = bass.AP(dt_in["fw_wo"], l * 512 * D, [[D, 128], [D * 128, 4], [1, D]])
    p.dma("gpsimd", bass.AP(wo_s, 0, [[4 * 1024, 128], [1024, 4], [1, 1024]]),
          ap, writes=(N("wo"),))
    p.dma("sync", bass.AP(fmask, 0, [[5 * 640, 128], [640, 5], [1, 640]]),
          dt_in["c_fmask"].ap().rearrange("(m p) n -> p m n", p=128),
          writes=(N("fmask"),))
    ctx.bcast(qsb[:, 0:64], dt_in["fw_qs"], 64, (N("qsb"),), offset=l * 64)
    ctx.bcast(qsb[:, 64:128], dt_in["fw_ks"], 64, (N("qsb"),), offset=l * 64)
    for ti, nm in enumerate(("c_xqA", "c_xqB", "c_xkA", "c_xkB")):
        p.dma("sync", bass.AP(scr[3], 0, [[1024, 128], [64, 16], [1, 64]]),
              dt_in[nm].ap().rearrange("(t p) n -> p t n", p=128),
              writes=(N("tabb"),))
        qcol = 0 if ti < 2 else 64
        swap = (ti % 2 == 1)
        out3 = bass.AP(tabl, ti * 1024, [[4096, 128], [64, 16], [32, 2], [1, 32]])
        in3 = bass.AP(scr[3], 0, [[1024, 128], [64, 16], [32, 2], [1, 32]])
        if swap:
            sc3 = bass.AP(qsb, qcol + 32, [[128, 128], [0, 16], [-32, 2], [1, 32]])
        else:
            sc3 = bass.AP(qsb, qcol, [[128, 128], [0, 16], [32, 2], [1, 32]])
        ctx.tt(out3, in3, sc3, ALU.mult, reads=(N("tabb"), N("qsb")),
               writes=(N("tabl"),))

    for b in range(B_LOC):
        p.dma("sync", Kw[1][:],
              bass.AP(dt_in["c_zrow"], 0, [[0, 128], [0, 2], [1, 1024]]),
              writes=(N("K1"),))
        p.dma("sync", Vw[1][:],
              bass.AP(dt_in["c_zrow"], 0, [[0, 128], [0, 2], [1, 1024]]),
              writes=(N("V1"),))
        cur, prv = 0, 1
        for wi in range(4):
            kK, kV = N(f"K{cur}"), N(f"V{cur}")
            pK, pV = N(f"K{prv}"), N(f"V{prv}")
            for tt in range(4):
                t = wi * 4 + tt
                g = b * NT + t
                pa = tt
                kyl = N(f"yl{pa}")
                khb, kht = N(f"hbf{pa % 2}"), N(f"hT{pa}")
                kqf, kkf = N(f"qf{pa}"), N(f"kf{pa % 2}")
                kqr, kkr = N(f"qrot{pa % 2}"), N(f"krot{pa % 2}")
                load_shifted(ctx, yl[pa], yr, g, t, kyl, dt_in, "yr")
                ctx.cp(hbf[pa % 2][:], yl[pa][:], reads=(kyl,), writes=(khb,))
                tb = ctx.tbank()
                for c in range(8):
                    ctx.tr(PST[tb][:, c * 128:(c + 1) * 128],
                           hbf[pa % 2][:, c * 128:(c + 1) * 128],
                           reads=(khb,), wkey=f"psT{tb}")
                ctx.cp(hT[pa][:], PST[tb][:], reads=(f"psT{tb}",), writes=(kht,))
                for o3 in range(3):
                    for c in range(8):
                        ctx.mm(PS[o3][:], hT[pa][:, c * 128:(c + 1) * 128],
                               wqkv[:, c * 1536 + o3 * 512:
                                    c * 1536 + (o3 + 1) * 512],
                               reads=(kht, N("wqkv")), wkey=f"ps{o3}",
                               start=(c == 0), stop=(c == 7))
                ctx.cp(qf[pa][:], PS[0][:], reads=("ps0",), writes=(kqf,),
                       eng="scalar")
                ctx.cp(kf[pa % 2][:], PS[1][:], reads=("ps1",),
                       writes=(kkf,), eng="scalar")
                ctx.cp(Vw[cur][:, tt * 512:(tt + 1) * 512], PS[2][:],
                       reads=("ps2",), writes=(kV,))
                ctx.act(scr[pa][:, 0:512], qf[pa][:], AF.Square,
                        reads=(kqf, N("tabb")), writes=(N(f"scrA{pa}"),))
                ctx.act(scr[pa][:, 512:1024], kf[pa % 2][:], AF.Square,
                        reads=(kkf,), writes=(N(f"scrA{pa}"),))
                ctx.red(stats[pa][:, 0:8],
                        bass.AP(scr[pa], 0, [[1024, 128], [64, 8], [1, 64]]),
                        ALU.add, reads=(N(f"scrA{pa}"),),
                        writes=(N(f"stQ{pa}"),))
                ctx.red(stats[pa][:, 8:16],
                        bass.AP(scr[pa], 512, [[1024, 128], [64, 8], [1, 64]]),
                        ALU.add, reads=(N(f"scrA{pa}"),),
                        writes=(N(f"stQ{pa}"),))
                ctx.rsqrt(stats[pa][:, 16:24], stats[pa][:, 0:8],
                          reads=(N(f"stQ{pa}"),), writes=(N(f"stF{pa}"),),
                          minv=1e12)
                ctx.rsqrt(stats[pa][:, 24:32], stats[pa][:, 8:16],
                          reads=(N(f"stQ{pa}"),), writes=(N(f"stF{pa}"),),
                          minv=1e12)
                ctx.tt(bass.AP(qf[pa], 0, [[512, 128], [64, 8], [1, 64]]),
                       bass.AP(qf[pa], 0, [[512, 128], [64, 8], [1, 64]]),
                       bass.AP(stats[pa], 16, [[32, 128], [1, 8], [0, 64]]),
                       ALU.mult, reads=(kqf, N(f"stF{pa}")), writes=(kqf,))
                ctx.tt(bass.AP(kf[pa % 2], 0, [[512, 128], [64, 8], [1, 64]]),
                       bass.AP(kf[pa % 2], 0, [[512, 128], [64, 8], [1, 64]]),
                       bass.AP(stats[pa], 24, [[32, 128], [1, 8], [0, 64]]),
                       ALU.mult, reads=(kkf, N(f"stF{pa}")), writes=(kkf,),
                       eng="vector")
                xrope(ctx, qf[pa], t1r[pa % 2], t2r[pa % 2], qrot[pa % 2],
                      tabl, 0, t, kqf, kqr, N("tabl"), eng="vector",
                      pfx2=N(f"r{pa % 2}"))
                xrope(ctx, kf[pa % 2], t1r[pa % 2], t2r[pa % 2],
                      krot[pa % 2], tabl, 2, t, kkf, kkr, N("tabl"),
                      eng="vector", pfx2=N(f"r{pa % 2}"))
                tb = ctx.tbank()
                for c in range(4):
                    ctx.tr(PST[tb][:, c * 128:(c + 1) * 128],
                           qrot[pa % 2][:, c * 128:(c + 1) * 128],
                           reads=(kqr,), wkey=f"psT{tb}")
                ctx.cp(bass.AP(Qw, tt * 128, [[4 * 512, 128], [512, 4], [1, 128]]),
                       bass.AP(PST[tb], 0, [[1024, 128], [128, 4], [1, 128]]),
                       reads=(f"psT{tb}",), writes=(N("Q"),))
                tb = ctx.tbank()
                for c in range(4):
                    ctx.tr(PST[tb][:, c * 128:(c + 1) * 128],
                           krot[pa % 2][:, c * 128:(c + 1) * 128],
                           reads=(kkr,), wkey=f"psT{tb}")
                ctx.cp(bass.AP(Kw[cur], tt * 128,
                               [[4 * 512, 128], [512, 4], [1, 128]]),
                       bass.AP(PST[tb], 0, [[1024, 128], [128, 4], [1, 128]]),
                       reads=(f"psT{tb}",), writes=(kK,), eng="scalar")
            def c2_sm1(h, qt):
                hb = (h % 2) * 64
                gp = h // 2
                sp = qt
                sci = qt % 2
                pnn = (h % 2) * 4 + qt
                ksc, kpn, kstm = (N(f"sc{sci}"), N(f"pn{pnn}"), N(f"stM{sp}"))
                Na = 512 - 128 * qt
                o0 = Na if wi == 0 else 0
                wdt = 640 - o0
                mrow = qt if wi == 0 else 4
                qsl = Qw[hb:hb + 64, gp * 512 + qt * 128:
                         gp * 512 + (qt + 1) * 128]
                rot = (h * 4 + qt) * 2
                if wi > 0:
                    pbA = 2 + (rot % 3)
                    ctx.mm(PS[pbA][:, 0:Na], qsl,
                           Kw[prv][hb:hb + 64,
                                   gp * 512 + 128 * qt:(gp + 1) * 512],
                           reads=(N("Q"), pK), wkey=f"ps{pbA}",
                           start=True, stop=True)
                    ctx.tt(sc[sp][:, 0:Na], PS[pbA][:, 0:Na],
                           fmask[:, mrow * 640:mrow * 640 + Na],
                           ALU.add, reads=(f"ps{pbA}", N("fmask")),
                           writes=(ksc,))
                pbB = 2 + ((rot + 1) % 3)
                wb_ = 128 * qt + 128
                ctx.mm(PS[pbB][:, 0:wb_], qsl,
                       Kw[cur][hb:hb + 64, gp * 512:gp * 512 + wb_],
                       reads=(N("Q"), kK), wkey=f"ps{pbB}",
                       start=True, stop=True)
                ctx.tt(sc[sp][:, Na - o0:Na - o0 + wb_], PS[pbB][:, 0:wb_],
                       fmask[:, mrow * 640 + Na:mrow * 640 + Na + wb_],
                       ALU.add, reads=(f"ps{pbB}", N("fmask")),
                       writes=(ksc,))
                ctx.red(st2[sp][:, 4:5], sc[sp][:, 0:wdt], ALU.max,
                        reads=(ksc,), writes=(kstm,), negate=True)
                ctx.act(Pn[pnn][:, 0:wdt], sc[sp][:, 0:wdt], AF.Exp,
                        reads=(ksc, kstm),
                        writes=(kpn, kstm), bias=st2[sp][:, 4:5],
                        accum=st2[sp][:, 5:6])

            def c2_sm2(h, qt):
                sp = qt
                pnn = (h % 2) * 4 + qt
                kpn, kstm = N(f"pn{pnn}"), N(f"stM{sp}")
                wdt = 640 - (512 - 128 * qt if wi == 0 else 0)
                ctx.recip(st2[sp][:, 5:6], st2[sp][:, 5:6], reads=(kstm,),
                          writes=(kstm,))
                ctx.ts(Pn[pnn][:, 0:wdt], Pn[pnn][:, 0:wdt], st2[sp][:, 5:6],
                       ALU.mult, reads=(kpn, kstm), writes=(kpn,),
                       eng="vector")

            def c2_softmax(h):
                for qt in range(4):
                    c2_sm1(h, qt)
                    if ctx.sm_pend is not None:
                        c2_sm2(*ctx.sm_pend)
                    ctx.sm_pend = (h, qt)

            def c2_pv(h):
                hb = (h % 2) * 64
                gp = h // 2
                k0 = [qt + ((512 - 128 * qt) // 128 if wi == 0 else 0)
                      for qt in range(4)]
                nj_ = [(640 - (512 - 128 * qt if wi == 0 else 0)) // 128
                       for qt in range(4)]
                for qt in range(4):
                    pnn = (h % 2) * 4 + qt
                    kpn = N(f"pn{pnn}")
                    nj = nj_[qt]
                    tb = ctx.tbank()
                    for j in range(nj):
                        ctx.tr(PST[tb][:, j * 128:(j + 1) * 128],
                               Pn[pnn][:, j * 128:(j + 1) * 128],
                               reads=(kpn,), wkey=f"psT{tb}")
                    ctx.cp(bass.AP(PT, qt * 128 + k0[qt] * 512,
                                   [[8 * 512, 128], [512, nj], [1, 128]]),
                           bass.AP(PST[tb], 0,
                                   [[1024, 128], [128, nj], [1, 128]]),
                           reads=(f"psT{tb}",),
                           writes=tuple(f"{pfx}PT:{k0[qt] + j}:{qt}"
                                        for j in range(nj)),
                           eng=("scalar" if qt % 2 else "vector"))
                pb0 = 0 if h % 2 == 0 else 64
                for qt in range(4):
                    for j in range(nj_[qt]):
                        kl = k0[qt] + j
                        vb = Vw[prv] if kl < 4 else Vw[cur]
                        vkey = pV if kl < 4 else kV
                        ctx.mm(PS[0][pb0:pb0 + 64,
                                     qt * 128:(qt + 1) * 128],
                               vb[:, (kl % 4) * 512 + h * 64:
                                  (kl % 4) * 512 + (h + 1) * 64],
                               PT[:, kl * 512 + qt * 128:
                                  kl * 512 + (qt + 1) * 128],
                               reads=(vkey, f"{pfx}PT:{kl}:{qt}"),
                               wkey="ps0", start=(j == 0),
                               stop=(j == nj_[qt] - 1),
                               skip=True)
                if h % 2 == 1:
                    ctx.cp(oall[:, gp * 512:(gp + 1) * 512], PS[0][:],
                           reads=("ps0",), writes=(f"{pfx}oall:{gp}",),
                           eng="scalar")

            ctx.sm_pend = None
            c2_softmax(0)
            for h in range(8):
                if h + 1 < 8:
                    c2_softmax(h + 1)
                else:
                    c2_sm2(*ctx.sm_pend)
                    ctx.sm_pend = None
                c2_pv(h)
            if getattr(ctx, "dbg_c1", None) is not None and (b, wi) == (0, 0)                     and inv == 0 and l == 0:
                dbgd = ctx.dbg_c1
                for nm, buf in (("Qw", Qw), ("Kw", Kw[cur]), ("Vw", Vw[cur]),
                                ("oall", oall), ("sc", sc[0]), ("pn", Pn[0]),
                                ("qrot", qrot[0]), ("krot", krot[0]),
                                ("qf", qf[0]), ("kf", kf[0]),
                                ("hbf", hbf[0])):
                    p.dma("sync", dbgd[nm].ap(), buf[:],
                          reads=tuple(ctx.p.writer.keys()), writes=())
            for qt in range(4):
                t = wi * 4 + qt
                g = b * NT + t
                pa = qt % 2
                ky2 = N(f"yl2{pa}")
                p.dma("sync", yl2[pa][:], gr_ap(yr, g),
                      reads=(f"yr:{g}",), writes=(ky2,))
                for n in range(2):
                    for c in range(4):
                        ctx.mm(PS[2 + n][:],
                               oall[:, c * 512 + qt * 128:c * 512 + (qt + 1) * 128],
                               wo_s[:, c * 1024 + n * 512:c * 1024 + (n + 1) * 512],
                               reads=(f"{pfx}oall:{c}", N("wo")),
                               wkey=f"ps{2 + n}", start=(c == 0), stop=(c == 3))
                for n in range(2):
                    ctx.tt(y1o[:, n * 512:(n + 1) * 512], PS[2 + n][:],
                           yl2[pa][:, n * 512:(n + 1) * 512], ALU.add,
                           reads=(f"ps{2 + n}", ky2), writes=(N("y1o"),))
                p.dma("sync", gr_ap(y1r, g), y1o[:], reads=(N("y1o"),),
                      writes=(f"y1r:{g}",))
            cur, prv = prv, cur


def xrope(ctx, src, t1r, t2r, outb, tabl, ti0, t, ksrc, kout, ktab,
          eng="vector", pfx2="r"):
    At = bass.AP(tabl, ti0 * 1024 + t * 64, [[4096, 128], [0, 8], [1, 64]])
    s3 = bass.AP(src, 0, [[512, 128], [64, 8], [1, 64]])
    ctx.tt(bass.AP(t1r, 0, [[512, 128], [64, 8], [1, 64]]), s3, At, ALU.mult,
           reads=(ksrc, ktab), writes=(pfx2 + "t1",), eng=eng)
    o2 = bass.AP(t2r, 0, [[512, 128], [64, 8], [32, 2], [1, 32]])
    prt = bass.AP(src, 32, [[512, 128], [64, 8], [-32, 2], [1, 32]])
    Bt = bass.AP(tabl, (ti0 + 1) * 1024 + t * 64,
                 [[4096, 128], [0, 8], [32, 2], [1, 32]])
    ctx.tt(o2, prt, Bt, ALU.mult, reads=(ksrc, ktab), writes=(pfx2 + "t2",),
           eng=eng)
    ctx.tt(outb[:], t1r[:], t2r[:], ALU.add,
           reads=(pfx2 + "t1", pfx2 + "t2"), writes=(kout,), eng=eng)


def pass_ffn_mlp(ctx, dt_in, sb, xr, yr, y1r, w1b, inv, l, last, out_d):
    p = ctx.p
    PS, PST = ctx.ps, ctx.psT
    endinv = (l == 1)
    pfx = f"F{inv}{l}m"
    N = lambda s: pfx + s

    lng = sb(N("lng"), [128, 1024], F32)
    lnb = sb(N("lnb"), [128, 1024], F32)
    b2b = sb(N("b2b"), [128, 1024], F32)
    b1c = sb(N("b1c"), [128, 2 * NHC], F32)
    h2T = sb(N("h2T"), [128, 8 * 1024], BF16)
    h4 = sb(N("h4"), [128, NHC * 1024], BF16)
    w2s = sb(N("w2s"), [128, NHC * 1024], BF16)
    w1s = [sb(N(f"w1s{i}"), [128, 1024], BF16) for i in range(4)]
    gact = sb(N("gact"), [128, 1024], BF16)
    abf = sb(N("abf"), [128, 1024], BF16)
    h2bf = [sb(N(f"h2bf{i}"), [128, 1024], BF16) for i in range(2)]
    hl = [sb(N(f"hl{i}"), [128, 1024], F32) for i in range(4)]
    h2n = [sb(N(f"h2n{i}"), [128, 1024], F32) for i in range(4)]
    scr = [sb(N(f"scr{i}"), [128, 1024], F32) for i in range(2)]
    scr2 = [sb(N(f"scr2{i}"), [128, 1024], F32) for i in range(2)]
    stats = [sb(N(f"stats{i}"), [128, 32], F32) for i in range(4)]
    y1l = [sb(N(f"y1l{i}"), [128, 1024], F32) for i in range(2)]
    xl = [sb(N(f"xl{i}"), [128, 1024], F32) for i in range(2)]
    yo = sb(N("yo"), [128, 1024], F32)

    ctx.bcast(lng[:], dt_in["fw_lng"], 1024, (N("lng"),), offset=l * D)
    ctx.bcast(lnb[:], dt_in["fw_lnb"], 1024, (N("lnb"),), offset=l * D)
    ctx.bcast(b2b[:], dt_in["fw_b2"], 1024, (N("b2b"),), offset=l * D)
    for half in range(2):
        base = l * 2 * HID + half * HID
        p.dma("gpsimd", scr2[0][0:22, 0:128],
              bass.AP(dt_in["c_zrow"], 0, [[0, 22], [1, 128]]),
              reads=("scrB0",), writes=("scrB0",))
        p.dma("sync", scr2[0][0:21, 0:128],
              bass.AP(dt_in["fw_b1"], base, [[128, 21], [1, 128]]),
              reads=("scrB0",), writes=("scrB0",))
        p.dma("sync", scr2[0][21:22, 0:42],
              bass.AP(dt_in["fw_b1"], base + 21 * 128, [[42, 1], [1, 42]]),
              reads=("scrB0",), writes=("scrB0",))
        ctx.p.op("tensor",
                 lambda q, o=PS[4][:, 0:22], i=scr2[0][0:22, 0:128],
                 idn=ctx.identf: q.transpose(o, i, idn[0:22, 0:22]),
                 reads=("scrB0", "identf"), writes=("ps4",), tag="tr")
        ctx.cp(b1c[:, half * NHC:(half + 1) * NHC], PS[4][:, 0:22],
               reads=("ps4",), writes=(N("b1c"),))

    for blk in range(4):
        b = blk // 2
        if blk == 0:
            for i in range(NHC):
                w = min(128, HID - i * 128)
                if w < 128:
                    p.dma("gpsimd", w2s[:, i * 1024:(i + 1) * 1024],
                          bass.AP(dt_in["c_zrow"], 0, [[0, 128], [1, 1024]]),
                          reads=(N("w2s"),), writes=(N("w2s"),))
                p.dma("gpsimd", w2s[0:w, i * 1024:(i + 1) * 1024],
                      bass.AP(dt_in["fw_w2"], l * HID * D + i * 128 * D,
                              [[D, w], [1, D]]),
                      reads=(N("w2s"),), writes=(N("w2s"),),
                      partial=(w < 128))
        for tt in range(8):
            t = (blk % 2) * 8 + tt
            g = b * NT + t
            pa = tt % 4
            khl = N(f"hl{pa}")
            kh2, kh2b = N(f"h2n{pa}"), N(f"h2bf{pa % 2}")
            load_shifted(ctx, hl[pa], y1r, g, t, khl, dt_in, "y1r")
            ctx.ln_tile(hl[pa][:], h2n[pa][:], khl, kh2, scr[pa % 2],
                        scr2[pa % 2], stats[pa], 0, N(f"stL{pa}"),
                        ksc=N(f"sl{pa % 2}"))
            ctx.stt(h2n[pa][:], h2n[pa][:], 1.0, lng[:], ALU.mult, ALU.mult,
                    reads=(kh2, N("lng")), writes=(kh2,))
            ctx.tt(h2bf[pa % 2][:], h2n[pa][:], lnb[:], ALU.add,
                   reads=(kh2, N("lnb")), writes=(kh2b,),
                   eng="vector")
            tb = ctx.tbank()
            for c in range(8):
                ctx.tr(PST[tb][:, c * 128:(c + 1) * 128],
                       h2bf[pa % 2][:, c * 128:(c + 1) * 128],
                       reads=(kh2b,), wkey=f"psT{tb}")
            ctx.cp(bass.AP(h2T, tt * 128, [[8 * 1024, 128], [1024, 8], [1, 128]]),
                   bass.AP(PST[tb], 0, [[1024, 128], [128, 8], [1, 128]]),
                   reads=(f"psT{tb}",), writes=(f"{pfx}h2T:{tt}",), eng="scalar")
        for i in range(NHC):
            pa = i % 2
            ka, kg = N(f"w1a{pa}"), N(f"w1g{pa}")
            p.dma("gpsimd", w1s[pa][:],
                  bass.AP(w1b, (l * 2 * NHC + i) * 128 * 1024,
                          [[1024, 128], [1, 1024]]),
                  reads=(f"w1b:{l}:{i}",), writes=(ka,))
            p.dma("gpsimd", w1s[2 + pa][:],
                  bass.AP(w1b, (l * 2 * NHC + NHC + i) * 128 * 1024,
                          [[1024, 128], [1, 1024]]),
                  reads=(f"w1b:{l}:{NHC + i}",), writes=(kg,))
            kpa, kpg = f"ps{pa}", f"ps{2 + pa}"
            for nn in range(2):
                psa, psg = PS[pa], PS[2 + pa]
                for c in range(8):
                    ctx.mm(psa[:], w1s[pa][:, c * 128:(c + 1) * 128],
                           h2T[:, c * 1024 + nn * 512:c * 1024 + (nn + 1) * 512],
                           reads=(ka,) + tuple(f"{pfx}h2T:{t_}"
                                               for t_ in range(8)),
                           wkey=kpa, start=(c == 0), stop=(c == 7))
                for c in range(8):
                    ctx.mm(psg[:], w1s[2 + pa][:, c * 128:(c + 1) * 128],
                           h2T[:, c * 1024 + nn * 512:c * 1024 + (nn + 1) * 512],
                           reads=(kg,) + tuple(f"{pfx}h2T:{t_}"
                                               for t_ in range(8)),
                           wkey=kpg, start=(c == 0), stop=(c == 7))
                ctx.act(gact[:, nn * 512:(nn + 1) * 512], psg[:], AF.Gelu,
                        reads=(kpg, N("b1c")), writes=(N("gact"),),
                        bias=b1c[:, NHC + i:NHC + i + 1])
                ctx.ts(abf[:, nn * 512:(nn + 1) * 512], psa[:],
                       b1c[:, i:i + 1], ALU.add, reads=(kpa, N("b1c")),
                       writes=(N("abf"),))
            ctx.tt(h4[:, i * 1024:(i + 1) * 1024], abf[:], gact[:], ALU.mult,
                   reads=(N("abf"), N("gact")), writes=(f"{pfx}h4:{i}",),
                   eng="vector")
        for tt in range(8):
            t = (blk % 2) * 8 + tt
            g = b * NT + t
            pa = tt % 2
            ky1, kx2 = N(f"y1l{pa}"), N(f"xl{pa}")
            p.dma("sync", y1l[pa][:], gr_ap(y1r, g),
                  reads=(f"y1r:{g}",), writes=(ky1,))
            if endinv:
                p.dma("sync", xl[pa][:], dram_row_ap(xr, g),
                      reads=(f"xr:{g}",), writes=(kx2,))
            pq = (4, 0) if tt % 2 == 0 else (1, 2)
            for n in range(2):
                for i in range(NHC):
                    ctx.mm(PS[pq[n]][:],
                           h4[:, i * 1024 + tt * 128:i * 1024 + (tt + 1) * 128],
                           w2s[:, i * 1024 + n * 512:i * 1024 + (n + 1) * 512],
                           reads=(f"{pfx}h4:{i}", N("w2s")),
                           wkey=f"ps{pq[n]}", start=(i == 0),
                           stop=(i == NHC - 1))
            for n in range(2):
                ctx.tt(yo[:, n * 512:(n + 1) * 512], PS[pq[n]][:],
                       y1l[pa][:, n * 512:(n + 1) * 512], ALU.add,
                       reads=(f"ps{pq[n]}", ky1), writes=(N("yo"),))
            ctx.tt(yo[:], yo[:], b2b[:], ALU.add, reads=(N("yo"), N("b2b")),
                   writes=(N("yo"),), eng="vector")
            if not endinv:
                p.dma("sync", gr_ap(yr, g), yo[:], reads=(N("yo"),),
                      writes=(f"yr:{g}",))
            else:
                ctx.tt(xl[pa][:], yo[:], xl[pa][:], ALU.add,
                       reads=(N("yo"), kx2), writes=(kx2,), eng="vector")
                wb_store(ctx, xl[pa][:], kx2, g, xr, yr, scr[pa % 2],
                         scr2[pa % 2], stats[pa], last, out_d, pa % 2)


def pass_attn_b(ctx, dt_in, sb, xr, yr, wTd, last, out_d):
    p = ctx.p
    PS, PST = ctx.ps, ctx.psT

    wsl = [sb(f"B_w{i}", [128, 8 * 1024], BF16) for i in range(2)]
    cmask = sb("B_cmask", [128, 4 * 512], BF16)
    rtab = sb("B_rtab", [128, 2 * 1024], F32)
    QFM = sb("B_qfm", [128, 8 * 512], BF16)
    KFM = sb("B_kfm", [128, 8 * 2048], BF16)
    VTM = sb("B_vtm", [128, 16 * 1024], BF16)
    yl = [sb(f"B_yl{i}", [128, 1024], F32) for i in range(2)]
    ybf = sb("B_ybf", [128, 1024], BF16)
    yT = sb("B_yT", [128, 1024], BF16)
    t1r = sb("B_t1r", [128, 1024], F32)
    t2r = sb("B_t2r", [128, 1024], F32)
    qrot = sb("B_qrot", [128, 1024], BF16)
    krot = sb("B_krot", [128, 1024], BF16)
    kf = sb("B_kf", [128, 1024], F32)
    st2 = sb("B_st2", [128, 8], F32)
    sc = sb("B_sc", [128, 2048], F32)
    Pn = [sb(f"B_pn{i}", [128, 2048], BF16) for i in range(2)]
    PT = sb("B_ptb", [128, 16 * 512], BF16)
    oall = sb("B_oall", [128, 8 * 512], BF16)
    xt2 = [sb(f"B_xt2{i}", [128, 1024], F32) for i in range(2)]
    scr = sb("B_scr", [128, 1024], F32)
    scr2 = t2r
    stats = sb("B_stats", [128, 32], F32)

    p.dma("sync", bass.AP(cmask, 0, [[2048, 128], [512, 4], [1, 512]]),
          dt_in["c_cmask"].ap().rearrange("(m p) n -> p m n", p=128),
          writes=("cmask",))
    for ti, nm in enumerate(("c_ropeA", "c_ropeB")):
        p.dma("sync", bass.AP(rtab, ti * 1024, [[2048, 128], [64, 16], [1, 64]]),
              dt_in[nm].ap().rearrange("(t p) n -> p t n", p=128),
              writes=("Brtab",))

    wnames = ("bw_q", "bw_k", "bw_v", "bw_o")

    def wTload(slot, wi, key):
        nm = wnames[wi]
        for g2 in range(2):
            ap = bass.AP(wTd, wi * D * D + g2 * 4 * 128 * D,
                         [[D, 128], [D * 128, 4], [1, D]])
            dst = bass.AP(wsl[slot], g2 * 4 * 1024,
                          [[8 * 1024, 128], [1024, 4], [1, 1024]])
            p.dma("gpsimd", dst, ap,
                  reads=tuple(f"wTd:{nm}:{j}" for j in range(8)),
                  writes=(key,))

    for b in range(B_LOC):
        wTload(0, 1, "Bw0")
        wTload(1, 2, "Bw1")
        for t in range(NT):
            g = b * NT + t
            pa = t % 2
            kyl = f"Byl{pa}"
            p.dma("sync", yl[pa][:], gr_ap(yr, g), reads=(f"yr:{g}",),
                  writes=(kyl,))
            ctx.cp(ybf[:], yl[pa][:], reads=(kyl,), writes=("Bybf",))
            tb = ctx.tbank()
            for c in range(8):
                ctx.tr(PST[tb][:, c * 128:(c + 1) * 128],
                       ybf[:, c * 128:(c + 1) * 128], reads=("Bybf",),
                       wkey=f"psT{tb}")
            ctx.cp(yT[:], PST[tb][:], reads=(f"psT{tb}",), writes=("ByT",))
            for n in range(2):
                for c in range(8):
                    ctx.mm(PS[n][:], yT[:, c * 128:(c + 1) * 128],
                           wsl[0][:, c * 1024 + n * 512:c * 1024 + (n + 1) * 512],
                           reads=("ByT", "Bw0"), wkey=f"ps{n}",
                           start=(c == 0), stop=(c == 7))
            for n in range(2):
                for c in range(8):
                    ctx.mm(PS[2 + n][:], yT[:, c * 128:(c + 1) * 128],
                           wsl[1][:, c * 1024 + n * 512:c * 1024 + (n + 1) * 512],
                           reads=("ByT", "Bw1"), wkey=f"ps{2 + n}",
                           start=(c == 0), stop=(c == 7))
            ctx.cp(kf[:, 0:512], PS[0][:], reads=("ps0",), writes=("Bkf",),
                   eng="scalar")
            ctx.cp(kf[:, 512:1024], PS[1][:], reads=("ps1",), writes=("Bkf",),
                   eng="scalar")
            brope(ctx, kf, "Bkf", t1r, t2r, krot, rtab, t, "Bkrot")
            ctx.cp(VTM[:, t * 1024:t * 1024 + 512], PS[2][:], reads=("ps2",),
                   writes=(f"BVTM:{t}:0",))
            ctx.cp(VTM[:, t * 1024 + 512:(t + 1) * 1024], PS[3][:],
                   reads=("ps3",), writes=(f"BVTM:{t}:1",), eng="scalar")
            tb = ctx.tbank()
            for c in range(8):
                ctx.tr(PST[tb][:, c * 128:(c + 1) * 128],
                       krot[:, c * 128:(c + 1) * 128], reads=("Bkrot",),
                       wkey=f"psT{tb}")
            ctx.cp(bass.AP(KFM, t * 128, [[8 * 2048, 128], [2048, 8], [1, 128]]),
                   bass.AP(PST[tb], 0, [[1024, 128], [128, 8], [1, 128]]),
                   reads=(f"psT{tb}",), writes=(f"BKFM:{t}",))
        wTload(0, 0, "Bw0")
        wTload(1, 3, "Bw1")
        for qb in range(4):
            for tt in range(4):
                t = qb * 4 + tt
                g = b * NT + t
                pa = t % 2
                kyl = f"Byl{pa}"
                p.dma("sync", yl[pa][:], gr_ap(yr, g),
                      reads=(f"yr:{g}",), writes=(kyl,))
                ctx.cp(ybf[:], yl[pa][:], reads=(kyl,), writes=("Bybf",))
                tb = ctx.tbank()
                for c in range(8):
                    ctx.tr(PST[tb][:, c * 128:(c + 1) * 128],
                           ybf[:, c * 128:(c + 1) * 128], reads=("Bybf",),
                           wkey=f"psT{tb}")
                ctx.cp(yT[:], PST[tb][:], reads=(f"psT{tb}",), writes=("ByT",))
                for n in range(2):
                    for c in range(8):
                        ctx.mm(PS[n][:], yT[:, c * 128:(c + 1) * 128],
                               wsl[0][:, c * 1024 + n * 512:
                                    c * 1024 + (n + 1) * 512],
                               reads=("ByT", "Bw0"), wkey=f"ps{n}",
                               start=(c == 0), stop=(c == 7))
                ctx.cp(kf[:, 0:512], PS[0][:], reads=("ps0",),
                       writes=("Bkf",), eng="scalar")
                ctx.cp(kf[:, 512:1024], PS[1][:], reads=("ps1",),
                       writes=("Bkf",), eng="scalar")
                brope(ctx, kf, "Bkf", t1r, t2r, qrot, rtab, t, "Bqrot")
                tb = ctx.tbank()
                for c in range(8):
                    ctx.tr(PST[tb][:, c * 128:(c + 1) * 128],
                           qrot[:, c * 128:(c + 1) * 128], reads=("Bqrot",),
                           wkey=f"psT{tb}")
                ctx.cp(bass.AP(QFM, tt * 128,
                               [[8 * 512, 128], [512, 8], [1, 128]]),
                       bass.AP(PST[tb], 0, [[1024, 128], [128, 8], [1, 128]]),
                       reads=(f"psT{tb}",), writes=(f"BQFM:{tt}",))
            nch = qb + 1
            nkc = nch * 4

            def b2_sm1(gp, h2, qt):
                hb = h2 * 64
                sp = qt % 2
                for c in range(nch):
                    pb_ = 2 + ((((gp * 2 + h2) * 4 + qt) * nch + c) % 3)
                    ctx.mm(PS[pb_][:],
                           QFM[hb:hb + 64, gp * 512 + qt * 128:
                               gp * 512 + (qt + 1) * 128],
                           KFM[hb:hb + 64, gp * 2048 + c * 512:
                               gp * 2048 + (c + 1) * 512],
                           reads=(f"BQFM:{qt}",) + tuple(
                               f"BKFM:{4 * c + i}" for i in range(4)),
                           wkey=f"ps{pb_}", start=True, stop=True)
                    if c == qb:
                        ctx.tt(sc[:, c * 512:(c + 1) * 512],
                               PS[pb_][:],
                               cmask[:, qt * 512:(qt + 1) * 512],
                               ALU.add, reads=(f"ps{pb_}", "cmask"),
                               writes=(f"Bsc:{c}",))
                    else:
                        ctx.cp(sc[:, c * 512:(c + 1) * 512],
                               PS[pb_][:], reads=(f"ps{pb_}",),
                               writes=(f"Bsc:{c}",), eng="scalar")
                ctx.red(st2[:, 4 + sp:5 + sp], sc[:, :nch * 512], ALU.max,
                        reads=tuple(f"Bsc:{c}" for c in range(nch)),
                        writes=(f"BstM{sp}",), negate=True)
                ctx.act(Pn[sp][:, :nch * 512], sc[:, :nch * 512], AF.Exp,
                        reads=(f"BstM{sp}",) + tuple(
                            f"Bsc:{c}" for c in range(nch)),
                        writes=(f"BPn{sp}", f"BstM{sp}"),
                        bias=st2[:, 4 + sp:5 + sp], accum=st2[:, 6 + sp:7 + sp])

            def b2_sm2pt(gp, h2, qt):
                sp = qt % 2
                ctx.recip(st2[:, 6 + sp:7 + sp], st2[:, 6 + sp:7 + sp],
                          reads=(f"BstM{sp}",), writes=(f"BstM{sp}",))
                ctx.ts(Pn[sp][:, :nch * 512], Pn[sp][:, :nch * 512],
                       st2[:, 6 + sp:7 + sp], ALU.mult,
                       reads=(f"BPn{sp}", f"BstM{sp}"),
                       writes=(f"BPn{sp}",), eng="vector")
                for kg in range((nkc + 7) // 8):
                    kcs = list(range(kg * 8, min(nkc, kg * 8 + 8)))
                    tb = ctx.tbank()
                    for kc in kcs:
                        ctx.tr(PST[tb][:, (kc % 8) * 128:(kc % 8 + 1) * 128],
                               Pn[sp][:, kc * 128:(kc + 1) * 128],
                               reads=(f"BPn{sp}",), wkey=f"psT{tb}")
                    nn = len(kcs)
                    ctx.cp(bass.AP(PT, kcs[0] * 512 + qt * 128,
                                   [[16 * 512, 128], [512, nn], [1, 128]]),
                           bass.AP(PST[tb], 0,
                                   [[1024, 128], [128, nn], [1, 128]]),
                           reads=(f"psT{tb}",),
                           writes=tuple(f"BPT:{kc}:{qt}" for kc in kcs),
                           eng=("scalar" if qt % 2 else "vector"))

            for gp in range(8):
                for h2 in range(2):
                    h = gp * 2 + h2
                    pend = None
                    for qt in range(4):
                        b2_sm1(gp, h2, qt)
                        if pend is not None:
                            b2_sm2pt(*pend)
                        pend = (gp, h2, qt)
                    b2_sm2pt(*pend)
                    psel = PS[0][0:64, :] if h2 == 0 else PS[0][64:128, :]
                    for kc in range(nkc):
                        ctx.mm(psel,
                               VTM[:, kc * 1024 + h * 64:
                                   kc * 1024 + (h + 1) * 64],
                               PT[:, kc * 512:(kc + 1) * 512],
                               reads=(f"BVTM:{kc}:{h // 8}",) + tuple(
                                   f"BPT:{kc}:{q_}" for q_ in range(4)),
                               wkey="ps0", start=(kc == 0),
                               stop=(kc == nkc - 1), skip=True)
                ctx.cp(oall[:, gp * 512:(gp + 1) * 512], PS[0][:],
                       reads=("ps0",), writes=(f"Boall:{gp}",))
            for qt in range(4):
                t = qb * 4 + qt
                g = b * NT + t
                pa = t % 2
                kx2 = f"Bxt2{pa}"
                p.dma("sync", xt2[pa][:], dram_row_ap(xr, g),
                      reads=(f"xr:{g}",), writes=(kx2,))
                for n in range(2):
                    for c in range(8):
                        ctx.mm(PS[2 + n][:],
                               oall[:, c * 512 + qt * 128:c * 512 + (qt + 1) * 128],
                               wsl[1][:, c * 1024 + n * 512:
                                    c * 1024 + (n + 1) * 512],
                               reads=(f"Boall:{c}", "Bw1"), wkey=f"ps{2 + n}",
                               start=(c == 0), stop=(c == 7))
                for n in range(2):
                    ctx.tt(xt2[pa][:, n * 512:(n + 1) * 512], PS[2 + n][:],
                           xt2[pa][:, n * 512:(n + 1) * 512], ALU.add,
                           reads=(f"ps{2 + n}", kx2), writes=(kx2,))
                wb_store(ctx, xt2[pa][:], kx2, g, xr, yr, scr, scr2, stats,
                         last, out_d)


def brope(ctx, src, ksrc, t1r, t2r, outb, rtab, t, kout):
    At = bass.AP(rtab, t * 64, [[2048, 128], [0, 16], [1, 64]])
    s3 = bass.AP(src, 0, [[1024, 128], [64, 16], [1, 64]])
    ctx.tt(bass.AP(t1r, 0, [[1024, 128], [64, 16], [1, 64]]), s3, At,
           ALU.mult, reads=(ksrc, "Brtab"), writes=("Brt1",))
    oe = bass.AP(t2r, 0, [[1024, 128], [64, 16], [2, 32]])
    pe = bass.AP(src, 1, [[1024, 128], [64, 16], [2, 32]])
    be = bass.AP(rtab, 1024 + t * 64, [[2048, 128], [0, 16], [2, 32]])
    ctx.tt(oe, pe, be, ALU.mult, reads=(ksrc, "Brtab"), writes=("Brt2",),
           eng="vector")
    oo = bass.AP(t2r, 1, [[1024, 128], [64, 16], [2, 32]])
    po = bass.AP(src, 0, [[1024, 128], [64, 16], [2, 32]])
    bo = bass.AP(rtab, 1024 + t * 64 + 1, [[2048, 128], [0, 16], [2, 32]])
    ctx.tt(oo, po, bo, ALU.mult, reads=(ksrc, "Brtab"), writes=("Brt2",),
           eng="vector")
    ctx.tt(outb[:], t1r[:], t2r[:], ALU.add, reads=("Brt1", "Brt2"),
           writes=(kout,), eng="vector")


def kernel(**inputs):
    consts = host_consts()
    x = np.ascontiguousarray(inputs["L0_attn_a_x"], dtype=np.float32)
    base = {
        "aw_normw": inputs["L0_attn_a_norm_w"],
        "aw_q": inputs["L0_attn_a_Wq"], "aw_k": inputs["L0_attn_a_Wk"],
        "aw_v": inputs["L0_attn_a_Wv"], "aw_o": inputs["L0_attn_a_Wo"],
        "aw_gate": np.reshape(inputs["L0_attn_a_gate"], [1]),
        "aw_lt": np.reshape(inputs["L0_attn_a_log_temperature"], [1]),
        "bw_q": inputs["L0_attn_b_Wq"], "bw_k": inputs["L0_attn_b_Wk"],
        "bw_v": inputs["L0_attn_b_Wv"], "bw_o": inputs["L0_attn_b_Wo"],
        "fw_qkv": inputs["L0_ffn_wqkv"], "fw_qs": inputs["L0_ffn_q_scale"],
        "fw_ks": inputs["L0_ffn_k_scale"], "fw_wo": inputs["L0_ffn_wo"],
        "fw_lng": inputs["L0_ffn_ln_g"], "fw_lnb": inputs["L0_ffn_ln_b"],
        "fw_w1": inputs["L0_ffn_w1"], "fw_b1": inputs["L0_ffn_b1"],
        "fw_w2": inputs["L0_ffn_w2"], "fw_b2": inputs["L0_ffn_b2"],
        "c_ident": consts["ident"], "c_cmask": consts["cmask"],
        "c_fmask": consts["fmask"], "c_ropeA": consts["ropeA"],
        "c_ropeB": consts["ropeB"], "c_xqA": consts["xqA"],
        "c_xqB": consts["xqB"], "c_xkA": consts["xkA"],
        "c_xkB": consts["xkB"], "c_zrow": consts["zrow"],
        "c_eps": consts["eps"],
    }
    base = {k: np.ascontiguousarray(v) for k, v in base.items()}
    nc = build(PASSES)
    in_maps = []
    for c in range(8):
        m = dict(base)
        m["x"] = np.ascontiguousarray(x[2 * c:2 * c + 2].reshape(B_LOC * S, D))
        in_maps.append(m)
    res = run_bass_kernel_spmd(nc, in_maps, core_ids=list(range(8)))
    if PASSES == "A1":
        kernel.dbg = [{k: np.asarray(v) for k, v in res.results[c].items()}
                      for c in range(8)]
        return np.zeros((16, S, D), np.float32)
    if PASSES in ("F1", "F1m"):
        kernel.dbg = [
            {k: np.asarray(res.results[c][f"dbg_{k}"])
             for k in ("xr", "yr", "y1r")} for c in range(8)]
        if PASSES == "F1":
            for c in range(8):
                for k in list(res.results[c].keys()):
                    if k.startswith("dbgC_"):
                        kernel.dbg[c][k] = np.asarray(res.results[c][k])
        return np.zeros((16, S, D), np.float32)
    outs = [np.asarray(res.results[c]["out"]).reshape(B_LOC, S, D)
            for c in range(8)]
    full = np.concatenate(outs, axis=0).astype(np.float32)
    if PASSES != "full":
        kernel.dbg = [
            {k: np.asarray(res.results[c][f"dbg_{k}"]) for k in ("xr", "yr")}
            for c in range(8)]
    return full


# version 13
# speedup vs baseline: 1.4209x; 1.0255x over previous
import numpy as np
import ml_dtypes
from contextlib import ExitStack
import concourse.bass as bass
import concourse.mybir as mybir
from concourse.bass_utils import run_bass_kernel_spmd

F32 = mybir.dt.float32
BF16 = mybir.dt.bfloat16
AF = mybir.ActivationFunctionType
ALU = mybir.AluOpType
AX = mybir.AxisListType

NEG = -1.0e30
B_LOC = 2
S = 2048
D = 1024
NT = 16
HID = 2730
NHC = 22

PASSES = "full"
FORCE_SYNC = False

ENGINES = ("sync", "tensor", "vector", "scalar", "gpsimd")
DMA_POOL = {"sync": 32, "tensor": 2, "vector": 2, "scalar": 8, "gpsimd": 24}


class Prog:
    def __init__(self):
        self.ops = {e: [] for e in ENGINES}
        self.writer = {}
        self.readers = {}
        self.fence_deps = ()

    def op(self, eng, fn, reads=(), writes=(), dma=False, tag="", accum=False):
        deps = set(self.fence_deps)
        for k in reads:
            for w in self.writer.get(k, {}).values():
                deps.add(w)
        for k in writes:
            for w in self.writer.get(k, {}).values():
                deps.add(w)
            for r in self.readers.get(k, ()):
                deps.add(r)
        idx = len(self.ops[eng])
        me = (eng, idx)
        deps.discard(me)
        self.ops[eng].append([fn, dma, deps, False, tag, accum])
        for k in writes:
            self.writer.setdefault(k, {})[eng] = me
            self.readers[k] = []
        for k in reads:
            self.readers.setdefault(k, []).append(me)

    def dma(self, queue, out, in_, reads=(), writes=(), tag="", partial=False,
            **kw):
        self.op(queue, (lambda q, o=out, i=in_, kw=kw: q.dma_start(o, i, **kw)),
                reads=reads, writes=writes, dma=True, tag=tag)

    def fence(self):
        self.fence_deps = tuple(
            (e, len(self.ops[e]) - 1) for e in ENGINES if self.ops[e])

    def finalize(self):
        need = {e: set() for e in ENGINES}
        for e in ENGINES:
            for i, o in enumerate(self.ops[e]):
                for (de, di) in o[2]:
                    dop = self.ops[de][di]
                    if de != e or dop[1] or dop[5] or FORCE_SYNC:
                        need[de].add(di)
        self.mnum = {}
        self.dma_final = {}
        for e in ENGINES:
            c = 0
            d = 0
            K = DMA_POOL[e]
            for i, o in enumerate(self.ops[e]):
                if o[1]:
                    s = d % K
                    r = d // K + 1
                    self.mnum[(e, i)] = ("s", s, r)
                    self.dma_final[(e, s)] = r
                    d += 1
                elif i in need[e]:
                    c += 1
                    o[3] = True
                    self.mnum[(e, i)] = ("c", c)

    def emit_engine(self, q, e, sems):
        wm = {}
        for i, (fn, dma, deps, ms, tag, accum) in enumerate(self.ops[e]):
            req = {}
            for (de, di) in deps:
                dop = self.ops[de][di]
                if de == e and not dop[1] and not dop[5] and not FORCE_SYNC:
                    continue
                mn = self.mnum[(de, di)]
                if mn[0] == "c":
                    kk = (de, "c")
                    if mn[1] > req.get(kk, 0):
                        req[kk] = mn[1]
                else:
                    kk = (de, "s", mn[1])
                    if mn[2] > req.get(kk, 0):
                        req[kk] = mn[2]
            for kk in sorted(req):
                val = req[kk]
                if wm.get(kk, 0) >= val:
                    continue
                if kk[1] == "c":
                    q.wait_ge(sems[kk], val)
                else:
                    q.wait_ge(sems[kk], val * 16)
                wm[kk] = val
            if dma:
                kind, s, r = self.mnum[(e, i)]
                kk = (e, "s", s)
                if r > 1 and wm.get(kk, 0) < r - 1:
                    q.wait_ge(sems[kk], 16 * (r - 1))
                    wm[kk] = r - 1
            inst = fn(q)
            if dma:
                kind, s, r = self.mnum[(e, i)]
                inst.then_inc(sems[(e, "s", s)], 16)
            elif ms:
                inst.then_inc(sems[(e, "c")], 1)
        if e == "sync":
            for (qe, s), r in sorted(self.dma_final.items()):
                kk = (qe, "s", s)
                if wm.get(kk, 0) < r:
                    q.wait_ge(sems[kk], 16 * r)


def host_consts():
    c = {}
    c["ident"] = np.eye(128, dtype=ml_dtypes.bfloat16)
    cm = np.zeros((4, 128, 512), dtype=np.float32)
    for d_ in range(4):
        i = np.arange(128)[:, None]
        j = np.arange(512)[None, :]
        cm[d_] = np.where(j <= d_ * 128 + i, 0.0, NEG)
    c["cmask"] = cm.astype(ml_dtypes.bfloat16).reshape(4 * 128, 512)
    fm = np.zeros((5, 128, 640), dtype=np.float32)
    pp = np.arange(128)[:, None]
    m = np.arange(640)[None, :]
    band = (m >= pp) & (m <= pp + 512)
    for jt in range(4):
        fm[jt] = np.where(band & (m >= 512 - 128 * jt), 0.0, NEG)
    fm[4] = np.where(band, 0.0, NEG)
    c["fmask"] = fm.astype(ml_dtypes.bfloat16).reshape(5 * 128, 640)
    hd = 64
    pos = np.arange(S, dtype=np.float64)
    inv = 1.0 / (10000.0 ** (np.arange(0, hd, 2, dtype=np.float64) / hd))
    fr = np.concatenate([np.outer(pos, inv), np.outer(pos, inv)], -1)
    cosb, sinb = np.cos(fr), np.sin(fr)
    Bt = sinb.copy()
    Bt[:, 0::2] = -sinb[:, 0::2]
    c["ropeA"] = cosb.astype(np.float32)
    c["ropeB"] = Bt.astype(np.float32)
    half = np.arange(0, hd, 2, dtype=np.float64)
    invf = 1.0 / (10000.0 ** (half / hd))
    fr2 = np.outer(pos, invf)
    fr2 = np.concatenate([fr2, fr2], -1)
    base = (half + 0.4 * hd) / (1.4 * hd)
    power = (pos - S // 2) / 256.0
    scale = base[None, :] ** power[:, None]
    scale = np.concatenate([scale, scale], -1)
    cosx, sinx = np.cos(fr2), np.sin(fr2)
    sgn = np.ones(hd)
    sgn[:32] = -1.0
    c["xqA"] = (cosx * scale * 8.0).astype(np.float32)
    c["xqB"] = (sgn[None, :] * sinx * scale * 8.0).astype(np.float32)
    c["xkA"] = (cosx / scale).astype(np.float32)
    c["xkB"] = (sgn[None, :] * sinx / scale).astype(np.float32)
    c["zrow"] = np.zeros((1, 1024), dtype=ml_dtypes.bfloat16)
    c["eps"] = np.array([1e-5, 1e-6], dtype=np.float32)
    return c


def _is_small(out):
    try:
        return out.free_size() <= 64
    except Exception:
        return False


class Ctx:
    def __init__(self, nc, prog):
        self.nc = nc
        self.p = prog
        self.ident = None
        self.ps = []
        self.psT = []
        self.pro_steps = []

    def tbank(self):
        self._tb = (getattr(self, "_tb", -1) + 1) % 3
        return self._tb

    def step_prologue(self, n=1):
        for _ in range(n):
            if self.pro_steps:
                self.pro_steps.pop(0)()

    def mm(self, ps_ap, lhsT_ap, rhs_ap, reads, wkey, start, stop, skip=False):
        self.p.op("tensor",
                  lambda q, o=ps_ap, a=lhsT_ap, b=rhs_ap, s1=start, s2=stop,
                  sk=skip: q.matmul(o, a, b, start=s1, stop=s2,
                                    skip_group_check=sk),
                  reads=reads, writes=(wkey,), tag="mm")

    def tr(self, ps_ap, in_ap, reads, wkey):
        self.p.op("tensor",
                  lambda q, o=ps_ap, i=in_ap, idn=self.ident:
                  q.transpose(o, i, idn[:]),
                  reads=reads + ("ident",), writes=(wkey,), tag="tr")

    def act(self, out, in_, func, reads, writes, bias=None, scale=None,
            accum=None):
        def f(q, o=out, i=in_, fu=func, b=bias, s=scale, a=accum):
            kw = {}
            if b is not None:
                kw["bias"] = b
            if s is not None:
                kw["scale"] = s
            if a is not None:
                kw["accum_out"] = a
            return q.activation(o, i, fu, **kw)
        self.p.op("scalar", f, reads=reads, writes=writes, tag="act",
                  accum=(accum is not None) or _is_small(out))

    def tt(self, out, a, b, op, reads, writes, eng="vector"):
        self.p.op(eng,
                  lambda q, o=out, x=a, y=b, p_=op: q.tensor_tensor(o, x, y, p_),
                  reads=reads, writes=writes, tag="tt", accum=_is_small(out))

    def ts(self, out, a, s1, op0, reads, writes, s2=None, op1=None, accum=None,
           eng="vector"):
        def f(q, o=out, x=a, s1=s1, s2=s2, o0=op0, o1=op1, ac=accum):
            kw = {}
            if ac is not None:
                kw["accum_out"] = ac
            if o1 is None:
                return q.tensor_scalar(o, x, s1, s2, o0, **kw)
            return q.tensor_scalar(o, x, s1, s2, o0, o1, **kw)
        self.p.op(eng, f, reads=reads, writes=writes, tag="ts",
                  accum=(accum is not None) or _is_small(out))

    def ttr(self, out, a, b, op0, op1, accum, reads, writes, scale=1.0,
            seed=0.0):
        self.p.op("vector",
                  lambda q, o=out, x=a, y=b, s=scale, sd=seed, o0=op0, o1=op1,
                  ac=accum: q.tensor_tensor_reduce(o, x, y, s, sd, o0, o1, ac),
                  reads=reads, writes=writes, tag="ttr")

    def stt(self, out, a, s, b, op0, op1, reads, writes, eng="vector"):
        self.p.op(eng,
                  lambda q, o=out, x=a, sc=s, y=b, o0=op0, o1=op1:
                  q.scalar_tensor_tensor(o, x, sc, y, o0, o1),
                  reads=reads, writes=writes, tag="stt", accum=_is_small(out))

    def cp(self, out, in_, reads, writes, eng="vector"):
        if eng == "scalar":
            self.p.op(eng, lambda q, o=out, i=in_: q.copy(o, i),
                      reads=reads, writes=writes, tag="cp",
                      accum=_is_small(out))
        else:
            self.p.op(eng, lambda q, o=out, i=in_: q.tensor_copy(o, i),
                      reads=reads, writes=writes, tag="cp",
                      accum=_is_small(out))

    def red(self, out, in_, op, reads, writes, negate=False):
        self.p.op("vector",
                  lambda q, o=out, i=in_, p_=op, n=negate:
                  q.tensor_reduce(o, i, axis=AX.X, op=p_, negate=n),
                  reads=reads, writes=writes, tag="red")

    def recip(self, out, in_, reads, writes):
        self.p.op("vector", lambda q, o=out, i=in_: q.reciprocal(o, i),
                  reads=reads, writes=writes, tag="recip",
                  accum=_is_small(out))

    def rsqrt(self, out, in_, reads, writes, scale=None, bias=None, minv=None):
        self.act(out, in_, AF.Ln, reads, writes, bias=bias, scale=scale)
        self.act(out, out, AF.Exp, writes, writes, scale=-0.5)
        if minv is not None:
            self.ts(out, out, minv, ALU.min, writes, writes)

    def bcast(self, dst, src_t, n, writes, offset=0, queue="sync"):
        ap = bass.AP(src_t, offset, [[0, 128], [1, n]])
        self.p.dma(queue, dst, ap, writes=writes)

    def ln_tile(self, x_ap, out_ap, kx, kout, scr, scr2, stats, sl, kst,
                ksc=None):
        ksc = ksc or kst
        st = lambda i: stats[:, sl + i: sl + i + 1]
        self.act(scr[:], x_ap, AF.Square, reads=(kx,), writes=(ksc + "sA", kst),
                 accum=st(0))
        self.ts(scr2[:], x_ap, 0.0, ALU.add, reads=(kx,), writes=(ksc + "sB", kst),
                s2=0.0, op1=ALU.add, accum=st(1))
        self.ts(st(2), st(1), 1.0 / D, ALU.mult, reads=(kst,), writes=(kst,))
        self.tt(st(5), st(2), st(2), ALU.mult, reads=(kst,), writes=(kst,))
        self.stt(st(3), st(0), 1.0 / D, st(5), ALU.mult, ALU.subtract,
                 reads=(kst,), writes=(kst,))
        self.rsqrt(st(4), st(3), reads=(kst, "epsc"), writes=(kst,),
                   bias=self.eps5)
        self.ts(out_ap, x_ap, st(2), ALU.subtract, reads=(kx, kst),
                writes=(kout,), s2=st(4), op1=ALU.mult)


def dram_row_ap(t, g, width=D):
    return bass.AP(t, g * 128 * width, [[width, 128], [1, width]])


def grow(g):
    return (g // NT) * (S + 128) + 128 + (g % NT) * 128


def gr_ap(t, g, width=D):
    return bass.AP(t, grow(g) * width, [[width, 128], [1, width]])


def gr_shift_ap(t, g):
    return bass.AP(t, (grow(g) - 1) * D + 512, [[D, 128], [1, 512]])


def build(passes=PASSES):
    nc = bass.Bass("TRN2", target_bir_lowering=False)
    prog = Prog()
    ctx = Ctx(nc, prog)
    p = prog

    dt_in = {}

    def din(name, shape, dtype=F32):
        dt_in[name] = nc.dram_tensor(name, list(shape), dtype,
                                     kind="ExternalInput")
        return dt_in[name]

    x_in = din("x", [B_LOC * S, D])
    din("aw_normw", [D]); din("aw_q", [D, D]); din("aw_k", [D, 512])
    din("aw_v", [D, 512]); din("aw_o", [D, D]); din("aw_gate", [1])
    din("aw_lt", [1])
    din("bw_q", [D, D]); din("bw_k", [D, D]); din("bw_v", [D, D])
    din("bw_o", [D, D])
    din("fw_qkv", [2, D, 1536]); din("fw_qs", [2, 64]); din("fw_ks", [2, 64])
    din("fw_wo", [2, 512, D]); din("fw_lng", [2, D]); din("fw_lnb", [2, D])
    din("fw_w1", [2, D, 2 * HID]); din("fw_b1", [2, 2 * HID])
    din("fw_w2", [2, HID, D]); din("fw_b2", [2, D])
    din("c_ident", [128, 128], BF16)
    din("c_cmask", [4 * 128, 512], BF16)
    din("c_fmask", [5 * 128, 640], BF16)
    for nm in ("c_ropeA", "c_ropeB", "c_xqA", "c_xqB", "c_xkA", "c_xkB"):
        din(nm, [S, 64])
    din("c_zrow", [1, 1024], BF16)
    din("c_eps", [2])

    out_d = nc.dram_tensor("out", [B_LOC * S, D], F32, kind="ExternalOutput")
    dbg = {}
    if passes == "F1":
        ctx.dbg_c1 = {}
        for nm, shape, dt_ in (("Qw", [128, 4 * 512], BF16),
                               ("Kw", [128, 4 * 512], BF16),
                               ("Vw", [128, 4 * 512], BF16),
                               ("oall", [128, 4 * 512], BF16),
                               ("sc", [128, 640], F32),
                               ("pn", [128, 640], BF16),
                               ("qrot", [128, 512], BF16),
                               ("krot", [128, 512], BF16),
                               ("qf", [128, 512], F32),
                               ("kf", [128, 512], F32),
                               ("hbf", [128, 1024], BF16),
                               ("yl0", [128, 1024], F32),
                               ("qf_raw", [128, 512], F32),
                               ("kf_raw", [128, 512], F32),
                               ("qf_n", [128, 512], F32),
                               ("qrot0", [128, 512], BF16),
                               ("fac0", [128, 32], F32),
                               ("hbf0", [128, 1024], BF16),
                               ("hT0", [128, 1024], BF16)):
            ctx.dbg_c1[nm] = nc.dram_tensor("dbgC_" + nm, shape, dt_,
                                            kind="ExternalOutput")
    if passes == "A1":
        ctx.dbg_a1 = {}
        for nm, shape, dt_ in (("kfm", [128, 4 * 2048], BF16),
                               ("vtm", [128, 16 * 512], BF16),
                               ("qfm", [128, 8 * 512], BF16),
                               ("qn", [128, 1024], BF16),
                               ("kn", [128, 512], BF16),
                               ("qf", [128, 1024], F32),
                               ("hbf", [128, 1024], BF16),
                               ("sc", [128, 2048], F32),
                               ("pn", [128, 2048], BF16),
                               ("oall", [128, 8 * 512], BF16)):
            ctx.dbg_a1[nm] = nc.dram_tensor("dbgA_" + nm, shape, dt_,
                                            kind="ExternalOutput")
    if passes != "full":
        dbg["xr"] = nc.dram_tensor("dbg_xr", [B_LOC * S, D], F32,
                                   kind="ExternalOutput")
        dbg["yr"] = nc.dram_tensor("dbg_yr", [B_LOC * S, D], F32,
                                   kind="ExternalOutput")
        dbg["y1r"] = nc.dram_tensor("dbg_y1r", [B_LOC * S, D], F32,
                                    kind="ExternalOutput")

    xr = nc.dram_tensor("xr", [B_LOC * S, D], F32)
    yr = nc.dram_tensor("yr", [B_LOC * (S + 128), D], F32)
    y1r = nc.dram_tensor("y1r", [B_LOC * (S + 128), D], F32)
    wTd = nc.dram_tensor("wTd", [4, D, D], BF16)
    w1b = nc.dram_tensor("w1b", [2, 2 * NHC, 128, 1024], BF16)

    es = ExitStack()
    ident = es.enter_context(nc.sbuf_tensor("ident", [128, 128], BF16))
    identf = es.enter_context(nc.sbuf_tensor("identf", [128, 128], F32))
    scal = es.enter_context(nc.sbuf_tensor("scal", [128, 8], F32))
    ctx.ident = ident
    ctx.ps = [es.enter_context(nc.psum_tensor(f"ps{i}", [128, 512], F32))
              for i in range(5)]
    ctx.psT = [es.enter_context(nc.psum_tensor(f"psT{i}", [128, 1024], BF16))
               for i in range(3)]

    p.dma("sync", ident[:], dt_in["c_ident"].ap(), writes=("ident",))
    ctx.cp(identf[:], ident[:], reads=("ident",), writes=("identf",))
    ctx.identf = identf
    ctx.bcast(scal[:, 0:1], dt_in["aw_gate"], 1, ("gate",))
    ctx.bcast(scal[:, 1:2], dt_in["aw_lt"], 1, ("temp",))
    ctx.act(scal[:, 1:2], scal[:, 1:2], AF.Exp, reads=("temp",),
            writes=("temp",))
    p.dma("sync", scal[:, 2:4],
          bass.AP(dt_in["c_eps"], 0, [[0, 128], [1, 2]]), writes=("epsc",))
    ctx.eps5 = scal[:, 2:3]
    ctx.eps6 = scal[:, 3:4]

    for tns, knm in ((yr, "yr"), (y1r, "y1r")):
        for b in range(B_LOC):
            p.dma("gpsimd",
                  bass.AP(tns, b * (S + 128) * D, [[D, 128], [1, D]]),
                  bass.AP(dt_in["c_zrow"], 0, [[0, 128], [1, 1024]]),
                  writes=(f"{knm}guard:{b}",))

    make_prologue(ctx, dt_in, wTd, w1b, es)

    def scoped(fn, *args, **kw):
        st = ExitStack()
        sb = lambda name, shape, dtype: st.enter_context(
            nc.sbuf_tensor(name, shape, dtype))
        fn(ctx, dt_in, sb, *args, **kw)
        st.close()
        prog.fence()

    ncut = {"A": 1, "A1": 1, "F1": 2, "F1m": 3, "AF0": 5, "AF0B": 6,
            "full": 10}[passes]
    seq = [
        lambda last: scoped(pass_attn_a, x_in, xr, yr, scal, last=last,
                            out_d=out_d),
        lambda last: scoped(pass_ffn_attn, xr, yr, y1r, inv=0, l=0),
        lambda last: scoped(pass_ffn_mlp, xr, yr, y1r, w1b, inv=0, l=0,
                            last=last, out_d=out_d),
        lambda last: scoped(pass_ffn_attn, xr, yr, y1r, inv=0, l=1),
        lambda last: scoped(pass_ffn_mlp, xr, yr, y1r, w1b, inv=0, l=1,
                            last=last, out_d=out_d),
        lambda last: scoped(pass_attn_b, xr, yr, wTd, last=last, out_d=out_d),
        lambda last: scoped(pass_ffn_attn, xr, yr, y1r, inv=1, l=0),
        lambda last: scoped(pass_ffn_mlp, xr, yr, y1r, w1b, inv=1, l=0,
                            last=last, out_d=out_d),
        lambda last: scoped(pass_ffn_attn, xr, yr, y1r, inv=1, l=1),
        lambda last: scoped(pass_ffn_mlp, xr, yr, y1r, w1b, inv=1, l=1,
                            last=last, out_d=out_d),
    ]
    for i in range(ncut):
        seq[i](i == ncut - 1)

    if passes != "full" and passes != "A1":
        for g in range(B_LOC * NT):
            p.dma("sync", dram_row_ap(dbg["xr"], g), dram_row_ap(xr, g),
                  reads=(f"xr:{g}",))
            p.dma("sync", dram_row_ap(dbg["yr"], g), gr_ap(yr, g),
                  reads=(f"yr:{g}",))
            if passes in ("F1", "F1m"):
                p.dma("sync", dram_row_ap(dbg["y1r"], g),
                      gr_ap(y1r, g), reads=(f"y1r:{g}",))

    prog.finalize()
    sems = {}
    for e in ENGINES:
        sems[(e, "c")] = es.enter_context(nc.semaphore(f"s_{e}_c"))
        for s in range(DMA_POOL[e]):
            sems[(e, "s", s)] = es.enter_context(
                nc.semaphore(f"s_{e}_d{s}"))
    with nc.Block() as blk:
        for e in ENGINES:
            if prog.ops[e] or e == "sync":
                getattr(blk, e)(lambda q, ee=e: prog.emit_engine(q, ee, sems))
    es.close()
    return nc


def make_prologue(ctx, dt_in, wTd, w1b, es):
    nc = ctx.nc
    p = ctx.p
    stg = es.enter_context(nc.sbuf_tensor("pro_stg", [128, 1024], BF16))
    stg2 = es.enter_context(nc.sbuf_tensor("pro_stg2", [128, 1024], BF16))
    steps = ctx.pro_steps

    def w1_step(l, hc):
        def go():
            half, hcl = divmod(hc, NHC)
            col0 = half * HID + hcl * 128
            w = min(128, HID - hcl * 128)
            if w < 128:
                p.dma("gpsimd", stg[:],
                      bass.AP(dt_in["c_zrow"], 0, [[0, 128], [1, 1024]]),
                      reads=("pstg",), writes=("pstg",))
            ap = bass.AP(dt_in["fw_w1"], l * D * 2 * HID + col0,
                         [[2 * HID, 128], [2 * HID * 128, 8], [1, w]])
            dst = bass.AP(stg, 0, [[1024, 128], [128, 8], [1, w]])
            p.dma("gpsimd", dst, ap, reads=("pstg",), writes=("pstg",))
            p.dma("sync",
                  bass.AP(w1b, (l * 2 * NHC + hc) * 128 * 1024,
                          [[1024, 128], [1, 1024]]),
                  stg[:], reads=("pstg",), writes=(f"w1b:{l}:{hc}",))
        return go

    for l in range(2):
        for hc in range(2 * NHC):
            steps.append(w1_step(l, hc))

    def wt_step(wi, nm, i):
        def go():
            ap = bass.AP(dt_in[nm], i * 128 * D, [[D, 128], [1, 1024]])
            p.dma("gpsimd", stg2[:], ap, reads=("pstg2",), writes=("pstg2",))
            if nm == "bw_q":
                ctx.ts(stg2[:], stg2[:], 0.125, ALU.mult, reads=("pstg2",),
                       writes=("pstg2",))
            for jh in range(2):
                pk = f"psT{jh}"
                for j4 in range(4):
                    j = jh * 4 + j4
                    ctx.tr(ctx.psT[jh][:, j4 * 128:(j4 + 1) * 128],
                           stg2[:, j * 128:(j + 1) * 128],
                           reads=("pstg2",), wkey=pk)
            ctx.cp(stg[:, 0:512], ctx.psT[0][:, 0:512], reads=("psT0",),
                   writes=("pstg",))
            ctx.cp(stg[:, 512:1024], ctx.psT[1][:, 0:512], reads=("psT1",),
                   writes=("pstg2e",))
            p.dma("sync",
                  bass.AP(wTd, wi * D * D + i * 128,
                          [[D, 128], [D * 128, 4], [1, 128]]),
                  bass.AP(stg, 0, [[1024, 128], [128, 4], [1, 128]]),
                  reads=("pstg",), writes=(f"wTd:{nm}:{i}",))
            p.dma("sync",
                  bass.AP(wTd, wi * D * D + 4 * 128 * D + i * 128,
                          [[D, 128], [D * 128, 4], [1, 128]]),
                  bass.AP(stg, 512, [[1024, 128], [128, 4], [1, 128]]),
                  reads=("pstg2e",), writes=(f"wTd:{nm}:{i}",))
        return go

    for wi, nm in enumerate(["bw_q", "bw_k", "bw_v", "bw_o"]):
        for i in range(8):
            steps.append(wt_step(wi, nm, i))


def pass_attn_a(ctx, dt_in, sb, x_in, xr, yr, scal, last, out_d):
    p = ctx.p
    PS, PST = ctx.ps, ctx.psT

    wq_s = sb("A_wq", [128, 8 * 1024], BF16)
    wk_s = sb("A_wk", [128, 8 * 512], BF16)
    wv_s = sb("A_wv", [128, 8 * 512], BF16)
    wo_s = sb("A_wo", [128, 8 * 1024], BF16)
    normw = sb("A_normw", [128, 1024], F32)
    cmask = sb("A_cmask", [128, 4 * 512], BF16)
    QFM = sb("A_qfm", [128, 8 * 512], BF16)
    KFM = sb("A_kfm", [128, 4 * 2048], BF16)
    VTM = sb("A_vtm", [128, 16 * 512], BF16)
    xin = [sb(f"A_xin{i}", [128, 1024], F32) for i in range(2)]
    ybuf = [sb(f"A_y{i}", [128, 1024], F32) for i in range(2)]
    hbf = [sb(f"A_hbf{i}", [128, 1024], BF16) for i in range(2)]
    hT = sb("A_hT", [128, 1024], BF16)
    qf = sb("A_qf", [128, 1024], F32)
    kf = sb("A_kf", [128, 512], F32)
    qn = sb("A_qn", [128, 1024], BF16)
    kn = sb("A_kn", [128, 512], BF16)
    scr = [sb(f"A_scr{i}", [128, 1024], F32) for i in range(2)]
    scr2 = [sb(f"A_scr2{i}", [128, 1024], F32) for i in range(2)]
    stats = [sb(f"A_stats{i}", [128, 32], F32) for i in range(2)]
    st2 = [sb(f"A_st2{i}", [128, 8], F32) for i in range(2)]
    sc = [sb(f"A_sc{i}", [128, 2048], F32) for i in range(2)]
    Pn = [sb(f"A_pn{i}", [128, 2048], BF16) for i in range(2)]
    PT = sb("A_ptb", [128, 16 * 512], BF16)
    oall = sb("A_oall", [128, 8 * 512], BF16)
    xt2 = [sb("A_xt2", [128, 1024], F32)] * 2
    yt2 = [sb("A_yt2", [128, 1024], F32)] * 2

    def wload(dst, name, cols, key):
        t = dt_in[name]
        for g2 in range(2):
            ap = bass.AP(t, g2 * 4 * 128 * cols,
                         [[cols, 128], [cols * 128, 4], [1, cols]])
            dstap = bass.AP(dst, g2 * 4 * cols,
                            [[8 * cols, 128], [cols, 4], [1, cols]])
            p.dma("gpsimd", dstap, ap, writes=(key,))

    wload(wq_s, "aw_q", 1024, "wq")
    wload(wk_s, "aw_k", 512, "wk")
    wload(wv_s, "aw_v", 512, "wv")
    wload(wo_s, "aw_o", 1024, "wo")
    ctx.bcast(normw[:], dt_in["aw_normw"], 1024, ("normw",))
    p.dma("sync", bass.AP(cmask, 0, [[2048, 128], [512, 4], [1, 512]]),
          dt_in["c_cmask"].ap().rearrange("(m p) n -> p m n", p=128),
          writes=("cmask",))

    for b in range(B_LOC):
        for qb in range(4):
            if getattr(ctx, "dbg_a1", None) is not None and (b, qb) == (0, 1):
                dbgd = ctx.dbg_a1
                for nm, buf in (("kfm", KFM), ("vtm", VTM), ("qfm", QFM),
                                ("qn", qn), ("kn", kn), ("qf", qf),
                                ("hbf", hbf[1]), ("sc", sc), ("pn", Pn),
                                ("oall", oall)):
                    p.dma("sync", dbgd[nm].ap(), buf[:],
                          reads=tuple(ctx.p.writer.keys()), writes=())
                return
            for tt in range(4):
                t = qb * 4 + tt
                g = b * NT + t
                pa = t % 2
                ctx.step_prologue(4)
                kx, ky = f"xin{pa}", f"ybuf{pa}"
                p.dma("sync", xin[pa][:], dram_row_ap(x_in, g), writes=(kx,))
                ctx.ln_tile(xin[pa][:], ybuf[pa][:], kx, ky, scr[pa],
                            scr2[pa], stats[pa], 0, f"stL{pa}")
                p.dma("sync", gr_ap(yr, g), ybuf[pa][:], reads=(ky,),
                      writes=(f"yr:{g}",))
                ctx.act(scr[pa][:], ybuf[pa][:], AF.Square, reads=(ky,),
                        writes=(f"scrA{pa}", f"stR{pa}"), scale=0.03125,
                        accum=stats[pa][:, 8:9])
                ctx.rsqrt(stats[pa][:, 9:10], stats[pa][:, 8:9],
                          reads=(f"stR{pa}", "epsc"), writes=(f"stR{pa}",),
                          bias=ctx.eps6)
                kh = f"hbf{pa}"
                ctx.stt(hbf[pa][:], ybuf[pa][:], stats[pa][:, 9:10], normw[:],
                        ALU.mult, ALU.mult, reads=(ky, f"stR{pa}", "normw"),
                        writes=(kh,))
                tb = ctx.tbank()
                for c in range(8):
                    ctx.tr(PST[tb][:, c * 128:(c + 1) * 128],
                           hbf[pa][:, c * 128:(c + 1) * 128],
                           reads=(kh,), wkey=f"psT{tb}")
                ctx.cp(hT[:], PST[tb][:], reads=(f"psT{tb}",), writes=("hT",))
                for n in range(2):
                    for c in range(8):
                        ctx.mm(PS[n][:], hT[:, c * 128:(c + 1) * 128],
                               wq_s[:, c * 1024 + n * 512:c * 1024 + (n + 1) * 512],
                               reads=("hT", "wq"), wkey=f"ps{n}",
                               start=(c == 0), stop=(c == 7))
                for c in range(8):
                    ctx.mm(PS[2][:], hT[:, c * 128:(c + 1) * 128],
                           wk_s[:, c * 512:(c + 1) * 512],
                           reads=("hT", "wk"), wkey="ps2",
                           start=(c == 0), stop=(c == 7))
                for c in range(8):
                    ctx.mm(PS[3][:], hT[:, c * 128:(c + 1) * 128],
                           wv_s[:, c * 512:(c + 1) * 512],
                           reads=("hT", "wv"), wkey="ps3",
                           start=(c == 0), stop=(c == 7))
                ctx.cp(qf[:, 0:512], PS[0][:], reads=("ps0",), writes=("qf",),
                       eng="scalar")
                ctx.cp(qf[:, 512:1024], PS[1][:], reads=("ps1",),
                       writes=("qf",), eng="scalar")
                ctx.cp(kf[:], PS[2][:], reads=("ps2",), writes=("kf",),
                       eng="scalar")
                ctx.cp(VTM[:, t * 512:(t + 1) * 512], PS[3][:], reads=("ps3",),
                       writes=(f"VTM:{t}",))
                ctx.act(scr[pa][:], qf[:], AF.Square, reads=("qf",),
                        writes=(f"scrA{pa}",))
                ctx.act(scr2[pa][:, 0:512], kf[:], AF.Square, reads=("kf",),
                        writes=(f"scrB{pa}",))
                ctx.red(stats[pa][:, 16:20],
                        bass.AP(scr[pa], 0, [[1024, 128], [256, 4], [1, 256]]),
                        ALU.add, reads=(f"scrA{pa}",), writes=(f"stQ{pa}",))
                ctx.red(stats[pa][:, 20:22],
                        bass.AP(scr2[pa], 0, [[1024, 128], [256, 2], [1, 256]]),
                        ALU.add, reads=(f"scrB{pa}",), writes=(f"stQ{pa}",))
                ctx.rsqrt(stats[pa][:, 24:28], stats[pa][:, 16:20],
                          reads=(f"stQ{pa}",), writes=(f"stF{pa}",), minv=1e12)
                ctx.rsqrt(stats[pa][:, 28:30], stats[pa][:, 20:22],
                          reads=(f"stQ{pa}",), writes=(f"stF{pa}",), minv=1e12)
                ctx.ts(stats[pa][:, 24:28], stats[pa][:, 24:28], scal[:, 1:2],
                       ALU.mult, reads=(f"stF{pa}", "temp"),
                       writes=(f"stF{pa}",))
                ctx.tt(bass.AP(qn, 0, [[1024, 128], [256, 4], [1, 256]]),
                       bass.AP(qf, 0, [[1024, 128], [256, 4], [1, 256]]),
                       bass.AP(stats[pa], 24, [[32, 128], [1, 4], [0, 256]]),
                       ALU.mult, reads=("qf", f"stF{pa}"), writes=("qn",))
                ctx.tt(bass.AP(kn, 0, [[512, 128], [256, 2], [1, 256]]),
                       bass.AP(kf, 0, [[512, 128], [256, 2], [1, 256]]),
                       bass.AP(stats[pa], 28, [[32, 128], [1, 2], [0, 256]]),
                       ALU.mult, reads=("kf", f"stF{pa}"), writes=("kn",),
                       eng="vector")
                tb = ctx.tbank()
                for c in range(8):
                    ctx.tr(PST[tb][:, c * 128:(c + 1) * 128],
                           qn[:, c * 128:(c + 1) * 128],
                           reads=("qn",), wkey=f"psT{tb}")
                ctx.cp(bass.AP(QFM, tt * 128,
                               [[8 * 512, 128], [512, 8], [1, 128]]),
                       bass.AP(PST[tb], 0, [[1024, 128], [128, 8], [1, 128]]),
                       reads=(f"psT{tb}",), writes=(f"QFM:{tt}",))
                tb = ctx.tbank()
                for c in range(4):
                    ctx.tr(PST[tb][:, c * 128:(c + 1) * 128],
                           kn[:, c * 128:(c + 1) * 128],
                           reads=("kn",), wkey=f"psT{tb}")
                ctx.cp(bass.AP(KFM, t * 128,
                               [[4 * 2048, 128], [2048, 4], [1, 128]]),
                       bass.AP(PST[tb], 0, [[1024, 128], [128, 4], [1, 128]]),
                       reads=(f"psT{tb}",), writes=(f"KFM:{t}",), eng="scalar")
            nch = qb + 1
            nkc = nch * 4

            def a2_sm1(h, qt):
                kvh = h // 2
                kq = f"QFM:{qt}"
                sp = qt % 2
                for c in range(nch):
                    pb_ = 2 + ((h * nch + c) % 3)
                    for cc in range(2):
                        ctx.mm(PS[pb_][:],
                               QFM[:, (2 * h + cc) * 512 + qt * 128:
                                   (2 * h + cc) * 512 + (qt + 1) * 128],
                               KFM[:, (2 * kvh + cc) * 2048 + c * 512:
                                   (2 * kvh + cc) * 2048 + (c + 1) * 512],
                               reads=(kq,) + tuple(
                                   f"KFM:{4 * c + i}" for i in range(4)),
                               wkey=f"ps{pb_}", start=(cc == 0),
                               stop=(cc == 1))
                    if c == qb:
                        ctx.tt(sc[sp][:, c * 512:(c + 1) * 512], PS[pb_][:],
                               cmask[:, qt * 512:(qt + 1) * 512], ALU.add,
                               reads=(f"ps{pb_}", "cmask"),
                               writes=(f"sc{sp}:{c}",))
                    else:
                        ctx.cp(sc[sp][:, c * 512:(c + 1) * 512], PS[pb_][:],
                               reads=(f"ps{pb_}",), writes=(f"sc{sp}:{c}",),
                               eng="scalar")
                ctx.red(st2[sp][:, 4:5], sc[sp][:, :nch * 512], ALU.max,
                        reads=tuple(f"sc{sp}:{c}" for c in range(nch)),
                        writes=(f"stM{sp}",), negate=True)
                ctx.act(Pn[sp][:, :nch * 512], sc[sp][:, :nch * 512], AF.Exp,
                        reads=(f"stM{sp}",) + tuple(
                            f"sc{sp}:{c}" for c in range(nch)),
                        writes=(f"Pn{sp}", f"stM{sp}"),
                        bias=st2[sp][:, 4:5], accum=st2[sp][:, 5:6])

            def a2_sm2pt(h, qt):
                sp = qt % 2
                ctx.recip(st2[sp][:, 5:6], st2[sp][:, 5:6],
                          reads=(f"stM{sp}",), writes=(f"stM{sp}",))
                ctx.ts(Pn[sp][:, :nch * 512], Pn[sp][:, :nch * 512],
                       st2[sp][:, 5:6], ALU.mult,
                       reads=(f"Pn{sp}", f"stM{sp}"), writes=(f"Pn{sp}",),
                       eng="vector")
                for kg in range((nkc + 7) // 8):
                    kcs = list(range(kg * 8, min(nkc, kg * 8 + 8)))
                    tb = ctx.tbank()
                    for kc in kcs:
                        ctx.tr(PST[tb][:, (kc % 8) * 128:(kc % 8 + 1) * 128],
                               Pn[sp][:, kc * 128:(kc + 1) * 128],
                               reads=(f"Pn{sp}",), wkey=f"psT{tb}")
                    nn = len(kcs)
                    ctx.cp(bass.AP(PT, kcs[0] * 512 + qt * 128,
                                   [[16 * 512, 128], [512, nn], [1, 128]]),
                           bass.AP(PST[tb], 0,
                                   [[1024, 128], [128, nn], [1, 128]]),
                           reads=(f"psT{tb}",),
                           writes=tuple(f"PT:{kc}:{qt}" for kc in kcs),
                           eng=("scalar" if qt % 2 else "vector"))

            def a2_pv(h):
                kvh = h // 2
                for vc in range(2):
                    for kc in range(nkc):
                        ctx.mm(PS[vc][:],
                               VTM[:, kc * 512 + kvh * 256 + vc * 128:
                                   kc * 512 + kvh * 256 + (vc + 1) * 128],
                               PT[:, kc * 512:(kc + 1) * 512],
                               reads=(f"VTM:{kc}",) + tuple(
                                   f"PT:{kc}:{q_}" for q_ in range(4)),
                               wkey=f"ps{vc}", start=(kc == 0),
                               stop=(kc == nkc - 1))
                    ctx.cp(oall[:, (2 * h + vc) * 512:(2 * h + vc + 1) * 512],
                           PS[vc][:], reads=(f"ps{vc}",),
                           writes=(f"oall:{2 * h + vc}",),
                           eng=("vector" if vc == 0 else "scalar"))

            for v in range(18):
                u = v - 2
                if u >= 0:
                    a2_sm2pt(*divmod(u, 4))
                    if u % 4 == 3:
                        a2_pv(u // 4)
                if v < 16:
                    a2_sm1(*divmod(v, 4))
            for qt in range(4):
                t = qb * 4 + qt
                g = b * NT + t
                pa = t % 2
                kx2, ky2 = "xt2s", "yt2s"
                p.dma("sync", xt2[pa][:], dram_row_ap(x_in, g), writes=(kx2,))
                p.dma("sync", yt2[pa][:], gr_ap(yr, g),
                      reads=(f"yr:{g}",), writes=(ky2,))
                for n in range(2):
                    for c in range(8):
                        ctx.mm(PS[2 + n][:],
                               oall[:, c * 512 + qt * 128:c * 512 + (qt + 1) * 128],
                               wo_s[:, c * 1024 + n * 512:c * 1024 + (n + 1) * 512],
                               reads=(f"oall:{c}", "wo"), wkey=f"ps{2 + n}",
                               start=(c == 0), stop=(c == 7))
                for n in range(2):
                    ctx.stt(ybuf[pa][:, n * 512:(n + 1) * 512], PS[2 + n][:],
                            scal[:, 0:1], yt2[pa][:, n * 512:(n + 1) * 512],
                            ALU.mult, ALU.add,
                            reads=(f"ps{2 + n}", "gate", ky2),
                            writes=(f"ybuf{pa}",))
                ctx.tt(xt2[pa][:], ybuf[pa][:], xt2[pa][:], ALU.add,
                       reads=(f"ybuf{pa}", kx2), writes=(kx2,), eng="vector")
                wb_store(ctx, xt2[pa][:], kx2, g, xr, yr, scr[pa], scr2[pa],
                         stats[pa], last, out_d, pa)


def wb_store(ctx, x1_ap, kx1, g, xr, yr, scr, scr2, stats, last, out_d,
             pa=0):
    p = ctx.p
    if last:
        p.dma("sync", dram_row_ap(out_d, g), x1_ap, reads=(kx1,),
              writes=(f"out:{g}",))
        p.dma("sync", dram_row_ap(xr, g), x1_ap, reads=(kx1,),
              writes=(f"xr:{g}",))
        return
    p.dma("sync", dram_row_ap(xr, g), x1_ap, reads=(kx1,), writes=(f"xr:{g}",))
    ctx.ln_tile(x1_ap, scr[:], kx1, f"scrA{pa}w", scr, scr2, stats, 10,
                f"stW{pa}")
    p.dma("sync", gr_ap(yr, g), scr[:], reads=(f"scrA{pa}w",),
          writes=(f"yr:{g}",))


def load_shifted(ctx, dst, src_t, g, t, key, dt_in, kpre):
    p = ctx.p
    b = g // NT
    p.dma("sync", dst[:, 0:512],
          bass.AP(src_t, grow(g) * D, [[D, 128], [1, 512]]),
          reads=(f"{kpre}:{g}",), writes=(key,))
    prev = (f"{kpre}guard:{b}",) if t == 0 else (f"{kpre}:{g - 1}",)
    p.dma("sync", dst[:, 512:1024], gr_shift_ap(src_t, g),
          reads=prev + (f"{kpre}:{g}",), writes=(key,))


def pass_ffn_attn(ctx, dt_in, sb, xr, yr, y1r, inv, l):
    p = ctx.p
    PS, PST = ctx.ps, ctx.psT
    pfx = f"F{inv}{l}a"
    N = lambda s: pfx + s

    wqkv = sb(N("wqkv"), [128, 8 * 1536], BF16)
    wo_s = sb(N("wo"), [128, 4 * 1024], BF16)
    fmask = sb(N("fmask"), [128, 5 * 640], BF16)
    tabl = sb(N("tabl"), [128, 4 * 1024], F32)
    qsb = sb(N("qsb"), [128, 128], F32)
    Kw = [sb(N(f"K{i}"), [128, 4 * 512], BF16) for i in range(2)]
    Vw = [sb(N(f"V{i}"), [128, 4 * 512], BF16) for i in range(2)]
    Qw = sb(N("Q"), [128, 4 * 512], BF16)
    yl = [sb(N(f"yl{i}"), [128, 1024], F32) for i in range(4)]
    hbf = [sb(N(f"hbf{i}"), [128, 1024], BF16) for i in range(2)]
    hT = [sb(N(f"hT{i}"), [128, 1024], BF16) for i in range(4)]
    qf = [sb(N(f"qf{i}"), [128, 512], F32) for i in range(4)]
    kf = [sb(N(f"kf{i}"), [128, 512], F32) for i in range(2)]
    t1r = [sb(N(f"t1r{i}"), [128, 512], F32) for i in range(2)]
    t2r = [sb(N(f"t2r{i}"), [128, 512], F32) for i in range(2)]
    qrot = [sb(N(f"qrot{i}"), [128, 512], BF16) for i in range(2)]
    krot = [sb(N(f"krot{i}"), [128, 512], BF16) for i in range(2)]
    scr = [sb(N(f"scr{i}"), [128, 1024], F32) for i in range(4)]
    stats = [sb(N(f"stats{i}"), [128, 32], F32) for i in range(4)]
    st2 = [sb(N(f"st2{i}"), [128, 8], F32) for i in range(4)]
    sc = [sb(N(f"sc{i}"), [128, 640], F32) for i in range(2)] * 2
    Pn = [sb(N(f"pn{i}"), [128, 640], BF16) for i in range(8)]
    PT = sb(N("pt"), [128, 8 * 512], BF16)
    oall = sb(N("oall"), [128, 4 * 512], BF16)
    yl2 = [sb(N(f"yl2{i}"), [128, 1024], F32) for i in range(2)]
    y1o = sb(N("y1o"), [128, 1024], F32)

    for g2 in range(4):
        ap = bass.AP(dt_in["fw_qkv"], l * D * 1536 + g2 * 2 * 128 * 1536,
                     [[1536, 128], [1536 * 128, 2], [1, 1536]])
        dst = bass.AP(wqkv, g2 * 2 * 1536,
                      [[8 * 1536, 128], [1536, 2], [1, 1536]])
        p.dma("gpsimd", dst, ap, writes=(N("wqkv"),))
    ap = bass.AP(dt_in["fw_wo"], l * 512 * D, [[D, 128], [D * 128, 4], [1, D]])
    p.dma("gpsimd", bass.AP(wo_s, 0, [[4 * 1024, 128], [1024, 4], [1, 1024]]),
          ap, writes=(N("wo"),))
    p.dma("sync", bass.AP(fmask, 0, [[5 * 640, 128], [640, 5], [1, 640]]),
          dt_in["c_fmask"].ap().rearrange("(m p) n -> p m n", p=128),
          writes=(N("fmask"),))
    ctx.bcast(qsb[:, 0:64], dt_in["fw_qs"], 64, (N("qsb"),), offset=l * 64)
    ctx.bcast(qsb[:, 64:128], dt_in["fw_ks"], 64, (N("qsb"),), offset=l * 64)
    for ti, nm in enumerate(("c_xqA", "c_xqB", "c_xkA", "c_xkB")):
        p.dma("sync", bass.AP(scr[3], 0, [[1024, 128], [64, 16], [1, 64]]),
              dt_in[nm].ap().rearrange("(t p) n -> p t n", p=128),
              writes=(N("tabb"),))
        qcol = 0 if ti < 2 else 64
        swap = (ti % 2 == 1)
        out3 = bass.AP(tabl, ti * 1024, [[4096, 128], [64, 16], [32, 2], [1, 32]])
        in3 = bass.AP(scr[3], 0, [[1024, 128], [64, 16], [32, 2], [1, 32]])
        if swap:
            sc3 = bass.AP(qsb, qcol + 32, [[128, 128], [0, 16], [-32, 2], [1, 32]])
        else:
            sc3 = bass.AP(qsb, qcol, [[128, 128], [0, 16], [32, 2], [1, 32]])
        ctx.tt(out3, in3, sc3, ALU.mult, reads=(N("tabb"), N("qsb")),
               writes=(N("tabl"),))

    for b in range(B_LOC):
        p.dma("sync", Kw[1][:],
              bass.AP(dt_in["c_zrow"], 0, [[0, 128], [0, 2], [1, 1024]]),
              writes=(N("K1"),))
        p.dma("sync", Vw[1][:],
              bass.AP(dt_in["c_zrow"], 0, [[0, 128], [0, 2], [1, 1024]]),
              writes=(N("V1"),))
        cur, prv = 0, 1
        for wi in range(4):
            kK, kV = N(f"K{cur}"), N(f"V{cur}")
            pK, pV = N(f"K{prv}"), N(f"V{prv}")
            for tt in range(4):
                t = wi * 4 + tt
                g = b * NT + t
                pa = tt
                kyl = N(f"yl{pa}")
                khb, kht = N(f"hbf{pa % 2}"), N(f"hT{pa}")
                kqf, kkf = N(f"qf{pa}"), N(f"kf{pa % 2}")
                kqr, kkr = N(f"qrot{pa % 2}"), N(f"krot{pa % 2}")
                load_shifted(ctx, yl[pa], yr, g, t, kyl, dt_in, "yr")
                ctx.cp(hbf[pa % 2][:], yl[pa][:], reads=(kyl,), writes=(khb,))
                tb = ctx.tbank()
                for c in range(8):
                    ctx.tr(PST[tb][:, c * 128:(c + 1) * 128],
                           hbf[pa % 2][:, c * 128:(c + 1) * 128],
                           reads=(khb,), wkey=f"psT{tb}")
                ctx.cp(hT[pa][:], PST[tb][:], reads=(f"psT{tb}",), writes=(kht,))
                for o3 in range(3):
                    for c in range(8):
                        ctx.mm(PS[o3][:], hT[pa][:, c * 128:(c + 1) * 128],
                               wqkv[:, c * 1536 + o3 * 512:
                                    c * 1536 + (o3 + 1) * 512],
                               reads=(kht, N("wqkv")), wkey=f"ps{o3}",
                               start=(c == 0), stop=(c == 7))
                ctx.cp(qf[pa][:], PS[0][:], reads=("ps0",), writes=(kqf,),
                       eng="scalar")
                ctx.cp(kf[pa % 2][:], PS[1][:], reads=("ps1",),
                       writes=(kkf,), eng="scalar")
                ctx.cp(Vw[cur][:, tt * 512:(tt + 1) * 512], PS[2][:],
                       reads=("ps2",), writes=(kV,))
                ctx.act(scr[pa][:, 0:512], qf[pa][:], AF.Square,
                        reads=(kqf, N("tabb")), writes=(N(f"scrA{pa}"),))
                ctx.act(scr[pa][:, 512:1024], kf[pa % 2][:], AF.Square,
                        reads=(kkf,), writes=(N(f"scrA{pa}"),))
                ctx.red(stats[pa][:, 0:8],
                        bass.AP(scr[pa], 0, [[1024, 128], [64, 8], [1, 64]]),
                        ALU.add, reads=(N(f"scrA{pa}"),),
                        writes=(N(f"stQ{pa}"),))
                ctx.red(stats[pa][:, 8:16],
                        bass.AP(scr[pa], 512, [[1024, 128], [64, 8], [1, 64]]),
                        ALU.add, reads=(N(f"scrA{pa}"),),
                        writes=(N(f"stQ{pa}"),))
                ctx.rsqrt(stats[pa][:, 16:24], stats[pa][:, 0:8],
                          reads=(N(f"stQ{pa}"),), writes=(N(f"stF{pa}"),),
                          minv=1e12)
                ctx.rsqrt(stats[pa][:, 24:32], stats[pa][:, 8:16],
                          reads=(N(f"stQ{pa}"),), writes=(N(f"stF{pa}"),),
                          minv=1e12)
                ctx.tt(bass.AP(qf[pa], 0, [[512, 128], [64, 8], [1, 64]]),
                       bass.AP(qf[pa], 0, [[512, 128], [64, 8], [1, 64]]),
                       bass.AP(stats[pa], 16, [[32, 128], [1, 8], [0, 64]]),
                       ALU.mult, reads=(kqf, N(f"stF{pa}")), writes=(kqf,))
                ctx.tt(bass.AP(kf[pa % 2], 0, [[512, 128], [64, 8], [1, 64]]),
                       bass.AP(kf[pa % 2], 0, [[512, 128], [64, 8], [1, 64]]),
                       bass.AP(stats[pa], 24, [[32, 128], [1, 8], [0, 64]]),
                       ALU.mult, reads=(kkf, N(f"stF{pa}")), writes=(kkf,),
                       eng="vector")
                xrope(ctx, qf[pa], t1r[pa % 2], t2r[pa % 2], qrot[pa % 2],
                      tabl, 0, t, kqf, kqr, N("tabl"), eng="vector",
                      pfx2=N(f"r{pa % 2}"))
                xrope(ctx, kf[pa % 2], t1r[pa % 2], t2r[pa % 2],
                      krot[pa % 2], tabl, 2, t, kkf, kkr, N("tabl"),
                      eng="vector", pfx2=N(f"r{pa % 2}"))
                tb = ctx.tbank()
                for c in range(4):
                    ctx.tr(PST[tb][:, c * 128:(c + 1) * 128],
                           qrot[pa % 2][:, c * 128:(c + 1) * 128],
                           reads=(kqr,), wkey=f"psT{tb}")
                ctx.cp(bass.AP(Qw, tt * 128, [[4 * 512, 128], [512, 4], [1, 128]]),
                       bass.AP(PST[tb], 0, [[1024, 128], [128, 4], [1, 128]]),
                       reads=(f"psT{tb}",), writes=(N("Q"),))
                tb = ctx.tbank()
                for c in range(4):
                    ctx.tr(PST[tb][:, c * 128:(c + 1) * 128],
                           krot[pa % 2][:, c * 128:(c + 1) * 128],
                           reads=(kkr,), wkey=f"psT{tb}")
                ctx.cp(bass.AP(Kw[cur], tt * 128,
                               [[4 * 512, 128], [512, 4], [1, 128]]),
                       bass.AP(PST[tb], 0, [[1024, 128], [128, 4], [1, 128]]),
                       reads=(f"psT{tb}",), writes=(kK,), eng="scalar")
            def c2_sm1(h, qt):
                hb = (h % 2) * 64
                gp = h // 2
                sp = qt
                sci = qt % 2
                pnn = (h % 2) * 4 + qt
                ksc, kpn, kstm = (N(f"sc{sci}"), N(f"pn{pnn}"), N(f"stM{sp}"))
                Na = 512 - 128 * qt
                o0 = Na if wi == 0 else 0
                wdt = 640 - o0
                mrow = qt if wi == 0 else 4
                qsl = Qw[hb:hb + 64, gp * 512 + qt * 128:
                         gp * 512 + (qt + 1) * 128]
                rot = (h * 4 + qt) * 2
                if wi > 0:
                    pbA = 2 + (rot % 3)
                    ctx.mm(PS[pbA][:, 0:Na], qsl,
                           Kw[prv][hb:hb + 64,
                                   gp * 512 + 128 * qt:(gp + 1) * 512],
                           reads=(N("Q"), pK), wkey=f"ps{pbA}",
                           start=True, stop=True)
                    ctx.tt(sc[sp][:, 0:Na], PS[pbA][:, 0:Na],
                           fmask[:, mrow * 640:mrow * 640 + Na],
                           ALU.add, reads=(f"ps{pbA}", N("fmask")),
                           writes=(ksc,))
                pbB = 2 + ((rot + 1) % 3)
                wb_ = 128 * qt + 128
                ctx.mm(PS[pbB][:, 0:wb_], qsl,
                       Kw[cur][hb:hb + 64, gp * 512:gp * 512 + wb_],
                       reads=(N("Q"), kK), wkey=f"ps{pbB}",
                       start=True, stop=True)
                ctx.tt(sc[sp][:, Na - o0:Na - o0 + wb_], PS[pbB][:, 0:wb_],
                       fmask[:, mrow * 640 + Na:mrow * 640 + Na + wb_],
                       ALU.add, reads=(f"ps{pbB}", N("fmask")),
                       writes=(ksc,))
                ctx.red(st2[sp][:, 4:5], sc[sp][:, 0:wdt], ALU.max,
                        reads=(ksc,), writes=(kstm,), negate=True)
                ctx.act(Pn[pnn][:, 0:wdt], sc[sp][:, 0:wdt], AF.Exp,
                        reads=(ksc, kstm),
                        writes=(kpn, kstm), bias=st2[sp][:, 4:5],
                        accum=st2[sp][:, 5:6])

            def c2_sm2(h, qt):
                sp = qt
                pnn = (h % 2) * 4 + qt
                kpn, kstm = N(f"pn{pnn}"), N(f"stM{sp}")
                wdt = 640 - (512 - 128 * qt if wi == 0 else 0)
                ctx.recip(st2[sp][:, 5:6], st2[sp][:, 5:6], reads=(kstm,),
                          writes=(kstm,))
                ctx.ts(Pn[pnn][:, 0:wdt], Pn[pnn][:, 0:wdt], st2[sp][:, 5:6],
                       ALU.mult, reads=(kpn, kstm), writes=(kpn,),
                       eng="vector")

            def c2_softmax(h):
                for qt in range(4):
                    c2_sm1(h, qt)
                    if ctx.sm_pend is not None:
                        c2_sm2(*ctx.sm_pend)
                    ctx.sm_pend = (h, qt)

            def c2_pv(h):
                hb = (h % 2) * 64
                gp = h // 2
                k0 = [qt + ((512 - 128 * qt) // 128 if wi == 0 else 0)
                      for qt in range(4)]
                nj_ = [(640 - (512 - 128 * qt if wi == 0 else 0)) // 128
                       for qt in range(4)]
                for qt in range(4):
                    pnn = (h % 2) * 4 + qt
                    kpn = N(f"pn{pnn}")
                    nj = nj_[qt]
                    tb = ctx.tbank()
                    for j in range(nj):
                        ctx.tr(PST[tb][:, j * 128:(j + 1) * 128],
                               Pn[pnn][:, j * 128:(j + 1) * 128],
                               reads=(kpn,), wkey=f"psT{tb}")
                    ctx.cp(bass.AP(PT, qt * 128 + k0[qt] * 512,
                                   [[8 * 512, 128], [512, nj], [1, 128]]),
                           bass.AP(PST[tb], 0,
                                   [[1024, 128], [128, nj], [1, 128]]),
                           reads=(f"psT{tb}",),
                           writes=tuple(f"{pfx}PT:{k0[qt] + j}:{qt}"
                                        for j in range(nj)),
                           eng=("scalar" if qt % 2 else "vector"))
                pb0 = 0 if h % 2 == 0 else 64
                for qt in range(4):
                    for j in range(nj_[qt]):
                        kl = k0[qt] + j
                        vb = Vw[prv] if kl < 4 else Vw[cur]
                        vkey = pV if kl < 4 else kV
                        ctx.mm(PS[0][pb0:pb0 + 64,
                                     qt * 128:(qt + 1) * 128],
                               vb[:, (kl % 4) * 512 + h * 64:
                                  (kl % 4) * 512 + (h + 1) * 64],
                               PT[:, kl * 512 + qt * 128:
                                  kl * 512 + (qt + 1) * 128],
                               reads=(vkey, f"{pfx}PT:{kl}:{qt}"),
                               wkey="ps0", start=(j == 0),
                               stop=(j == nj_[qt] - 1),
                               skip=True)
                if h % 2 == 1:
                    ctx.cp(oall[:, gp * 512:(gp + 1) * 512], PS[0][:],
                           reads=("ps0",), writes=(f"{pfx}oall:{gp}",),
                           eng="scalar")

            for v in range(34):
                if v < 32:
                    c2_sm1(*divmod(v, 4))
                u = v - 2
                if u >= 0:
                    c2_sm2(*divmod(u, 4))
                    if u % 4 == 3:
                        c2_pv(u // 4)
            if getattr(ctx, "dbg_c1", None) is not None and (b, wi) == (0, 0)                     and inv == 0 and l == 0:
                dbgd = ctx.dbg_c1
                for nm, buf in (("Qw", Qw), ("Kw", Kw[cur]), ("Vw", Vw[cur]),
                                ("oall", oall), ("sc", sc[0]), ("pn", Pn[0]),
                                ("qrot", qrot[0]), ("krot", krot[0]),
                                ("qf", qf[0]), ("kf", kf[0]),
                                ("hbf", hbf[0])):
                    p.dma("sync", dbgd[nm].ap(), buf[:],
                          reads=tuple(ctx.p.writer.keys()), writes=())
            for qt in range(4):
                t = wi * 4 + qt
                g = b * NT + t
                pa = qt % 2
                ky2 = N(f"yl2{pa}")
                p.dma("sync", yl2[pa][:], gr_ap(yr, g),
                      reads=(f"yr:{g}",), writes=(ky2,))
                for n in range(2):
                    for c in range(4):
                        ctx.mm(PS[2 + n][:],
                               oall[:, c * 512 + qt * 128:c * 512 + (qt + 1) * 128],
                               wo_s[:, c * 1024 + n * 512:c * 1024 + (n + 1) * 512],
                               reads=(f"{pfx}oall:{c}", N("wo")),
                               wkey=f"ps{2 + n}", start=(c == 0), stop=(c == 3))
                for n in range(2):
                    ctx.tt(y1o[:, n * 512:(n + 1) * 512], PS[2 + n][:],
                           yl2[pa][:, n * 512:(n + 1) * 512], ALU.add,
                           reads=(f"ps{2 + n}", ky2), writes=(N("y1o"),))
                p.dma("sync", gr_ap(y1r, g), y1o[:], reads=(N("y1o"),),
                      writes=(f"y1r:{g}",))
            cur, prv = prv, cur


def xrope(ctx, src, t1r, t2r, outb, tabl, ti0, t, ksrc, kout, ktab,
          eng="vector", pfx2="r"):
    At = bass.AP(tabl, ti0 * 1024 + t * 64, [[4096, 128], [0, 8], [1, 64]])
    s3 = bass.AP(src, 0, [[512, 128], [64, 8], [1, 64]])
    ctx.tt(bass.AP(t1r, 0, [[512, 128], [64, 8], [1, 64]]), s3, At, ALU.mult,
           reads=(ksrc, ktab), writes=(pfx2 + "t1",), eng=eng)
    o2 = bass.AP(t2r, 0, [[512, 128], [64, 8], [32, 2], [1, 32]])
    prt = bass.AP(src, 32, [[512, 128], [64, 8], [-32, 2], [1, 32]])
    Bt = bass.AP(tabl, (ti0 + 1) * 1024 + t * 64,
                 [[4096, 128], [0, 8], [32, 2], [1, 32]])
    ctx.tt(o2, prt, Bt, ALU.mult, reads=(ksrc, ktab), writes=(pfx2 + "t2",),
           eng=eng)
    ctx.tt(outb[:], t1r[:], t2r[:], ALU.add,
           reads=(pfx2 + "t1", pfx2 + "t2"), writes=(kout,), eng=eng)


def pass_ffn_mlp(ctx, dt_in, sb, xr, yr, y1r, w1b, inv, l, last, out_d):
    p = ctx.p
    PS, PST = ctx.ps, ctx.psT
    endinv = (l == 1)
    pfx = f"F{inv}{l}m"
    N = lambda s: pfx + s

    lng = sb(N("lng"), [128, 1024], F32)
    lnb = sb(N("lnb"), [128, 1024], F32)
    b2b = sb(N("b2b"), [128, 1024], F32)
    b1c = sb(N("b1c"), [128, 2 * NHC], F32)
    h2T = sb(N("h2T"), [128, 8 * 1024], BF16)
    h4 = sb(N("h4"), [128, NHC * 1024], BF16)
    w2s = sb(N("w2s"), [128, NHC * 1024], BF16)
    w1s = [sb(N(f"w1s{i}"), [128, 1024], BF16) for i in range(4)]
    gact = sb(N("gact"), [128, 1024], BF16)
    abf = sb(N("abf"), [128, 1024], BF16)
    h2bf = [sb(N(f"h2bf{i}"), [128, 1024], BF16) for i in range(2)]
    hl = [sb(N(f"hl{i}"), [128, 1024], F32) for i in range(4)]
    h2n = [sb(N(f"h2n{i}"), [128, 1024], F32) for i in range(4)]
    scr = [sb(N(f"scr{i}"), [128, 1024], F32) for i in range(2)]
    scr2 = [sb(N(f"scr2{i}"), [128, 1024], F32) for i in range(2)]
    stats = [sb(N(f"stats{i}"), [128, 32], F32) for i in range(4)]
    y1l = [sb(N(f"y1l{i}"), [128, 1024], F32) for i in range(2)]
    xl = [sb(N(f"xl{i}"), [128, 1024], F32) for i in range(2)]
    yo = sb(N("yo"), [128, 1024], F32)

    ctx.bcast(lng[:], dt_in["fw_lng"], 1024, (N("lng"),), offset=l * D)
    ctx.bcast(lnb[:], dt_in["fw_lnb"], 1024, (N("lnb"),), offset=l * D)
    ctx.bcast(b2b[:], dt_in["fw_b2"], 1024, (N("b2b"),), offset=l * D)
    for half in range(2):
        base = l * 2 * HID + half * HID
        p.dma("gpsimd", scr2[0][0:22, 0:128],
              bass.AP(dt_in["c_zrow"], 0, [[0, 22], [1, 128]]),
              reads=("scrB0",), writes=("scrB0",))
        p.dma("sync", scr2[0][0:21, 0:128],
              bass.AP(dt_in["fw_b1"], base, [[128, 21], [1, 128]]),
              reads=("scrB0",), writes=("scrB0",))
        p.dma("sync", scr2[0][21:22, 0:42],
              bass.AP(dt_in["fw_b1"], base + 21 * 128, [[42, 1], [1, 42]]),
              reads=("scrB0",), writes=("scrB0",))
        ctx.p.op("tensor",
                 lambda q, o=PS[4][:, 0:22], i=scr2[0][0:22, 0:128],
                 idn=ctx.identf: q.transpose(o, i, idn[0:22, 0:22]),
                 reads=("scrB0", "identf"), writes=("ps4",), tag="tr")
        ctx.cp(b1c[:, half * NHC:(half + 1) * NHC], PS[4][:, 0:22],
               reads=("ps4",), writes=(N("b1c"),))

    for blk in range(4):
        b = blk // 2
        if blk == 0:
            for i in range(NHC):
                w = min(128, HID - i * 128)
                if w < 128:
                    p.dma("gpsimd", w2s[:, i * 1024:(i + 1) * 1024],
                          bass.AP(dt_in["c_zrow"], 0, [[0, 128], [1, 1024]]),
                          reads=(N("w2s"),), writes=(N("w2s"),))
                p.dma("gpsimd", w2s[0:w, i * 1024:(i + 1) * 1024],
                      bass.AP(dt_in["fw_w2"], l * HID * D + i * 128 * D,
                              [[D, w], [1, D]]),
                      reads=(N("w2s"),), writes=(N("w2s"),),
                      partial=(w < 128))
        for tt in range(8):
            t = (blk % 2) * 8 + tt
            g = b * NT + t
            pa = tt % 4
            khl = N(f"hl{pa}")
            kh2, kh2b = N(f"h2n{pa}"), N(f"h2bf{pa % 2}")
            load_shifted(ctx, hl[pa], y1r, g, t, khl, dt_in, "y1r")
            ctx.ln_tile(hl[pa][:], h2n[pa][:], khl, kh2, scr[pa % 2],
                        scr2[pa % 2], stats[pa], 0, N(f"stL{pa}"),
                        ksc=N(f"sl{pa % 2}"))
            ctx.stt(h2n[pa][:], h2n[pa][:], 1.0, lng[:], ALU.mult, ALU.mult,
                    reads=(kh2, N("lng")), writes=(kh2,))
            ctx.tt(h2bf[pa % 2][:], h2n[pa][:], lnb[:], ALU.add,
                   reads=(kh2, N("lnb")), writes=(kh2b,),
                   eng="vector")
            tb = ctx.tbank()
            for c in range(8):
                ctx.tr(PST[tb][:, c * 128:(c + 1) * 128],
                       h2bf[pa % 2][:, c * 128:(c + 1) * 128],
                       reads=(kh2b,), wkey=f"psT{tb}")
            ctx.cp(bass.AP(h2T, tt * 128, [[8 * 1024, 128], [1024, 8], [1, 128]]),
                   bass.AP(PST[tb], 0, [[1024, 128], [128, 8], [1, 128]]),
                   reads=(f"psT{tb}",), writes=(f"{pfx}h2T:{tt}",), eng="scalar")
        for i in range(NHC):
            pa = i % 2
            ka, kg = N(f"w1a{pa}"), N(f"w1g{pa}")
            p.dma("gpsimd", w1s[pa][:],
                  bass.AP(w1b, (l * 2 * NHC + i) * 128 * 1024,
                          [[1024, 128], [1, 1024]]),
                  reads=(f"w1b:{l}:{i}",), writes=(ka,))
            p.dma("gpsimd", w1s[2 + pa][:],
                  bass.AP(w1b, (l * 2 * NHC + NHC + i) * 128 * 1024,
                          [[1024, 128], [1, 1024]]),
                  reads=(f"w1b:{l}:{NHC + i}",), writes=(kg,))
            kpa, kpg = f"ps{pa}", f"ps{2 + pa}"
            for nn in range(2):
                psa, psg = PS[pa], PS[2 + pa]
                for c in range(8):
                    ctx.mm(psa[:], w1s[pa][:, c * 128:(c + 1) * 128],
                           h2T[:, c * 1024 + nn * 512:c * 1024 + (nn + 1) * 512],
                           reads=(ka,) + tuple(f"{pfx}h2T:{t_}"
                                               for t_ in range(8)),
                           wkey=kpa, start=(c == 0), stop=(c == 7))
                for c in range(8):
                    ctx.mm(psg[:], w1s[2 + pa][:, c * 128:(c + 1) * 128],
                           h2T[:, c * 1024 + nn * 512:c * 1024 + (nn + 1) * 512],
                           reads=(kg,) + tuple(f"{pfx}h2T:{t_}"
                                               for t_ in range(8)),
                           wkey=kpg, start=(c == 0), stop=(c == 7))
                ctx.act(gact[:, nn * 512:(nn + 1) * 512], psg[:], AF.Gelu,
                        reads=(kpg, N("b1c")), writes=(N("gact"),),
                        bias=b1c[:, NHC + i:NHC + i + 1])
                ctx.ts(abf[:, nn * 512:(nn + 1) * 512], psa[:],
                       b1c[:, i:i + 1], ALU.add, reads=(kpa, N("b1c")),
                       writes=(N("abf"),))
            ctx.tt(h4[:, i * 1024:(i + 1) * 1024], abf[:], gact[:], ALU.mult,
                   reads=(N("abf"), N("gact")), writes=(f"{pfx}h4:{i}",),
                   eng="vector")
        for tt in range(8):
            t = (blk % 2) * 8 + tt
            g = b * NT + t
            pa = tt % 2
            ky1, kx2 = N(f"y1l{pa}"), N(f"xl{pa}")
            p.dma("sync", y1l[pa][:], gr_ap(y1r, g),
                  reads=(f"y1r:{g}",), writes=(ky1,))
            if endinv:
                p.dma("sync", xl[pa][:], dram_row_ap(xr, g),
                      reads=(f"xr:{g}",), writes=(kx2,))
            pq = (4, 0) if tt % 2 == 0 else (1, 2)
            for n in range(2):
                for i in range(NHC):
                    ctx.mm(PS[pq[n]][:],
                           h4[:, i * 1024 + tt * 128:i * 1024 + (tt + 1) * 128],
                           w2s[:, i * 1024 + n * 512:i * 1024 + (n + 1) * 512],
                           reads=(f"{pfx}h4:{i}", N("w2s")),
                           wkey=f"ps{pq[n]}", start=(i == 0),
                           stop=(i == NHC - 1))
            for n in range(2):
                ctx.tt(yo[:, n * 512:(n + 1) * 512], PS[pq[n]][:],
                       y1l[pa][:, n * 512:(n + 1) * 512], ALU.add,
                       reads=(f"ps{pq[n]}", ky1), writes=(N("yo"),))
            ctx.tt(yo[:], yo[:], b2b[:], ALU.add, reads=(N("yo"), N("b2b")),
                   writes=(N("yo"),), eng="vector")
            if not endinv:
                p.dma("sync", gr_ap(yr, g), yo[:], reads=(N("yo"),),
                      writes=(f"yr:{g}",))
            else:
                ctx.tt(xl[pa][:], yo[:], xl[pa][:], ALU.add,
                       reads=(N("yo"), kx2), writes=(kx2,), eng="vector")
                wb_store(ctx, xl[pa][:], kx2, g, xr, yr, scr[pa % 2],
                         scr2[pa % 2], stats[pa], last, out_d, pa % 2)


def pass_attn_b(ctx, dt_in, sb, xr, yr, wTd, last, out_d):
    p = ctx.p
    PS, PST = ctx.ps, ctx.psT

    wsl = [sb(f"B_w{i}", [128, 8 * 1024], BF16) for i in range(2)]
    cmask = sb("B_cmask", [128, 4 * 512], BF16)
    rtab = sb("B_rtab", [128, 2 * 1024], F32)
    QFM = sb("B_qfm", [128, 8 * 512], BF16)
    KFM = sb("B_kfm", [128, 8 * 2048], BF16)
    VTM = sb("B_vtm", [128, 16 * 1024], BF16)
    yl = [sb(f"B_yl{i}", [128, 1024], F32) for i in range(2)]
    ybf = sb("B_ybf", [128, 1024], BF16)
    yT = sb("B_yT", [128, 1024], BF16)
    t2r = sb("B_t2r", [128, 1024], F32)
    qrot = sb("B_qrot", [128, 1024], BF16)
    krot = sb("B_krot", [128, 1024], BF16)
    kf = sb("B_kf", [128, 1024], F32)
    st2 = sb("B_st2", [128, 8], F32)
    sc = [sb(f"B_sc{i}", [128, 2048], F32) for i in range(2)]
    Pn = [sb(f"B_pn{i}", [128, 2048], BF16) for i in range(2)]
    PT = sb("B_ptb", [128, 16 * 512], BF16)
    oall = sb("B_oall", [128, 8 * 512], BF16)
    xt2 = [sb(f"B_xt2{i}", [128, 1024], F32) for i in range(2)]
    scr = sb("B_scr", [128, 1024], F32)
    scr2 = t2r
    stats = sb("B_stats", [128, 32], F32)

    p.dma("sync", bass.AP(cmask, 0, [[2048, 128], [512, 4], [1, 512]]),
          dt_in["c_cmask"].ap().rearrange("(m p) n -> p m n", p=128),
          writes=("cmask",))
    for ti, nm in enumerate(("c_ropeA", "c_ropeB")):
        p.dma("sync", bass.AP(rtab, ti * 1024, [[2048, 128], [64, 16], [1, 64]]),
              dt_in[nm].ap().rearrange("(t p) n -> p t n", p=128),
              writes=("Brtab",))

    wnames = ("bw_q", "bw_k", "bw_v", "bw_o")

    def wTload(slot, wi, key):
        nm = wnames[wi]
        for g2 in range(2):
            ap = bass.AP(wTd, wi * D * D + g2 * 4 * 128 * D,
                         [[D, 128], [D * 128, 4], [1, D]])
            dst = bass.AP(wsl[slot], g2 * 4 * 1024,
                          [[8 * 1024, 128], [1024, 4], [1, 1024]])
            p.dma("gpsimd", dst, ap,
                  reads=tuple(f"wTd:{nm}:{j}" for j in range(8)),
                  writes=(key,))

    for b in range(B_LOC):
        wTload(0, 1, "Bw0")
        wTload(1, 2, "Bw1")
        for t in range(NT):
            g = b * NT + t
            pa = t % 2
            kyl = f"Byl{pa}"
            p.dma("sync", yl[pa][:], gr_ap(yr, g), reads=(f"yr:{g}",),
                  writes=(kyl,))
            ctx.cp(ybf[:], yl[pa][:], reads=(kyl,), writes=("Bybf",))
            tb = ctx.tbank()
            for c in range(8):
                ctx.tr(PST[tb][:, c * 128:(c + 1) * 128],
                       ybf[:, c * 128:(c + 1) * 128], reads=("Bybf",),
                       wkey=f"psT{tb}")
            ctx.cp(yT[:], PST[tb][:], reads=(f"psT{tb}",), writes=("ByT",))
            for n in range(2):
                for c in range(8):
                    ctx.mm(PS[n][:], yT[:, c * 128:(c + 1) * 128],
                           wsl[0][:, c * 1024 + n * 512:c * 1024 + (n + 1) * 512],
                           reads=("ByT", "Bw0"), wkey=f"ps{n}",
                           start=(c == 0), stop=(c == 7))
            for n in range(2):
                for c in range(8):
                    ctx.mm(PS[2 + n][:], yT[:, c * 128:(c + 1) * 128],
                           wsl[1][:, c * 1024 + n * 512:c * 1024 + (n + 1) * 512],
                           reads=("ByT", "Bw1"), wkey=f"ps{2 + n}",
                           start=(c == 0), stop=(c == 7))
            ctx.cp(kf[:, 0:512], PS[0][:], reads=("ps0",), writes=("Bkf",),
                   eng="scalar")
            ctx.cp(kf[:, 512:1024], PS[1][:], reads=("ps1",), writes=("Bkf",),
                   eng="scalar")
            brope(ctx, kf, "Bkf", t2r, krot, rtab, t, "Bkrot")
            ctx.cp(VTM[:, t * 1024:t * 1024 + 512], PS[2][:], reads=("ps2",),
                   writes=(f"BVTM:{t}:0",))
            ctx.cp(VTM[:, t * 1024 + 512:(t + 1) * 1024], PS[3][:],
                   reads=("ps3",), writes=(f"BVTM:{t}:1",), eng="scalar")
            tb = ctx.tbank()
            for c in range(8):
                ctx.tr(PST[tb][:, c * 128:(c + 1) * 128],
                       krot[:, c * 128:(c + 1) * 128], reads=("Bkrot",),
                       wkey=f"psT{tb}")
            ctx.cp(bass.AP(KFM, t * 128, [[8 * 2048, 128], [2048, 8], [1, 128]]),
                   bass.AP(PST[tb], 0, [[1024, 128], [128, 8], [1, 128]]),
                   reads=(f"psT{tb}",), writes=(f"BKFM:{t}",))
        wTload(0, 0, "Bw0")
        wTload(1, 3, "Bw1")
        for qb in range(4):
            for tt in range(4):
                t = qb * 4 + tt
                g = b * NT + t
                pa = t % 2
                kyl = f"Byl{pa}"
                p.dma("sync", yl[pa][:], gr_ap(yr, g),
                      reads=(f"yr:{g}",), writes=(kyl,))
                ctx.cp(ybf[:], yl[pa][:], reads=(kyl,), writes=("Bybf",))
                tb = ctx.tbank()
                for c in range(8):
                    ctx.tr(PST[tb][:, c * 128:(c + 1) * 128],
                           ybf[:, c * 128:(c + 1) * 128], reads=("Bybf",),
                           wkey=f"psT{tb}")
                ctx.cp(yT[:], PST[tb][:], reads=(f"psT{tb}",), writes=("ByT",))
                for n in range(2):
                    for c in range(8):
                        ctx.mm(PS[n][:], yT[:, c * 128:(c + 1) * 128],
                               wsl[0][:, c * 1024 + n * 512:
                                    c * 1024 + (n + 1) * 512],
                               reads=("ByT", "Bw0"), wkey=f"ps{n}",
                               start=(c == 0), stop=(c == 7))
                ctx.cp(kf[:, 0:512], PS[0][:], reads=("ps0",),
                       writes=("Bkf",), eng="scalar")
                ctx.cp(kf[:, 512:1024], PS[1][:], reads=("ps1",),
                       writes=("Bkf",), eng="scalar")
                brope(ctx, kf, "Bkf", t2r, qrot, rtab, t, "Bqrot")
                tb = ctx.tbank()
                for c in range(8):
                    ctx.tr(PST[tb][:, c * 128:(c + 1) * 128],
                           qrot[:, c * 128:(c + 1) * 128], reads=("Bqrot",),
                           wkey=f"psT{tb}")
                ctx.cp(bass.AP(QFM, tt * 128,
                               [[8 * 512, 128], [512, 8], [1, 128]]),
                       bass.AP(PST[tb], 0, [[1024, 128], [128, 8], [1, 128]]),
                       reads=(f"psT{tb}",), writes=(f"BQFM:{tt}",))
            nch = qb + 1
            nkc = nch * 4

            def b2_sm1(gp, h2, qt):
                hb = h2 * 64
                sp = qt % 2
                for c in range(nch):
                    pb_ = 2 + ((((gp * 2 + h2) * 4 + qt) * nch + c) % 3)
                    ctx.mm(PS[pb_][:],
                           QFM[hb:hb + 64, gp * 512 + qt * 128:
                               gp * 512 + (qt + 1) * 128],
                           KFM[hb:hb + 64, gp * 2048 + c * 512:
                               gp * 2048 + (c + 1) * 512],
                           reads=(f"BQFM:{qt}",) + tuple(
                               f"BKFM:{4 * c + i}" for i in range(4)),
                           wkey=f"ps{pb_}", start=True, stop=True)
                    if c == qb:
                        ctx.tt(sc[sp][:, c * 512:(c + 1) * 512],
                               PS[pb_][:],
                               cmask[:, qt * 512:(qt + 1) * 512],
                               ALU.add, reads=(f"ps{pb_}", "cmask"),
                               writes=(f"Bsc{sp}:{c}",))
                    else:
                        ctx.cp(sc[sp][:, c * 512:(c + 1) * 512],
                               PS[pb_][:], reads=(f"ps{pb_}",),
                               writes=(f"Bsc{sp}:{c}",), eng="scalar")
                ctx.red(st2[:, 4 + sp:5 + sp], sc[sp][:, :nch * 512], ALU.max,
                        reads=tuple(f"Bsc{sp}:{c}" for c in range(nch)),
                        writes=(f"BstM{sp}",), negate=True)
                ctx.act(Pn[sp][:, :nch * 512], sc[sp][:, :nch * 512], AF.Exp,
                        reads=(f"BstM{sp}",) + tuple(
                            f"Bsc{sp}:{c}" for c in range(nch)),
                        writes=(f"BPn{sp}", f"BstM{sp}"),
                        bias=st2[:, 4 + sp:5 + sp], accum=st2[:, 6 + sp:7 + sp])

            def b2_sm2pt(gp, h2, qt):
                sp = qt % 2
                ctx.recip(st2[:, 6 + sp:7 + sp], st2[:, 6 + sp:7 + sp],
                          reads=(f"BstM{sp}",), writes=(f"BstM{sp}",))
                ctx.ts(Pn[sp][:, :nch * 512], Pn[sp][:, :nch * 512],
                       st2[:, 6 + sp:7 + sp], ALU.mult,
                       reads=(f"BPn{sp}", f"BstM{sp}"),
                       writes=(f"BPn{sp}",), eng="vector")
                for kg in range((nkc + 7) // 8):
                    kcs = list(range(kg * 8, min(nkc, kg * 8 + 8)))
                    tb = ctx.tbank()
                    for kc in kcs:
                        ctx.tr(PST[tb][:, (kc % 8) * 128:(kc % 8 + 1) * 128],
                               Pn[sp][:, kc * 128:(kc + 1) * 128],
                               reads=(f"BPn{sp}",), wkey=f"psT{tb}")
                    nn = len(kcs)
                    ctx.cp(bass.AP(PT, kcs[0] * 512 + qt * 128,
                                   [[16 * 512, 128], [512, nn], [1, 128]]),
                           bass.AP(PST[tb], 0,
                                   [[1024, 128], [128, nn], [1, 128]]),
                           reads=(f"psT{tb}",),
                           writes=tuple(f"BPT:{kc}:{qt}" for kc in kcs),
                           eng=("scalar" if qt % 2 else "vector"))

            def b2_pv(gp, h2):
                h = gp * 2 + h2
                psel = PS[0][0:64, :] if h2 == 0 else PS[0][64:128, :]
                for kc in range(nkc):
                    ctx.mm(psel,
                           VTM[:, kc * 1024 + h * 64:
                               kc * 1024 + (h + 1) * 64],
                           PT[:, kc * 512:(kc + 1) * 512],
                           reads=(f"BVTM:{kc}:{h // 8}",) + tuple(
                               f"BPT:{kc}:{q_}" for q_ in range(4)),
                           wkey="ps0", start=(kc == 0),
                           stop=(kc == nkc - 1), skip=True)
                if h2 == 1:
                    ctx.cp(oall[:, gp * 512:(gp + 1) * 512], PS[0][:],
                           reads=("ps0",), writes=(f"Boall:{gp}",))

            units = [(gp, h2, qt) for gp in range(8)
                     for h2 in range(2) for qt in range(4)]
            for v in range(len(units) + 2):
                u = v - 2
                if u >= 0:
                    b2_sm2pt(*units[u])
                    if units[u][2] == 3:
                        b2_pv(units[u][0], units[u][1])
                if v < len(units):
                    b2_sm1(*units[v])
            for qt in range(4):
                t = qb * 4 + qt
                g = b * NT + t
                pa = t % 2
                kx2 = f"Bxt2{pa}"
                p.dma("sync", xt2[pa][:], dram_row_ap(xr, g),
                      reads=(f"xr:{g}",), writes=(kx2,))
                for n in range(2):
                    for c in range(8):
                        ctx.mm(PS[2 + n][:],
                               oall[:, c * 512 + qt * 128:c * 512 + (qt + 1) * 128],
                               wsl[1][:, c * 1024 + n * 512:
                                    c * 1024 + (n + 1) * 512],
                               reads=(f"Boall:{c}", "Bw1"), wkey=f"ps{2 + n}",
                               start=(c == 0), stop=(c == 7))
                for n in range(2):
                    ctx.tt(xt2[pa][:, n * 512:(n + 1) * 512], PS[2 + n][:],
                           xt2[pa][:, n * 512:(n + 1) * 512], ALU.add,
                           reads=(f"ps{2 + n}", kx2), writes=(kx2,))
                wb_store(ctx, xt2[pa][:], kx2, g, xr, yr, scr, scr2, stats,
                         last, out_d)


def brope(ctx, src, ksrc, t2r, outb, rtab, t, kout):
    oe = bass.AP(t2r, 0, [[1024, 128], [64, 16], [2, 32]])
    pe = bass.AP(src, 1, [[1024, 128], [64, 16], [2, 32]])
    be = bass.AP(rtab, 1024 + t * 64, [[2048, 128], [0, 16], [2, 32]])
    ctx.tt(oe, pe, be, ALU.mult, reads=(ksrc, "Brtab"), writes=("Brt2",),
           eng="vector")
    oo = bass.AP(t2r, 1, [[1024, 128], [64, 16], [2, 32]])
    po = bass.AP(src, 0, [[1024, 128], [64, 16], [2, 32]])
    bo = bass.AP(rtab, 1024 + t * 64 + 1, [[2048, 128], [0, 16], [2, 32]])
    ctx.tt(oo, po, bo, ALU.mult, reads=(ksrc, "Brtab"), writes=("Brt2",),
           eng="vector")
    At = bass.AP(rtab, t * 64, [[2048, 128], [0, 16], [1, 64]])
    s3 = bass.AP(src, 0, [[1024, 128], [64, 16], [1, 64]])
    ctx.tt(s3, s3, At, ALU.mult, reads=(ksrc, "Brtab"), writes=(ksrc,),
           eng="vector")
    ctx.tt(outb[:], src[:], t2r[:], ALU.add, reads=(ksrc, "Brt2"),
           writes=(kout,), eng="vector")


def kernel(**inputs):
    consts = host_consts()
    x = np.ascontiguousarray(inputs["L0_attn_a_x"], dtype=np.float32)
    base = {
        "aw_normw": inputs["L0_attn_a_norm_w"],
        "aw_q": inputs["L0_attn_a_Wq"], "aw_k": inputs["L0_attn_a_Wk"],
        "aw_v": inputs["L0_attn_a_Wv"], "aw_o": inputs["L0_attn_a_Wo"],
        "aw_gate": np.reshape(inputs["L0_attn_a_gate"], [1]),
        "aw_lt": np.reshape(inputs["L0_attn_a_log_temperature"], [1]),
        "bw_q": inputs["L0_attn_b_Wq"], "bw_k": inputs["L0_attn_b_Wk"],
        "bw_v": inputs["L0_attn_b_Wv"], "bw_o": inputs["L0_attn_b_Wo"],
        "fw_qkv": inputs["L0_ffn_wqkv"], "fw_qs": inputs["L0_ffn_q_scale"],
        "fw_ks": inputs["L0_ffn_k_scale"], "fw_wo": inputs["L0_ffn_wo"],
        "fw_lng": inputs["L0_ffn_ln_g"], "fw_lnb": inputs["L0_ffn_ln_b"],
        "fw_w1": inputs["L0_ffn_w1"], "fw_b1": inputs["L0_ffn_b1"],
        "fw_w2": inputs["L0_ffn_w2"], "fw_b2": inputs["L0_ffn_b2"],
        "c_ident": consts["ident"], "c_cmask": consts["cmask"],
        "c_fmask": consts["fmask"], "c_ropeA": consts["ropeA"],
        "c_ropeB": consts["ropeB"], "c_xqA": consts["xqA"],
        "c_xqB": consts["xqB"], "c_xkA": consts["xkA"],
        "c_xkB": consts["xkB"], "c_zrow": consts["zrow"],
        "c_eps": consts["eps"],
    }
    base = {k: np.ascontiguousarray(v) for k, v in base.items()}
    nc = build(PASSES)
    in_maps = []
    for c in range(8):
        m = dict(base)
        m["x"] = np.ascontiguousarray(x[2 * c:2 * c + 2].reshape(B_LOC * S, D))
        in_maps.append(m)
    res = run_bass_kernel_spmd(nc, in_maps, core_ids=list(range(8)))
    if PASSES == "A1":
        kernel.dbg = [{k: np.asarray(v) for k, v in res.results[c].items()}
                      for c in range(8)]
        return np.zeros((16, S, D), np.float32)
    if PASSES in ("F1", "F1m"):
        kernel.dbg = [
            {k: np.asarray(res.results[c][f"dbg_{k}"])
             for k in ("xr", "yr", "y1r")} for c in range(8)]
        if PASSES == "F1":
            for c in range(8):
                for k in list(res.results[c].keys()):
                    if k.startswith("dbgC_"):
                        kernel.dbg[c][k] = np.asarray(res.results[c][k])
        return np.zeros((16, S, D), np.float32)
    outs = [np.asarray(res.results[c]["out"]).reshape(B_LOC, S, D)
            for c in range(8)]
    full = np.concatenate(outs, axis=0).astype(np.float32)
    if PASSES != "full":
        kernel.dbg = [
            {k: np.asarray(res.results[c][f"dbg_{k}"]) for k in ("xr", "yr")}
            for c in range(8)]
    return full
